# version 01
# baseline (speedup 1.0000x reference)
import numpy as np
import ml_dtypes
from contextlib import ExitStack

import concourse.bass as bass
import concourse.mybir as mybir
from concourse.bass_utils import run_bass_kernel_spmd

F32 = mybir.dt.float32
BF16 = mybir.dt.bfloat16
AF = mybir.ActivationFunctionType
ALU = mybir.AluOpType
NPBF = ml_dtypes.bfloat16

D = 1024
T = 4096
TH = 2048
NH = 16
HD = 64
EPS = 1e-5


class Prog:
    ENG = ("sync", "tensor", "vector", "scalar", "gpsimd")

    def __init__(self, name):
        self.nc = bass.Bass("TRN2", target_bir_lowering=False)
        self.name = name
        self.ctx = ExitStack()
        self.q = {e: [] for e in self.ENG}
        self.cnt = {e: 0 for e in self.ENG}
        self.last = None
        self._n = 0

    def din(self, name, shape, dt=F32):
        return self.nc.dram_tensor(name, list(shape), dt, kind="ExternalInput").ap()

    def dout(self, name, shape, dt=F32):
        return self.nc.dram_tensor(name, list(shape), dt, kind="ExternalOutput").ap()

    def sb(self, name, shape, dt=F32):
        return self.ctx.enter_context(self.nc.sbuf_tensor(name, list(shape), dt))

    def ps(self, name, shape, dt=F32):
        return self.ctx.enter_context(self.nc.psum_tensor(name, list(shape), dt))

    def _emit(self, eng, fn, inc, deps):
        my = list(deps) if deps is not None else ([self.last] if self.last else [])
        def closure(h, _fn=fn, _deps=my, _eng=eng, _inc=inc):
            for d in _deps:
                if d is not None and not (d[0] == _eng and d[1] <= self._done[_eng]):
                    h.wait_ge(self._sems[d[0]], d[1])
            r = _fn(h)
            if _inc:
                r.then_inc(self._sems[_eng], _inc)
        self.q[eng].append(closure)
        if inc:
            self.cnt[eng] += inc
        tok = (eng, self.cnt[eng])
        self.last = tok
        return tok

    def op(self, eng, fn, deps=None):
        return self._emit(eng, fn, 1, deps)

    def dma(self, eng, fn, deps=None):
        return self._emit(eng, fn, 16, deps)

    def build(self):
        sems = {}
        for e in self.ENG:
            sems[e] = self.ctx.enter_context(self.nc.semaphore("sem_" + e))
        self._sems = sems
        self._done = {e: 0 for e in self.ENG}
        block = self.ctx.enter_context(self.nc.Block())

        def runq(e):
            def body(h):
                for f in self.q[e]:
                    f(h)
            return body

        for e in self.ENG:
            deco = getattr(block, e)
            deco(runq(e))
        self.ctx.close()
        return self.nc


def _np_f32(x):
    return np.ascontiguousarray(x, np.float32)


def _np_bf(x):
    return np.ascontiguousarray(np.asarray(x).astype(NPBF))


def build_ln(name, ncols, gamma=False):
    p = Prog(name)
    xin = p.din("xin", [D, ncols])
    xout = p.dout("xout", [D, ncols])
    if gamma:
        gcol = p.din("gcol", [128, 8])
        bcol = p.din("bcol", [128, 8])

    CT = 512
    nct = ncols // CT
    xs = [p.sb(f"x{i}", [128, CT]) for i in range(8)]
    sq = p.sb("sq", [128, CT])
    mrow = p.sb("mrow", [1, CT])
    vrow = p.sb("vrow", [1, CT])
    rrow = p.sb("rrow", [1, CT])
    t1 = p.sb("t1", [128, CT])
    ones_col = p.sb("ones_col", [128, 1])
    eps_col = p.sb("eps_col", [128, 1])
    ones_row1 = p.sb("ones_row1", [1, 128])
    outbuf = p.sb("outbuf", [128, CT])
    sgc = p.sb("sgc", [128, 8]) if gamma else None
    sbc = p.sb("sbc", [128, 8]) if gamma else None
    pm = p.ps("pm", [128, CT])
    pv = p.ps("pv", [128, CT])
    pb1 = p.ps("pb1", [128, CT])
    pb2 = p.ps("pb2", [128, CT])

    p.op("gpsimd", lambda h: h.memset(ones_col[:], 1.0 / 1024.0))
    p.op("gpsimd", lambda h: h.memset(ones_row1[:], 1.0))
    p.op("gpsimd", lambda h: h.memset(eps_col[:], EPS))
    if gamma:
        p.dma("sync", lambda h: h.dma_start(out=sgc[:], in_=gcol))
        p.dma("sync", lambda h: h.dma_start(out=sbc[:], in_=bcol))

    for tt in range(nct):
        c0 = tt * CT
        for c in range(8):
            p.dma("sync", lambda h, c=c, c0=c0: h.dma_start(
                out=xs[c][:], in_=xin[128 * c:128 * c + 128, c0:c0 + CT]))
        for c in range(8):
            p.op("tensor", lambda h, c=c: h.matmul(
                pm[0:1, :], ones_col[:], xs[c][:], start=(c == 0), stop=(c == 7)))
        for c in range(8):
            p.op("scalar", lambda h, c=c: h.activation(sq[:], xs[c][:], AF.Square))
            p.op("tensor", lambda h, c=c: h.matmul(
                pv[0:1, :], ones_col[:], sq[:], start=(c == 0), stop=(c == 7)))
        p.op("vector", lambda h: h.tensor_copy(mrow[:], pm[0:1, :]))
        p.op("vector", lambda h: h.scalar_tensor_tensor(
            vrow[:], mrow[:], -1.0, mrow[:], op0=ALU.mult, op1=ALU.mult))
        p.op("vector", lambda h: h.tensor_add(vrow[:], vrow[:], pv[0:1, :]))
        p.op("scalar", lambda h: h.activation(rrow[:], vrow[:], AF.Ln, bias=eps_col[0:1, :], scale=1.0))
        p.op("scalar", lambda h: h.activation(rrow[:], rrow[:], AF.Exp, scale=-0.5))
        p.op("tensor", lambda h: h.matmul(pb1[:], ones_row1[:], mrow[:], start=True, stop=True))
        p.op("tensor", lambda h: h.matmul(pb2[:], ones_row1[:], rrow[:], start=True, stop=True))
        for c in range(8):
            p.op("vector", lambda h, c=c: h.tensor_sub(t1[:], xs[c][:], pb1[:]))
            p.op("vector", lambda h, c=c: h.tensor_mul(outbuf[:], t1[:], pb2[:]))
            if gamma:
                p.op("vector", lambda h, c=c: h.tensor_scalar(
                    outbuf[:], outbuf[:], sgc[:, c:c + 1], sbc[:, c:c + 1],
                    op0=ALU.mult, op1=ALU.add))
            p.dma("gpsimd", lambda h, c=c, c0=c0: h.dma_start(
                out=xout[128 * c:128 * c + 128, c0:c0 + CT], in_=outbuf[:]))
    last = p.last
    p.op("gpsimd", lambda h: h.engine_nop(), deps=[last])
    return p


def build_attn(name, outer, debug=False):
    p = Prog(name)
    xnT = p.din("xnT", [D, T])
    if outer:
        wq = p.din("wq", [D, D]); wk = p.din("wk", [D, D])
        wv = p.din("wv", [D, D]); wo = p.din("wo", [D, D])
        brows = p.din("brows", [1, 4 * D])
        augK = p.din("augK", [4, T], BF16)
        augQ = p.din("augQ", [4, T], BF16)
    else:
        wqkv = p.din("wqkv", [D, 3 * D])
        wo = p.din("wo", [D, D])
        gcol = p.din("gcol", [128, 8])
        bcol = p.din("bcol", [128, 8])
    masks = p.din("masks", [4, 128, 512])
    sout = p.dout("sout", [D, T])

    xbf = p.sb("xbf", [128, 8 * T], BF16)
    wstage = p.sb("wstage", [128, TH])
    wbf = p.sb("wbf", [128, 8 * 128], BF16)
    qg = p.sb("qg", [128, 2 * T], BF16)
    kg = p.sb("kg", [128, 2 * T], BF16)
    vsb = p.sb("vsb", [128, 32 * 260], BF16)
    psb = p.sb("psb", [128, 512], BF16)
    osb = p.sb("osb", [128, 512])
    rd = p.sb("rd", [1, 512])
    onesr = p.sb("onesr", [1, 512], BF16)
    onesc = p.sb("onesc", [1, 64])
    brsb = p.sb("brsb", [1, 4 * D], BF16) if outer else None
    brst = p.sb("brst", [1, 4 * D]) if outer else None
    maskb = p.sb("maskb", [128, 4 * 512])
    ontd = p.nc.dram_tensor("ontd", [D, T], BF16,
                            kind="Internal" if not debug else "ExternalOutput").ap()
    ontb = p.sb("ontb", [128, 512], BF16)
    onrm = p.sb("onrm", [64, 512], BF16)
    aKb = p.sb("aKb", [4, T], BF16) if outer else None
    aQb = p.sb("aQb", [4, T], BF16) if outer else None
    stg = p.sb("stg", [128, 512])
    vrhs = p.sb("vrhs", [128, 8 * 256], BF16)
    x32 = p.sb("x32", [128, 512])
    sq = p.sb("sq", [128, 512])
    mrow = p.sb("mrow", [1, 512]); rrow = p.sb("rrow", [1, 512])
    vrow = p.sb("vrow", [1, 512])
    onescol = p.sb("onescol", [128, 1])
    eps_col = p.sb("eps_col", [128, 1])
    ones_row1 = p.sb("ones_row1", [1, 128])
    sgc = p.sb("sgc", [128, 8]) if not outer else None
    sbc = p.sb("sbc", [128, 8]) if not outer else None

    pA = p.ps("pA", [128, 512])
    pB = p.ps("pB", [128, 512])
    pC = p.ps("pC", [128, 512])
    pD = p.ps("pD", [128, 512])

    p.op("gpsimd", lambda h: h.memset(onesr[:], 1.0))
    p.op("gpsimd", lambda h: h.memset(onesc[:], 1.0))
    p.op("gpsimd", lambda h: h.memset(onescol[:], 1.0 / 1024.0))
    p.op("gpsimd", lambda h: h.memset(eps_col[:], EPS))
    p.op("gpsimd", lambda h: h.memset(ones_row1[:], 1.0))
    p.op("gpsimd", lambda h: h.memset(
        vsb[:].rearrange("p (t h c) -> p t h c", t=32, h=4)[:, :, :, 64:65], 1.0))
    for d in range(4):
        p.dma("sync", lambda h, d=d: h.dma_start(out=maskb[:, 512 * d:512 * d + 512], in_=masks[d]))
    if outer:
        p.dma("sync", lambda h: h.dma_start(out=brst[:], in_=brows))
        p.op("vector", lambda h: h.tensor_copy(brsb[:], brst[:]))
        p.dma("sync", lambda h: h.dma_start(out=aKb[:], in_=augK))
        p.dma("sync", lambda h: h.dma_start(out=aQb[:], in_=augQ))
    else:
        p.dma("sync", lambda h: h.dma_start(out=sgc[:], in_=gcol))
        p.dma("sync", lambda h: h.dma_start(out=sbc[:], in_=bcol))

    if not outer:
        for tt in range(8):
            c0 = tt * 512
            for c in range(8):
                p.dma("sync", lambda h, c=c, c0=c0: h.dma_start(
                    out=x32[:], in_=xnT[128 * c:128 * c + 128, c0:c0 + 512]))
                p.op("tensor", lambda h, c=c: h.matmul(
                    pC[0:1, :], onescol[:], x32[:], start=(c == 0), stop=(c == 7)))
                p.op("scalar", lambda h: h.activation(sq[:], x32[:], AF.Square))
                p.op("tensor", lambda h, c=c: h.matmul(
                    pD[0:1, :], onescol[:], sq[:], start=(c == 0), stop=(c == 7)))
            p.op("vector", lambda h: h.tensor_copy(mrow[:], pC[0:1, :]))
            p.op("vector", lambda h: h.scalar_tensor_tensor(
                vrow[:], mrow[:], -1.0, mrow[:], op0=ALU.mult, op1=ALU.mult))
            p.op("vector", lambda h: h.tensor_add(vrow[:], vrow[:], pD[0:1, :]))
            p.op("scalar", lambda h: h.activation(vrow[:], vrow[:], AF.Ln, bias=eps_col[0:1, :], scale=1.0))
            p.op("scalar", lambda h: h.activation(rrow[:], vrow[:], AF.Exp, scale=-0.5))
            p.op("tensor", lambda h: h.matmul(pC[:], ones_row1[:], mrow[:], start=True, stop=True))
            p.op("tensor", lambda h: h.matmul(pD[:], ones_row1[:], rrow[:], start=True, stop=True))
            for c in range(8):
                p.dma("sync", lambda h, c=c, c0=c0: h.dma_start(
                    out=x32[:], in_=xnT[128 * c:128 * c + 128, c0:c0 + 512]))
                p.op("vector", lambda h: h.tensor_sub(stg[:], x32[:], pC[:]))
                p.op("vector", lambda h: h.tensor_mul(stg[:], stg[:], pD[:]))
                p.op("vector", lambda h, c=c, c0=c0: h.tensor_scalar(
                    xbf[:, c * T + c0:c * T + c0 + 512], stg[:],
                    sgc[:, c:c + 1], sbc[:, c:c + 1], op0=ALU.mult, op1=ALU.add))
    else:
        for c in range(8):
            for hh2 in range(2):
                p.dma("sync", lambda h, c=c, hh2=hh2: h.dma_start(
                    out=wstage[:], in_=xnT[128 * c:128 * c + 128, TH * hh2:TH * hh2 + TH]))
                p.op("vector", lambda h, c=c, hh2=hh2: h.tensor_copy(
                    xbf[:, c * T + TH * hh2:c * T + TH * hh2 + TH], wstage[:]))

    def load_lhsT(src_ap, k, mcol0, tag):
        p.dma("sync", lambda h: h.dma_start(
            out=wstage[:, 0:128], in_=src_ap[128 * k:128 * k + 128, mcol0:mcol0 + 128]))
        p.op("vector", lambda h: h.tensor_copy(wbf[:, 128 * k:128 * k + 128], wstage[:, 0:128]))

    NKT = T // 128
    for grp in range(4):
        h0 = grp * 4
        r0 = h0 * HD
        for which, dst, scale in (("q", qg, 0.125), ("k", kg, 1.0)):
            if outer:
                wsrc = wq if which == "q" else wk
                coff = r0
            else:
                wsrc = wqkv
                coff = r0 if which == "q" else D + r0
            for m in range(2):
                for k in range(8):
                    load_lhsT(wsrc, k, coff + 128 * m, which)
                for n in range(8):
                    for k in range(8):
                        p.op("tensor", lambda h, k=k, n=n: h.matmul(
                            pA[:], wbf[:, 128 * k:128 * k + 128],
                            xbf[:, k * T + 512 * n:k * T + 512 * n + 512],
                            start=(k == 0), stop=(k == 7 and not outer)))
                    if outer:
                        bi = 0 if which == "q" else 1
                        p.op("tensor", lambda h, m=m, bi=bi, coff=coff: h.matmul(
                            pA[:], brsb[0:1, bi * D + coff + 128 * m:bi * D + coff + 128 * m + 128],
                            onesr[:], start=False, stop=True))
                    p.op("scalar", lambda h, dst=dst, m=m, n=n, scale=scale: h.activation(
                        dst[:, m * T + 512 * n:m * T + 512 * n + 512], pA[:],
                        AF.Copy, scale=scale))
        if outer:
            vw = wv; vcoff = r0
        else:
            vw = wqkv; vcoff = 2 * D + r0
        for k in range(8):
            p.dma("sync", lambda h, k=k, vcoff=vcoff: h.dma_start(
                out=wstage[:, 0:256], in_=vw[128 * k:128 * k + 128, vcoff:vcoff + 256]))
            p.op("vector", lambda h, k=k: h.tensor_copy(
                vrhs[:, 256 * k:256 * k + 256], wstage[:, 0:256]))
        for kt in range(NKT):
            for k in range(8):
                p.op("tensor", lambda h, k=k, kt=kt: h.matmul(
                    pA[:, 0:256], xbf[:, k * T + 128 * kt:k * T + 128 * kt + 128],
                    vrhs[:, 256 * k:256 * k + 256], start=(k == 0), stop=(k == 7 and not outer)))
            if outer:
                p.op("tensor", lambda h, vcoff=vcoff: h.matmul(
                    pA[:, 0:256], onesr[0:1, 0:128], brsb[0:1, 2 * D + vcoff:2 * D + vcoff + 256],
                    start=False, stop=True))
            for hh in range(4):
                p.op("vector", lambda h, kt=kt, hh=hh: h.tensor_copy(
                    vsb[:, 260 * kt + 65 * hh:260 * kt + 65 * hh + 64],
                    pA[:, 64 * hh:64 * hh + 64]))
        if debug and grp == 0:
            qdump = p.dout("qdump", [128, 2 * T], BF16)
            kdump = p.dout("kdump", [128, 2 * T], BF16)
            vdump = p.dout("vdump", [128, 32 * 260], BF16)
            p.dma("gpsimd", lambda h: h.dma_start(out=qdump, in_=qg[:]))
            p.dma("gpsimd", lambda h: h.dma_start(out=kdump, in_=kg[:]))
            p.dma("gpsimd", lambda h: h.dma_start(out=vdump, in_=vsb[:]))
        for qt in range(8):
            q0 = qt * 512
            nkb = min(32, 4 * qt + 4)
            for hh in range(4):
                hloc_t = hh // 2
                hbase = 64 * (hh % 2)
                for kb in range(nkb):
                    p.op("tensor", lambda h, hloc_t=hloc_t, hbase=hbase, kb=kb, q0=q0: h.matmul(
                        pA[:], kg[hbase:hbase + 64, hloc_t * T + 128 * kb:hloc_t * T + 128 * kb + 128],
                        qg[hbase:hbase + 64, hloc_t * T + q0:hloc_t * T + q0 + 512],
                        start=True, stop=not outer))
                    if outer:
                        p.op("tensor", lambda h, kb=kb, q0=q0: h.matmul(
                            pA[:], aKb[:, 128 * kb:128 * kb + 128], aQb[:, q0:q0 + 512],
                            start=False, stop=True))
                    d = 128 * kb - q0
                    if 0 <= d < 512:
                        mi = d // 128
                        p.op("vector", lambda h, mi=mi: h.tensor_add(
                            pA[:], pA[:], maskb[:, 512 * mi:512 * mi + 512]))
                    p.op("scalar", lambda h: h.activation(psb[:], pA[:], AF.Exp))
                    if debug and grp == 0 and qt == 0 and hh == 0 and kb == 0:
                        scd = p.dout("scd", [128, 512])
                        psd = p.dout("psd", [128, 512], BF16)
                        p.op("vector", lambda h: h.tensor_copy(stg[:], pA[:]))
                        p.dma("gpsimd", lambda h: h.dma_start(out=scd, in_=stg[:]))
                        p.dma("gpsimd", lambda h: h.dma_start(out=psd, in_=psb[:]))
                    p.op("tensor", lambda h, kb=kb, hh=hh: h.matmul(
                        pB[0:65, :], vsb[:, 260 * kb + 65 * hh:260 * kb + 65 * hh + 65],
                        psb[:], start=(kb == 0), stop=(kb == nkb - 1)))
                p.op("vector", lambda h: h.tensor_copy(osb[0:65, :], pB[0:65, :]))
                if debug and grp == 0 and qt == 0 and hh == 0:
                    osd = p.dout("osd", [128, 512])
                    p.dma("gpsimd", lambda h: h.dma_start(out=osd, in_=osb[:]))
                p.op("vector", lambda h: h.reciprocal(rd[:], osb[64:65, :]))
                p.op("tensor", lambda h: h.matmul(
                    pC[0:64, :], onesc[:], rd[:], start=True, stop=True))
                p.op("vector", lambda h: h.tensor_mul(onrm[:], osb[0:64, :], pC[0:64, :]))
                p.dma("gpsimd", lambda h, h0=h0, hh=hh, q0=q0: h.dma_start(
                    out=ontd[64 * (h0 + hh):64 * (h0 + hh) + 64, q0:q0 + 512], in_=onrm[:]))

    for m in range(8):
        for k in range(8):
            load_lhsT(wo, k, 128 * m, "wo")
        for n in range(8):
            for k in range(8):
                p.dma("sync", lambda h, k=k, n=n: h.dma_start(
                    out=ontb[:], in_=ontd[128 * k:128 * k + 128, 512 * n:512 * n + 512]))
                p.op("tensor", lambda h, k=k, n=n: h.matmul(
                    pD[:], wbf[:, 128 * k:128 * k + 128], ontb[:],
                    start=(k == 0), stop=(k == 7 and not outer)))
            if outer:
                p.op("tensor", lambda h, m=m: h.matmul(
                    pD[:], brsb[0:1, 3 * D + 128 * m:3 * D + 128 * m + 128], onesr[:],
                    start=False, stop=True))
                p.op("vector", lambda h: h.tensor_copy(stg[:], pD[:]))
            else:
                p.dma("sync", lambda h, m=m, n=n: h.dma_start(
                    out=x32[:], in_=xnT[128 * m:128 * m + 128, 512 * n:512 * n + 512]))
                p.op("vector", lambda h: h.tensor_add(stg[:], pD[:], x32[:]))
            p.dma("gpsimd", lambda h, m=m, n=n: h.dma_start(
                out=sout[128 * m:128 * m + 128, 512 * n:512 * n + 512], in_=stg[:]))
    last = p.last
    p.op("gpsimd", lambda h: h.engine_nop(), deps=[last])
    return p


def run_prog(p, in_maps, n=None):
    nc = p.build()
    ids = list(range(len(in_maps)))
    res = run_bass_kernel_spmd(nc, in_maps, core_ids=ids)
    return res.results


def build_moe(name, first):
    HID = 4096 if first else 2816
    NMH = HID // 128
    p = Prog(name)
    if first:
        xr = p.din("xr", [D, TH])
        sT = p.din("sT", [D, TH])
        xT = p.din("xT", [D, TH])
        gcol = p.din("gcol", [128, 8]); bcol = p.din("bcol", [128, 8])
        b1c = p.din("b1c", [128, 8 * (HID // 128)])
        b2m = p.din("b2m", [8, D])
        x1o = p.dout("x1o", [D, TH])
        xn1o = p.dout("xn1o", [D, TH])
    else:
        xn1 = p.din("xn1", [D, TH])
        x1 = p.din("x1", [D, TH])
        s1 = p.din("s1", [D, TH])
        x2o = p.dout("x2o", [D, TH])
    wg = p.din("wg", [D, 8])
    w1 = p.din("w1", [8, D, HID])
    w2 = p.din("w2", [8, HID, D])
    if not first:
        w3 = p.din("w3", [8, D, HID])
    selm = p.din("selm", [8, 8 * 128], BF16)
    ident = p.din("ident", [128, 128])

    mbf = p.sb("mbf", [128, 8 * TH], BF16)
    wst = p.sb("wst", [128, HID // 2])
    w1bf = p.sb("w1bf", [128, 8 * HID], BF16)
    w3bf = p.sb("w3bf", [128, 8 * HID], BF16) if not first else None
    w2bf = p.sb("w2bf", [128, D], BF16)
    hebf = p.sb("hebf", [128, NMH * 512], BF16)
    sabf = p.sb("sabf", [128, 512], BF16) if not first else None
    acc = p.sb("acc", [128, 8 * 512])
    tmp = p.sb("tmp", [128, 512])
    gb = p.sb("gb", [128, 512])
    stg = p.sb("stg", [128, 512])
    x32 = p.sb("x32", [128, 512])
    y32 = p.sb("y32", [128, 512])
    lgs = p.sb("lgs", [128, 16 * 8])
    gts = p.sb("gts", [128, 16 * 8])
    gt8b = p.sb("gt8b", [8, TH], BF16)
    selb = p.sb("selb", [8, 8 * 128], BF16)
    idb = p.sb("idb", [128, 128])
    wgb = p.sb("wgb", [128, 64], BF16)
    srt = p.sb("srt", [128, 8])
    c1 = p.sb("c1", [128, 1]); c2 = p.sb("c2", [128, 1]); c3 = p.sb("c3", [128, 1])
    sq = p.sb("sq", [128, 512])
    mrow = p.sb("mrow", [1, 512]); vrow = p.sb("vrow", [1, 512]); rrow = p.sb("rrow", [1, 512])
    onescol = p.sb("onescol", [128, 1])
    ones_row1 = p.sb("ones_row1", [1, 128])
    eps_col = p.sb("eps_col", [128, 1])
    b1s = p.sb("b1s", [128, 8 * (HID // 128)]) if first else None
    b2s = p.sb("b2s", [8, D]) if first else None
    b2sb = p.sb("b2sb", [8, D], BF16) if first else None
    sgc = p.sb("sgc", [128, 8]) if first else None
    sbc = p.sb("sbc", [128, 8]) if first else None


    pZ1 = p.ps("pZ1", [128, 2048])
    pZ2 = p.ps("pZ2", [128, 2048])

    p.op("gpsimd", lambda h: h.memset(onescol[:], 1.0 / 1024.0))
    p.op("gpsimd", lambda h: h.memset(ones_row1[:], 1.0))
    p.op("gpsimd", lambda h: h.memset(eps_col[:], EPS))
    p.dma("sync", lambda h: h.dma_start(out=selb[:], in_=selm))
    p.dma("sync", lambda h: h.dma_start(out=idb[:], in_=ident))
    if first:
        p.dma("sync", lambda h: h.dma_start(out=sgc[:], in_=gcol))
        p.dma("sync", lambda h: h.dma_start(out=sbc[:], in_=bcol))
        p.dma("sync", lambda h: h.dma_start(out=b1s[:], in_=b1c))
        p.dma("sync", lambda h: h.dma_start(out=b2s[:], in_=b2m))
        p.op("vector", lambda h: h.tensor_copy(b2sb[:], b2s[:]))
    for c in range(8):
        p.dma("sync", lambda h, c=c: h.dma_start(out=wst[:, 0:8], in_=wg[128 * c:128 * c + 128, :]))
        p.op("vector", lambda h, c=c: h.tensor_copy(wgb[:, 8 * c:8 * c + 8], wst[:, 0:8]))

    src_in = xr if first else xn1
    for tt in range(4):
        c0 = tt * 512
        if first:
            for c in range(8):
                p.dma("sync", lambda h, c=c, c0=c0: h.dma_start(
                    out=x32[:], in_=src_in[128 * c:128 * c + 128, c0:c0 + 512]))
                p.op("tensor", lambda h, c=c: h.matmul(
                    pZ1[0:1, 0:512], onescol[:], x32[:], start=(c == 0), stop=(c == 7)))
                p.op("scalar", lambda h: h.activation(sq[:], x32[:], AF.Square))
                p.op("tensor", lambda h, c=c: h.matmul(
                    pZ1[0:1, 512:1024], onescol[:], sq[:], start=(c == 0), stop=(c == 7)))
            p.op("vector", lambda h: h.tensor_copy(mrow[:], pZ1[0:1, 0:512]))
            p.op("vector", lambda h: h.scalar_tensor_tensor(
                vrow[:], mrow[:], -1.0, mrow[:], op0=ALU.mult, op1=ALU.mult))
            p.op("vector", lambda h: h.tensor_add(vrow[:], vrow[:], pZ1[0:1, 512:1024]))
            p.op("scalar", lambda h: h.activation(vrow[:], vrow[:], AF.Ln, bias=eps_col[0:1, :], scale=1.0))
            p.op("scalar", lambda h: h.activation(rrow[:], vrow[:], AF.Exp, scale=-0.5))
            p.op("tensor", lambda h: h.matmul(pZ1[:, 0:512], ones_row1[:], mrow[:], start=True, stop=True))
            p.op("tensor", lambda h: h.matmul(pZ1[:, 512:1024], ones_row1[:], rrow[:], start=True, stop=True))
            for c in range(8):
                p.dma("sync", lambda h, c=c, c0=c0: h.dma_start(
                    out=x32[:], in_=src_in[128 * c:128 * c + 128, c0:c0 + 512]))
                p.op("vector", lambda h: h.tensor_sub(stg[:], x32[:], pZ1[:, 0:512]))
                p.op("vector", lambda h: h.tensor_mul(stg[:], stg[:], pZ1[:, 512:1024]))
                p.op("vector", lambda h, c=c, c0=c0: h.tensor_scalar(
                    mbf[:, c * TH + c0:c * TH + c0 + 512], stg[:],
                    sgc[:, c:c + 1], sbc[:, c:c + 1], op0=ALU.mult, op1=ALU.add))
        else:
            for c in range(8):
                p.dma("sync", lambda h, c=c, c0=c0: h.dma_start(
                    out=x32[:], in_=src_in[128 * c:128 * c + 128, c0:c0 + 512]))
                p.op("vector", lambda h, c=c, c0=c0: h.tensor_copy(
                    mbf[:, c * TH + c0:c * TH + c0 + 512], x32[:]))

    for tt in range(4):
        c0 = tt * 512
        for c in range(8):
            p.op("tensor", lambda h, c=c, c0=c0: h.matmul(
                pZ1[0:8, 0:512], wgb[:, 8 * c:8 * c + 8],
                mbf[:, c * TH + c0:c * TH + c0 + 512], start=(c == 0), stop=(c == 7)))
        p.op("vector", lambda h: h.tensor_copy(stg[0:8, :], pZ1[0:8, 0:512]))
        for q in range(4):
            ti = tt * 4 + q
            p.op("tensor", lambda h, q=q: h.transpose(
                pZ2[0:128, 0:8], stg[0:8, 128 * q:128 * q + 128], idb[0:8, 0:8]))
            p.op("vector", lambda h, ti=ti: h.tensor_copy(lgs[:, 8 * ti:8 * ti + 8], pZ2[0:128, 0:8]))
            p.op("vector", lambda h, ti=ti: h.max(srt[:], lgs[:, 8 * ti:8 * ti + 8]))
            p.op("vector", lambda h: h.tensor_sub(c1[:], srt[:, 1:2], srt[:, 0:1]))
            p.op("scalar", lambda h: h.activation(c1[:], c1[:], AF.Exp))
            p.op("vector", lambda h: h.tensor_scalar(c2[:], c1[:], 1.0, None, op0=ALU.add))
            p.op("vector", lambda h: h.reciprocal(c2[:], c2[:]))
            p.op("vector", lambda h: h.tensor_scalar(c3[:], c2[:], -1.0, 1.0, op0=ALU.mult, op1=ALU.add))
            p.op("vector", lambda h, ti=ti: h.tensor_scalar(
                gts[:, 8 * ti:8 * ti + 8], lgs[:, 8 * ti:8 * ti + 8], srt[:, 0:1], c2[:],
                op0=ALU.is_equal, op1=ALU.mult))
            p.op("vector", lambda h, ti=ti: h.tensor_scalar(
                tmp[:, 0:8], lgs[:, 8 * ti:8 * ti + 8], srt[:, 1:2], c3[:],
                op0=ALU.is_equal, op1=ALU.mult))
            p.op("vector", lambda h, ti=ti: h.tensor_add(
                gts[:, 8 * ti:8 * ti + 8], gts[:, 8 * ti:8 * ti + 8], tmp[:, 0:8]))
            p.op("tensor", lambda h, ti=ti: h.transpose(
                pZ2[0:8, 0:128], gts[:, 8 * ti:8 * ti + 8], idb[:]))
            p.op("vector", lambda h, ti=ti: h.tensor_copy(
                gt8b[:, 128 * ti:128 * ti + 128], pZ2[0:8, 0:128]))

    for tq in range(4):
        c0 = tq * 512
        for mo in range(8):
            p.op("gpsimd", lambda h, mo=mo: h.memset(acc[:, 512 * mo:512 * mo + 512], 0.0))
        for e in range(8):
            HH = HID // 2
            for k in range(8):
                for z in range(2):
                    p.dma("sync", lambda h, e=e, k=k, z=z: h.dma_start(
                        out=wst[:], in_=w1[e, 128 * k:128 * k + 128, z * HH:z * HH + HH]))
                    p.op("vector", lambda h, k=k, z=z, HH=HH: h.tensor_copy(
                        w1bf[:, HID * k + z * HH:HID * k + z * HH + HH], wst[:]))
                    if not first:
                        p.dma("sync", lambda h, e=e, k=k, z=z: h.dma_start(
                            out=wst[:], in_=w3[e, 128 * k:128 * k + 128, z * HH:z * HH + HH]))
                        p.op("vector", lambda h, k=k, z=z, HH=HH: h.tensor_copy(
                            w3bf[:, HID * k + z * HH:HID * k + z * HH + HH], wst[:]))
            for mh in range(NMH):
                for k in range(8):
                    p.op("tensor", lambda h, k=k, mh=mh, c0=c0: h.matmul(
                        pZ1[:, 0:512], w1bf[:, HID * k + 128 * mh:HID * k + 128 * mh + 128],
                        mbf[:, k * TH + c0:k * TH + c0 + 512], start=(k == 0), stop=(k == 7)))
                if first:
                    p.op("scalar", lambda h, mh=mh, e=e: h.activation(
                        hebf[:, 512 * mh:512 * mh + 512], pZ1[:, 0:512],
                        AF.Gelu_apprx_tanh, bias=b1s[:, NMH * e + mh:NMH * e + mh + 1], scale=1.0))
                else:
                    for k in range(8):
                        p.op("tensor", lambda h, k=k, mh=mh, c0=c0: h.matmul(
                            pZ2[:, 0:512], w3bf[:, HID * k + 128 * mh:HID * k + 128 * mh + 128],
                            mbf[:, k * TH + c0:k * TH + c0 + 512], start=(k == 0), stop=(k == 7)))
                    p.op("scalar", lambda h: h.activation(sabf[:], pZ1[:, 0:512], AF.Silu))
                    p.op("vector", lambda h, mh=mh: h.tensor_tensor(
                        hebf[:, 512 * mh:512 * mh + 512], sabf[:], pZ2[:, 0:512], op=ALU.mult))
            p.op("tensor", lambda h, e=e, c0=c0: h.matmul(
                pZ1[:, 0:512], selb[:, 128 * e:128 * e + 128], gt8b[:, c0:c0 + 512],
                start=True, stop=True))
            p.op("vector", lambda h: h.tensor_copy(gb[:], pZ1[:, 0:512]))
            for k in range(NMH):
                p.dma("sync", lambda h, e=e, k=k: h.dma_start(
                    out=wst[:, 0:D], in_=w2[e, 128 * k:128 * k + 128, :]))
                p.op("vector", lambda h, k=k: h.tensor_copy(w2bf[:], wst[:, 0:D]))
                for mo in range(8):
                    pz = pZ1 if mo < 4 else pZ2
                    p.op("tensor", lambda h, k=k, mo=mo, pz=pz: h.matmul(
                        pz[:, 512 * (mo % 4):512 * (mo % 4) + 512],
                        w2bf[:, 128 * mo:128 * mo + 128],
                        hebf[:, 512 * k:512 * k + 512], start=(k == 0), stop=(k == NMH - 1)))
            for mo in range(8):
                pz = pZ1 if mo < 4 else pZ2
                p.op("vector", lambda h, mo=mo, pz=pz: h.tensor_mul(
                    tmp[:], pz[:, 512 * (mo % 4):512 * (mo % 4) + 512], gb[:]))
                p.op("vector", lambda h, mo=mo: h.tensor_add(
                    acc[:, 512 * mo:512 * mo + 512], acc[:, 512 * mo:512 * mo + 512], tmp[:]))
        if first:
            for mo in range(8):
                p.op("tensor", lambda h, mo=mo, c0=c0: h.matmul(
                    pZ1[:, 0:512], b2sb[:, 128 * mo:128 * mo + 128], gt8b[:, c0:c0 + 512],
                    start=True, stop=True))
                p.op("vector", lambda h, mo=mo: h.tensor_add(
                    acc[:, 512 * mo:512 * mo + 512], acc[:, 512 * mo:512 * mo + 512], pZ1[:, 0:512]))
            for mo in range(8):
                p.dma("sync", lambda h, mo=mo, c0=c0: h.dma_start(
                    out=x32[:], in_=xr[128 * mo:128 * mo + 128, c0:c0 + 512]))
                p.op("vector", lambda h, mo=mo: h.tensor_add(
                    y32[:], acc[:, 512 * mo:512 * mo + 512], x32[:]))
                p.op("scalar", lambda h: h.activation(stg[:], y32[:], AF.Silu))
                p.dma("sync", lambda h, mo=mo, c0=c0: h.dma_start(
                    out=x32[:], in_=sT[128 * mo:128 * mo + 128, c0:c0 + 512]))
                p.op("vector", lambda h: h.tensor_mul(stg[:], stg[:], x32[:]))
                p.dma("sync", lambda h, mo=mo, c0=c0: h.dma_start(
                    out=x32[:], in_=xT[128 * mo:128 * mo + 128, c0:c0 + 512]))
                p.op("vector", lambda h: h.tensor_add(y32[:], stg[:], x32[:]))
                p.dma("gpsimd", lambda h, mo=mo, c0=c0: h.dma_start(
                    out=x1o[128 * mo:128 * mo + 128, c0:c0 + 512], in_=y32[:]))
                p.op("tensor", lambda h, mo=mo: h.matmul(
                    pZ1[0:1, 0:512], onescol[:], y32[:], start=(mo == 0), stop=(mo == 7)))
                p.op("scalar", lambda h: h.activation(sq[:], y32[:], AF.Square))
                p.op("tensor", lambda h, mo=mo: h.matmul(
                    pZ1[0:1, 512:1024], onescol[:], sq[:], start=(mo == 0), stop=(mo == 7)))
            p.op("vector", lambda h: h.tensor_copy(mrow[:], pZ1[0:1, 0:512]))
            p.op("vector", lambda h: h.scalar_tensor_tensor(
                vrow[:], mrow[:], -1.0, mrow[:], op0=ALU.mult, op1=ALU.mult))
            p.op("vector", lambda h: h.tensor_add(vrow[:], vrow[:], pZ1[0:1, 512:1024]))
            p.op("scalar", lambda h: h.activation(vrow[:], vrow[:], AF.Ln, bias=eps_col[0:1, :], scale=1.0))
            p.op("scalar", lambda h: h.activation(rrow[:], vrow[:], AF.Exp, scale=-0.5))
            p.op("tensor", lambda h: h.matmul(pZ1[:, 0:512], ones_row1[:], mrow[:], start=True, stop=True))
            p.op("tensor", lambda h: h.matmul(pZ1[:, 512:1024], ones_row1[:], rrow[:], start=True, stop=True))
            for mo in range(8):
                p.dma("sync", lambda h, mo=mo, c0=c0: h.dma_start(
                    out=x32[:], in_=x1o[128 * mo:128 * mo + 128, c0:c0 + 512]))
                p.op("vector", lambda h: h.tensor_sub(stg[:], x32[:], pZ1[:, 0:512]))
                p.op("vector", lambda h: h.tensor_mul(stg[:], stg[:], pZ1[:, 512:1024]))
                p.dma("gpsimd", lambda h, mo=mo, c0=c0: h.dma_start(
                    out=xn1o[128 * mo:128 * mo + 128, c0:c0 + 512], in_=stg[:]))
        else:
            for mo in range(8):
                p.op("scalar", lambda h, mo=mo: h.activation(
                    stg[:], acc[:, 512 * mo:512 * mo + 512], AF.Silu))
                p.dma("sync", lambda h, mo=mo, c0=c0: h.dma_start(
                    out=x32[:], in_=s1[128 * mo:128 * mo + 128, c0:c0 + 512]))
                p.op("vector", lambda h: h.tensor_mul(stg[:], stg[:], x32[:]))
                p.dma("sync", lambda h, mo=mo, c0=c0: h.dma_start(
                    out=x32[:], in_=x1[128 * mo:128 * mo + 128, c0:c0 + 512]))
                p.op("vector", lambda h: h.tensor_add(stg[:], stg[:], x32[:]))
                p.dma("gpsimd", lambda h, mo=mo, c0=c0: h.dma_start(
                    out=x2o[128 * mo:128 * mo + 128, c0:c0 + 512], in_=stg[:]))
    last = p.last
    p.op("gpsimd", lambda h: h.engine_nop(), deps=[last])
    return p


def build_ssmk(name):
    p = Prog(name)
    ldt = p.din("ldt", [128, 1])
    lar = p.din("lar", [128, 32])
    aim = p.din("aim", [128, 32])
    cre = p.din("cre", [128, 32])
    cim = p.din("cim", [128, 32])
    Ko = p.dout("Ko", [128, 4096])

    PI = float(np.pi)
    dt = p.sb("dt", [128, 1])
    er = p.sb("er", [128, 32])
    ai = p.sb("ai", [128, 32])
    cr = p.sb("cr", [128, 32]); ci = p.sb("ci", [128, 32])
    dar = p.sb("dar", [128, 32])
    dai = p.sb("dai", [128, 32])
    t1 = p.sb("t1", [128, 32]); t2 = p.sb("t2", [128, 32]); t3 = p.sb("t3", [128, 32])
    sn = p.sb("sn", [128, 32]); cs = p.sb("cs", [128, 32])
    wr = p.sb("wr", [128, 32]); wi = p.sb("wi", [128, 32])
    wpr = p.sb("wpr", [128, 32]); wpi = p.sb("wpi", [128, 32])
    nwpi = p.sb("nwpi", [128, 32])
    cwr = p.sb("cwr", [128, 32]); cwi = p.sb("cwi", [128, 32]); ncwi = p.sb("ncwi", [128, 32])
    ur = p.sb("ur", [128, 2048]); ui = p.sb("ui", [128, 2048])
    vr = p.sb("vr", [128, 2048]); vi = p.sb("vi", [128, 2048]); nvi = p.sb("nvi", [128, 2048])
    pr = p.sb("pr", [128, 2048]); pi = p.sb("pi", [128, 2048])
    Ks = p.sb("Ks", [128, 4096])
    tl = p.sb("tl", [128, 64])

    for nm, dst, srcp in ((0, dt, ldt), (1, er, lar), (2, ai, aim), (3, cr, cre), (4, ci, cim)):
        p.dma("sync", lambda h, dst=dst, srcp=srcp: h.dma_start(out=dst[:], in_=srcp))
    p.op("scalar", lambda h: h.activation(dt[:], dt[:], AF.Exp))
    p.op("scalar", lambda h: h.activation(er[:], er[:], AF.Exp))
    p.op("vector", lambda h: h.tensor_scalar(dar[:], er[:], dt[:], -1.0, op0=ALU.mult, op1=ALU.mult))
    p.op("vector", lambda h: h.tensor_scalar(dai[:], ai[:], dt[:], None, op0=ALU.mult))

    ti32 = p.sb("ti32", [128, 32], mybir.dt.int32)

    def wrap_sin(dstt, add_half_pi):
        off = PI / 2 if add_half_pi else 0.0
        p.op("vector", lambda h, off=off: h.tensor_scalar(t3[:], dai[:], off, None, op0=ALU.add))
        p.op("vector", lambda h: h.tensor_scalar(t1[:], t3[:], 1.0 / (2 * PI), None, op0=ALU.mult))
        p.op("vector", lambda h: h.tensor_copy(ti32[:], t1[:]))
        p.op("vector", lambda h: h.tensor_copy(t1[:], ti32[:]))
        p.op("vector", lambda h: h.scalar_tensor_tensor(t1[:], t1[:], -2 * PI, t3[:], op0=ALU.mult, op1=ALU.add))
        p.op("vector", lambda h: h.tensor_scalar(t2[:], t1[:], PI, None, op0=ALU.is_gt))
        p.op("vector", lambda h: h.scalar_tensor_tensor(t1[:], t2[:], -2 * PI, t1[:], op0=ALU.mult, op1=ALU.add))
        p.op("vector", lambda h: h.tensor_scalar(t2[:], t1[:], -PI, None, op0=ALU.is_lt))
        p.op("vector", lambda h: h.scalar_tensor_tensor(t1[:], t2[:], 2 * PI, t1[:], op0=ALU.mult, op1=ALU.add))
        p.op("scalar", lambda h, dstt=dstt: h.activation(dstt[:], t1[:], AF.Sin))
    wrap_sin(sn, False)
    wrap_sin(cs, True)
    p.op("scalar", lambda h: h.activation(t1[:], dar[:], AF.Exp))
    p.op("vector", lambda h: h.tensor_mul(wr[:], t1[:], cs[:]))
    p.op("vector", lambda h: h.tensor_mul(wi[:], t1[:], sn[:]))
    p.op("vector", lambda h: h.tensor_scalar(t1[:], wr[:], -1.0, None, op0=ALU.add))
    p.op("vector", lambda h: h.tensor_mul(t2[:], cr[:], t1[:]))
    p.op("vector", lambda h: h.scalar_tensor_tensor(t2[:], ci[:], -1.0, t2[:], op0=ALU.mult, op1=ALU.bypass)
         if False else h.tensor_mul(t3[:], ci[:], wi[:]))
    p.op("vector", lambda h: h.tensor_sub(t2[:], t2[:], t3[:]))
    p.op("vector", lambda h: h.tensor_mul(t3[:], cr[:], wi[:]))
    p.op("vector", lambda h: h.tensor_mul(t1[:], ci[:], t1[:]))
    p.op("vector", lambda h: h.tensor_add(t3[:], t3[:], t1[:]))
    p.op("vector", lambda h: h.tensor_mul(t1[:], er[:], er[:]))
    p.op("vector", lambda h: h.scalar_tensor_tensor(t1[:], ai[:], 1.0, t1[:], op0=ALU.bypass, op1=ALU.bypass)
         if False else h.tensor_mul(cwr[:], ai[:], ai[:]))
    p.op("vector", lambda h: h.tensor_add(t1[:], t1[:], cwr[:]))
    p.op("vector", lambda h: h.reciprocal(t1[:], t1[:]))
    p.op("vector", lambda h: h.tensor_mul(cwr[:], t2[:], er[:]))
    p.op("vector", lambda h: h.tensor_mul(cwi[:], t3[:], ai[:]))
    p.op("vector", lambda h: h.tensor_sub(cwr[:], cwi[:], cwr[:]))
    p.op("vector", lambda h: h.tensor_mul(cwr[:], cwr[:], t1[:]))
    p.op("vector", lambda h: h.tensor_scalar(cwr[:], cwr[:], 2.0, None, op0=ALU.mult))
    p.op("vector", lambda h: h.tensor_mul(cwi[:], t3[:], er[:]))
    p.op("vector", lambda h: h.tensor_mul(t3[:], t2[:], ai[:]))
    p.op("vector", lambda h: h.tensor_add(cwi[:], cwi[:], t3[:]))
    p.op("vector", lambda h: h.tensor_mul(cwi[:], cwi[:], t1[:]))
    p.op("vector", lambda h: h.tensor_scalar(cwi[:], cwi[:], -2.0, None, op0=ALU.mult))
    p.op("vector", lambda h: h.tensor_scalar(ncwi[:], cwi[:], -1.0, None, op0=ALU.mult))

    p.op("vector", lambda h: h.tensor_copy(wpr[:], wr[:]))
    p.op("vector", lambda h: h.tensor_copy(wpi[:], wi[:]))
    for arr_r, arr_i in ((ur, ui), (vr, vi)):
        p.op("gpsimd", lambda h, arr_r=arr_r: h.memset(arr_r[:], 0.0))
        p.op("gpsimd", lambda h, arr_i=arr_i: h.memset(arr_i[:], 0.0))
        p.op("gpsimd", lambda h, arr_r=arr_r: h.memset(
            arr_r[:].rearrange("p (n l) -> p n l", n=32)[:, :, 0:1], 1.0))
        have = 1
        while have < 64:
            cnt = min(have, 64 - have)
            p.op("vector", lambda h: h.tensor_scalar(nwpi[:], wpi[:], -1.0, None, op0=ALU.mult))
            for n in range(32):
                b0 = 64 * n
                p.op("vector", lambda h, b0=b0, have=have, cnt=cnt, arr_r=arr_r, n=n: h.tensor_scalar(
                    tl[:, 0:cnt], arr_r[:, b0:b0 + cnt], wpr[:, n:n + 1], None, op0=ALU.mult))
                p.op("vector", lambda h, b0=b0, have=have, cnt=cnt, arr_i=arr_i, n=n: h.scalar_tensor_tensor(
                    tl[:, 0:cnt], arr_i[:, b0:b0 + cnt], nwpi[:, n:n + 1], tl[:, 0:cnt],
                    op0=ALU.mult, op1=ALU.add))
                p.op("vector", lambda h, b0=b0, have=have, cnt=cnt, arr_r=arr_r, arr_i=arr_i, n=n: h.tensor_scalar(
                    tl[:, 64 - cnt:64], arr_i[:, b0:b0 + cnt], wpr[:, n:n + 1], None, op0=ALU.mult))
                p.op("vector", lambda h, b0=b0, have=have, cnt=cnt, arr_r=arr_r, arr_i=arr_i, n=n: h.scalar_tensor_tensor(
                    arr_i[:, b0 + have:b0 + have + cnt], arr_r[:, b0:b0 + cnt], wpi[:, n:n + 1],
                    tl[:, 64 - cnt:64], op0=ALU.mult, op1=ALU.add))
                p.op("vector", lambda h, b0=b0, have=have, cnt=cnt, arr_r=arr_r, n=n: h.tensor_copy(
                    arr_r[:, b0 + have:b0 + have + cnt], tl[:, 0:cnt]))
            p.op("vector", lambda h: h.tensor_mul(t1[:], wpr[:], wpr[:]))
            p.op("vector", lambda h: h.tensor_mul(t2[:], wpi[:], wpi[:]))
            p.op("vector", lambda h: h.tensor_mul(t3[:], wpr[:], wpi[:]))
            p.op("vector", lambda h: h.tensor_sub(wpr[:], t1[:], t2[:]))
            p.op("vector", lambda h: h.tensor_scalar(wpi[:], t3[:], 2.0, None, op0=ALU.mult))
            have *= 2
    for n in range(32):
        b0 = 64 * n
        p.op("vector", lambda h, b0=b0, n=n: h.tensor_scalar(
            tl[:], ur[:, b0:b0 + 64], cwr[:, n:n + 1], None, op0=ALU.mult))
        p.op("vector", lambda h, b0=b0, n=n: h.scalar_tensor_tensor(
            pr[:, b0:b0 + 64], ui[:, b0:b0 + 64], ncwi[:, n:n + 1], tl[:],
            op0=ALU.mult, op1=ALU.add))
        p.op("vector", lambda h, b0=b0, n=n: h.tensor_scalar(
            tl[:], ur[:, b0:b0 + 64], cwi[:, n:n + 1], None, op0=ALU.mult))
        p.op("vector", lambda h, b0=b0, n=n: h.scalar_tensor_tensor(
            pi[:, b0:b0 + 64], ui[:, b0:b0 + 64], cwr[:, n:n + 1], tl[:],
            op0=ALU.mult, op1=ALU.add))
    p.op("vector", lambda h: h.tensor_scalar(nvi[:], vi[:], -1.0, None, op0=ALU.mult))
    p.op("gpsimd", lambda h: h.memset(Ks[:], 0.0))
    for n in range(32):
        b0 = 64 * n
        for l2 in range(64):
            p.op("vector", lambda h, b0=b0, l2=l2, n=n: h.scalar_tensor_tensor(
                Ks[:, 64 * l2:64 * l2 + 64], pr[:, b0:b0 + 64], vr[:, b0 + l2:b0 + l2 + 1],
                Ks[:, 64 * l2:64 * l2 + 64], op0=ALU.mult, op1=ALU.add))
            p.op("vector", lambda h, b0=b0, l2=l2, n=n: h.scalar_tensor_tensor(
                Ks[:, 64 * l2:64 * l2 + 64], pi[:, b0:b0 + 64], nvi[:, b0 + l2:b0 + l2 + 1],
                Ks[:, 64 * l2:64 * l2 + 64], op0=ALU.mult, op1=ALU.add))
    p.dma("gpsimd", lambda h: h.dma_start(out=Ko, in_=Ks[:]))
    last = p.last
    p.op("gpsimd", lambda h: h.engine_nop(), deps=[last])
    return p


NS = 640


def build_fft1(name):
    p = Prog(name)
    za = p.din("za", [64, NS * 64])
    wr = p.din("wr", [64, 128], BF16)
    wi = p.din("wi", [64, 128], BF16)
    gr = p.dout("gr", [128, NS * 64])
    gi = p.dout("gi", [128, NS * 64])
    zb = p.sb("zb", [64, 512], BF16)
    zs = p.sb("zs", [64, 512])
    wrs = p.sb("wrs", [64, 128], BF16)
    wis = p.sb("wis", [64, 128], BF16)
    ev = p.sb("ev", [128, 512])
    pA = p.ps("pA", [128, 512])
    pB = p.ps("pB", [128, 512])
    p.dma("sync", lambda h: h.dma_start(out=wrs[:], in_=wr))
    p.dma("sync", lambda h: h.dma_start(out=wis[:], in_=wi))
    NTL = NS * 64 // 512
    for t in range(NTL):
        c0 = t * 512
        p.dma("sync", lambda h, c0=c0: h.dma_start(out=zs[:], in_=za[:, c0:c0 + 512]))
        p.op("vector", lambda h: h.tensor_copy(zb[:], zs[:]))
        p.op("tensor", lambda h: h.matmul(pA[:], wrs[:], zb[:], start=True, stop=True))
        p.op("tensor", lambda h: h.matmul(pB[:], wis[:], zb[:], start=True, stop=True))
        p.op("vector", lambda h: h.tensor_copy(ev[:], pA[:]))
        p.dma("gpsimd", lambda h, c0=c0: h.dma_start(out=gr[:, c0:c0 + 512], in_=ev[:]))
        p.op("vector", lambda h: h.tensor_copy(ev[:], pB[:]))
        p.dma("gpsimd", lambda h, c0=c0: h.dma_start(out=gi[:, c0:c0 + 512], in_=ev[:]))
    last = p.last
    p.op("gpsimd", lambda h: h.engine_nop(), deps=[last])
    return p


def build_fft2(name):
    p = Prog(name)
    gt = p.din("gt", [64, 128 * 2 * NS])
    ma = p.din("ma", [64, 128 * 128], BF16)
    mb = p.din("mb", [64, 128 * 128], BF16)
    xo = p.dout("xo", [128, 128 * NS])
    gb = p.sb("gb", [64, 2 * NS], BF16)
    gs = p.sb("gs", [64, 2 * NS])
    mas = p.sb("mas", [64, 128], BF16)
    mbs = p.sb("mbs", [64, 128], BF16)
    ev = p.sb("ev", [128, 512])
    pA = p.ps("pA", [128, 512])
    for k2 in range(128):
        p.dma("sync", lambda h, k2=k2: h.dma_start(out=gs[:], in_=gt[:, 2 * NS * k2:2 * NS * k2 + 2 * NS]))
        p.op("vector", lambda h: h.tensor_copy(gb[:], gs[:]))
        p.dma("sync", lambda h, k2=k2: h.dma_start(out=mas[:], in_=ma[:, 128 * k2:128 * k2 + 128]))
        p.dma("sync", lambda h, k2=k2: h.dma_start(out=mbs[:], in_=mb[:, 128 * k2:128 * k2 + 128]))
        for nt in range((NS + 511) // 512):
            n0 = nt * 512
            nc_ = min(512, NS - n0)
            p.op("tensor", lambda h, n0=n0, nc_=nc_: h.matmul(
                pA[:, 0:nc_], mas[:], gb[:, n0:n0 + nc_], start=True, stop=False))
            p.op("tensor", lambda h, n0=n0, nc_=nc_: h.matmul(
                pA[:, 0:nc_], mbs[:], gb[:, NS + n0:NS + n0 + nc_], start=False, stop=True))
            p.op("vector", lambda h, nc_=nc_: h.tensor_copy(ev[:, 0:nc_], pA[:, 0:nc_]))
            p.dma("gpsimd", lambda h, k2=k2, n0=n0, nc_=nc_: h.dma_start(
                out=xo[:, NS * k2 + n0:NS * k2 + n0 + nc_], in_=ev[:, 0:nc_]))
    last = p.last
    p.op("gpsimd", lambda h: h.engine_nop(), deps=[last])
    return p


NSI = 512
NKC = 33


def build_inv(name):
    p = Prog(name)
    xxr = p.din("xxr", [NKC * 128, NSI])
    xxi = p.din("xxi", [NKC * 128, NSI])
    kfr = p.din("kfr", [NKC * 128, NSI])
    kfi = p.din("kfi", [NKC * 128, NSI])
    cc = p.din("cc", [NKC * 128, 4096], BF16)
    cs = p.din("cs", [NKC * 128, 4096], BF16)
    xns = p.din("xns", [4096, NSI])
    drow = p.din("drow", [1, NSI])
    yo = p.dout("yo", [4096, NSI])

    a1 = p.sb("a1", [128, NSI]); a2_ = p.sb("a2_", [128, NSI])
    b1 = p.sb("b1", [128, NSI]); b2_ = p.sb("b2_", [128, NSI])
    t1 = p.sb("t1", [128, NSI]); t2 = p.sb("t2", [128, NSI])
    pbr = p.sb("pbr", [128, NKC * NSI], BF16)
    pbi = p.sb("pbi", [128, NKC * NSI], BF16)
    ccs = p.sb("ccs", [128, 128], BF16)
    css = p.sb("css", [128, 128], BF16)
    dsb = p.sb("dsb", [1, NSI])
    dbc = p.sb("dbc", [128, NSI])
    ones_row1 = p.sb("ones_row1", [1, 128])
    ev = p.sb("ev", [128, NSI])
    pA = p.ps("pA", [128, NSI])
    pB = p.ps("pB", [128, NSI])

    p.op("gpsimd", lambda h: h.memset(ones_row1[:], 1.0))
    p.dma("sync", lambda h: h.dma_start(out=dsb[:], in_=drow))
    p.op("tensor", lambda h: h.matmul(pB[:], ones_row1[:], dsb[:], start=True, stop=True))
    p.op("vector", lambda h: h.tensor_copy(dbc[:], pB[:]))
    for ch in range(NKC):
        r0 = ch * 128
        p.dma("sync", lambda h, r0=r0: h.dma_start(out=a1[:], in_=xxr[r0:r0 + 128, :]))
        p.dma("sync", lambda h, r0=r0: h.dma_start(out=a2_[:], in_=xxi[r0:r0 + 128, :]))
        p.dma("sync", lambda h, r0=r0: h.dma_start(out=b1[:], in_=kfr[r0:r0 + 128, :]))
        p.dma("sync", lambda h, r0=r0: h.dma_start(out=b2_[:], in_=kfi[r0:r0 + 128, :]))
        p.op("vector", lambda h: h.tensor_mul(t1[:], a1[:], b1[:]))
        p.op("vector", lambda h: h.tensor_mul(t2[:], a2_[:], b2_[:]))
        p.op("vector", lambda h: h.tensor_sub(t1[:], t1[:], t2[:]))
        p.op("vector", lambda h, ch=ch: h.tensor_copy(pbr[:, NSI * ch:NSI * ch + NSI], t1[:]))
        p.op("vector", lambda h: h.tensor_mul(t1[:], a1[:], b2_[:]))
        p.op("vector", lambda h: h.tensor_mul(t2[:], a2_[:], b1[:]))
        p.op("vector", lambda h: h.tensor_add(t1[:], t1[:], t2[:]))
        p.op("vector", lambda h, ch=ch: h.tensor_copy(pbi[:, NSI * ch:NSI * ch + NSI], t1[:]))
    for mt in range(32):
        for ch in range(NKC):
            p.dma("sync", lambda h, mt=mt, ch=ch: h.dma_start(
                out=ccs[:], in_=cc[128 * ch:128 * ch + 128, 128 * mt:128 * mt + 128]))
            p.dma("sync", lambda h, mt=mt, ch=ch: h.dma_start(
                out=css[:], in_=cs[128 * ch:128 * ch + 128, 128 * mt:128 * mt + 128]))
            p.op("tensor", lambda h, ch=ch: h.matmul(
                pA[:], ccs[:], pbr[:, NSI * ch:NSI * ch + NSI], start=(ch == 0), stop=False))
            p.op("tensor", lambda h, ch=ch: h.matmul(
                pA[:], css[:], pbi[:, NSI * ch:NSI * ch + NSI],
                start=False, stop=(ch == NKC - 1)))
        p.dma("sync", lambda h, mt=mt: h.dma_start(out=t1[:], in_=xns[128 * mt:128 * mt + 128, :]))
        p.op("vector", lambda h: h.tensor_mul(t2[:], t1[:], dbc[:]))
        p.op("vector", lambda h: h.tensor_add(ev[:], pA[:], t2[:]))
        p.dma("gpsimd", lambda h, mt=mt: h.dma_start(out=yo[128 * mt:128 * mt + 128, :], in_=ev[:]))
    last = p.last
    p.op("gpsimd", lambda h: h.engine_nop(), deps=[last])
    return p


def _consts_attn():
    j = np.arange(T, dtype=np.float64)
    jhi, jlo = np.floor(j / 64), j % 64
    augK = np.stack([jhi, jlo, np.full(T, 64.0), np.ones(T)]).astype(np.float32)
    augQ = np.stack([np.full(T, 64.0), np.ones(T), -jhi, -jlo]).astype(np.float32)
    masks = np.stack([np.where(np.arange(128)[:, None] + d > np.arange(512)[None, :],
                               -30000.0, 0.0) for d in (0, 128, 256, 384)]).astype(np.float32)
    return _np_bf(augK), _np_bf(augQ), masks


def _consts_fft():
    a2 = np.arange(64); k2 = np.arange(128)
    W = np.exp(-2j * np.pi * np.outer(a2, k2) / 128.0)
    wr, wi = _np_bf(W.real), _np_bf(W.imag)
    a1 = np.arange(64); k1 = np.arange(64)
    tw = np.exp(-2j * np.pi * np.outer(a1, k2) / 8192.0)
    W64 = np.exp(-2j * np.pi * np.outer(a1, k1) / 64.0)
    M = np.einsum('ak,al->kal', tw, W64)
    MA = np.concatenate([M.real, M.imag], axis=2)
    MB = np.concatenate([-M.imag, M.real], axis=2)
    MA = _np_bf(MA.transpose(1, 0, 2).reshape(64, 128 * 128))
    MB = _np_bf(MB.transpose(1, 0, 2).reshape(64, 128 * 128))
    k = np.arange(4224); m = np.arange(4096)
    wk = np.where((k == 0) | (k == 4096), 1.0, 2.0) * (k <= 4096)
    th = 2.0 * np.pi * np.outer(k, m) / 8192.0
    cC = _np_bf(wk[:, None] * np.cos(th) / 8192.0)
    cS = _np_bf(-wk[:, None] * np.sin(th) / 8192.0)
    return wr, wi, MA, MB, cC, cS


_PROGS = {}

def _prog(key, builder, *a, **kw):
    if key not in _PROGS:
        p = builder(*a, **kw)
        _PROGS[key] = (p, p.build())
    return _PROGS[key]


def _run(key, in_maps):
    p, nc = _PROGS[key]
    res = run_bass_kernel_spmd(nc, in_maps, core_ids=list(range(len(in_maps))))
    return res.results


def ssm_launches(xn1_all, inp):
    wr, wi, MA, MB, cC, cS = _consts_fft()
    _prog("ssmk", build_ssmk, "ssmk")
    ldt = _np_f32(inp["L1_attn_log_dt"])[:, None]
    lar = _np_f32(inp["L1_attn_log_A_real"])
    aim = _np_f32(inp["L1_attn_A_imag"])
    Cp = _np_f32(inp["L1_attn_C"])
    ins = [dict(ldt=np.ascontiguousarray(ldt[128*c:128*c+128]),
                lar=np.ascontiguousarray(lar[128*c:128*c+128]),
                aim=np.ascontiguousarray(aim[128*c:128*c+128]),
                cre=np.ascontiguousarray(Cp[128*c:128*c+128, :, 0]),
                cim=np.ascontiguousarray(Cp[128*c:128*c+128, :, 1])) for c in range(8)]
    r = _run("ssmk", ins)
    Kfull = np.concatenate([np.asarray(r[c]["Ko"], np.float32) for c in range(8)], axis=0)
    sigs = np.concatenate([xn1_all.reshape(4 * 1024, 4096), Kfull], axis=0)
    _prog("fft1", build_fft1, "fft1")
    ins = []
    for c in range(8):
        sl = sigs[NS * c:NS * c + NS]
        za = sl.reshape(NS, 64, 64).transpose(1, 0, 2).reshape(64, NS * 64)
        ins.append(dict(za=np.ascontiguousarray(za), wr=wr, wi=wi))
    r = _run("fft1", ins)
    Gr = np.stack([np.asarray(r[c]["gr"], np.float32) for c in range(8)])
    Gi = np.stack([np.asarray(r[c]["gi"], np.float32) for c in range(8)])
    _prog("fft2", build_fft2, "fft2")
    ins = []
    for c in range(8):
        grc = Gr[c].reshape(128, NS, 64)
        gic = Gi[c].reshape(128, NS, 64)
        gt = np.stack([grc, gic], axis=1)
        gt = gt.transpose(3, 0, 1, 2).reshape(64, 128 * 2 * NS)
        ins.append(dict(gt=np.ascontiguousarray(gt), ma=MA, mb=MB))
    r = _run("fft2", ins)
    Xr = np.zeros((5120, 8192), np.float32)
    Xi = np.zeros((5120, 8192), np.float32)
    for c in range(8):
        xo = np.asarray(r[c]["xo"], np.float32).reshape(128, 128, NS)
        Xr[NS*c:NS*c+NS] = xo[0:64].transpose(2, 0, 1).reshape(NS, 8192)
        Xi[NS*c:NS*c+NS] = xo[64:128].transpose(2, 0, 1).reshape(NS, 8192)
    _prog("inv", build_inv, "inv")
    Dp = _np_f32(inp["L1_attn_D"])
    ins = []
    for c in range(8):
        ss = np.arange(NSI * c, NSI * c + NSI)
        hh = ss % 1024
        xxr = np.zeros((NKC * 128, NSI), np.float32); xxr[0:4097] = Xr[ss, 0:4097].T
        xxi = np.zeros((NKC * 128, NSI), np.float32); xxi[0:4097] = Xi[ss, 0:4097].T
        kfr = np.zeros((NKC * 128, NSI), np.float32); kfr[0:4097] = Xr[4096 + hh][:, 0:4097].T
        kfi = np.zeros((NKC * 128, NSI), np.float32); kfi[0:4097] = Xi[4096 + hh][:, 0:4097].T
        xns = sigs[ss].T.copy()
        drow = Dp[hh][None, :]
        ins.append(dict(xxr=xxr, xxi=xxi, kfr=kfr, kfi=kfi, cc=cC, cs=cS,
                        xns=np.ascontiguousarray(xns), drow=np.ascontiguousarray(drow)))
    r = _run("inv", ins)
    s1_all = np.zeros((4, 1024, 4096), np.float32)
    for c in range(8):
        y = np.asarray(r[c]["yo"], np.float32)
        ss = np.arange(NSI * c, NSI * c + NSI)
        s1_all.reshape(4096, 4096)[ss] = y.T
    return s1_all


def kernel(**inp):
    inp = {k: np.asarray(v) for k, v in inp.items()}
    x = _np_f32(inp["L0_attn_x"])
    B = x.shape[0]
    xT_all = np.ascontiguousarray(x.transpose(0, 2, 1))
    augK, augQ, masks = _consts_attn()
    ident = np.eye(128, dtype=np.float32)
    selm = np.zeros((8, 8, 128), np.float32)
    for e in range(8):
        selm[e, e, :] = 1.0
    selm = _np_bf(selm.reshape(8, 8 * 128))

    _prog("ln0", build_ln, "ln0", TH)
    ins = [dict(xin=np.ascontiguousarray(xT_all[c // 2][:, TH * (c % 2):TH * (c % 2) + TH]))
           for c in range(8)]
    r = _run("ln0", ins)
    xn_all = np.stack([np.concatenate([np.asarray(r[2 * b]["xout"], np.float32),
                                       np.asarray(r[2 * b + 1]["xout"], np.float32)], axis=1)
                       for b in range(B)])

    _prog("attn0", build_attn, "attn0", True)
    wq = np.ascontiguousarray(_np_f32(inp["L0_attn_Wq"]).T)
    wk = np.ascontiguousarray(_np_f32(inp["L0_attn_Wk"]).T)
    wv = np.ascontiguousarray(_np_f32(inp["L0_attn_Wv"]).T)
    wo = np.ascontiguousarray(_np_f32(inp["L0_attn_Wo"]).T)
    brows = np.ascontiguousarray(np.concatenate(
        [_np_f32(inp["L0_attn_bq"]), _np_f32(inp["L0_attn_bk"]),
         _np_f32(inp["L0_attn_bv"]), _np_f32(inp["L0_attn_bo"])])[None, :])
    ins = [dict(xnT=np.ascontiguousarray(xn_all[c // 2]), wq=wq, wk=wk, wv=wv, wo=wo,
                brows=brows, augK=augK, augQ=augQ, masks=masks) for c in range(8)]
    r = _run("attn0", ins)
    s_all = np.stack([np.asarray(r[2 * b]["sout"], np.float32) for b in range(B)])

    _prog("attn1", build_attn, "attn1", False)
    g1c = np.ascontiguousarray(_np_f32(inp["L0_ffn_ln1_g"]).reshape(8, 128).T)
    b1c_ = np.ascontiguousarray(_np_f32(inp["L0_ffn_ln1_b"]).reshape(8, 128).T)
    wqkv = np.ascontiguousarray(_np_f32(inp["L0_ffn_w_qkv"]))
    wo1 = np.ascontiguousarray(_np_f32(inp["L0_ffn_w_o"]))
    ins = [dict(xnT=np.ascontiguousarray(xn_all[c // 2]), wqkv=wqkv, wo=wo1,
                gcol=g1c, bcol=b1c_, masks=masks) for c in range(8)]
    r = _run("attn1", ins)
    xr_all = np.stack([np.asarray(r[2 * b]["sout"], np.float32) for b in range(B)])

    _prog("moe0", build_moe, "moe0", True)
    g2c = np.ascontiguousarray(_np_f32(inp["L0_ffn_ln2_g"]).reshape(8, 128).T)
    b2c = np.ascontiguousarray(_np_f32(inp["L0_ffn_ln2_b"]).reshape(8, 128).T)
    b1cols = np.ascontiguousarray(_np_f32(inp["L0_ffn_b1"]).reshape(8, 32, 128)
                                  .transpose(2, 0, 1).reshape(128, 256))
    ins = []
    for c in range(8):
        b, hh = c // 2, c % 2
        sl = slice(TH * hh, TH * hh + TH)
        ins.append(dict(xr=np.ascontiguousarray(xr_all[b][:, sl]),
                        sT=np.ascontiguousarray(s_all[b][:, sl]),
                        xT=np.ascontiguousarray(xT_all[b][:, sl]),
                        gcol=g2c, bcol=b2c, b1c=b1cols,
                        b2m=np.ascontiguousarray(_np_f32(inp["L0_ffn_b2"])),
                        wg=np.ascontiguousarray(_np_f32(inp["L0_ffn_w_gate"])),
                        w1=np.ascontiguousarray(_np_f32(inp["L0_ffn_w1"])),
                        w2=np.ascontiguousarray(_np_f32(inp["L0_ffn_w2"])),
                        selm=selm, ident=ident))
    r = _run("moe0", ins)
    x1_all = np.stack([np.concatenate([np.asarray(r[2 * b]["x1o"], np.float32),
                                       np.asarray(r[2 * b + 1]["x1o"], np.float32)], axis=1)
                       for b in range(B)])
    xn1_all = np.stack([np.concatenate([np.asarray(r[2 * b]["xn1o"], np.float32),
                                        np.asarray(r[2 * b + 1]["xn1o"], np.float32)], axis=1)
                        for b in range(B)])

    s1_all = ssm_launches(xn1_all, inp)

    _prog("moe1", build_moe, "moe1", False)
    HID = 2816
    pad1 = np.zeros((8, 1024, HID), np.float32); pad1[:, :, :2730] = _np_f32(inp["L1_ffn_w1"])
    pad3 = np.zeros((8, 1024, HID), np.float32); pad3[:, :, :2730] = _np_f32(inp["L1_ffn_w3"])
    pad2 = np.zeros((8, HID, 1024), np.float32); pad2[:, :2730, :] = _np_f32(inp["L1_ffn_w2"])
    ins = []
    for c in range(8):
        b, hh = c // 2, c % 2
        sl = slice(TH * hh, TH * hh + TH)
        ins.append(dict(xn1=np.ascontiguousarray(xn1_all[b][:, sl]),
                        x1=np.ascontiguousarray(x1_all[b][:, sl]),
                        s1=np.ascontiguousarray(s1_all[b][:, sl]),
                        wg=np.ascontiguousarray(_np_f32(inp["L1_ffn_gate_w"])),
                        w1=pad1, w2=pad2, w3=pad3, selm=selm, ident=ident))
    r = _run("moe1", ins)
    out = np.zeros_like(x)
    for b in range(B):
        x2 = np.concatenate([np.asarray(r[2 * b]["x2o"], np.float32),
                             np.asarray(r[2 * b + 1]["x2o"], np.float32)], axis=1)
        out[b] = x2.T
    return out


# version 02
# speedup vs baseline: 1.1434x; 1.1434x over previous
import numpy as np
import ml_dtypes
from contextlib import ExitStack

import concourse.bass as bass
import concourse.mybir as mybir
from concourse.bass_utils import run_bass_kernel_spmd

F32 = mybir.dt.float32
BF16 = mybir.dt.bfloat16
AF = mybir.ActivationFunctionType
ALU = mybir.AluOpType
NPBF = ml_dtypes.bfloat16

D = 1024
T = 4096
TH = 2048
NH = 16
HD = 64
EPS = 1e-5


class Prog:
    ENG = ("sync", "tensor", "vector", "scalar", "gpsimd")

    def __init__(self, name):
        self.nc = bass.Bass("TRN2", target_bir_lowering=False)
        self.name = name
        self.ctx = ExitStack()
        self.q = {e: [] for e in self.ENG}
        self.cnt = {e: 0 for e in self.ENG}
        self.last = None
        self._n = 0

    def din(self, name, shape, dt=F32):
        return self.nc.dram_tensor(name, list(shape), dt, kind="ExternalInput").ap()

    def dout(self, name, shape, dt=F32):
        return self.nc.dram_tensor(name, list(shape), dt, kind="ExternalOutput").ap()

    def sb(self, name, shape, dt=F32):
        return self.ctx.enter_context(self.nc.sbuf_tensor(name, list(shape), dt))

    def ps(self, name, shape, dt=F32):
        return self.ctx.enter_context(self.nc.psum_tensor(name, list(shape), dt))

    def _emit(self, eng, fn, inc, deps):
        my = list(deps) if deps is not None else ([self.last] if self.last else [])
        def closure(h, _fn=fn, _deps=my, _eng=eng, _inc=inc):
            for d in _deps:
                if d is not None and not (d[0] == _eng and d[1] <= self._done[_eng]):
                    h.wait_ge(self._sems[d[0]], d[1])
            r = _fn(h)
            if _inc:
                r.then_inc(self._sems[_eng], _inc)
        self.q[eng].append(closure)
        if inc:
            self.cnt[eng] += inc
        tok = (eng, self.cnt[eng])
        self.last = tok
        return tok

    def op(self, eng, fn, deps=None):
        return self._emit(eng, fn, 1, deps)

    def dma(self, eng, fn, deps=None):
        return self._emit(eng, fn, 16, deps)

    def build(self):
        sems = {}
        for e in self.ENG:
            sems[e] = self.ctx.enter_context(self.nc.semaphore("sem_" + e))
        self._sems = sems
        self._done = {e: 0 for e in self.ENG}
        block = self.ctx.enter_context(self.nc.Block())

        def runq(e):
            def body(h):
                for f in self.q[e]:
                    f(h)
            return body

        for e in self.ENG:
            deco = getattr(block, e)
            deco(runq(e))
        self.ctx.close()
        return self.nc


def _np_f32(x):
    return np.ascontiguousarray(x, np.float32)


def _np_bf(x):
    return np.ascontiguousarray(np.asarray(x).astype(NPBF))


def build_ln(name, ncols, gamma=False):
    p = Prog(name)
    xin = p.din("xin", [D, ncols])
    xout = p.dout("xout", [D, ncols])
    if gamma:
        gcol = p.din("gcol", [128, 8])
        bcol = p.din("bcol", [128, 8])

    CT = 512
    nct = ncols // CT
    xs = [p.sb(f"x{i}", [128, CT]) for i in range(8)]
    sq = p.sb("sq", [128, CT])
    mrow = p.sb("mrow", [1, CT])
    vrow = p.sb("vrow", [1, CT])
    rrow = p.sb("rrow", [1, CT])
    t1 = p.sb("t1", [128, CT])
    ones_col = p.sb("ones_col", [128, 1])
    eps_col = p.sb("eps_col", [128, 1])
    ones_row1 = p.sb("ones_row1", [1, 128])
    outbuf = p.sb("outbuf", [128, CT])
    sgc = p.sb("sgc", [128, 8]) if gamma else None
    sbc = p.sb("sbc", [128, 8]) if gamma else None
    pm = p.ps("pm", [128, CT])
    pv = p.ps("pv", [128, CT])
    pb1 = p.ps("pb1", [128, CT])
    pb2 = p.ps("pb2", [128, CT])

    p.op("gpsimd", lambda h: h.memset(ones_col[:], 1.0 / 1024.0))
    p.op("gpsimd", lambda h: h.memset(ones_row1[:], 1.0))
    p.op("gpsimd", lambda h: h.memset(eps_col[:], EPS))
    if gamma:
        p.dma("sync", lambda h: h.dma_start(out=sgc[:], in_=gcol))
        p.dma("sync", lambda h: h.dma_start(out=sbc[:], in_=bcol))

    for tt in range(nct):
        c0 = tt * CT
        for c in range(8):
            p.dma("sync", lambda h, c=c, c0=c0: h.dma_start(
                out=xs[c][:], in_=xin[128 * c:128 * c + 128, c0:c0 + CT]))
        for c in range(8):
            p.op("tensor", lambda h, c=c: h.matmul(
                pm[0:1, :], ones_col[:], xs[c][:], start=(c == 0), stop=(c == 7)))
        for c in range(8):
            p.op("scalar", lambda h, c=c: h.activation(sq[:], xs[c][:], AF.Square))
            p.op("tensor", lambda h, c=c: h.matmul(
                pv[0:1, :], ones_col[:], sq[:], start=(c == 0), stop=(c == 7)))
        p.op("vector", lambda h: h.tensor_copy(mrow[:], pm[0:1, :]))
        p.op("vector", lambda h: h.scalar_tensor_tensor(
            vrow[:], mrow[:], -1.0, mrow[:], op0=ALU.mult, op1=ALU.mult))
        p.op("vector", lambda h: h.tensor_add(vrow[:], vrow[:], pv[0:1, :]))
        p.op("scalar", lambda h: h.activation(rrow[:], vrow[:], AF.Ln, bias=eps_col[0:1, :], scale=1.0))
        p.op("scalar", lambda h: h.activation(rrow[:], rrow[:], AF.Exp, scale=-0.5))
        p.op("tensor", lambda h: h.matmul(pb1[:], ones_row1[:], mrow[:], start=True, stop=True))
        p.op("tensor", lambda h: h.matmul(pb2[:], ones_row1[:], rrow[:], start=True, stop=True))
        for c in range(8):
            p.op("vector", lambda h, c=c: h.tensor_sub(t1[:], xs[c][:], pb1[:]))
            p.op("vector", lambda h, c=c: h.tensor_mul(outbuf[:], t1[:], pb2[:]))
            if gamma:
                p.op("vector", lambda h, c=c: h.tensor_scalar(
                    outbuf[:], outbuf[:], sgc[:, c:c + 1], sbc[:, c:c + 1],
                    op0=ALU.mult, op1=ALU.add))
            p.dma("gpsimd", lambda h, c=c, c0=c0: h.dma_start(
                out=xout[128 * c:128 * c + 128, c0:c0 + CT], in_=outbuf[:]))
    last = p.last
    p.op("gpsimd", lambda h: h.engine_nop(), deps=[last])
    return p


def build_attn(name, outer, debug=False):
    p = Prog(name)
    xnT = p.din("xnT", [D, T])
    if outer:
        wq = p.din("wq", [D, D]); wk = p.din("wk", [D, D])
        wv = p.din("wv", [D, D]); wo = p.din("wo", [D, D])
        brows = p.din("brows", [1, 4 * D])
        augK = p.din("augK", [4, T], BF16)
        augQ = p.din("augQ", [4, T], BF16)
    else:
        wqkv = p.din("wqkv", [D, 3 * D])
        wo = p.din("wo", [D, D])
        gcol = p.din("gcol", [128, 8])
        bcol = p.din("bcol", [128, 8])
    masks = p.din("masks", [4, 128, 512])
    sout = p.dout("sout", [D, T])

    xbf = p.sb("xbf", [128, 8 * T], BF16)
    wstage = p.sb("wstage", [128, TH])
    wbf = p.sb("wbf", [128, 8 * 128], BF16)
    qg = p.sb("qg", [128, 2 * T], BF16)
    kg = p.sb("kg", [128, 2 * T], BF16)
    vsb = p.sb("vsb", [128, 32 * 260], BF16)
    psb = p.sb("psb", [128, 512], BF16)
    osb = p.sb("osb", [128, 512])
    rd = p.sb("rd", [1, 512])
    onesr = p.sb("onesr", [1, 512], BF16)
    onesc = p.sb("onesc", [1, 64])
    brsb = p.sb("brsb", [1, 4 * D], BF16) if outer else None
    brst = p.sb("brst", [1, 4 * D]) if outer else None
    maskb = p.sb("maskb", [128, 4 * 512])
    ontd = p.nc.dram_tensor("ontd", [D, T], BF16,
                            kind="Internal" if not debug else "ExternalOutput").ap()
    ontb = p.sb("ontb", [128, 512], BF16)
    onrm = p.sb("onrm", [64, 512], BF16)
    aKb = p.sb("aKb", [4, T], BF16) if outer else None
    aQb = p.sb("aQb", [4, T], BF16) if outer else None
    stg = p.sb("stg", [128, 512])
    vrhs = p.sb("vrhs", [128, 8 * 256], BF16)
    x32 = p.sb("x32", [128, 512])
    sq = p.sb("sq", [128, 512])
    mrow = p.sb("mrow", [1, 512]); rrow = p.sb("rrow", [1, 512])
    vrow = p.sb("vrow", [1, 512])
    onescol = p.sb("onescol", [128, 1])
    eps_col = p.sb("eps_col", [128, 1])
    ones_row1 = p.sb("ones_row1", [1, 128])
    sgc = p.sb("sgc", [128, 8]) if not outer else None
    sbc = p.sb("sbc", [128, 8]) if not outer else None

    pA = p.ps("pA", [128, 512])
    pB = p.ps("pB", [128, 512])
    pC = p.ps("pC", [128, 512])
    pD = p.ps("pD", [128, 512])

    p.op("gpsimd", lambda h: h.memset(onesr[:], 1.0))
    p.op("gpsimd", lambda h: h.memset(onesc[:], 1.0))
    p.op("gpsimd", lambda h: h.memset(onescol[:], 1.0 / 1024.0))
    p.op("gpsimd", lambda h: h.memset(eps_col[:], EPS))
    p.op("gpsimd", lambda h: h.memset(ones_row1[:], 1.0))
    p.op("gpsimd", lambda h: h.memset(
        vsb[:].rearrange("p (t h c) -> p t h c", t=32, h=4)[:, :, :, 64:65], 1.0))
    for d in range(4):
        p.dma("sync", lambda h, d=d: h.dma_start(out=maskb[:, 512 * d:512 * d + 512], in_=masks[d]))
    if outer:
        p.dma("sync", lambda h: h.dma_start(out=brst[:], in_=brows))
        p.op("vector", lambda h: h.tensor_copy(brsb[:], brst[:]))
        p.dma("sync", lambda h: h.dma_start(out=aKb[:], in_=augK))
        p.dma("sync", lambda h: h.dma_start(out=aQb[:], in_=augQ))
    else:
        p.dma("sync", lambda h: h.dma_start(out=sgc[:], in_=gcol))
        p.dma("sync", lambda h: h.dma_start(out=sbc[:], in_=bcol))

    if not outer:
        for tt in range(8):
            c0 = tt * 512
            for c in range(8):
                p.dma("sync", lambda h, c=c, c0=c0: h.dma_start(
                    out=x32[:], in_=xnT[128 * c:128 * c + 128, c0:c0 + 512]))
                p.op("tensor", lambda h, c=c: h.matmul(
                    pC[0:1, :], onescol[:], x32[:], start=(c == 0), stop=(c == 7)))
                p.op("scalar", lambda h: h.activation(sq[:], x32[:], AF.Square))
                p.op("tensor", lambda h, c=c: h.matmul(
                    pD[0:1, :], onescol[:], sq[:], start=(c == 0), stop=(c == 7)))
            p.op("vector", lambda h: h.tensor_copy(mrow[:], pC[0:1, :]))
            p.op("vector", lambda h: h.scalar_tensor_tensor(
                vrow[:], mrow[:], -1.0, mrow[:], op0=ALU.mult, op1=ALU.mult))
            p.op("vector", lambda h: h.tensor_add(vrow[:], vrow[:], pD[0:1, :]))
            p.op("scalar", lambda h: h.activation(vrow[:], vrow[:], AF.Ln, bias=eps_col[0:1, :], scale=1.0))
            p.op("scalar", lambda h: h.activation(rrow[:], vrow[:], AF.Exp, scale=-0.5))
            p.op("tensor", lambda h: h.matmul(pC[:], ones_row1[:], mrow[:], start=True, stop=True))
            p.op("tensor", lambda h: h.matmul(pD[:], ones_row1[:], rrow[:], start=True, stop=True))
            for c in range(8):
                p.dma("sync", lambda h, c=c, c0=c0: h.dma_start(
                    out=x32[:], in_=xnT[128 * c:128 * c + 128, c0:c0 + 512]))
                p.op("vector", lambda h: h.tensor_sub(stg[:], x32[:], pC[:]))
                p.op("vector", lambda h: h.tensor_mul(stg[:], stg[:], pD[:]))
                p.op("vector", lambda h, c=c, c0=c0: h.tensor_scalar(
                    xbf[:, c * T + c0:c * T + c0 + 512], stg[:],
                    sgc[:, c:c + 1], sbc[:, c:c + 1], op0=ALU.mult, op1=ALU.add))
    else:
        for c in range(8):
            for hh2 in range(2):
                p.dma("sync", lambda h, c=c, hh2=hh2: h.dma_start(
                    out=wstage[:], in_=xnT[128 * c:128 * c + 128, TH * hh2:TH * hh2 + TH]))
                p.op("vector", lambda h, c=c, hh2=hh2: h.tensor_copy(
                    xbf[:, c * T + TH * hh2:c * T + TH * hh2 + TH], wstage[:]))

    def load_lhsT(src_ap, k, mcol0, tag):
        p.dma("sync", lambda h: h.dma_start(
            out=wstage[:, 0:128], in_=src_ap[128 * k:128 * k + 128, mcol0:mcol0 + 128]))
        p.op("vector", lambda h: h.tensor_copy(wbf[:, 128 * k:128 * k + 128], wstage[:, 0:128]))

    NKT = T // 128
    for grp in range(4):
        h0 = grp * 4
        r0 = h0 * HD
        for which, dst, scale in (("q", qg, 0.125), ("k", kg, 1.0)):
            if outer:
                wsrc = wq if which == "q" else wk
                coff = r0
            else:
                wsrc = wqkv
                coff = r0 if which == "q" else D + r0
            for m in range(2):
                for k in range(8):
                    load_lhsT(wsrc, k, coff + 128 * m, which)
                for n in range(8):
                    for k in range(8):
                        p.op("tensor", lambda h, k=k, n=n: h.matmul(
                            pA[:], wbf[:, 128 * k:128 * k + 128],
                            xbf[:, k * T + 512 * n:k * T + 512 * n + 512],
                            start=(k == 0), stop=(k == 7 and not outer)))
                    if outer:
                        bi = 0 if which == "q" else 1
                        p.op("tensor", lambda h, m=m, bi=bi, coff=coff: h.matmul(
                            pA[:], brsb[0:1, bi * D + coff + 128 * m:bi * D + coff + 128 * m + 128],
                            onesr[:], start=False, stop=True))
                    p.op("scalar", lambda h, dst=dst, m=m, n=n, scale=scale: h.activation(
                        dst[:, m * T + 512 * n:m * T + 512 * n + 512], pA[:],
                        AF.Copy, scale=scale))
        if outer:
            vw = wv; vcoff = r0
        else:
            vw = wqkv; vcoff = 2 * D + r0
        for k in range(8):
            p.dma("sync", lambda h, k=k, vcoff=vcoff: h.dma_start(
                out=wstage[:, 0:256], in_=vw[128 * k:128 * k + 128, vcoff:vcoff + 256]))
            p.op("vector", lambda h, k=k: h.tensor_copy(
                vrhs[:, 256 * k:256 * k + 256], wstage[:, 0:256]))
        for kt in range(NKT):
            for k in range(8):
                p.op("tensor", lambda h, k=k, kt=kt: h.matmul(
                    pA[:, 0:256], xbf[:, k * T + 128 * kt:k * T + 128 * kt + 128],
                    vrhs[:, 256 * k:256 * k + 256], start=(k == 0), stop=(k == 7 and not outer)))
            if outer:
                p.op("tensor", lambda h, vcoff=vcoff: h.matmul(
                    pA[:, 0:256], onesr[0:1, 0:128], brsb[0:1, 2 * D + vcoff:2 * D + vcoff + 256],
                    start=False, stop=True))
            for hh in range(4):
                p.op("vector", lambda h, kt=kt, hh=hh: h.tensor_copy(
                    vsb[:, 260 * kt + 65 * hh:260 * kt + 65 * hh + 64],
                    pA[:, 64 * hh:64 * hh + 64]))
        if debug and grp == 0:
            qdump = p.dout("qdump", [128, 2 * T], BF16)
            kdump = p.dout("kdump", [128, 2 * T], BF16)
            vdump = p.dout("vdump", [128, 32 * 260], BF16)
            p.dma("gpsimd", lambda h: h.dma_start(out=qdump, in_=qg[:]))
            p.dma("gpsimd", lambda h: h.dma_start(out=kdump, in_=kg[:]))
            p.dma("gpsimd", lambda h: h.dma_start(out=vdump, in_=vsb[:]))
        for qt in range(8):
            q0 = qt * 512
            nkb = min(32, 4 * qt + 4)
            for hh in range(4):
                hloc_t = hh // 2
                hbase = 64 * (hh % 2)
                for kb in range(nkb):
                    p.op("tensor", lambda h, hloc_t=hloc_t, hbase=hbase, kb=kb, q0=q0: h.matmul(
                        pA[:], kg[hbase:hbase + 64, hloc_t * T + 128 * kb:hloc_t * T + 128 * kb + 128],
                        qg[hbase:hbase + 64, hloc_t * T + q0:hloc_t * T + q0 + 512],
                        start=True, stop=not outer))
                    if outer:
                        p.op("tensor", lambda h, kb=kb, q0=q0: h.matmul(
                            pA[:], aKb[:, 128 * kb:128 * kb + 128], aQb[:, q0:q0 + 512],
                            start=False, stop=True))
                    d = 128 * kb - q0
                    if 0 <= d < 512:
                        mi = d // 128
                        p.op("vector", lambda h, mi=mi: h.tensor_add(
                            pA[:], pA[:], maskb[:, 512 * mi:512 * mi + 512]))
                    p.op("scalar", lambda h: h.activation(psb[:], pA[:], AF.Exp))
                    if debug and grp == 0 and qt == 0 and hh == 0 and kb == 0:
                        scd = p.dout("scd", [128, 512])
                        psd = p.dout("psd", [128, 512], BF16)
                        p.op("vector", lambda h: h.tensor_copy(stg[:], pA[:]))
                        p.dma("gpsimd", lambda h: h.dma_start(out=scd, in_=stg[:]))
                        p.dma("gpsimd", lambda h: h.dma_start(out=psd, in_=psb[:]))
                    p.op("tensor", lambda h, kb=kb, hh=hh: h.matmul(
                        pB[0:65, :], vsb[:, 260 * kb + 65 * hh:260 * kb + 65 * hh + 65],
                        psb[:], start=(kb == 0), stop=(kb == nkb - 1)))
                p.op("vector", lambda h: h.tensor_copy(osb[0:65, :], pB[0:65, :]))
                if debug and grp == 0 and qt == 0 and hh == 0:
                    osd = p.dout("osd", [128, 512])
                    p.dma("gpsimd", lambda h: h.dma_start(out=osd, in_=osb[:]))
                p.op("vector", lambda h: h.reciprocal(rd[:], osb[64:65, :]))
                p.op("tensor", lambda h: h.matmul(
                    pC[0:64, :], onesc[:], rd[:], start=True, stop=True))
                p.op("vector", lambda h: h.tensor_mul(onrm[:], osb[0:64, :], pC[0:64, :]))
                p.dma("gpsimd", lambda h, h0=h0, hh=hh, q0=q0: h.dma_start(
                    out=ontd[64 * (h0 + hh):64 * (h0 + hh) + 64, q0:q0 + 512], in_=onrm[:]))

    for m in range(8):
        for k in range(8):
            load_lhsT(wo, k, 128 * m, "wo")
        for n in range(8):
            for k in range(8):
                p.dma("sync", lambda h, k=k, n=n: h.dma_start(
                    out=ontb[:], in_=ontd[128 * k:128 * k + 128, 512 * n:512 * n + 512]))
                p.op("tensor", lambda h, k=k, n=n: h.matmul(
                    pD[:], wbf[:, 128 * k:128 * k + 128], ontb[:],
                    start=(k == 0), stop=(k == 7 and not outer)))
            if outer:
                p.op("tensor", lambda h, m=m: h.matmul(
                    pD[:], brsb[0:1, 3 * D + 128 * m:3 * D + 128 * m + 128], onesr[:],
                    start=False, stop=True))
                p.op("vector", lambda h: h.tensor_copy(stg[:], pD[:]))
            else:
                p.dma("sync", lambda h, m=m, n=n: h.dma_start(
                    out=x32[:], in_=xnT[128 * m:128 * m + 128, 512 * n:512 * n + 512]))
                p.op("vector", lambda h: h.tensor_add(stg[:], pD[:], x32[:]))
            p.dma("gpsimd", lambda h, m=m, n=n: h.dma_start(
                out=sout[128 * m:128 * m + 128, 512 * n:512 * n + 512], in_=stg[:]))
    last = p.last
    p.op("gpsimd", lambda h: h.engine_nop(), deps=[last])
    return p


def run_prog(p, in_maps, n=None):
    nc = p.build()
    ids = list(range(len(in_maps)))
    res = run_bass_kernel_spmd(nc, in_maps, core_ids=ids)
    return res.results


def build_moe(name, first):
    HID = 4096 if first else 2816
    NMH = HID // 128
    p = Prog(name)
    if first:
        xr = p.din("xr", [D, TH])
        sT = p.din("sT", [D, TH])
        xT = p.din("xT", [D, TH])
        gcol = p.din("gcol", [128, 8]); bcol = p.din("bcol", [128, 8])
        b1c = p.din("b1c", [128, 8 * (HID // 128)])
        b2m = p.din("b2m", [8, D])
        x1o = p.dout("x1o", [D, TH])
        xn1o = p.dout("xn1o", [D, TH])
    else:
        xn1 = p.din("xn1", [D, TH])
        x1 = p.din("x1", [D, TH])
        s1 = p.din("s1", [D, TH])
        x2o = p.dout("x2o", [D, TH])
    wg = p.din("wg", [D, 8])
    w1 = p.din("w1", [8, D, HID])
    w2 = p.din("w2", [8, HID, D])
    if not first:
        w3 = p.din("w3", [8, D, HID])
    selm = p.din("selm", [8, 8 * 128], BF16)
    ident = p.din("ident", [128, 128])

    mbf = p.sb("mbf", [128, 8 * TH], BF16)
    wst = p.sb("wst", [128, HID // 2])
    w1bf = p.sb("w1bf", [128, 8 * HID], BF16)
    w3bf = p.sb("w3bf", [128, 8 * HID], BF16) if not first else None
    w2bf = p.sb("w2bf", [128, D], BF16)
    hebf = p.sb("hebf", [128, NMH * 512], BF16)
    sabf = p.sb("sabf", [128, 512], BF16) if not first else None
    acc = p.sb("acc", [128, 8 * 512])
    tmp = p.sb("tmp", [128, 512])
    gb = p.sb("gb", [128, 512])
    stg = p.sb("stg", [128, 512])
    x32 = p.sb("x32", [128, 512])
    y32 = p.sb("y32", [128, 512])
    lgs = p.sb("lgs", [128, 16 * 8])
    gts = p.sb("gts", [128, 16 * 8])
    gt8b = p.sb("gt8b", [8, TH], BF16)
    selb = p.sb("selb", [8, 8 * 128], BF16)
    idb = p.sb("idb", [128, 128])
    wgb = p.sb("wgb", [128, 64], BF16)
    srt = p.sb("srt", [128, 8])
    c1 = p.sb("c1", [128, 1]); c2 = p.sb("c2", [128, 1]); c3 = p.sb("c3", [128, 1])
    sq = p.sb("sq", [128, 512])
    mrow = p.sb("mrow", [1, 512]); vrow = p.sb("vrow", [1, 512]); rrow = p.sb("rrow", [1, 512])
    onescol = p.sb("onescol", [128, 1])
    ones_row1 = p.sb("ones_row1", [1, 128])
    eps_col = p.sb("eps_col", [128, 1])
    b1s = p.sb("b1s", [128, 8 * (HID // 128)]) if first else None
    b2s = p.sb("b2s", [8, D]) if first else None
    b2sb = p.sb("b2sb", [8, D], BF16) if first else None
    sgc = p.sb("sgc", [128, 8]) if first else None
    sbc = p.sb("sbc", [128, 8]) if first else None


    pZ1 = p.ps("pZ1", [128, 2048])
    pZ2 = p.ps("pZ2", [128, 2048])

    p.op("gpsimd", lambda h: h.memset(onescol[:], 1.0 / 1024.0))
    p.op("gpsimd", lambda h: h.memset(ones_row1[:], 1.0))
    p.op("gpsimd", lambda h: h.memset(eps_col[:], EPS))
    p.dma("sync", lambda h: h.dma_start(out=selb[:], in_=selm))
    p.dma("sync", lambda h: h.dma_start(out=idb[:], in_=ident))
    if first:
        p.dma("sync", lambda h: h.dma_start(out=sgc[:], in_=gcol))
        p.dma("sync", lambda h: h.dma_start(out=sbc[:], in_=bcol))
        p.dma("sync", lambda h: h.dma_start(out=b1s[:], in_=b1c))
        p.dma("sync", lambda h: h.dma_start(out=b2s[:], in_=b2m))
        p.op("vector", lambda h: h.tensor_copy(b2sb[:], b2s[:]))
    for c in range(8):
        p.dma("sync", lambda h, c=c: h.dma_start(out=wst[:, 0:8], in_=wg[128 * c:128 * c + 128, :]))
        p.op("vector", lambda h, c=c: h.tensor_copy(wgb[:, 8 * c:8 * c + 8], wst[:, 0:8]))

    src_in = xr if first else xn1
    for tt in range(4):
        c0 = tt * 512
        if first:
            for c in range(8):
                p.dma("sync", lambda h, c=c, c0=c0: h.dma_start(
                    out=x32[:], in_=src_in[128 * c:128 * c + 128, c0:c0 + 512]))
                p.op("tensor", lambda h, c=c: h.matmul(
                    pZ1[0:1, 0:512], onescol[:], x32[:], start=(c == 0), stop=(c == 7)))
                p.op("scalar", lambda h: h.activation(sq[:], x32[:], AF.Square))
                p.op("tensor", lambda h, c=c: h.matmul(
                    pZ1[0:1, 512:1024], onescol[:], sq[:], start=(c == 0), stop=(c == 7)))
            p.op("vector", lambda h: h.tensor_copy(mrow[:], pZ1[0:1, 0:512]))
            p.op("vector", lambda h: h.scalar_tensor_tensor(
                vrow[:], mrow[:], -1.0, mrow[:], op0=ALU.mult, op1=ALU.mult))
            p.op("vector", lambda h: h.tensor_add(vrow[:], vrow[:], pZ1[0:1, 512:1024]))
            p.op("scalar", lambda h: h.activation(vrow[:], vrow[:], AF.Ln, bias=eps_col[0:1, :], scale=1.0))
            p.op("scalar", lambda h: h.activation(rrow[:], vrow[:], AF.Exp, scale=-0.5))
            p.op("tensor", lambda h: h.matmul(pZ1[:, 0:512], ones_row1[:], mrow[:], start=True, stop=True))
            p.op("tensor", lambda h: h.matmul(pZ1[:, 512:1024], ones_row1[:], rrow[:], start=True, stop=True))
            for c in range(8):
                p.dma("sync", lambda h, c=c, c0=c0: h.dma_start(
                    out=x32[:], in_=src_in[128 * c:128 * c + 128, c0:c0 + 512]))
                p.op("vector", lambda h: h.tensor_sub(stg[:], x32[:], pZ1[:, 0:512]))
                p.op("vector", lambda h: h.tensor_mul(stg[:], stg[:], pZ1[:, 512:1024]))
                p.op("vector", lambda h, c=c, c0=c0: h.tensor_scalar(
                    mbf[:, c * TH + c0:c * TH + c0 + 512], stg[:],
                    sgc[:, c:c + 1], sbc[:, c:c + 1], op0=ALU.mult, op1=ALU.add))
        else:
            for c in range(8):
                p.dma("sync", lambda h, c=c, c0=c0: h.dma_start(
                    out=x32[:], in_=src_in[128 * c:128 * c + 128, c0:c0 + 512]))
                p.op("vector", lambda h, c=c, c0=c0: h.tensor_copy(
                    mbf[:, c * TH + c0:c * TH + c0 + 512], x32[:]))

    for tt in range(4):
        c0 = tt * 512
        for c in range(8):
            p.op("tensor", lambda h, c=c, c0=c0: h.matmul(
                pZ1[0:8, 0:512], wgb[:, 8 * c:8 * c + 8],
                mbf[:, c * TH + c0:c * TH + c0 + 512], start=(c == 0), stop=(c == 7)))
        p.op("vector", lambda h: h.tensor_copy(stg[0:8, :], pZ1[0:8, 0:512]))
        for q in range(4):
            ti = tt * 4 + q
            p.op("tensor", lambda h, q=q: h.transpose(
                pZ2[0:128, 0:8], stg[0:8, 128 * q:128 * q + 128], idb[0:8, 0:8]))
            p.op("vector", lambda h, ti=ti: h.tensor_copy(lgs[:, 8 * ti:8 * ti + 8], pZ2[0:128, 0:8]))
            p.op("vector", lambda h, ti=ti: h.max(srt[:], lgs[:, 8 * ti:8 * ti + 8]))
            p.op("vector", lambda h: h.tensor_sub(c1[:], srt[:, 1:2], srt[:, 0:1]))
            p.op("scalar", lambda h: h.activation(c1[:], c1[:], AF.Exp))
            p.op("vector", lambda h: h.tensor_scalar(c2[:], c1[:], 1.0, None, op0=ALU.add))
            p.op("vector", lambda h: h.reciprocal(c2[:], c2[:]))
            p.op("vector", lambda h: h.tensor_scalar(c3[:], c2[:], -1.0, 1.0, op0=ALU.mult, op1=ALU.add))
            p.op("vector", lambda h, ti=ti: h.tensor_scalar(
                gts[:, 8 * ti:8 * ti + 8], lgs[:, 8 * ti:8 * ti + 8], srt[:, 0:1], c2[:],
                op0=ALU.is_equal, op1=ALU.mult))
            p.op("vector", lambda h, ti=ti: h.tensor_scalar(
                tmp[:, 0:8], lgs[:, 8 * ti:8 * ti + 8], srt[:, 1:2], c3[:],
                op0=ALU.is_equal, op1=ALU.mult))
            p.op("vector", lambda h, ti=ti: h.tensor_add(
                gts[:, 8 * ti:8 * ti + 8], gts[:, 8 * ti:8 * ti + 8], tmp[:, 0:8]))
            p.op("tensor", lambda h, ti=ti: h.transpose(
                pZ2[0:8, 0:128], gts[:, 8 * ti:8 * ti + 8], idb[:]))
            p.op("vector", lambda h, ti=ti: h.tensor_copy(
                gt8b[:, 128 * ti:128 * ti + 128], pZ2[0:8, 0:128]))

    for tq in range(4):
        c0 = tq * 512
        for mo in range(8):
            p.op("gpsimd", lambda h, mo=mo: h.memset(acc[:, 512 * mo:512 * mo + 512], 0.0))
        for e in range(8):
            HH = HID // 2
            for k in range(8):
                for z in range(2):
                    p.dma("sync", lambda h, e=e, k=k, z=z: h.dma_start(
                        out=wst[:], in_=w1[e, 128 * k:128 * k + 128, z * HH:z * HH + HH]))
                    p.op("vector", lambda h, k=k, z=z, HH=HH: h.tensor_copy(
                        w1bf[:, HID * k + z * HH:HID * k + z * HH + HH], wst[:]))
                    if not first:
                        p.dma("sync", lambda h, e=e, k=k, z=z: h.dma_start(
                            out=wst[:], in_=w3[e, 128 * k:128 * k + 128, z * HH:z * HH + HH]))
                        p.op("vector", lambda h, k=k, z=z, HH=HH: h.tensor_copy(
                            w3bf[:, HID * k + z * HH:HID * k + z * HH + HH], wst[:]))
            for mh in range(NMH):
                for k in range(8):
                    p.op("tensor", lambda h, k=k, mh=mh, c0=c0: h.matmul(
                        pZ1[:, 0:512], w1bf[:, HID * k + 128 * mh:HID * k + 128 * mh + 128],
                        mbf[:, k * TH + c0:k * TH + c0 + 512], start=(k == 0), stop=(k == 7)))
                if first:
                    p.op("scalar", lambda h, mh=mh, e=e: h.activation(
                        hebf[:, 512 * mh:512 * mh + 512], pZ1[:, 0:512],
                        AF.Gelu_apprx_tanh, bias=b1s[:, NMH * e + mh:NMH * e + mh + 1], scale=1.0))
                else:
                    for k in range(8):
                        p.op("tensor", lambda h, k=k, mh=mh, c0=c0: h.matmul(
                            pZ2[:, 0:512], w3bf[:, HID * k + 128 * mh:HID * k + 128 * mh + 128],
                            mbf[:, k * TH + c0:k * TH + c0 + 512], start=(k == 0), stop=(k == 7)))
                    p.op("scalar", lambda h: h.activation(sabf[:], pZ1[:, 0:512], AF.Silu))
                    p.op("vector", lambda h, mh=mh: h.tensor_tensor(
                        hebf[:, 512 * mh:512 * mh + 512], sabf[:], pZ2[:, 0:512], op=ALU.mult))
            p.op("tensor", lambda h, e=e, c0=c0: h.matmul(
                pZ1[:, 0:512], selb[:, 128 * e:128 * e + 128], gt8b[:, c0:c0 + 512],
                start=True, stop=True))
            p.op("vector", lambda h: h.tensor_copy(gb[:], pZ1[:, 0:512]))
            for k in range(NMH):
                p.dma("sync", lambda h, e=e, k=k: h.dma_start(
                    out=wst[:, 0:D], in_=w2[e, 128 * k:128 * k + 128, :]))
                p.op("vector", lambda h, k=k: h.tensor_copy(w2bf[:], wst[:, 0:D]))
                for mo in range(8):
                    pz = pZ1 if mo < 4 else pZ2
                    p.op("tensor", lambda h, k=k, mo=mo, pz=pz: h.matmul(
                        pz[:, 512 * (mo % 4):512 * (mo % 4) + 512],
                        w2bf[:, 128 * mo:128 * mo + 128],
                        hebf[:, 512 * k:512 * k + 512], start=(k == 0), stop=(k == NMH - 1)))
            for mo in range(8):
                pz = pZ1 if mo < 4 else pZ2
                p.op("vector", lambda h, mo=mo, pz=pz: h.tensor_mul(
                    tmp[:], pz[:, 512 * (mo % 4):512 * (mo % 4) + 512], gb[:]))
                p.op("vector", lambda h, mo=mo: h.tensor_add(
                    acc[:, 512 * mo:512 * mo + 512], acc[:, 512 * mo:512 * mo + 512], tmp[:]))
        if first:
            for mo in range(8):
                p.op("tensor", lambda h, mo=mo, c0=c0: h.matmul(
                    pZ1[:, 0:512], b2sb[:, 128 * mo:128 * mo + 128], gt8b[:, c0:c0 + 512],
                    start=True, stop=True))
                p.op("vector", lambda h, mo=mo: h.tensor_add(
                    acc[:, 512 * mo:512 * mo + 512], acc[:, 512 * mo:512 * mo + 512], pZ1[:, 0:512]))
            for mo in range(8):
                p.dma("sync", lambda h, mo=mo, c0=c0: h.dma_start(
                    out=x32[:], in_=xr[128 * mo:128 * mo + 128, c0:c0 + 512]))
                p.op("vector", lambda h, mo=mo: h.tensor_add(
                    y32[:], acc[:, 512 * mo:512 * mo + 512], x32[:]))
                p.op("scalar", lambda h: h.activation(stg[:], y32[:], AF.Silu))
                p.dma("sync", lambda h, mo=mo, c0=c0: h.dma_start(
                    out=x32[:], in_=sT[128 * mo:128 * mo + 128, c0:c0 + 512]))
                p.op("vector", lambda h: h.tensor_mul(stg[:], stg[:], x32[:]))
                p.dma("sync", lambda h, mo=mo, c0=c0: h.dma_start(
                    out=x32[:], in_=xT[128 * mo:128 * mo + 128, c0:c0 + 512]))
                p.op("vector", lambda h: h.tensor_add(y32[:], stg[:], x32[:]))
                p.dma("gpsimd", lambda h, mo=mo, c0=c0: h.dma_start(
                    out=x1o[128 * mo:128 * mo + 128, c0:c0 + 512], in_=y32[:]))
                p.op("tensor", lambda h, mo=mo: h.matmul(
                    pZ1[0:1, 0:512], onescol[:], y32[:], start=(mo == 0), stop=(mo == 7)))
                p.op("scalar", lambda h: h.activation(sq[:], y32[:], AF.Square))
                p.op("tensor", lambda h, mo=mo: h.matmul(
                    pZ1[0:1, 512:1024], onescol[:], sq[:], start=(mo == 0), stop=(mo == 7)))
            p.op("vector", lambda h: h.tensor_copy(mrow[:], pZ1[0:1, 0:512]))
            p.op("vector", lambda h: h.scalar_tensor_tensor(
                vrow[:], mrow[:], -1.0, mrow[:], op0=ALU.mult, op1=ALU.mult))
            p.op("vector", lambda h: h.tensor_add(vrow[:], vrow[:], pZ1[0:1, 512:1024]))
            p.op("scalar", lambda h: h.activation(vrow[:], vrow[:], AF.Ln, bias=eps_col[0:1, :], scale=1.0))
            p.op("scalar", lambda h: h.activation(rrow[:], vrow[:], AF.Exp, scale=-0.5))
            p.op("tensor", lambda h: h.matmul(pZ1[:, 0:512], ones_row1[:], mrow[:], start=True, stop=True))
            p.op("tensor", lambda h: h.matmul(pZ1[:, 512:1024], ones_row1[:], rrow[:], start=True, stop=True))
            for mo in range(8):
                p.dma("sync", lambda h, mo=mo, c0=c0: h.dma_start(
                    out=x32[:], in_=x1o[128 * mo:128 * mo + 128, c0:c0 + 512]))
                p.op("vector", lambda h: h.tensor_sub(stg[:], x32[:], pZ1[:, 0:512]))
                p.op("vector", lambda h: h.tensor_mul(stg[:], stg[:], pZ1[:, 512:1024]))
                p.dma("gpsimd", lambda h, mo=mo, c0=c0: h.dma_start(
                    out=xn1o[128 * mo:128 * mo + 128, c0:c0 + 512], in_=stg[:]))
        else:
            for mo in range(8):
                p.op("scalar", lambda h, mo=mo: h.activation(
                    stg[:], acc[:, 512 * mo:512 * mo + 512], AF.Silu))
                p.dma("sync", lambda h, mo=mo, c0=c0: h.dma_start(
                    out=x32[:], in_=s1[128 * mo:128 * mo + 128, c0:c0 + 512]))
                p.op("vector", lambda h: h.tensor_mul(stg[:], stg[:], x32[:]))
                p.dma("sync", lambda h, mo=mo, c0=c0: h.dma_start(
                    out=x32[:], in_=x1[128 * mo:128 * mo + 128, c0:c0 + 512]))
                p.op("vector", lambda h: h.tensor_add(stg[:], stg[:], x32[:]))
                p.dma("gpsimd", lambda h, mo=mo, c0=c0: h.dma_start(
                    out=x2o[128 * mo:128 * mo + 128, c0:c0 + 512], in_=stg[:]))
    last = p.last
    p.op("gpsimd", lambda h: h.engine_nop(), deps=[last])
    return p


def build_ssmk(name):
    p = Prog(name)
    ldt = p.din("ldt", [128, 1])
    lar = p.din("lar", [128, 32])
    aim = p.din("aim", [128, 32])
    cre = p.din("cre", [128, 32])
    cim = p.din("cim", [128, 32])
    Ko = p.dout("Ko", [128, 4096])

    PI = float(np.pi)
    dt = p.sb("dt", [128, 1])
    er = p.sb("er", [128, 32])
    ai = p.sb("ai", [128, 32])
    cr = p.sb("cr", [128, 32]); ci = p.sb("ci", [128, 32])
    dar = p.sb("dar", [128, 32])
    dai = p.sb("dai", [128, 32])
    t1 = p.sb("t1", [128, 32]); t2 = p.sb("t2", [128, 32]); t3 = p.sb("t3", [128, 32])
    sn = p.sb("sn", [128, 32]); cs = p.sb("cs", [128, 32])
    wr = p.sb("wr", [128, 32]); wi = p.sb("wi", [128, 32])
    wpr = p.sb("wpr", [128, 32]); wpi = p.sb("wpi", [128, 32])
    nwpi = p.sb("nwpi", [128, 32])
    cwr = p.sb("cwr", [128, 32]); cwi = p.sb("cwi", [128, 32]); ncwi = p.sb("ncwi", [128, 32])
    ur = p.sb("ur", [128, 2048]); ui = p.sb("ui", [128, 2048])
    vr = p.sb("vr", [128, 2048]); vi = p.sb("vi", [128, 2048]); nvi = p.sb("nvi", [128, 2048])
    pr = p.sb("pr", [128, 2048]); pi = p.sb("pi", [128, 2048])
    Ks = p.sb("Ks", [128, 4096])
    tl = p.sb("tl", [128, 64])

    for nm, dst, srcp in ((0, dt, ldt), (1, er, lar), (2, ai, aim), (3, cr, cre), (4, ci, cim)):
        p.dma("sync", lambda h, dst=dst, srcp=srcp: h.dma_start(out=dst[:], in_=srcp))
    p.op("scalar", lambda h: h.activation(dt[:], dt[:], AF.Exp))
    p.op("scalar", lambda h: h.activation(er[:], er[:], AF.Exp))
    p.op("vector", lambda h: h.tensor_scalar(dar[:], er[:], dt[:], -1.0, op0=ALU.mult, op1=ALU.mult))
    p.op("vector", lambda h: h.tensor_scalar(dai[:], ai[:], dt[:], None, op0=ALU.mult))

    ti32 = p.sb("ti32", [128, 32], mybir.dt.int32)

    def wrap_sin(dstt, add_half_pi):
        off = PI / 2 if add_half_pi else 0.0
        p.op("vector", lambda h, off=off: h.tensor_scalar(t3[:], dai[:], off, None, op0=ALU.add))
        p.op("vector", lambda h: h.tensor_scalar(t1[:], t3[:], 1.0 / (2 * PI), None, op0=ALU.mult))
        p.op("vector", lambda h: h.tensor_copy(ti32[:], t1[:]))
        p.op("vector", lambda h: h.tensor_copy(t1[:], ti32[:]))
        p.op("vector", lambda h: h.scalar_tensor_tensor(t1[:], t1[:], -2 * PI, t3[:], op0=ALU.mult, op1=ALU.add))
        p.op("vector", lambda h: h.tensor_scalar(t2[:], t1[:], PI, None, op0=ALU.is_gt))
        p.op("vector", lambda h: h.scalar_tensor_tensor(t1[:], t2[:], -2 * PI, t1[:], op0=ALU.mult, op1=ALU.add))
        p.op("vector", lambda h: h.tensor_scalar(t2[:], t1[:], -PI, None, op0=ALU.is_lt))
        p.op("vector", lambda h: h.scalar_tensor_tensor(t1[:], t2[:], 2 * PI, t1[:], op0=ALU.mult, op1=ALU.add))
        p.op("scalar", lambda h, dstt=dstt: h.activation(dstt[:], t1[:], AF.Sin))
    wrap_sin(sn, False)
    wrap_sin(cs, True)
    p.op("scalar", lambda h: h.activation(t1[:], dar[:], AF.Exp))
    p.op("vector", lambda h: h.tensor_mul(wr[:], t1[:], cs[:]))
    p.op("vector", lambda h: h.tensor_mul(wi[:], t1[:], sn[:]))
    p.op("vector", lambda h: h.tensor_scalar(t1[:], wr[:], -1.0, None, op0=ALU.add))
    p.op("vector", lambda h: h.tensor_mul(t2[:], cr[:], t1[:]))
    p.op("vector", lambda h: h.scalar_tensor_tensor(t2[:], ci[:], -1.0, t2[:], op0=ALU.mult, op1=ALU.bypass)
         if False else h.tensor_mul(t3[:], ci[:], wi[:]))
    p.op("vector", lambda h: h.tensor_sub(t2[:], t2[:], t3[:]))
    p.op("vector", lambda h: h.tensor_mul(t3[:], cr[:], wi[:]))
    p.op("vector", lambda h: h.tensor_mul(t1[:], ci[:], t1[:]))
    p.op("vector", lambda h: h.tensor_add(t3[:], t3[:], t1[:]))
    p.op("vector", lambda h: h.tensor_mul(t1[:], er[:], er[:]))
    p.op("vector", lambda h: h.scalar_tensor_tensor(t1[:], ai[:], 1.0, t1[:], op0=ALU.bypass, op1=ALU.bypass)
         if False else h.tensor_mul(cwr[:], ai[:], ai[:]))
    p.op("vector", lambda h: h.tensor_add(t1[:], t1[:], cwr[:]))
    p.op("vector", lambda h: h.reciprocal(t1[:], t1[:]))
    p.op("vector", lambda h: h.tensor_mul(cwr[:], t2[:], er[:]))
    p.op("vector", lambda h: h.tensor_mul(cwi[:], t3[:], ai[:]))
    p.op("vector", lambda h: h.tensor_sub(cwr[:], cwi[:], cwr[:]))
    p.op("vector", lambda h: h.tensor_mul(cwr[:], cwr[:], t1[:]))
    p.op("vector", lambda h: h.tensor_scalar(cwr[:], cwr[:], 2.0, None, op0=ALU.mult))
    p.op("vector", lambda h: h.tensor_mul(cwi[:], t3[:], er[:]))
    p.op("vector", lambda h: h.tensor_mul(t3[:], t2[:], ai[:]))
    p.op("vector", lambda h: h.tensor_add(cwi[:], cwi[:], t3[:]))
    p.op("vector", lambda h: h.tensor_mul(cwi[:], cwi[:], t1[:]))
    p.op("vector", lambda h: h.tensor_scalar(cwi[:], cwi[:], -2.0, None, op0=ALU.mult))
    p.op("vector", lambda h: h.tensor_scalar(ncwi[:], cwi[:], -1.0, None, op0=ALU.mult))

    p.op("vector", lambda h: h.tensor_copy(wpr[:], wr[:]))
    p.op("vector", lambda h: h.tensor_copy(wpi[:], wi[:]))
    for arr_r, arr_i in ((ur, ui), (vr, vi)):
        p.op("gpsimd", lambda h, arr_r=arr_r: h.memset(arr_r[:], 0.0))
        p.op("gpsimd", lambda h, arr_i=arr_i: h.memset(arr_i[:], 0.0))
        p.op("gpsimd", lambda h, arr_r=arr_r: h.memset(
            arr_r[:].rearrange("p (n l) -> p n l", n=32)[:, :, 0:1], 1.0))
        have = 1
        while have < 64:
            cnt = min(have, 64 - have)
            p.op("vector", lambda h: h.tensor_scalar(nwpi[:], wpi[:], -1.0, None, op0=ALU.mult))
            for n in range(32):
                b0 = 64 * n
                p.op("vector", lambda h, b0=b0, have=have, cnt=cnt, arr_r=arr_r, n=n: h.tensor_scalar(
                    tl[:, 0:cnt], arr_r[:, b0:b0 + cnt], wpr[:, n:n + 1], None, op0=ALU.mult))
                p.op("vector", lambda h, b0=b0, have=have, cnt=cnt, arr_i=arr_i, n=n: h.scalar_tensor_tensor(
                    tl[:, 0:cnt], arr_i[:, b0:b0 + cnt], nwpi[:, n:n + 1], tl[:, 0:cnt],
                    op0=ALU.mult, op1=ALU.add))
                p.op("vector", lambda h, b0=b0, have=have, cnt=cnt, arr_r=arr_r, arr_i=arr_i, n=n: h.tensor_scalar(
                    tl[:, 64 - cnt:64], arr_i[:, b0:b0 + cnt], wpr[:, n:n + 1], None, op0=ALU.mult))
                p.op("vector", lambda h, b0=b0, have=have, cnt=cnt, arr_r=arr_r, arr_i=arr_i, n=n: h.scalar_tensor_tensor(
                    arr_i[:, b0 + have:b0 + have + cnt], arr_r[:, b0:b0 + cnt], wpi[:, n:n + 1],
                    tl[:, 64 - cnt:64], op0=ALU.mult, op1=ALU.add))
                p.op("vector", lambda h, b0=b0, have=have, cnt=cnt, arr_r=arr_r, n=n: h.tensor_copy(
                    arr_r[:, b0 + have:b0 + have + cnt], tl[:, 0:cnt]))
            p.op("vector", lambda h: h.tensor_mul(t1[:], wpr[:], wpr[:]))
            p.op("vector", lambda h: h.tensor_mul(t2[:], wpi[:], wpi[:]))
            p.op("vector", lambda h: h.tensor_mul(t3[:], wpr[:], wpi[:]))
            p.op("vector", lambda h: h.tensor_sub(wpr[:], t1[:], t2[:]))
            p.op("vector", lambda h: h.tensor_scalar(wpi[:], t3[:], 2.0, None, op0=ALU.mult))
            have *= 2
    for n in range(32):
        b0 = 64 * n
        p.op("vector", lambda h, b0=b0, n=n: h.tensor_scalar(
            tl[:], ur[:, b0:b0 + 64], cwr[:, n:n + 1], None, op0=ALU.mult))
        p.op("vector", lambda h, b0=b0, n=n: h.scalar_tensor_tensor(
            pr[:, b0:b0 + 64], ui[:, b0:b0 + 64], ncwi[:, n:n + 1], tl[:],
            op0=ALU.mult, op1=ALU.add))
        p.op("vector", lambda h, b0=b0, n=n: h.tensor_scalar(
            tl[:], ur[:, b0:b0 + 64], cwi[:, n:n + 1], None, op0=ALU.mult))
        p.op("vector", lambda h, b0=b0, n=n: h.scalar_tensor_tensor(
            pi[:, b0:b0 + 64], ui[:, b0:b0 + 64], cwr[:, n:n + 1], tl[:],
            op0=ALU.mult, op1=ALU.add))
    p.op("vector", lambda h: h.tensor_scalar(nvi[:], vi[:], -1.0, None, op0=ALU.mult))
    p.op("gpsimd", lambda h: h.memset(Ks[:], 0.0))
    for n in range(32):
        b0 = 64 * n
        for l2 in range(64):
            p.op("vector", lambda h, b0=b0, l2=l2, n=n: h.scalar_tensor_tensor(
                Ks[:, 64 * l2:64 * l2 + 64], pr[:, b0:b0 + 64], vr[:, b0 + l2:b0 + l2 + 1],
                Ks[:, 64 * l2:64 * l2 + 64], op0=ALU.mult, op1=ALU.add))
            p.op("vector", lambda h, b0=b0, l2=l2, n=n: h.scalar_tensor_tensor(
                Ks[:, 64 * l2:64 * l2 + 64], pi[:, b0:b0 + 64], nvi[:, b0 + l2:b0 + l2 + 1],
                Ks[:, 64 * l2:64 * l2 + 64], op0=ALU.mult, op1=ALU.add))
    p.dma("gpsimd", lambda h: h.dma_start(out=Ko, in_=Ks[:]))
    last = p.last
    p.op("gpsimd", lambda h: h.engine_nop(), deps=[last])
    return p


NS = 640


def build_fft1(name):
    p = Prog(name)
    za = p.din("za", [64, NS * 64])
    wr = p.din("wr", [64, 128], BF16)
    wi = p.din("wi", [64, 128], BF16)
    gr = p.dout("gr", [128, NS * 64])
    gi = p.dout("gi", [128, NS * 64])
    zb = p.sb("zb", [64, 512], BF16)
    zs = p.sb("zs", [64, 512])
    wrs = p.sb("wrs", [64, 128], BF16)
    wis = p.sb("wis", [64, 128], BF16)
    ev = p.sb("ev", [128, 512])
    pA = p.ps("pA", [128, 512])
    pB = p.ps("pB", [128, 512])
    p.dma("sync", lambda h: h.dma_start(out=wrs[:], in_=wr))
    p.dma("sync", lambda h: h.dma_start(out=wis[:], in_=wi))
    NTL = NS * 64 // 512
    for t in range(NTL):
        c0 = t * 512
        p.dma("sync", lambda h, c0=c0: h.dma_start(out=zs[:], in_=za[:, c0:c0 + 512]))
        p.op("vector", lambda h: h.tensor_copy(zb[:], zs[:]))
        p.op("tensor", lambda h: h.matmul(pA[:], wrs[:], zb[:], start=True, stop=True))
        p.op("tensor", lambda h: h.matmul(pB[:], wis[:], zb[:], start=True, stop=True))
        p.op("vector", lambda h: h.tensor_copy(ev[:], pA[:]))
        p.dma("gpsimd", lambda h, c0=c0: h.dma_start(out=gr[:, c0:c0 + 512], in_=ev[:]))
        p.op("vector", lambda h: h.tensor_copy(ev[:], pB[:]))
        p.dma("gpsimd", lambda h, c0=c0: h.dma_start(out=gi[:, c0:c0 + 512], in_=ev[:]))
    last = p.last
    p.op("gpsimd", lambda h: h.engine_nop(), deps=[last])
    return p


def build_fft2(name):
    p = Prog(name)
    gt = p.din("gt", [64, 128 * 2 * NS])
    ma = p.din("ma", [64, 128 * 128], BF16)
    mb = p.din("mb", [64, 128 * 128], BF16)
    xo = p.dout("xo", [128, 128 * NS])
    gb = p.sb("gb", [64, 2 * NS], BF16)
    gs = p.sb("gs", [64, 2 * NS])
    mas = p.sb("mas", [64, 128], BF16)
    mbs = p.sb("mbs", [64, 128], BF16)
    ev = p.sb("ev", [128, 512])
    pA = p.ps("pA", [128, 512])
    for k2 in range(128):
        p.dma("sync", lambda h, k2=k2: h.dma_start(out=gs[:], in_=gt[:, 2 * NS * k2:2 * NS * k2 + 2 * NS]))
        p.op("vector", lambda h: h.tensor_copy(gb[:], gs[:]))
        p.dma("sync", lambda h, k2=k2: h.dma_start(out=mas[:], in_=ma[:, 128 * k2:128 * k2 + 128]))
        p.dma("sync", lambda h, k2=k2: h.dma_start(out=mbs[:], in_=mb[:, 128 * k2:128 * k2 + 128]))
        for nt in range((NS + 511) // 512):
            n0 = nt * 512
            nc_ = min(512, NS - n0)
            p.op("tensor", lambda h, n0=n0, nc_=nc_: h.matmul(
                pA[:, 0:nc_], mas[:], gb[:, n0:n0 + nc_], start=True, stop=False))
            p.op("tensor", lambda h, n0=n0, nc_=nc_: h.matmul(
                pA[:, 0:nc_], mbs[:], gb[:, NS + n0:NS + n0 + nc_], start=False, stop=True))
            p.op("vector", lambda h, nc_=nc_: h.tensor_copy(ev[:, 0:nc_], pA[:, 0:nc_]))
            p.dma("gpsimd", lambda h, k2=k2, n0=n0, nc_=nc_: h.dma_start(
                out=xo[:, NS * k2 + n0:NS * k2 + n0 + nc_], in_=ev[:, 0:nc_]))
    last = p.last
    p.op("gpsimd", lambda h: h.engine_nop(), deps=[last])
    return p


NSI = 512
NKC = 33


def build_inv(name):
    p = Prog(name)
    xxr = p.din("xxr", [NKC * 128, NSI])
    xxi = p.din("xxi", [NKC * 128, NSI])
    kfr = p.din("kfr", [NKC * 128, NSI])
    kfi = p.din("kfi", [NKC * 128, NSI])
    cc = p.din("cc", [NKC * 128, 4096], BF16)
    cs = p.din("cs", [NKC * 128, 4096], BF16)
    xns = p.din("xns", [4096, NSI])
    drow = p.din("drow", [1, NSI])
    yo = p.dout("yo", [4096, NSI])

    a1 = p.sb("a1", [128, NSI]); a2_ = p.sb("a2_", [128, NSI])
    b1 = p.sb("b1", [128, NSI]); b2_ = p.sb("b2_", [128, NSI])
    t1 = p.sb("t1", [128, NSI]); t2 = p.sb("t2", [128, NSI])
    pbr = p.sb("pbr", [128, NKC * NSI], BF16)
    pbi = p.sb("pbi", [128, NKC * NSI], BF16)
    ccs = p.sb("ccs", [128, 128], BF16)
    css = p.sb("css", [128, 128], BF16)
    dsb = p.sb("dsb", [1, NSI])
    dbc = p.sb("dbc", [128, NSI])
    ones_row1 = p.sb("ones_row1", [1, 128])
    ev = p.sb("ev", [128, NSI])
    pA = p.ps("pA", [128, NSI])
    pB = p.ps("pB", [128, NSI])

    p.op("gpsimd", lambda h: h.memset(ones_row1[:], 1.0))
    p.dma("sync", lambda h: h.dma_start(out=dsb[:], in_=drow))
    p.op("tensor", lambda h: h.matmul(pB[:], ones_row1[:], dsb[:], start=True, stop=True))
    p.op("vector", lambda h: h.tensor_copy(dbc[:], pB[:]))
    for ch in range(NKC):
        r0 = ch * 128
        p.dma("sync", lambda h, r0=r0: h.dma_start(out=a1[:], in_=xxr[r0:r0 + 128, :]))
        p.dma("sync", lambda h, r0=r0: h.dma_start(out=a2_[:], in_=xxi[r0:r0 + 128, :]))
        p.dma("sync", lambda h, r0=r0: h.dma_start(out=b1[:], in_=kfr[r0:r0 + 128, :]))
        p.dma("sync", lambda h, r0=r0: h.dma_start(out=b2_[:], in_=kfi[r0:r0 + 128, :]))
        p.op("vector", lambda h: h.tensor_mul(t1[:], a1[:], b1[:]))
        p.op("vector", lambda h: h.tensor_mul(t2[:], a2_[:], b2_[:]))
        p.op("vector", lambda h: h.tensor_sub(t1[:], t1[:], t2[:]))
        p.op("vector", lambda h, ch=ch: h.tensor_copy(pbr[:, NSI * ch:NSI * ch + NSI], t1[:]))
        p.op("vector", lambda h: h.tensor_mul(t1[:], a1[:], b2_[:]))
        p.op("vector", lambda h: h.tensor_mul(t2[:], a2_[:], b1[:]))
        p.op("vector", lambda h: h.tensor_add(t1[:], t1[:], t2[:]))
        p.op("vector", lambda h, ch=ch: h.tensor_copy(pbi[:, NSI * ch:NSI * ch + NSI], t1[:]))
    for mt in range(32):
        for ch in range(NKC):
            p.dma("sync", lambda h, mt=mt, ch=ch: h.dma_start(
                out=ccs[:], in_=cc[128 * ch:128 * ch + 128, 128 * mt:128 * mt + 128]))
            p.dma("sync", lambda h, mt=mt, ch=ch: h.dma_start(
                out=css[:], in_=cs[128 * ch:128 * ch + 128, 128 * mt:128 * mt + 128]))
            p.op("tensor", lambda h, ch=ch: h.matmul(
                pA[:], ccs[:], pbr[:, NSI * ch:NSI * ch + NSI], start=(ch == 0), stop=False))
            p.op("tensor", lambda h, ch=ch: h.matmul(
                pA[:], css[:], pbi[:, NSI * ch:NSI * ch + NSI],
                start=False, stop=(ch == NKC - 1)))
        p.dma("sync", lambda h, mt=mt: h.dma_start(out=t1[:], in_=xns[128 * mt:128 * mt + 128, :]))
        p.op("vector", lambda h: h.tensor_mul(t2[:], t1[:], dbc[:]))
        p.op("vector", lambda h: h.tensor_add(ev[:], pA[:], t2[:]))
        p.dma("gpsimd", lambda h, mt=mt: h.dma_start(out=yo[128 * mt:128 * mt + 128, :], in_=ev[:]))
    last = p.last
    p.op("gpsimd", lambda h: h.engine_nop(), deps=[last])
    return p


def _consts_attn():
    j = np.arange(T, dtype=np.float64)
    jhi, jlo = np.floor(j / 64), j % 64
    augK = np.stack([jhi, jlo, np.full(T, 64.0), np.ones(T)]).astype(np.float32)
    augQ = np.stack([np.full(T, 64.0), np.ones(T), -jhi, -jlo]).astype(np.float32)
    masks = np.stack([np.where(np.arange(128)[:, None] + d > np.arange(512)[None, :],
                               -30000.0, 0.0) for d in (0, 128, 256, 384)]).astype(np.float32)
    return _np_bf(augK), _np_bf(augQ), masks


def _consts_fft():
    a2 = np.arange(64); k2 = np.arange(128)
    W = np.exp(-2j * np.pi * np.outer(a2, k2) / 128.0)
    wr, wi = _np_bf(W.real), _np_bf(W.imag)
    a1 = np.arange(64); k1 = np.arange(64)
    tw = np.exp(-2j * np.pi * np.outer(a1, k2) / 8192.0)
    W64 = np.exp(-2j * np.pi * np.outer(a1, k1) / 64.0)
    M = np.einsum('ak,al->kal', tw, W64)
    MA = np.concatenate([M.real, M.imag], axis=2)
    MB = np.concatenate([-M.imag, M.real], axis=2)
    MA = _np_bf(MA.transpose(1, 0, 2).reshape(64, 128 * 128))
    MB = _np_bf(MB.transpose(1, 0, 2).reshape(64, 128 * 128))
    k = np.arange(4224); m = np.arange(4096)
    wk = np.where((k == 0) | (k == 4096), 1.0, 2.0) * (k <= 4096)
    th = 2.0 * np.pi * np.outer(k, m) / 8192.0
    cC = _np_bf(wk[:, None] * np.cos(th) / 8192.0)
    cS = _np_bf(-wk[:, None] * np.sin(th) / 8192.0)
    return wr, wi, MA, MB, cC, cS


_PROGS = {}

def _prog(key, builder, *a, **kw):
    if key not in _PROGS:
        p = builder(*a, **kw)
        _PROGS[key] = (p, p.build())
    return _PROGS[key]


def _run(key, in_maps):
    p, nc = _PROGS[key]
    res = run_bass_kernel_spmd(nc, in_maps, core_ids=list(range(len(in_maps))))
    return res.results


def ssm_launches(xn1_all, inp):
    wr, wi, MA, MB, cC, cS = _consts_fft()
    _prog("ssmk", build_ssmk, "ssmk")
    ldt = _np_f32(inp["L1_attn_log_dt"])[:, None]
    lar = _np_f32(inp["L1_attn_log_A_real"])
    aim = _np_f32(inp["L1_attn_A_imag"])
    Cp = _np_f32(inp["L1_attn_C"])
    ins = [dict(ldt=np.ascontiguousarray(ldt[128*c:128*c+128]),
                lar=np.ascontiguousarray(lar[128*c:128*c+128]),
                aim=np.ascontiguousarray(aim[128*c:128*c+128]),
                cre=np.ascontiguousarray(Cp[128*c:128*c+128, :, 0]),
                cim=np.ascontiguousarray(Cp[128*c:128*c+128, :, 1])) for c in range(8)]
    r = _run("ssmk", ins)
    Kfull = np.concatenate([np.asarray(r[c]["Ko"], np.float32) for c in range(8)], axis=0)
    sigs = np.concatenate([xn1_all.reshape(4 * 1024, 4096), Kfull], axis=0)
    _prog("fft1", build_fft1, "fft1")
    ins = []
    for c in range(8):
        sl = sigs[NS * c:NS * c + NS]
        za = sl.reshape(NS, 64, 64).transpose(1, 0, 2).reshape(64, NS * 64)
        ins.append(dict(za=np.ascontiguousarray(za), wr=wr, wi=wi))
    r = _run("fft1", ins)
    Gr = np.stack([np.asarray(r[c]["gr"], np.float32) for c in range(8)])
    Gi = np.stack([np.asarray(r[c]["gi"], np.float32) for c in range(8)])
    _prog("fft2", build_fft2, "fft2")
    ins = []
    for c in range(8):
        grc = Gr[c].reshape(128, NS, 64)
        gic = Gi[c].reshape(128, NS, 64)
        gt = np.stack([grc, gic], axis=1)
        gt = gt.transpose(3, 0, 1, 2).reshape(64, 128 * 2 * NS)
        ins.append(dict(gt=np.ascontiguousarray(gt), ma=MA, mb=MB))
    r = _run("fft2", ins)
    Xr = np.zeros((5120, 8192), np.float32)
    Xi = np.zeros((5120, 8192), np.float32)
    for c in range(8):
        xo = np.asarray(r[c]["xo"], np.float32).reshape(128, 128, NS)
        Xr[NS*c:NS*c+NS] = xo[0:64].transpose(2, 0, 1).reshape(NS, 8192)
        Xi[NS*c:NS*c+NS] = xo[64:128].transpose(2, 0, 1).reshape(NS, 8192)
    _prog("inv", build_inv, "inv")
    Dp = _np_f32(inp["L1_attn_D"])
    ins = []
    for c in range(8):
        ss = np.arange(NSI * c, NSI * c + NSI)
        hh = ss % 1024
        xxr = np.zeros((NKC * 128, NSI), np.float32); xxr[0:4097] = Xr[ss, 0:4097].T
        xxi = np.zeros((NKC * 128, NSI), np.float32); xxi[0:4097] = Xi[ss, 0:4097].T
        kfr = np.zeros((NKC * 128, NSI), np.float32); kfr[0:4097] = Xr[4096 + hh][:, 0:4097].T
        kfi = np.zeros((NKC * 128, NSI), np.float32); kfi[0:4097] = Xi[4096 + hh][:, 0:4097].T
        xns = sigs[ss].T.copy()
        drow = Dp[hh][None, :]
        ins.append(dict(xxr=xxr, xxi=xxi, kfr=kfr, kfi=kfi, cc=cC, cs=cS,
                        xns=np.ascontiguousarray(xns), drow=np.ascontiguousarray(drow)))
    r = _run("inv", ins)
    s1_all = np.zeros((4, 1024, 4096), np.float32)
    for c in range(8):
        y = np.asarray(r[c]["yo"], np.float32)
        ss = np.arange(NSI * c, NSI * c + NSI)
        s1_all.reshape(4096, 4096)[ss] = y.T
    return s1_all


def kernel(**inp):
    inp = {k: np.asarray(v) for k, v in inp.items()}
    x = _np_f32(inp["L0_attn_x"])
    B = x.shape[0]
    xT_all = np.ascontiguousarray(x.transpose(0, 2, 1))
    augK, augQ, masks = _consts_attn()
    ident = np.eye(128, dtype=np.float32)
    selm = np.zeros((8, 8, 128), np.float32)
    for e in range(8):
        selm[e, e, :] = 1.0
    selm = _np_bf(selm.reshape(8, 8 * 128))

    _prog("l0a", build_l0a, "l0a")
    wq = np.ascontiguousarray(_np_f32(inp["L0_attn_Wq"]).T)
    wk = np.ascontiguousarray(_np_f32(inp["L0_attn_Wk"]).T)
    wv = np.ascontiguousarray(_np_f32(inp["L0_attn_Wv"]).T)
    wo = np.ascontiguousarray(_np_f32(inp["L0_attn_Wo"]).T)
    brows = np.ascontiguousarray(np.concatenate(
        [_np_f32(inp["L0_attn_bq"]), _np_f32(inp["L0_attn_bk"]),
         _np_f32(inp["L0_attn_bv"]), _np_f32(inp["L0_attn_bo"])])[None, :])
    g1c = np.ascontiguousarray(_np_f32(inp["L0_ffn_ln1_g"]).reshape(8, 128).T)
    b1c_ = np.ascontiguousarray(_np_f32(inp["L0_ffn_ln1_b"]).reshape(8, 128).T)
    wqkv = np.ascontiguousarray(_np_f32(inp["L0_ffn_w_qkv"]))
    wo1 = np.ascontiguousarray(_np_f32(inp["L0_ffn_w_o"]))
    ins = [dict(xT=np.ascontiguousarray(xT_all[c // 2]), wq=wq, wk=wk, wv=wv, wo0=wo,
                brows=brows, augK=augK, augQ=augQ, wqkv=wqkv, wo1=wo1,
                gcol=g1c, bcol=b1c_, masks=masks) for c in range(8)]
    r = _run("l0a", ins)
    s_all = np.stack([np.asarray(r[2 * b]["sout0"], np.float32) for b in range(B)])
    xr_all = np.stack([np.asarray(r[2 * b]["sout1"], np.float32) for b in range(B)])
    xn_all = None

    _prog("moe0", build_moe, "moe0", True)
    g2c = np.ascontiguousarray(_np_f32(inp["L0_ffn_ln2_g"]).reshape(8, 128).T)
    b2c = np.ascontiguousarray(_np_f32(inp["L0_ffn_ln2_b"]).reshape(8, 128).T)
    b1cols = np.ascontiguousarray(_np_f32(inp["L0_ffn_b1"]).reshape(8, 32, 128)
                                  .transpose(2, 0, 1).reshape(128, 256))
    ins = []
    for c in range(8):
        b, hh = c // 2, c % 2
        sl = slice(TH * hh, TH * hh + TH)
        ins.append(dict(xr=np.ascontiguousarray(xr_all[b][:, sl]),
                        sT=np.ascontiguousarray(s_all[b][:, sl]),
                        xT=np.ascontiguousarray(xT_all[b][:, sl]),
                        gcol=g2c, bcol=b2c, b1c=b1cols,
                        b2m=np.ascontiguousarray(_np_f32(inp["L0_ffn_b2"])),
                        wg=np.ascontiguousarray(_np_f32(inp["L0_ffn_w_gate"])),
                        w1=np.ascontiguousarray(_np_f32(inp["L0_ffn_w1"])),
                        w2=np.ascontiguousarray(_np_f32(inp["L0_ffn_w2"])),
                        selm=selm, ident=ident))
    r = _run("moe0", ins)
    x1_all = np.stack([np.concatenate([np.asarray(r[2 * b]["x1o"], np.float32),
                                       np.asarray(r[2 * b + 1]["x1o"], np.float32)], axis=1)
                       for b in range(B)])
    xn1_all = np.stack([np.concatenate([np.asarray(r[2 * b]["xn1o"], np.float32),
                                        np.asarray(r[2 * b + 1]["xn1o"], np.float32)], axis=1)
                        for b in range(B)])

    s1_all = ssm_launches(xn1_all, inp)

    _prog("moe1", build_moe, "moe1", False)
    HID = 2816
    pad1 = np.zeros((8, 1024, HID), np.float32); pad1[:, :, :2730] = _np_f32(inp["L1_ffn_w1"])
    pad3 = np.zeros((8, 1024, HID), np.float32); pad3[:, :, :2730] = _np_f32(inp["L1_ffn_w3"])
    pad2 = np.zeros((8, HID, 1024), np.float32); pad2[:, :2730, :] = _np_f32(inp["L1_ffn_w2"])
    ins = []
    for c in range(8):
        b, hh = c // 2, c % 2
        sl = slice(TH * hh, TH * hh + TH)
        ins.append(dict(xn1=np.ascontiguousarray(xn1_all[b][:, sl]),
                        x1=np.ascontiguousarray(x1_all[b][:, sl]),
                        s1=np.ascontiguousarray(s1_all[b][:, sl]),
                        wg=np.ascontiguousarray(_np_f32(inp["L1_ffn_gate_w"])),
                        w1=pad1, w2=pad2, w3=pad3, selm=selm, ident=ident))
    r = _run("moe1", ins)
    out = np.zeros_like(x)
    for b in range(B):
        x2 = np.concatenate([np.asarray(r[2 * b]["x2o"], np.float32),
                             np.asarray(r[2 * b + 1]["x2o"], np.float32)], axis=1)
        out[b] = x2.T
    return out


def build_l0a(name):
    p = Prog(name)
    xT = p.din("xT", [D, T])
    wq = p.din("wq", [D, D]); wk = p.din("wk", [D, D])
    wv = p.din("wv", [D, D]); wo0 = p.din("wo0", [D, D])
    brows = p.din("brows", [1, 4 * D])
    augK = p.din("augK", [4, T], BF16)
    augQ = p.din("augQ", [4, T], BF16)
    wqkv = p.din("wqkv", [D, 3 * D])
    wo1 = p.din("wo1", [D, D])
    gcol = p.din("gcol", [128, 8])
    bcol = p.din("bcol", [128, 8])
    masks = p.din("masks", [4, 128, 512])
    sout0 = p.dout("sout0", [D, T])
    sout1 = p.dout("sout1", [D, T])
    xnd = p.nc.dram_tensor("xnd", [D, T], F32, kind="Internal").ap()

    xbf = p.sb("xbf", [128, 8 * T], BF16)
    wstage = p.sb("wstage", [128, TH])
    wbf = p.sb("wbf", [128, 8 * 128], BF16)
    qg = p.sb("qg", [128, 2 * T], BF16)
    kg = p.sb("kg", [128, 2 * T], BF16)
    vsb = p.sb("vsb", [128, 32 * 260], BF16)
    psb = p.sb("psb", [128, 512], BF16)
    osb = p.sb("osb", [128, 512])
    rd = p.sb("rd", [1, 512])
    onesr = p.sb("onesr", [1, 512], BF16)
    onesc = p.sb("onesc", [1, 64])
    brsb = p.sb("brsb", [1, 4 * D], BF16)
    brst = p.sb("brst", [1, 4 * D])
    maskb = p.sb("maskb", [128, 4 * 512])
    ontd = p.nc.dram_tensor("ontd", [D, T], BF16, kind="Internal").ap()
    ontb = p.sb("ontb", [128, 512], BF16)
    onrm = p.sb("onrm", [64, 512], BF16)
    aKb = p.sb("aKb", [4, T], BF16)
    aQb = p.sb("aQb", [4, T], BF16)
    stg = p.sb("stg", [128, 512])
    vrhs = p.sb("vrhs", [128, 8 * 256], BF16)
    x32 = p.sb("x32", [128, 512])
    sq = p.sb("sq", [128, 512])
    mrow = p.sb("mrow", [1, 512]); rrow = p.sb("rrow", [1, 512])
    vrow = p.sb("vrow", [1, 512])
    onescol = p.sb("onescol", [128, 1])
    eps_col = p.sb("eps_col", [128, 1])
    ones_row1 = p.sb("ones_row1", [1, 128])
    sgc = p.sb("sgc", [128, 8])
    sbc = p.sb("sbc", [128, 8])
    pA = p.ps("pA", [128, 512])
    pB = p.ps("pB", [128, 512])
    pC = p.ps("pC", [128, 512])
    pD = p.ps("pD", [128, 512])

    p.op("gpsimd", lambda h: h.memset(onesr[:], 1.0))
    p.op("gpsimd", lambda h: h.memset(onesc[:], 1.0))
    p.op("gpsimd", lambda h: h.memset(onescol[:], 1.0 / 1024.0))
    p.op("gpsimd", lambda h: h.memset(eps_col[:], EPS))
    p.op("gpsimd", lambda h: h.memset(ones_row1[:], 1.0))
    p.op("gpsimd", lambda h: h.memset(
        vsb[:].rearrange("p (t h c) -> p t h c", t=32, h=4)[:, :, :, 64:65], 1.0))
    for d in range(4):
        p.dma("sync", lambda h, d=d: h.dma_start(out=maskb[:, 512 * d:512 * d + 512], in_=masks[d]))
    p.dma("sync", lambda h: h.dma_start(out=brst[:], in_=brows))
    p.op("vector", lambda h: h.tensor_copy(brsb[:], brst[:]))
    p.dma("sync", lambda h: h.dma_start(out=aKb[:], in_=augK))
    p.dma("sync", lambda h: h.dma_start(out=aQb[:], in_=augQ))
    p.dma("sync", lambda h: h.dma_start(out=sgc[:], in_=gcol))
    p.dma("sync", lambda h: h.dma_start(out=sbc[:], in_=bcol))

    for tt in range(8):
        c0 = tt * 512
        for c in range(8):
            p.dma("sync", lambda h, c=c, c0=c0: h.dma_start(
                out=x32[:], in_=xT[128 * c:128 * c + 128, c0:c0 + 512]))
            p.op("tensor", lambda h, c=c: h.matmul(
                pA[0:1, :], onescol[:], x32[:], start=(c == 0), stop=(c == 7)))
            p.op("scalar", lambda h: h.activation(sq[:], x32[:], AF.Square))
            p.op("tensor", lambda h, c=c: h.matmul(
                pB[0:1, :], onescol[:], sq[:], start=(c == 0), stop=(c == 7)))
        p.op("vector", lambda h: h.tensor_copy(mrow[:], pA[0:1, :]))
        p.op("vector", lambda h: h.scalar_tensor_tensor(
            vrow[:], mrow[:], -1.0, mrow[:], op0=ALU.mult, op1=ALU.mult))
        p.op("vector", lambda h: h.tensor_add(vrow[:], vrow[:], pB[0:1, :]))
        p.op("scalar", lambda h: h.activation(vrow[:], vrow[:], AF.Ln, bias=eps_col[0:1, :], scale=1.0))
        p.op("scalar", lambda h: h.activation(rrow[:], vrow[:], AF.Exp, scale=-0.5))
        p.op("tensor", lambda h: h.matmul(pA[:], ones_row1[:], mrow[:], start=True, stop=True))
        p.op("tensor", lambda h: h.matmul(pB[:], ones_row1[:], rrow[:], start=True, stop=True))
        for c in range(8):
            p.dma("sync", lambda h, c=c, c0=c0: h.dma_start(
                out=x32[:], in_=xT[128 * c:128 * c + 128, c0:c0 + 512]))
            p.op("vector", lambda h: h.tensor_sub(stg[:], x32[:], pA[:]))
            p.op("vector", lambda h: h.tensor_mul(stg[:], stg[:], pB[:]))
            p.dma("gpsimd", lambda h, c=c, c0=c0: h.dma_start(
                out=xnd[128 * c:128 * c + 128, c0:c0 + 512], in_=stg[:]))

    def load_lhsT(src_ap, k, mcol0):
        p.dma("sync", lambda h, src_ap=src_ap, k=k, mcol0=mcol0: h.dma_start(
            out=wstage[:, 0:128], in_=src_ap[128 * k:128 * k + 128, mcol0:mcol0 + 128]))
        p.op("vector", lambda h: h.tensor_copy(wbf[:, 128 * k:128 * k + 128], wstage[:, 0:128]))

    NKT = T // 128
    for outer in (True, False):
        xnT = xnd
        sout = sout0 if outer else sout1
        wo = wo0 if outer else wo1
        if not outer:
            for tt in range(8):
                c0 = tt * 512
                for c in range(8):
                    p.dma("sync", lambda h, c=c, c0=c0: h.dma_start(
                        out=x32[:], in_=xnT[128 * c:128 * c + 128, c0:c0 + 512]))
                    p.op("tensor", lambda h, c=c: h.matmul(
                        pC[0:1, :], onescol[:], x32[:], start=(c == 0), stop=(c == 7)))
                    p.op("scalar", lambda h: h.activation(sq[:], x32[:], AF.Square))
                    p.op("tensor", lambda h, c=c: h.matmul(
                        pD[0:1, :], onescol[:], sq[:], start=(c == 0), stop=(c == 7)))
                p.op("vector", lambda h: h.tensor_copy(mrow[:], pC[0:1, :]))
                p.op("vector", lambda h: h.scalar_tensor_tensor(
                    vrow[:], mrow[:], -1.0, mrow[:], op0=ALU.mult, op1=ALU.mult))
                p.op("vector", lambda h: h.tensor_add(vrow[:], vrow[:], pD[0:1, :]))
                p.op("scalar", lambda h: h.activation(vrow[:], vrow[:], AF.Ln, bias=eps_col[0:1, :], scale=1.0))
                p.op("scalar", lambda h: h.activation(rrow[:], vrow[:], AF.Exp, scale=-0.5))
                p.op("tensor", lambda h: h.matmul(pC[:], ones_row1[:], mrow[:], start=True, stop=True))
                p.op("tensor", lambda h: h.matmul(pD[:], ones_row1[:], rrow[:], start=True, stop=True))
                for c in range(8):
                    p.dma("sync", lambda h, c=c, c0=c0: h.dma_start(
                        out=x32[:], in_=xnT[128 * c:128 * c + 128, c0:c0 + 512]))
                    p.op("vector", lambda h: h.tensor_sub(stg[:], x32[:], pC[:]))
                    p.op("vector", lambda h: h.tensor_mul(stg[:], stg[:], pD[:]))
                    p.op("vector", lambda h, c=c, c0=c0: h.tensor_scalar(
                        xbf[:, c * T + c0:c * T + c0 + 512], stg[:],
                        sgc[:, c:c + 1], sbc[:, c:c + 1], op0=ALU.mult, op1=ALU.add))
        else:
            for c in range(8):
                for hh2 in range(2):
                    p.dma("sync", lambda h, c=c, hh2=hh2: h.dma_start(
                        out=wstage[:], in_=xnT[128 * c:128 * c + 128, TH * hh2:TH * hh2 + TH]))
                    p.op("vector", lambda h, c=c, hh2=hh2: h.tensor_copy(
                        xbf[:, c * T + TH * hh2:c * T + TH * hh2 + TH], wstage[:]))
        for grp in range(4):
            h0 = grp * 4
            r0 = h0 * HD
            for which, dst, scale in (("q", qg, 0.125), ("k", kg, 1.0)):
                if outer:
                    wsrc = wq if which == "q" else wk
                    coff = r0
                else:
                    wsrc = wqkv
                    coff = r0 if which == "q" else D + r0
                for m in range(2):
                    for k in range(8):
                        load_lhsT(wsrc, k, coff + 128 * m)
                    for n in range(8):
                        for k in range(8):
                            p.op("tensor", lambda h, k=k, n=n: h.matmul(
                                pA[:], wbf[:, 128 * k:128 * k + 128],
                                xbf[:, k * T + 512 * n:k * T + 512 * n + 512],
                                start=(k == 0), stop=(k == 7 and not outer)))
                        if outer:
                            bi = 0 if which == "q" else 1
                            p.op("tensor", lambda h, m=m, bi=bi, coff=coff: h.matmul(
                                pA[:], brsb[0:1, bi * D + coff + 128 * m:bi * D + coff + 128 * m + 128],
                                onesr[:], start=False, stop=True))
                        p.op("scalar", lambda h, dst=dst, m=m, n=n, scale=scale: h.activation(
                            dst[:, m * T + 512 * n:m * T + 512 * n + 512], pA[:],
                            AF.Copy, scale=scale))
            if outer:
                vw = wv; vcoff = r0
            else:
                vw = wqkv; vcoff = 2 * D + r0
            for k in range(8):
                p.dma("sync", lambda h, k=k, vw=vw, vcoff=vcoff: h.dma_start(
                    out=wstage[:, 0:256], in_=vw[128 * k:128 * k + 128, vcoff:vcoff + 256]))
                p.op("vector", lambda h, k=k: h.tensor_copy(
                    vrhs[:, 256 * k:256 * k + 256], wstage[:, 0:256]))
            for kt in range(NKT):
                for k in range(8):
                    p.op("tensor", lambda h, k=k, kt=kt: h.matmul(
                        pA[:, 0:256], xbf[:, k * T + 128 * kt:k * T + 128 * kt + 128],
                        vrhs[:, 256 * k:256 * k + 256], start=(k == 0), stop=(k == 7 and not outer)))
                if outer:
                    p.op("tensor", lambda h, vcoff=vcoff: h.matmul(
                        pA[:, 0:256], onesr[0:1, 0:128], brsb[0:1, 2 * D + vcoff:2 * D + vcoff + 256],
                        start=False, stop=True))
                for hh in range(4):
                    p.op("vector", lambda h, kt=kt, hh=hh: h.tensor_copy(
                        vsb[:, 260 * kt + 65 * hh:260 * kt + 65 * hh + 64],
                        pA[:, 64 * hh:64 * hh + 64]))
            for qt in range(8):
                q0 = qt * 512
                nkb = min(32, 4 * qt + 4)
                for hh in range(4):
                    hloc_t = hh // 2
                    hbase = 64 * (hh % 2)
                    for kb in range(nkb):
                        p.op("tensor", lambda h, hloc_t=hloc_t, hbase=hbase, kb=kb, q0=q0, outer=outer: h.matmul(
                            pA[:], kg[hbase:hbase + 64, hloc_t * T + 128 * kb:hloc_t * T + 128 * kb + 128],
                            qg[hbase:hbase + 64, hloc_t * T + q0:hloc_t * T + q0 + 512],
                            start=True, stop=not outer))
                        if outer:
                            p.op("tensor", lambda h, kb=kb, q0=q0: h.matmul(
                                pA[:], aKb[:, 128 * kb:128 * kb + 128], aQb[:, q0:q0 + 512],
                                start=False, stop=True))
                        d = 128 * kb - q0
                        if 0 <= d < 512:
                            mi = d // 128
                            p.op("vector", lambda h, mi=mi: h.tensor_add(
                                pA[:], pA[:], maskb[:, 512 * mi:512 * mi + 512]))
                        p.op("scalar", lambda h: h.activation(psb[:], pA[:], AF.Exp))
                        p.op("tensor", lambda h, kb=kb, hh=hh, nkb=nkb: h.matmul(
                            pB[0:65, :], vsb[:, 260 * kb + 65 * hh:260 * kb + 65 * hh + 65],
                            psb[:], start=(kb == 0), stop=(kb == nkb - 1)))
                    p.op("vector", lambda h: h.tensor_copy(osb[0:65, :], pB[0:65, :]))
                    p.op("vector", lambda h: h.reciprocal(rd[:], osb[64:65, :]))
                    p.op("tensor", lambda h: h.matmul(
                        pC[0:64, :], onesc[:], rd[:], start=True, stop=True))
                    p.op("vector", lambda h: h.tensor_mul(onrm[:], osb[0:64, :], pC[0:64, :]))
                    p.dma("gpsimd", lambda h, h0=h0, hh=hh, q0=q0: h.dma_start(
                        out=ontd[64 * (h0 + hh):64 * (h0 + hh) + 64, q0:q0 + 512], in_=onrm[:]))
        for m in range(8):
            for k in range(8):
                load_lhsT(wo, k, 128 * m)
            for n in range(8):
                for k in range(8):
                    p.dma("sync", lambda h, k=k, n=n: h.dma_start(
                        out=ontb[:], in_=ontd[128 * k:128 * k + 128, 512 * n:512 * n + 512]))
                    p.op("tensor", lambda h, k=k, n=n, outer=outer: h.matmul(
                        pD[:], wbf[:, 128 * k:128 * k + 128], ontb[:],
                        start=(k == 0), stop=(k == 7 and not outer)))
                if outer:
                    p.op("tensor", lambda h, m=m: h.matmul(
                        pD[:], brsb[0:1, 3 * D + 128 * m:3 * D + 128 * m + 128], onesr[:],
                        start=False, stop=True))
                    p.op("vector", lambda h: h.tensor_copy(stg[:], pD[:]))
                else:
                    p.dma("sync", lambda h, m=m, n=n: h.dma_start(
                        out=x32[:], in_=xnT[128 * m:128 * m + 128, 512 * n:512 * n + 512]))
                    p.op("vector", lambda h: h.tensor_add(stg[:], pD[:], x32[:]))
                p.dma("gpsimd", lambda h, m=m, n=n, sout=sout: h.dma_start(
                    out=sout[128 * m:128 * m + 128, 512 * n:512 * n + 512], in_=stg[:]))
    last = p.last
    p.op("gpsimd", lambda h: h.engine_nop(), deps=[last])
    return p


# version 03
# speedup vs baseline: 1.3896x; 1.2153x over previous
import numpy as np
import ml_dtypes
from contextlib import ExitStack

import concourse.bass as bass
import concourse.mybir as mybir
from concourse.bass_utils import run_bass_kernel_spmd

F32 = mybir.dt.float32
BF16 = mybir.dt.bfloat16
AF = mybir.ActivationFunctionType
ALU = mybir.AluOpType
NPBF = ml_dtypes.bfloat16

D = 1024
T = 4096
TH = 2048
NH = 16
HD = 64
EPS = 1e-5


class Prog:
    ENG = ("sync", "tensor", "vector", "scalar", "gpsimd")

    def __init__(self, name):
        self.nc = bass.Bass("TRN2", target_bir_lowering=False)
        self.name = name
        self.ctx = ExitStack()
        self.q = {e: [] for e in self.ENG}
        self.cnt = {e: 0 for e in self.ENG}
        self.last = None
        self._n = 0

    def din(self, name, shape, dt=F32):
        return self.nc.dram_tensor(name, list(shape), dt, kind="ExternalInput").ap()

    def dout(self, name, shape, dt=F32):
        return self.nc.dram_tensor(name, list(shape), dt, kind="ExternalOutput").ap()

    def sb(self, name, shape, dt=F32):
        return self.ctx.enter_context(self.nc.sbuf_tensor(name, list(shape), dt))

    def ps(self, name, shape, dt=F32):
        return self.ctx.enter_context(self.nc.psum_tensor(name, list(shape), dt))

    def _emit(self, eng, fn, inc, deps, mode="chain"):
        if mode == "chain":
            my = list(deps) if deps is not None else ([self.last] if self.last else [])
        else:
            my = []
        if mode == "mid":
            inc = 0
        def closure(h, _fn=fn, _deps=my, _eng=eng, _inc=inc):
            for d in _deps:
                if d is not None and not (d[0] == _eng and d[1] <= self._done[_eng]):
                    h.wait_ge(self._sems[d[0]], d[1])
            r = _fn(h)
            if _inc:
                r.then_inc(self._sems[_eng], _inc)
        self.q[eng].append(closure)
        if inc:
            self.cnt[eng] += inc
            tok = (eng, self.cnt[eng])
            self.last = tok
        else:
            tok = (eng, self.cnt[eng])
        return tok

    def op(self, eng, fn, deps=None, mode="chain"):
        return self._emit(eng, fn, 1, deps, mode)

    def dma(self, eng, fn, deps=None, mode="chain"):
        return self._emit(eng, fn, 16, deps, mode)

    def build(self):
        sems = {}
        for e in self.ENG:
            sems[e] = self.ctx.enter_context(self.nc.semaphore("sem_" + e))
        self._sems = sems
        self._done = {e: 0 for e in self.ENG}
        block = self.ctx.enter_context(self.nc.Block())

        def runq(e):
            def body(h):
                for f in self.q[e]:
                    f(h)
            return body

        for e in self.ENG:
            deco = getattr(block, e)
            deco(runq(e))
        self.ctx.close()
        return self.nc


def _np_f32(x):
    return np.ascontiguousarray(x, np.float32)


def _np_bf(x):
    return np.ascontiguousarray(np.asarray(x).astype(NPBF))


def build_ln(name, ncols, gamma=False):
    p = Prog(name)
    xin = p.din("xin", [D, ncols])
    xout = p.dout("xout", [D, ncols])
    if gamma:
        gcol = p.din("gcol", [128, 8])
        bcol = p.din("bcol", [128, 8])

    CT = 512
    nct = ncols // CT
    xs = [p.sb(f"x{i}", [128, CT]) for i in range(8)]
    sq = p.sb("sq", [128, CT])
    mrow = p.sb("mrow", [1, CT])
    vrow = p.sb("vrow", [1, CT])
    rrow = p.sb("rrow", [1, CT])
    t1 = p.sb("t1", [128, CT])
    ones_col = p.sb("ones_col", [128, 1])
    eps_col = p.sb("eps_col", [128, 1])
    ones_row1 = p.sb("ones_row1", [1, 128])
    outbuf = p.sb("outbuf", [128, CT])
    sgc = p.sb("sgc", [128, 8]) if gamma else None
    sbc = p.sb("sbc", [128, 8]) if gamma else None
    pm = p.ps("pm", [128, CT])
    pv = p.ps("pv", [128, CT])
    pb1 = p.ps("pb1", [128, CT])
    pb2 = p.ps("pb2", [128, CT])

    p.op("gpsimd", lambda h: h.memset(ones_col[:], 1.0 / 1024.0))
    p.op("gpsimd", lambda h: h.memset(ones_row1[:], 1.0))
    p.op("gpsimd", lambda h: h.memset(eps_col[:], EPS))
    if gamma:
        p.dma("sync", lambda h: h.dma_start(out=sgc[:], in_=gcol))
        p.dma("sync", lambda h: h.dma_start(out=sbc[:], in_=bcol))

    for tt in range(nct):
        c0 = tt * CT
        for c in range(8):
            p.dma("sync", lambda h, c=c, c0=c0: h.dma_start(
                out=xs[c][:], in_=xin[128 * c:128 * c + 128, c0:c0 + CT]))
        for c in range(8):
            p.op("tensor", lambda h, c=c: h.matmul(
                pm[0:1, :], ones_col[:], xs[c][:], start=(c == 0), stop=(c == 7)))
        for c in range(8):
            p.op("scalar", lambda h, c=c: h.activation(sq[:], xs[c][:], AF.Square))
            p.op("tensor", lambda h, c=c: h.matmul(
                pv[0:1, :], ones_col[:], sq[:], start=(c == 0), stop=(c == 7)))
        p.op("vector", lambda h: h.tensor_copy(mrow[:], pm[0:1, :]))
        p.op("vector", lambda h: h.scalar_tensor_tensor(
            vrow[:], mrow[:], -1.0, mrow[:], op0=ALU.mult, op1=ALU.mult))
        p.op("vector", lambda h: h.tensor_add(vrow[:], vrow[:], pv[0:1, :]))
        p.op("scalar", lambda h: h.activation(rrow[:], vrow[:], AF.Ln, bias=eps_col[0:1, :], scale=1.0))
        p.op("scalar", lambda h: h.activation(rrow[:], rrow[:], AF.Exp, scale=-0.5))
        p.op("tensor", lambda h: h.matmul(pb1[:], ones_row1[:], mrow[:], start=True, stop=True))
        p.op("tensor", lambda h: h.matmul(pb2[:], ones_row1[:], rrow[:], start=True, stop=True))
        for c in range(8):
            p.op("vector", lambda h, c=c: h.tensor_sub(t1[:], xs[c][:], pb1[:]))
            p.op("vector", lambda h, c=c: h.tensor_mul(outbuf[:], t1[:], pb2[:]))
            if gamma:
                p.op("vector", lambda h, c=c: h.tensor_scalar(
                    outbuf[:], outbuf[:], sgc[:, c:c + 1], sbc[:, c:c + 1],
                    op0=ALU.mult, op1=ALU.add))
            p.dma("gpsimd", lambda h, c=c, c0=c0: h.dma_start(
                out=xout[128 * c:128 * c + 128, c0:c0 + CT], in_=outbuf[:]))
    last = p.last
    p.op("gpsimd", lambda h: h.engine_nop(), deps=[last])
    return p


def build_attn(name, outer, debug=False):
    p = Prog(name)
    xnT = p.din("xnT", [D, T])
    if outer:
        wq = p.din("wq", [D, D]); wk = p.din("wk", [D, D])
        wv = p.din("wv", [D, D]); wo = p.din("wo", [D, D])
        brows = p.din("brows", [1, 4 * D])
        augK = p.din("augK", [4, T], BF16)
        augQ = p.din("augQ", [4, T], BF16)
    else:
        wqkv = p.din("wqkv", [D, 3 * D])
        wo = p.din("wo", [D, D])
        gcol = p.din("gcol", [128, 8])
        bcol = p.din("bcol", [128, 8])
    masks = p.din("masks", [4, 128, 512])
    sout = p.dout("sout", [D, T])

    xbf = p.sb("xbf", [128, 8 * T], BF16)
    wstage = p.sb("wstage", [128, TH])
    wbf = p.sb("wbf", [128, 8 * 128], BF16)
    qg = p.sb("qg", [128, 2 * T], BF16)
    kg = p.sb("kg", [128, 2 * T], BF16)
    vsb = p.sb("vsb", [128, 32 * 260], BF16)
    psb = p.sb("psb", [128, 512], BF16)
    osb = p.sb("osb", [128, 512])
    rd = p.sb("rd", [1, 512])
    onesr = p.sb("onesr", [1, 512], BF16)
    onesc = p.sb("onesc", [1, 64])
    brsb = p.sb("brsb", [1, 4 * D], BF16) if outer else None
    brst = p.sb("brst", [1, 4 * D]) if outer else None
    maskb = p.sb("maskb", [128, 4 * 512])
    ontd = p.nc.dram_tensor("ontd", [D, T], BF16,
                            kind="Internal" if not debug else "ExternalOutput").ap()
    ontb = p.sb("ontb", [128, 512], BF16)
    onrm = p.sb("onrm", [64, 512], BF16)
    aKb = p.sb("aKb", [4, T], BF16) if outer else None
    aQb = p.sb("aQb", [4, T], BF16) if outer else None
    stg = p.sb("stg", [128, 512])
    vrhs = p.sb("vrhs", [128, 8 * 256], BF16)
    x32 = p.sb("x32", [128, 512])
    sq = p.sb("sq", [128, 512])
    mrow = p.sb("mrow", [1, 512]); rrow = p.sb("rrow", [1, 512])
    vrow = p.sb("vrow", [1, 512])
    onescol = p.sb("onescol", [128, 1])
    eps_col = p.sb("eps_col", [128, 1])
    ones_row1 = p.sb("ones_row1", [1, 128])
    sgc = p.sb("sgc", [128, 8]) if not outer else None
    sbc = p.sb("sbc", [128, 8]) if not outer else None

    pA = p.ps("pA", [128, 512])
    pB = p.ps("pB", [128, 512])
    pC = p.ps("pC", [128, 512])
    pD = p.ps("pD", [128, 512])

    p.op("gpsimd", lambda h: h.memset(onesr[:], 1.0))
    p.op("gpsimd", lambda h: h.memset(onesc[:], 1.0))
    p.op("gpsimd", lambda h: h.memset(onescol[:], 1.0 / 1024.0))
    p.op("gpsimd", lambda h: h.memset(eps_col[:], EPS))
    p.op("gpsimd", lambda h: h.memset(ones_row1[:], 1.0))
    p.op("gpsimd", lambda h: h.memset(
        vsb[:].rearrange("p (t h c) -> p t h c", t=32, h=4)[:, :, :, 64:65], 1.0))
    for d in range(4):
        p.dma("sync", lambda h, d=d: h.dma_start(out=maskb[:, 512 * d:512 * d + 512], in_=masks[d]))
    if outer:
        p.dma("sync", lambda h: h.dma_start(out=brst[:], in_=brows))
        p.op("vector", lambda h: h.tensor_copy(brsb[:], brst[:]))
        p.dma("sync", lambda h: h.dma_start(out=aKb[:], in_=augK))
        p.dma("sync", lambda h: h.dma_start(out=aQb[:], in_=augQ))
    else:
        p.dma("sync", lambda h: h.dma_start(out=sgc[:], in_=gcol))
        p.dma("sync", lambda h: h.dma_start(out=sbc[:], in_=bcol))

    if not outer:
        for tt in range(8):
            c0 = tt * 512
            for c in range(8):
                p.dma("sync", lambda h, c=c, c0=c0: h.dma_start(
                    out=x32[:], in_=xnT[128 * c:128 * c + 128, c0:c0 + 512]))
                p.op("tensor", lambda h, c=c: h.matmul(
                    pC[0:1, :], onescol[:], x32[:], start=(c == 0), stop=(c == 7)))
                p.op("scalar", lambda h: h.activation(sq[:], x32[:], AF.Square))
                p.op("tensor", lambda h, c=c: h.matmul(
                    pD[0:1, :], onescol[:], sq[:], start=(c == 0), stop=(c == 7)))
            p.op("vector", lambda h: h.tensor_copy(mrow[:], pC[0:1, :]))
            p.op("vector", lambda h: h.scalar_tensor_tensor(
                vrow[:], mrow[:], -1.0, mrow[:], op0=ALU.mult, op1=ALU.mult))
            p.op("vector", lambda h: h.tensor_add(vrow[:], vrow[:], pD[0:1, :]))
            p.op("scalar", lambda h: h.activation(vrow[:], vrow[:], AF.Ln, bias=eps_col[0:1, :], scale=1.0))
            p.op("scalar", lambda h: h.activation(rrow[:], vrow[:], AF.Exp, scale=-0.5))
            p.op("tensor", lambda h: h.matmul(pC[:], ones_row1[:], mrow[:], start=True, stop=True))
            p.op("tensor", lambda h: h.matmul(pD[:], ones_row1[:], rrow[:], start=True, stop=True))
            for c in range(8):
                p.dma("sync", lambda h, c=c, c0=c0: h.dma_start(
                    out=x32[:], in_=xnT[128 * c:128 * c + 128, c0:c0 + 512]))
                p.op("vector", lambda h: h.tensor_sub(stg[:], x32[:], pC[:]))
                p.op("vector", lambda h: h.tensor_mul(stg[:], stg[:], pD[:]))
                p.op("vector", lambda h, c=c, c0=c0: h.tensor_scalar(
                    xbf[:, c * T + c0:c * T + c0 + 512], stg[:],
                    sgc[:, c:c + 1], sbc[:, c:c + 1], op0=ALU.mult, op1=ALU.add))
    else:
        for c in range(8):
            for hh2 in range(2):
                p.dma("sync", lambda h, c=c, hh2=hh2: h.dma_start(
                    out=wstage[:], in_=xnT[128 * c:128 * c + 128, TH * hh2:TH * hh2 + TH]))
                p.op("vector", lambda h, c=c, hh2=hh2: h.tensor_copy(
                    xbf[:, c * T + TH * hh2:c * T + TH * hh2 + TH], wstage[:]))

    def load_lhsT(src_ap, k, mcol0, tag):
        p.dma("sync", lambda h: h.dma_start(
            out=wstage[:, 0:128], in_=src_ap[128 * k:128 * k + 128, mcol0:mcol0 + 128]))
        p.op("vector", lambda h: h.tensor_copy(wbf[:, 128 * k:128 * k + 128], wstage[:, 0:128]))

    NKT = T // 128
    for grp in range(4):
        h0 = grp * 4
        r0 = h0 * HD
        for which, dst, scale in (("q", qg, 0.125), ("k", kg, 1.0)):
            if outer:
                wsrc = wq if which == "q" else wk
                coff = r0
            else:
                wsrc = wqkv
                coff = r0 if which == "q" else D + r0
            for m in range(2):
                for k in range(8):
                    load_lhsT(wsrc, k, coff + 128 * m, which)
                for n in range(8):
                    for k in range(8):
                        p.op("tensor", lambda h, k=k, n=n: h.matmul(
                            pA[:], wbf[:, 128 * k:128 * k + 128],
                            xbf[:, k * T + 512 * n:k * T + 512 * n + 512],
                            start=(k == 0), stop=(k == 7 and not outer)))
                    if outer:
                        bi = 0 if which == "q" else 1
                        p.op("tensor", lambda h, m=m, bi=bi, coff=coff: h.matmul(
                            pA[:], brsb[0:1, bi * D + coff + 128 * m:bi * D + coff + 128 * m + 128],
                            onesr[:], start=False, stop=True))
                    p.op("scalar", lambda h, dst=dst, m=m, n=n, scale=scale: h.activation(
                        dst[:, m * T + 512 * n:m * T + 512 * n + 512], pA[:],
                        AF.Copy, scale=scale))
        if outer:
            vw = wv; vcoff = r0
        else:
            vw = wqkv; vcoff = 2 * D + r0
        for k in range(8):
            p.dma("sync", lambda h, k=k, vcoff=vcoff: h.dma_start(
                out=wstage[:, 0:256], in_=vw[128 * k:128 * k + 128, vcoff:vcoff + 256]))
            p.op("vector", lambda h, k=k: h.tensor_copy(
                vrhs[:, 256 * k:256 * k + 256], wstage[:, 0:256]))
        for kt in range(NKT):
            for k in range(8):
                p.op("tensor", lambda h, k=k, kt=kt: h.matmul(
                    pA[:, 0:256], xbf[:, k * T + 128 * kt:k * T + 128 * kt + 128],
                    vrhs[:, 256 * k:256 * k + 256], start=(k == 0), stop=(k == 7 and not outer)))
            if outer:
                p.op("tensor", lambda h, vcoff=vcoff: h.matmul(
                    pA[:, 0:256], onesr[0:1, 0:128], brsb[0:1, 2 * D + vcoff:2 * D + vcoff + 256],
                    start=False, stop=True))
            for hh in range(4):
                p.op("vector", lambda h, kt=kt, hh=hh: h.tensor_copy(
                    vsb[:, 260 * kt + 65 * hh:260 * kt + 65 * hh + 64],
                    pA[:, 64 * hh:64 * hh + 64]))
        if debug and grp == 0:
            qdump = p.dout("qdump", [128, 2 * T], BF16)
            kdump = p.dout("kdump", [128, 2 * T], BF16)
            vdump = p.dout("vdump", [128, 32 * 260], BF16)
            p.dma("gpsimd", lambda h: h.dma_start(out=qdump, in_=qg[:]))
            p.dma("gpsimd", lambda h: h.dma_start(out=kdump, in_=kg[:]))
            p.dma("gpsimd", lambda h: h.dma_start(out=vdump, in_=vsb[:]))
        for qt in range(8):
            q0 = qt * 512
            nkb = min(32, 4 * qt + 4)
            for hh in range(4):
                hloc_t = hh // 2
                hbase = 64 * (hh % 2)
                for kb in range(nkb):
                    p.op("tensor", lambda h, hloc_t=hloc_t, hbase=hbase, kb=kb, q0=q0: h.matmul(
                        pA[:], kg[hbase:hbase + 64, hloc_t * T + 128 * kb:hloc_t * T + 128 * kb + 128],
                        qg[hbase:hbase + 64, hloc_t * T + q0:hloc_t * T + q0 + 512],
                        start=True, stop=not outer))
                    if outer:
                        p.op("tensor", lambda h, kb=kb, q0=q0: h.matmul(
                            pA[:], aKb[:, 128 * kb:128 * kb + 128], aQb[:, q0:q0 + 512],
                            start=False, stop=True))
                    d = 128 * kb - q0
                    if 0 <= d < 512:
                        mi = d // 128
                        p.op("vector", lambda h, mi=mi: h.tensor_add(
                            pA[:], pA[:], maskb[:, 512 * mi:512 * mi + 512]))
                    p.op("scalar", lambda h: h.activation(psb[:], pA[:], AF.Exp))
                    if debug and grp == 0 and qt == 0 and hh == 0 and kb == 0:
                        scd = p.dout("scd", [128, 512])
                        psd = p.dout("psd", [128, 512], BF16)
                        p.op("vector", lambda h: h.tensor_copy(stg[:], pA[:]))
                        p.dma("gpsimd", lambda h: h.dma_start(out=scd, in_=stg[:]))
                        p.dma("gpsimd", lambda h: h.dma_start(out=psd, in_=psb[:]))
                    p.op("tensor", lambda h, kb=kb, hh=hh: h.matmul(
                        pB[0:65, :], vsb[:, 260 * kb + 65 * hh:260 * kb + 65 * hh + 65],
                        psb[:], start=(kb == 0), stop=(kb == nkb - 1)))
                p.op("vector", lambda h: h.tensor_copy(osb[0:65, :], pB[0:65, :]))
                if debug and grp == 0 and qt == 0 and hh == 0:
                    osd = p.dout("osd", [128, 512])
                    p.dma("gpsimd", lambda h: h.dma_start(out=osd, in_=osb[:]))
                p.op("vector", lambda h: h.reciprocal(rd[:], osb[64:65, :]))
                p.op("tensor", lambda h: h.matmul(
                    pC[0:64, :], onesc[:], rd[:], start=True, stop=True))
                p.op("vector", lambda h: h.tensor_mul(onrm[:], osb[0:64, :], pC[0:64, :]))
                p.dma("gpsimd", lambda h, h0=h0, hh=hh, q0=q0: h.dma_start(
                    out=ontd[64 * (h0 + hh):64 * (h0 + hh) + 64, q0:q0 + 512], in_=onrm[:]))

    for m in range(8):
        for k in range(8):
            load_lhsT(wo, k, 128 * m, "wo")
        for n in range(8):
            for k in range(8):
                p.dma("sync", lambda h, k=k, n=n: h.dma_start(
                    out=ontb[:], in_=ontd[128 * k:128 * k + 128, 512 * n:512 * n + 512]))
                p.op("tensor", lambda h, k=k, n=n: h.matmul(
                    pD[:], wbf[:, 128 * k:128 * k + 128], ontb[:],
                    start=(k == 0), stop=(k == 7 and not outer)))
            if outer:
                p.op("tensor", lambda h, m=m: h.matmul(
                    pD[:], brsb[0:1, 3 * D + 128 * m:3 * D + 128 * m + 128], onesr[:],
                    start=False, stop=True))
                p.op("vector", lambda h: h.tensor_copy(stg[:], pD[:]))
            else:
                p.dma("sync", lambda h, m=m, n=n: h.dma_start(
                    out=x32[:], in_=xnT[128 * m:128 * m + 128, 512 * n:512 * n + 512]))
                p.op("vector", lambda h: h.tensor_add(stg[:], pD[:], x32[:]))
            p.dma("gpsimd", lambda h, m=m, n=n: h.dma_start(
                out=sout[128 * m:128 * m + 128, 512 * n:512 * n + 512], in_=stg[:]))
    last = p.last
    p.op("gpsimd", lambda h: h.engine_nop(), deps=[last])
    return p


def run_prog(p, in_maps, n=None):
    nc = p.build()
    ids = list(range(len(in_maps)))
    res = run_bass_kernel_spmd(nc, in_maps, core_ids=ids)
    return res.results


def build_moe(name, first):
    HID = 4096 if first else 2816
    NMH = HID // 128
    p = Prog(name)
    if first:
        xr = p.din("xr", [D, TH])
        sT = p.din("sT", [D, TH])
        xT = p.din("xT", [D, TH])
        gcol = p.din("gcol", [128, 8]); bcol = p.din("bcol", [128, 8])
        b1c = p.din("b1c", [128, 8 * (HID // 128)])
        b2m = p.din("b2m", [8, D])
        x1o = p.dout("x1o", [D, TH])
        xn1o = p.dout("xn1o", [D, TH])
    else:
        xn1 = p.din("xn1", [D, TH])
        x1 = p.din("x1", [D, TH])
        s1 = p.din("s1", [D, TH])
        x2o = p.dout("x2o", [D, TH])
    wg = p.din("wg", [D, 8])
    w1 = p.din("w1", [8, D, HID])
    w2 = p.din("w2", [8, HID, D])
    if not first:
        w3 = p.din("w3", [8, D, HID])
    selm = p.din("selm", [8, 8 * 128], BF16)
    ident = p.din("ident", [128, 128])

    mbf = p.sb("mbf", [128, 8 * TH], BF16)
    wst = p.sb("wst", [128, HID // 2])
    w1bf = p.sb("w1bf", [128, 8 * HID], BF16)
    w3bf = p.sb("w3bf", [128, 8 * HID], BF16) if not first else None
    w2bf = p.sb("w2bf", [128, D], BF16)
    hebf = p.sb("hebf", [128, NMH * 512], BF16)
    sabf = p.sb("sabf", [128, 512], BF16) if not first else None
    acc = p.sb("acc", [128, 8 * 512])
    tmp = p.sb("tmp", [128, 512])
    gb = p.sb("gb", [128, 512])
    stg = p.sb("stg", [128, 512])
    x32 = p.sb("x32", [128, 512])
    y32 = p.sb("y32", [128, 512])
    lgs = p.sb("lgs", [128, 16 * 8])
    gts = p.sb("gts", [128, 16 * 8])
    gt8b = p.sb("gt8b", [8, TH], BF16)
    selb = p.sb("selb", [8, 8 * 128], BF16)
    idb = p.sb("idb", [128, 128])
    wgb = p.sb("wgb", [128, 64], BF16)
    srt = p.sb("srt", [128, 8])
    c1 = p.sb("c1", [128, 1]); c2 = p.sb("c2", [128, 1]); c3 = p.sb("c3", [128, 1])
    sq = p.sb("sq", [128, 512])
    mrow = p.sb("mrow", [1, 512]); vrow = p.sb("vrow", [1, 512]); rrow = p.sb("rrow", [1, 512])
    onescol = p.sb("onescol", [128, 1])
    ones_row1 = p.sb("ones_row1", [1, 128])
    eps_col = p.sb("eps_col", [128, 1])
    b1s = p.sb("b1s", [128, 8 * (HID // 128)]) if first else None
    b2s = p.sb("b2s", [8, D]) if first else None
    b2sb = p.sb("b2sb", [8, D], BF16) if first else None
    sgc = p.sb("sgc", [128, 8]) if first else None
    sbc = p.sb("sbc", [128, 8]) if first else None


    pZ1 = p.ps("pZ1", [128, 2048])
    pZ2 = p.ps("pZ2", [128, 2048])

    p.op("gpsimd", lambda h: h.memset(onescol[:], 1.0 / 1024.0))
    p.op("gpsimd", lambda h: h.memset(ones_row1[:], 1.0))
    p.op("gpsimd", lambda h: h.memset(eps_col[:], EPS))
    p.dma("sync", lambda h: h.dma_start(out=selb[:], in_=selm))
    p.dma("sync", lambda h: h.dma_start(out=idb[:], in_=ident))
    if first:
        p.dma("sync", lambda h: h.dma_start(out=sgc[:], in_=gcol))
        p.dma("sync", lambda h: h.dma_start(out=sbc[:], in_=bcol))
        p.dma("sync", lambda h: h.dma_start(out=b1s[:], in_=b1c))
        p.dma("sync", lambda h: h.dma_start(out=b2s[:], in_=b2m))
        p.op("vector", lambda h: h.tensor_copy(b2sb[:], b2s[:]))
    for c in range(8):
        p.dma("sync", lambda h, c=c: h.dma_start(out=wst[:, 0:8], in_=wg[128 * c:128 * c + 128, :]))
        p.op("vector", lambda h, c=c: h.tensor_copy(wgb[:, 8 * c:8 * c + 8], wst[:, 0:8]))

    src_in = xr if first else xn1
    for tt in range(4):
        c0 = tt * 512
        if first:
            for c in range(8):
                p.dma("sync", lambda h, c=c, c0=c0: h.dma_start(
                    out=x32[:], in_=src_in[128 * c:128 * c + 128, c0:c0 + 512]))
                p.op("tensor", lambda h, c=c: h.matmul(
                    pZ1[0:1, 0:512], onescol[:], x32[:], start=(c == 0), stop=(c == 7)))
                p.op("scalar", lambda h: h.activation(sq[:], x32[:], AF.Square))
                p.op("tensor", lambda h, c=c: h.matmul(
                    pZ1[0:1, 512:1024], onescol[:], sq[:], start=(c == 0), stop=(c == 7)))
            p.op("vector", lambda h: h.tensor_copy(mrow[:], pZ1[0:1, 0:512]))
            p.op("vector", lambda h: h.scalar_tensor_tensor(
                vrow[:], mrow[:], -1.0, mrow[:], op0=ALU.mult, op1=ALU.mult))
            p.op("vector", lambda h: h.tensor_add(vrow[:], vrow[:], pZ1[0:1, 512:1024]))
            p.op("scalar", lambda h: h.activation(vrow[:], vrow[:], AF.Ln, bias=eps_col[0:1, :], scale=1.0))
            p.op("scalar", lambda h: h.activation(rrow[:], vrow[:], AF.Exp, scale=-0.5))
            p.op("tensor", lambda h: h.matmul(pZ1[:, 0:512], ones_row1[:], mrow[:], start=True, stop=True))
            p.op("tensor", lambda h: h.matmul(pZ1[:, 512:1024], ones_row1[:], rrow[:], start=True, stop=True))
            for c in range(8):
                p.dma("sync", lambda h, c=c, c0=c0: h.dma_start(
                    out=x32[:], in_=src_in[128 * c:128 * c + 128, c0:c0 + 512]))
                p.op("vector", lambda h: h.tensor_sub(stg[:], x32[:], pZ1[:, 0:512]))
                p.op("vector", lambda h: h.tensor_mul(stg[:], stg[:], pZ1[:, 512:1024]))
                p.op("vector", lambda h, c=c, c0=c0: h.tensor_scalar(
                    mbf[:, c * TH + c0:c * TH + c0 + 512], stg[:],
                    sgc[:, c:c + 1], sbc[:, c:c + 1], op0=ALU.mult, op1=ALU.add))
        else:
            for c in range(8):
                p.dma("sync", lambda h, c=c, c0=c0: h.dma_start(
                    out=x32[:], in_=src_in[128 * c:128 * c + 128, c0:c0 + 512]))
                p.op("vector", lambda h, c=c, c0=c0: h.tensor_copy(
                    mbf[:, c * TH + c0:c * TH + c0 + 512], x32[:]))

    for tt in range(4):
        c0 = tt * 512
        for c in range(8):
            p.op("tensor", lambda h, c=c, c0=c0: h.matmul(
                pZ1[0:8, 0:512], wgb[:, 8 * c:8 * c + 8],
                mbf[:, c * TH + c0:c * TH + c0 + 512], start=(c == 0), stop=(c == 7)))
        p.op("vector", lambda h: h.tensor_copy(stg[0:8, :], pZ1[0:8, 0:512]))
        for q in range(4):
            ti = tt * 4 + q
            p.op("tensor", lambda h, q=q: h.transpose(
                pZ2[0:128, 0:8], stg[0:8, 128 * q:128 * q + 128], idb[0:8, 0:8]))
            p.op("vector", lambda h, ti=ti: h.tensor_copy(lgs[:, 8 * ti:8 * ti + 8], pZ2[0:128, 0:8]))
            p.op("vector", lambda h, ti=ti: h.max(srt[:], lgs[:, 8 * ti:8 * ti + 8]))
            p.op("vector", lambda h: h.tensor_sub(c1[:], srt[:, 1:2], srt[:, 0:1]))
            p.op("scalar", lambda h: h.activation(c1[:], c1[:], AF.Exp))
            p.op("vector", lambda h: h.tensor_scalar(c2[:], c1[:], 1.0, None, op0=ALU.add))
            p.op("vector", lambda h: h.reciprocal(c2[:], c2[:]))
            p.op("vector", lambda h: h.tensor_scalar(c3[:], c2[:], -1.0, 1.0, op0=ALU.mult, op1=ALU.add))
            p.op("vector", lambda h, ti=ti: h.tensor_scalar(
                gts[:, 8 * ti:8 * ti + 8], lgs[:, 8 * ti:8 * ti + 8], srt[:, 0:1], c2[:],
                op0=ALU.is_equal, op1=ALU.mult))
            p.op("vector", lambda h, ti=ti: h.tensor_scalar(
                tmp[:, 0:8], lgs[:, 8 * ti:8 * ti + 8], srt[:, 1:2], c3[:],
                op0=ALU.is_equal, op1=ALU.mult))
            p.op("vector", lambda h, ti=ti: h.tensor_add(
                gts[:, 8 * ti:8 * ti + 8], gts[:, 8 * ti:8 * ti + 8], tmp[:, 0:8]))
            p.op("tensor", lambda h, ti=ti: h.transpose(
                pZ2[0:8, 0:128], gts[:, 8 * ti:8 * ti + 8], idb[:]))
            p.op("vector", lambda h, ti=ti: h.tensor_copy(
                gt8b[:, 128 * ti:128 * ti + 128], pZ2[0:8, 0:128]))

    for tq in range(4):
        c0 = tq * 512
        for mo in range(8):
            p.op("gpsimd", lambda h, mo=mo: h.memset(acc[:, 512 * mo:512 * mo + 512], 0.0))
        for e in range(8):
            HH = HID // 2
            for k in range(8):
                for z in range(2):
                    p.dma("sync", lambda h, e=e, k=k, z=z: h.dma_start(
                        out=wst[:], in_=w1[e, 128 * k:128 * k + 128, z * HH:z * HH + HH]))
                    p.op("vector", lambda h, k=k, z=z, HH=HH: h.tensor_copy(
                        w1bf[:, HID * k + z * HH:HID * k + z * HH + HH], wst[:]))
                    if not first:
                        p.dma("sync", lambda h, e=e, k=k, z=z: h.dma_start(
                            out=wst[:], in_=w3[e, 128 * k:128 * k + 128, z * HH:z * HH + HH]))
                        p.op("vector", lambda h, k=k, z=z, HH=HH: h.tensor_copy(
                            w3bf[:, HID * k + z * HH:HID * k + z * HH + HH], wst[:]))
            for mh in range(NMH):
                for k in range(8):
                    md = "chain" if k == 0 else ("tail" if k == 7 else "mid")
                    p.op("tensor", lambda h, k=k, mh=mh, c0=c0: h.matmul(
                        pZ1[:, 0:512], w1bf[:, HID * k + 128 * mh:HID * k + 128 * mh + 128],
                        mbf[:, k * TH + c0:k * TH + c0 + 512], start=(k == 0), stop=(k == 7)), mode=md)
                if first:
                    p.op("scalar", lambda h, mh=mh, e=e: h.activation(
                        hebf[:, 512 * mh:512 * mh + 512], pZ1[:, 0:512],
                        AF.Gelu_apprx_tanh, bias=b1s[:, NMH * e + mh:NMH * e + mh + 1], scale=1.0))
                else:
                    for k in range(8):
                        md = "chain" if k == 0 else ("tail" if k == 7 else "mid")
                        p.op("tensor", lambda h, k=k, mh=mh, c0=c0: h.matmul(
                            pZ2[:, 0:512], w3bf[:, HID * k + 128 * mh:HID * k + 128 * mh + 128],
                            mbf[:, k * TH + c0:k * TH + c0 + 512], start=(k == 0), stop=(k == 7)), mode=md)
                    p.op("scalar", lambda h: h.activation(sabf[:], pZ1[:, 0:512], AF.Silu))
                    p.op("vector", lambda h, mh=mh: h.tensor_tensor(
                        hebf[:, 512 * mh:512 * mh + 512], sabf[:], pZ2[:, 0:512], op=ALU.mult))
            p.op("tensor", lambda h, e=e, c0=c0: h.matmul(
                pZ1[:, 0:512], selb[:, 128 * e:128 * e + 128], gt8b[:, c0:c0 + 512],
                start=True, stop=True))
            p.op("vector", lambda h: h.tensor_copy(gb[:], pZ1[:, 0:512]))
            for k in range(NMH):
                p.dma("sync", lambda h, e=e, k=k: h.dma_start(
                    out=wst[:, 0:D], in_=w2[e, 128 * k:128 * k + 128, :]))
                p.op("vector", lambda h, k=k: h.tensor_copy(w2bf[:], wst[:, 0:D]))
                for mo in range(8):
                    pz = pZ1 if mo < 4 else pZ2
                    md = "chain" if mo == 0 else ("tail" if mo == 7 else "mid")
                    p.op("tensor", lambda h, k=k, mo=mo, pz=pz: h.matmul(
                        pz[:, 512 * (mo % 4):512 * (mo % 4) + 512],
                        w2bf[:, 128 * mo:128 * mo + 128],
                        hebf[:, 512 * k:512 * k + 512], start=(k == 0), stop=(k == NMH - 1)), mode=md)
            for mo in range(8):
                pz = pZ1 if mo < 4 else pZ2
                p.op("vector", lambda h, mo=mo, pz=pz: h.tensor_mul(
                    tmp[:], pz[:, 512 * (mo % 4):512 * (mo % 4) + 512], gb[:]))
                p.op("vector", lambda h, mo=mo: h.tensor_add(
                    acc[:, 512 * mo:512 * mo + 512], acc[:, 512 * mo:512 * mo + 512], tmp[:]))
        if first:
            for mo in range(8):
                p.op("tensor", lambda h, mo=mo, c0=c0: h.matmul(
                    pZ1[:, 0:512], b2sb[:, 128 * mo:128 * mo + 128], gt8b[:, c0:c0 + 512],
                    start=True, stop=True))
                p.op("vector", lambda h, mo=mo: h.tensor_add(
                    acc[:, 512 * mo:512 * mo + 512], acc[:, 512 * mo:512 * mo + 512], pZ1[:, 0:512]))
            for mo in range(8):
                p.dma("sync", lambda h, mo=mo, c0=c0: h.dma_start(
                    out=x32[:], in_=xr[128 * mo:128 * mo + 128, c0:c0 + 512]))
                p.op("vector", lambda h, mo=mo: h.tensor_add(
                    y32[:], acc[:, 512 * mo:512 * mo + 512], x32[:]))
                p.op("scalar", lambda h: h.activation(stg[:], y32[:], AF.Silu))
                p.dma("sync", lambda h, mo=mo, c0=c0: h.dma_start(
                    out=x32[:], in_=sT[128 * mo:128 * mo + 128, c0:c0 + 512]))
                p.op("vector", lambda h: h.tensor_mul(stg[:], stg[:], x32[:]))
                p.dma("sync", lambda h, mo=mo, c0=c0: h.dma_start(
                    out=x32[:], in_=xT[128 * mo:128 * mo + 128, c0:c0 + 512]))
                p.op("vector", lambda h: h.tensor_add(y32[:], stg[:], x32[:]))
                p.dma("gpsimd", lambda h, mo=mo, c0=c0: h.dma_start(
                    out=x1o[128 * mo:128 * mo + 128, c0:c0 + 512], in_=y32[:]))
                p.op("tensor", lambda h, mo=mo: h.matmul(
                    pZ1[0:1, 0:512], onescol[:], y32[:], start=(mo == 0), stop=(mo == 7)))
                p.op("scalar", lambda h: h.activation(sq[:], y32[:], AF.Square))
                p.op("tensor", lambda h, mo=mo: h.matmul(
                    pZ1[0:1, 512:1024], onescol[:], sq[:], start=(mo == 0), stop=(mo == 7)))
            p.op("vector", lambda h: h.tensor_copy(mrow[:], pZ1[0:1, 0:512]))
            p.op("vector", lambda h: h.scalar_tensor_tensor(
                vrow[:], mrow[:], -1.0, mrow[:], op0=ALU.mult, op1=ALU.mult))
            p.op("vector", lambda h: h.tensor_add(vrow[:], vrow[:], pZ1[0:1, 512:1024]))
            p.op("scalar", lambda h: h.activation(vrow[:], vrow[:], AF.Ln, bias=eps_col[0:1, :], scale=1.0))
            p.op("scalar", lambda h: h.activation(rrow[:], vrow[:], AF.Exp, scale=-0.5))
            p.op("tensor", lambda h: h.matmul(pZ1[:, 0:512], ones_row1[:], mrow[:], start=True, stop=True))
            p.op("tensor", lambda h: h.matmul(pZ1[:, 512:1024], ones_row1[:], rrow[:], start=True, stop=True))
            for mo in range(8):
                p.dma("sync", lambda h, mo=mo, c0=c0: h.dma_start(
                    out=x32[:], in_=x1o[128 * mo:128 * mo + 128, c0:c0 + 512]))
                p.op("vector", lambda h: h.tensor_sub(stg[:], x32[:], pZ1[:, 0:512]))
                p.op("vector", lambda h: h.tensor_mul(stg[:], stg[:], pZ1[:, 512:1024]))
                p.dma("gpsimd", lambda h, mo=mo, c0=c0: h.dma_start(
                    out=xn1o[128 * mo:128 * mo + 128, c0:c0 + 512], in_=stg[:]))
        else:
            for mo in range(8):
                p.op("scalar", lambda h, mo=mo: h.activation(
                    stg[:], acc[:, 512 * mo:512 * mo + 512], AF.Silu))
                p.dma("sync", lambda h, mo=mo, c0=c0: h.dma_start(
                    out=x32[:], in_=s1[128 * mo:128 * mo + 128, c0:c0 + 512]))
                p.op("vector", lambda h: h.tensor_mul(stg[:], stg[:], x32[:]))
                p.dma("sync", lambda h, mo=mo, c0=c0: h.dma_start(
                    out=x32[:], in_=x1[128 * mo:128 * mo + 128, c0:c0 + 512]))
                p.op("vector", lambda h: h.tensor_add(stg[:], stg[:], x32[:]))
                p.dma("gpsimd", lambda h, mo=mo, c0=c0: h.dma_start(
                    out=x2o[128 * mo:128 * mo + 128, c0:c0 + 512], in_=stg[:]))
    last = p.last
    p.op("gpsimd", lambda h: h.engine_nop(), deps=[last])
    return p


def build_ssmk(name):
    p = Prog(name)
    ldt = p.din("ldt", [128, 1])
    lar = p.din("lar", [128, 32])
    aim = p.din("aim", [128, 32])
    cre = p.din("cre", [128, 32])
    cim = p.din("cim", [128, 32])
    Ko = p.dout("Ko", [128, 4096])

    PI = float(np.pi)
    dt = p.sb("dt", [128, 1])
    er = p.sb("er", [128, 32])
    ai = p.sb("ai", [128, 32])
    cr = p.sb("cr", [128, 32]); ci = p.sb("ci", [128, 32])
    dar = p.sb("dar", [128, 32])
    dai = p.sb("dai", [128, 32])
    t1 = p.sb("t1", [128, 32]); t2 = p.sb("t2", [128, 32]); t3 = p.sb("t3", [128, 32])
    sn = p.sb("sn", [128, 32]); cs = p.sb("cs", [128, 32])
    wr = p.sb("wr", [128, 32]); wi = p.sb("wi", [128, 32])
    wpr = p.sb("wpr", [128, 32]); wpi = p.sb("wpi", [128, 32])
    nwpi = p.sb("nwpi", [128, 32])
    cwr = p.sb("cwr", [128, 32]); cwi = p.sb("cwi", [128, 32]); ncwi = p.sb("ncwi", [128, 32])
    ur = p.sb("ur", [128, 2048]); ui = p.sb("ui", [128, 2048])
    vr = p.sb("vr", [128, 2048]); vi = p.sb("vi", [128, 2048]); nvi = p.sb("nvi", [128, 2048])
    pr = p.sb("pr", [128, 2048]); pi = p.sb("pi", [128, 2048])
    Ks = p.sb("Ks", [128, 4096])
    tl = p.sb("tl", [128, 64])

    for nm, dst, srcp in ((0, dt, ldt), (1, er, lar), (2, ai, aim), (3, cr, cre), (4, ci, cim)):
        p.dma("sync", lambda h, dst=dst, srcp=srcp: h.dma_start(out=dst[:], in_=srcp))
    p.op("scalar", lambda h: h.activation(dt[:], dt[:], AF.Exp))
    p.op("scalar", lambda h: h.activation(er[:], er[:], AF.Exp))
    p.op("vector", lambda h: h.tensor_scalar(dar[:], er[:], dt[:], -1.0, op0=ALU.mult, op1=ALU.mult))
    p.op("vector", lambda h: h.tensor_scalar(dai[:], ai[:], dt[:], None, op0=ALU.mult))

    ti32 = p.sb("ti32", [128, 32], mybir.dt.int32)

    def wrap_sin(dstt, add_half_pi):
        off = PI / 2 if add_half_pi else 0.0
        p.op("vector", lambda h, off=off: h.tensor_scalar(t3[:], dai[:], off, None, op0=ALU.add))
        p.op("vector", lambda h: h.tensor_scalar(t1[:], t3[:], 1.0 / (2 * PI), None, op0=ALU.mult))
        p.op("vector", lambda h: h.tensor_copy(ti32[:], t1[:]))
        p.op("vector", lambda h: h.tensor_copy(t1[:], ti32[:]))
        p.op("vector", lambda h: h.scalar_tensor_tensor(t1[:], t1[:], -2 * PI, t3[:], op0=ALU.mult, op1=ALU.add))
        p.op("vector", lambda h: h.tensor_scalar(t2[:], t1[:], PI, None, op0=ALU.is_gt))
        p.op("vector", lambda h: h.scalar_tensor_tensor(t1[:], t2[:], -2 * PI, t1[:], op0=ALU.mult, op1=ALU.add))
        p.op("vector", lambda h: h.tensor_scalar(t2[:], t1[:], -PI, None, op0=ALU.is_lt))
        p.op("vector", lambda h: h.scalar_tensor_tensor(t1[:], t2[:], 2 * PI, t1[:], op0=ALU.mult, op1=ALU.add))
        p.op("scalar", lambda h, dstt=dstt: h.activation(dstt[:], t1[:], AF.Sin))
    wrap_sin(sn, False)
    wrap_sin(cs, True)
    p.op("scalar", lambda h: h.activation(t1[:], dar[:], AF.Exp))
    p.op("vector", lambda h: h.tensor_mul(wr[:], t1[:], cs[:]))
    p.op("vector", lambda h: h.tensor_mul(wi[:], t1[:], sn[:]))
    p.op("vector", lambda h: h.tensor_scalar(t1[:], wr[:], -1.0, None, op0=ALU.add))
    p.op("vector", lambda h: h.tensor_mul(t2[:], cr[:], t1[:]))
    p.op("vector", lambda h: h.scalar_tensor_tensor(t2[:], ci[:], -1.0, t2[:], op0=ALU.mult, op1=ALU.bypass)
         if False else h.tensor_mul(t3[:], ci[:], wi[:]))
    p.op("vector", lambda h: h.tensor_sub(t2[:], t2[:], t3[:]))
    p.op("vector", lambda h: h.tensor_mul(t3[:], cr[:], wi[:]))
    p.op("vector", lambda h: h.tensor_mul(t1[:], ci[:], t1[:]))
    p.op("vector", lambda h: h.tensor_add(t3[:], t3[:], t1[:]))
    p.op("vector", lambda h: h.tensor_mul(t1[:], er[:], er[:]))
    p.op("vector", lambda h: h.scalar_tensor_tensor(t1[:], ai[:], 1.0, t1[:], op0=ALU.bypass, op1=ALU.bypass)
         if False else h.tensor_mul(cwr[:], ai[:], ai[:]))
    p.op("vector", lambda h: h.tensor_add(t1[:], t1[:], cwr[:]))
    p.op("vector", lambda h: h.reciprocal(t1[:], t1[:]))
    p.op("vector", lambda h: h.tensor_mul(cwr[:], t2[:], er[:]))
    p.op("vector", lambda h: h.tensor_mul(cwi[:], t3[:], ai[:]))
    p.op("vector", lambda h: h.tensor_sub(cwr[:], cwi[:], cwr[:]))
    p.op("vector", lambda h: h.tensor_mul(cwr[:], cwr[:], t1[:]))
    p.op("vector", lambda h: h.tensor_scalar(cwr[:], cwr[:], 2.0, None, op0=ALU.mult))
    p.op("vector", lambda h: h.tensor_mul(cwi[:], t3[:], er[:]))
    p.op("vector", lambda h: h.tensor_mul(t3[:], t2[:], ai[:]))
    p.op("vector", lambda h: h.tensor_add(cwi[:], cwi[:], t3[:]))
    p.op("vector", lambda h: h.tensor_mul(cwi[:], cwi[:], t1[:]))
    p.op("vector", lambda h: h.tensor_scalar(cwi[:], cwi[:], -2.0, None, op0=ALU.mult))
    p.op("vector", lambda h: h.tensor_scalar(ncwi[:], cwi[:], -1.0, None, op0=ALU.mult))

    p.op("vector", lambda h: h.tensor_copy(wpr[:], wr[:]))
    p.op("vector", lambda h: h.tensor_copy(wpi[:], wi[:]))
    for arr_r, arr_i in ((ur, ui), (vr, vi)):
        p.op("gpsimd", lambda h, arr_r=arr_r: h.memset(arr_r[:], 0.0))
        p.op("gpsimd", lambda h, arr_i=arr_i: h.memset(arr_i[:], 0.0))
        p.op("gpsimd", lambda h, arr_r=arr_r: h.memset(
            arr_r[:].rearrange("p (n l) -> p n l", n=32)[:, :, 0:1], 1.0))
        have = 1
        while have < 64:
            cnt = min(have, 64 - have)
            p.op("vector", lambda h: h.tensor_scalar(nwpi[:], wpi[:], -1.0, None, op0=ALU.mult))
            for n in range(32):
                b0 = 64 * n
                p.op("vector", lambda h, b0=b0, have=have, cnt=cnt, arr_r=arr_r, n=n: h.tensor_scalar(
                    tl[:, 0:cnt], arr_r[:, b0:b0 + cnt], wpr[:, n:n + 1], None, op0=ALU.mult))
                p.op("vector", lambda h, b0=b0, have=have, cnt=cnt, arr_i=arr_i, n=n: h.scalar_tensor_tensor(
                    tl[:, 0:cnt], arr_i[:, b0:b0 + cnt], nwpi[:, n:n + 1], tl[:, 0:cnt],
                    op0=ALU.mult, op1=ALU.add))
                p.op("vector", lambda h, b0=b0, have=have, cnt=cnt, arr_r=arr_r, arr_i=arr_i, n=n: h.tensor_scalar(
                    tl[:, 64 - cnt:64], arr_i[:, b0:b0 + cnt], wpr[:, n:n + 1], None, op0=ALU.mult))
                p.op("vector", lambda h, b0=b0, have=have, cnt=cnt, arr_r=arr_r, arr_i=arr_i, n=n: h.scalar_tensor_tensor(
                    arr_i[:, b0 + have:b0 + have + cnt], arr_r[:, b0:b0 + cnt], wpi[:, n:n + 1],
                    tl[:, 64 - cnt:64], op0=ALU.mult, op1=ALU.add))
                p.op("vector", lambda h, b0=b0, have=have, cnt=cnt, arr_r=arr_r, n=n: h.tensor_copy(
                    arr_r[:, b0 + have:b0 + have + cnt], tl[:, 0:cnt]))
            p.op("vector", lambda h: h.tensor_mul(t1[:], wpr[:], wpr[:]))
            p.op("vector", lambda h: h.tensor_mul(t2[:], wpi[:], wpi[:]))
            p.op("vector", lambda h: h.tensor_mul(t3[:], wpr[:], wpi[:]))
            p.op("vector", lambda h: h.tensor_sub(wpr[:], t1[:], t2[:]))
            p.op("vector", lambda h: h.tensor_scalar(wpi[:], t3[:], 2.0, None, op0=ALU.mult))
            have *= 2
    for n in range(32):
        b0 = 64 * n
        p.op("vector", lambda h, b0=b0, n=n: h.tensor_scalar(
            tl[:], ur[:, b0:b0 + 64], cwr[:, n:n + 1], None, op0=ALU.mult))
        p.op("vector", lambda h, b0=b0, n=n: h.scalar_tensor_tensor(
            pr[:, b0:b0 + 64], ui[:, b0:b0 + 64], ncwi[:, n:n + 1], tl[:],
            op0=ALU.mult, op1=ALU.add))
        p.op("vector", lambda h, b0=b0, n=n: h.tensor_scalar(
            tl[:], ur[:, b0:b0 + 64], cwi[:, n:n + 1], None, op0=ALU.mult))
        p.op("vector", lambda h, b0=b0, n=n: h.scalar_tensor_tensor(
            pi[:, b0:b0 + 64], ui[:, b0:b0 + 64], cwr[:, n:n + 1], tl[:],
            op0=ALU.mult, op1=ALU.add))
    p.op("vector", lambda h: h.tensor_scalar(nvi[:], vi[:], -1.0, None, op0=ALU.mult))
    p.op("gpsimd", lambda h: h.memset(Ks[:], 0.0))
    for n in range(32):
        b0 = 64 * n
        for l2 in range(64):
            p.op("vector", lambda h, b0=b0, l2=l2, n=n: h.scalar_tensor_tensor(
                Ks[:, 64 * l2:64 * l2 + 64], pr[:, b0:b0 + 64], vr[:, b0 + l2:b0 + l2 + 1],
                Ks[:, 64 * l2:64 * l2 + 64], op0=ALU.mult, op1=ALU.add))
            p.op("vector", lambda h, b0=b0, l2=l2, n=n: h.scalar_tensor_tensor(
                Ks[:, 64 * l2:64 * l2 + 64], pi[:, b0:b0 + 64], nvi[:, b0 + l2:b0 + l2 + 1],
                Ks[:, 64 * l2:64 * l2 + 64], op0=ALU.mult, op1=ALU.add))
    p.dma("gpsimd", lambda h: h.dma_start(out=Ko, in_=Ks[:]))
    last = p.last
    p.op("gpsimd", lambda h: h.engine_nop(), deps=[last])
    return p


NS = 640


def build_fft1(name):
    p = Prog(name)
    za = p.din("za", [64, NS * 64])
    wr = p.din("wr", [64, 128], BF16)
    wi = p.din("wi", [64, 128], BF16)
    gr = p.dout("gr", [128, NS * 64])
    gi = p.dout("gi", [128, NS * 64])
    zb = p.sb("zb", [64, 512], BF16)
    zs = p.sb("zs", [64, 512])
    wrs = p.sb("wrs", [64, 128], BF16)
    wis = p.sb("wis", [64, 128], BF16)
    ev = p.sb("ev", [128, 512])
    pA = p.ps("pA", [128, 512])
    pB = p.ps("pB", [128, 512])
    p.dma("sync", lambda h: h.dma_start(out=wrs[:], in_=wr))
    p.dma("sync", lambda h: h.dma_start(out=wis[:], in_=wi))
    NTL = NS * 64 // 512
    for t in range(NTL):
        c0 = t * 512
        p.dma("sync", lambda h, c0=c0: h.dma_start(out=zs[:], in_=za[:, c0:c0 + 512]))
        p.op("vector", lambda h: h.tensor_copy(zb[:], zs[:]))
        p.op("tensor", lambda h: h.matmul(pA[:], wrs[:], zb[:], start=True, stop=True))
        p.op("tensor", lambda h: h.matmul(pB[:], wis[:], zb[:], start=True, stop=True))
        p.op("vector", lambda h: h.tensor_copy(ev[:], pA[:]))
        p.dma("gpsimd", lambda h, c0=c0: h.dma_start(out=gr[:, c0:c0 + 512], in_=ev[:]))
        p.op("vector", lambda h: h.tensor_copy(ev[:], pB[:]))
        p.dma("gpsimd", lambda h, c0=c0: h.dma_start(out=gi[:, c0:c0 + 512], in_=ev[:]))
    last = p.last
    p.op("gpsimd", lambda h: h.engine_nop(), deps=[last])
    return p


def build_fft2(name):
    p = Prog(name)
    gt = p.din("gt", [64, 128 * 2 * NS])
    ma = p.din("ma", [64, 128 * 128], BF16)
    mb = p.din("mb", [64, 128 * 128], BF16)
    xo = p.dout("xo", [128, 128 * NS])
    gb = p.sb("gb", [64, 2 * NS], BF16)
    gs = p.sb("gs", [64, 2 * NS])
    mas = p.sb("mas", [64, 128], BF16)
    mbs = p.sb("mbs", [64, 128], BF16)
    ev = p.sb("ev", [128, 512])
    pA = p.ps("pA", [128, 512])
    for k2 in range(128):
        p.dma("sync", lambda h, k2=k2: h.dma_start(out=gs[:], in_=gt[:, 2 * NS * k2:2 * NS * k2 + 2 * NS]))
        p.op("vector", lambda h: h.tensor_copy(gb[:], gs[:]))
        p.dma("sync", lambda h, k2=k2: h.dma_start(out=mas[:], in_=ma[:, 128 * k2:128 * k2 + 128]))
        p.dma("sync", lambda h, k2=k2: h.dma_start(out=mbs[:], in_=mb[:, 128 * k2:128 * k2 + 128]))
        for nt in range((NS + 511) // 512):
            n0 = nt * 512
            nc_ = min(512, NS - n0)
            p.op("tensor", lambda h, n0=n0, nc_=nc_: h.matmul(
                pA[:, 0:nc_], mas[:], gb[:, n0:n0 + nc_], start=True, stop=False))
            p.op("tensor", lambda h, n0=n0, nc_=nc_: h.matmul(
                pA[:, 0:nc_], mbs[:], gb[:, NS + n0:NS + n0 + nc_], start=False, stop=True))
            p.op("vector", lambda h, nc_=nc_: h.tensor_copy(ev[:, 0:nc_], pA[:, 0:nc_]))
            p.dma("gpsimd", lambda h, k2=k2, n0=n0, nc_=nc_: h.dma_start(
                out=xo[:, NS * k2 + n0:NS * k2 + n0 + nc_], in_=ev[:, 0:nc_]))
    last = p.last
    p.op("gpsimd", lambda h: h.engine_nop(), deps=[last])
    return p


NSI = 512
NKC = 33


def build_inv(name):
    p = Prog(name)
    xxr = p.din("xxr", [NKC * 128, NSI])
    xxi = p.din("xxi", [NKC * 128, NSI])
    kfr = p.din("kfr", [NKC * 128, NSI])
    kfi = p.din("kfi", [NKC * 128, NSI])
    cc = p.din("cc", [NKC * 128, 4096], BF16)
    cs = p.din("cs", [NKC * 128, 4096], BF16)
    xns = p.din("xns", [4096, NSI])
    drow = p.din("drow", [1, NSI])
    yo = p.dout("yo", [4096, NSI])

    a1 = p.sb("a1", [128, NSI]); a2_ = p.sb("a2_", [128, NSI])
    b1 = p.sb("b1", [128, NSI]); b2_ = p.sb("b2_", [128, NSI])
    t1 = p.sb("t1", [128, NSI]); t2 = p.sb("t2", [128, NSI])
    pbr = p.sb("pbr", [128, NKC * NSI], BF16)
    pbi = p.sb("pbi", [128, NKC * NSI], BF16)
    ccs = p.sb("ccs", [128, 128], BF16)
    css = p.sb("css", [128, 128], BF16)
    dsb = p.sb("dsb", [1, NSI])
    dbc = p.sb("dbc", [128, NSI])
    ones_row1 = p.sb("ones_row1", [1, 128])
    ev = p.sb("ev", [128, NSI])
    pA = p.ps("pA", [128, NSI])
    pB = p.ps("pB", [128, NSI])

    p.op("gpsimd", lambda h: h.memset(ones_row1[:], 1.0))
    p.dma("sync", lambda h: h.dma_start(out=dsb[:], in_=drow))
    p.op("tensor", lambda h: h.matmul(pB[:], ones_row1[:], dsb[:], start=True, stop=True))
    p.op("vector", lambda h: h.tensor_copy(dbc[:], pB[:]))
    for ch in range(NKC):
        r0 = ch * 128
        p.dma("sync", lambda h, r0=r0: h.dma_start(out=a1[:], in_=xxr[r0:r0 + 128, :]))
        p.dma("sync", lambda h, r0=r0: h.dma_start(out=a2_[:], in_=xxi[r0:r0 + 128, :]))
        p.dma("sync", lambda h, r0=r0: h.dma_start(out=b1[:], in_=kfr[r0:r0 + 128, :]))
        p.dma("sync", lambda h, r0=r0: h.dma_start(out=b2_[:], in_=kfi[r0:r0 + 128, :]))
        p.op("vector", lambda h: h.tensor_mul(t1[:], a1[:], b1[:]))
        p.op("vector", lambda h: h.tensor_mul(t2[:], a2_[:], b2_[:]))
        p.op("vector", lambda h: h.tensor_sub(t1[:], t1[:], t2[:]))
        p.op("vector", lambda h, ch=ch: h.tensor_copy(pbr[:, NSI * ch:NSI * ch + NSI], t1[:]))
        p.op("vector", lambda h: h.tensor_mul(t1[:], a1[:], b2_[:]))
        p.op("vector", lambda h: h.tensor_mul(t2[:], a2_[:], b1[:]))
        p.op("vector", lambda h: h.tensor_add(t1[:], t1[:], t2[:]))
        p.op("vector", lambda h, ch=ch: h.tensor_copy(pbi[:, NSI * ch:NSI * ch + NSI], t1[:]))
    for mt in range(32):
        for ch in range(NKC):
            p.dma("sync", lambda h, mt=mt, ch=ch: h.dma_start(
                out=ccs[:], in_=cc[128 * ch:128 * ch + 128, 128 * mt:128 * mt + 128]))
            p.dma("sync", lambda h, mt=mt, ch=ch: h.dma_start(
                out=css[:], in_=cs[128 * ch:128 * ch + 128, 128 * mt:128 * mt + 128]))
            p.op("tensor", lambda h, ch=ch: h.matmul(
                pA[:], ccs[:], pbr[:, NSI * ch:NSI * ch + NSI], start=(ch == 0), stop=False))
            p.op("tensor", lambda h, ch=ch: h.matmul(
                pA[:], css[:], pbi[:, NSI * ch:NSI * ch + NSI],
                start=False, stop=(ch == NKC - 1)), mode="tail")
        p.dma("sync", lambda h, mt=mt: h.dma_start(out=t1[:], in_=xns[128 * mt:128 * mt + 128, :]))
        p.op("vector", lambda h: h.tensor_mul(t2[:], t1[:], dbc[:]))
        p.op("vector", lambda h: h.tensor_add(ev[:], pA[:], t2[:]))
        p.dma("gpsimd", lambda h, mt=mt: h.dma_start(out=yo[128 * mt:128 * mt + 128, :], in_=ev[:]))
    last = p.last
    p.op("gpsimd", lambda h: h.engine_nop(), deps=[last])
    return p


def _consts_attn():
    j = np.arange(T, dtype=np.float64)
    jhi, jlo = np.floor(j / 64), j % 64
    augK = np.stack([jhi, jlo, np.full(T, 64.0), np.ones(T)]).astype(np.float32)
    augQ = np.stack([np.full(T, 64.0), np.ones(T), -jhi, -jlo]).astype(np.float32)
    masks = np.stack([np.where(np.arange(128)[:, None] + d > np.arange(512)[None, :],
                               -30000.0, 0.0) for d in (0, 128, 256, 384)]).astype(np.float32)
    return _np_bf(augK), _np_bf(augQ), masks


def _consts_fft():
    a2 = np.arange(64); k2 = np.arange(128)
    W = np.exp(-2j * np.pi * np.outer(a2, k2) / 128.0)
    wr, wi = _np_bf(W.real), _np_bf(W.imag)
    a1 = np.arange(64); k1 = np.arange(64)
    tw = np.exp(-2j * np.pi * np.outer(a1, k2) / 8192.0)
    W64 = np.exp(-2j * np.pi * np.outer(a1, k1) / 64.0)
    M = np.einsum('ak,al->kal', tw, W64)
    MA = np.concatenate([M.real, M.imag], axis=2)
    MB = np.concatenate([-M.imag, M.real], axis=2)
    MA = _np_bf(MA.transpose(1, 0, 2).reshape(64, 128 * 128))
    MB = _np_bf(MB.transpose(1, 0, 2).reshape(64, 128 * 128))
    k = np.arange(4224); m = np.arange(4096)
    wk = np.where((k == 0) | (k == 4096), 1.0, 2.0) * (k <= 4096)
    th = 2.0 * np.pi * np.outer(k, m) / 8192.0
    cC = _np_bf(wk[:, None] * np.cos(th) / 8192.0)
    cS = _np_bf(-wk[:, None] * np.sin(th) / 8192.0)
    return wr, wi, MA, MB, cC, cS


_PROGS = {}

def _prog(key, builder, *a, **kw):
    if key not in _PROGS:
        p = builder(*a, **kw)
        _PROGS[key] = (p, p.build())
    return _PROGS[key]


def _run(key, in_maps):
    p, nc = _PROGS[key]
    res = run_bass_kernel_spmd(nc, in_maps, core_ids=list(range(len(in_maps))))
    return res.results


def ssm_launches(xn1_all, inp):
    wr, wi, MA, MB, cC, cS = _consts_fft()
    _prog("ssmk", build_ssmk, "ssmk")
    ldt = _np_f32(inp["L1_attn_log_dt"])[:, None]
    lar = _np_f32(inp["L1_attn_log_A_real"])
    aim = _np_f32(inp["L1_attn_A_imag"])
    Cp = _np_f32(inp["L1_attn_C"])
    ins = [dict(ldt=np.ascontiguousarray(ldt[128*c:128*c+128]),
                lar=np.ascontiguousarray(lar[128*c:128*c+128]),
                aim=np.ascontiguousarray(aim[128*c:128*c+128]),
                cre=np.ascontiguousarray(Cp[128*c:128*c+128, :, 0]),
                cim=np.ascontiguousarray(Cp[128*c:128*c+128, :, 1])) for c in range(8)]
    r = _run("ssmk", ins)
    Kfull = np.concatenate([np.asarray(r[c]["Ko"], np.float32) for c in range(8)], axis=0)
    sigs = np.concatenate([xn1_all.reshape(4 * 1024, 4096), Kfull], axis=0)
    _prog("fft1", build_fft1, "fft1")
    ins = []
    for c in range(8):
        sl = sigs[NS * c:NS * c + NS]
        za = sl.reshape(NS, 64, 64).transpose(1, 0, 2).reshape(64, NS * 64)
        ins.append(dict(za=np.ascontiguousarray(za), wr=wr, wi=wi))
    r = _run("fft1", ins)
    Gr = np.stack([np.asarray(r[c]["gr"], np.float32) for c in range(8)])
    Gi = np.stack([np.asarray(r[c]["gi"], np.float32) for c in range(8)])
    _prog("fft2", build_fft2, "fft2")
    ins = []
    for c in range(8):
        grc = Gr[c].reshape(128, NS, 64)
        gic = Gi[c].reshape(128, NS, 64)
        gt = np.stack([grc, gic], axis=1)
        gt = gt.transpose(3, 0, 1, 2).reshape(64, 128 * 2 * NS)
        ins.append(dict(gt=np.ascontiguousarray(gt), ma=MA, mb=MB))
    r = _run("fft2", ins)
    Xr = np.zeros((5120, 8192), np.float32)
    Xi = np.zeros((5120, 8192), np.float32)
    for c in range(8):
        xo = np.asarray(r[c]["xo"], np.float32).reshape(128, 128, NS)
        Xr[NS*c:NS*c+NS] = xo[0:64].transpose(2, 0, 1).reshape(NS, 8192)
        Xi[NS*c:NS*c+NS] = xo[64:128].transpose(2, 0, 1).reshape(NS, 8192)
    _prog("inv", build_inv, "inv")
    Dp = _np_f32(inp["L1_attn_D"])
    ins = []
    for c in range(8):
        ss = np.arange(NSI * c, NSI * c + NSI)
        hh = ss % 1024
        xxr = np.zeros((NKC * 128, NSI), np.float32); xxr[0:4097] = Xr[ss, 0:4097].T
        xxi = np.zeros((NKC * 128, NSI), np.float32); xxi[0:4097] = Xi[ss, 0:4097].T
        kfr = np.zeros((NKC * 128, NSI), np.float32); kfr[0:4097] = Xr[4096 + hh][:, 0:4097].T
        kfi = np.zeros((NKC * 128, NSI), np.float32); kfi[0:4097] = Xi[4096 + hh][:, 0:4097].T
        xns = sigs[ss].T.copy()
        drow = Dp[hh][None, :]
        ins.append(dict(xxr=xxr, xxi=xxi, kfr=kfr, kfi=kfi, cc=cC, cs=cS,
                        xns=np.ascontiguousarray(xns), drow=np.ascontiguousarray(drow)))
    r = _run("inv", ins)
    s1_all = np.zeros((4, 1024, 4096), np.float32)
    for c in range(8):
        y = np.asarray(r[c]["yo"], np.float32)
        ss = np.arange(NSI * c, NSI * c + NSI)
        s1_all.reshape(4096, 4096)[ss] = y.T
    return s1_all


def kernel(**inp):
    inp = {k: np.asarray(v) for k, v in inp.items()}
    x = _np_f32(inp["L0_attn_x"])
    B = x.shape[0]
    xT_all = np.ascontiguousarray(x.transpose(0, 2, 1))
    augK, augQ, masks = _consts_attn()
    ident = np.eye(128, dtype=np.float32)
    selm = np.zeros((8, 8, 128), np.float32)
    for e in range(8):
        selm[e, e, :] = 1.0
    selm = _np_bf(selm.reshape(8, 8 * 128))

    _prog("l0a", build_l0a, "l0a")
    wq = np.ascontiguousarray(_np_f32(inp["L0_attn_Wq"]).T)
    wk = np.ascontiguousarray(_np_f32(inp["L0_attn_Wk"]).T)
    wv = np.ascontiguousarray(_np_f32(inp["L0_attn_Wv"]).T)
    wo = np.ascontiguousarray(_np_f32(inp["L0_attn_Wo"]).T)
    brows = np.ascontiguousarray(np.concatenate(
        [_np_f32(inp["L0_attn_bq"]), _np_f32(inp["L0_attn_bk"]),
         _np_f32(inp["L0_attn_bv"]), _np_f32(inp["L0_attn_bo"])])[None, :])
    g1c = np.ascontiguousarray(_np_f32(inp["L0_ffn_ln1_g"]).reshape(8, 128).T)
    b1c_ = np.ascontiguousarray(_np_f32(inp["L0_ffn_ln1_b"]).reshape(8, 128).T)
    wqkv = np.ascontiguousarray(_np_f32(inp["L0_ffn_w_qkv"]))
    wo1 = np.ascontiguousarray(_np_f32(inp["L0_ffn_w_o"]))
    ins = [dict(xT=np.ascontiguousarray(xT_all[c // 2]), wq=wq, wk=wk, wv=wv, wo0=wo,
                brows=brows, augK=augK, augQ=augQ, wqkv=wqkv, wo1=wo1,
                gcol=g1c, bcol=b1c_, masks=masks) for c in range(8)]
    r = _run("l0a", ins)
    s_all = np.stack([np.asarray(r[2 * b]["sout0"], np.float32) for b in range(B)])
    xr_all = np.stack([np.asarray(r[2 * b]["sout1"], np.float32) for b in range(B)])
    xn_all = None

    _prog("moe0", build_moe, "moe0", True)
    g2c = np.ascontiguousarray(_np_f32(inp["L0_ffn_ln2_g"]).reshape(8, 128).T)
    b2c = np.ascontiguousarray(_np_f32(inp["L0_ffn_ln2_b"]).reshape(8, 128).T)
    b1cols = np.ascontiguousarray(_np_f32(inp["L0_ffn_b1"]).reshape(8, 32, 128)
                                  .transpose(2, 0, 1).reshape(128, 256))
    ins = []
    for c in range(8):
        b, hh = c // 2, c % 2
        sl = slice(TH * hh, TH * hh + TH)
        ins.append(dict(xr=np.ascontiguousarray(xr_all[b][:, sl]),
                        sT=np.ascontiguousarray(s_all[b][:, sl]),
                        xT=np.ascontiguousarray(xT_all[b][:, sl]),
                        gcol=g2c, bcol=b2c, b1c=b1cols,
                        b2m=np.ascontiguousarray(_np_f32(inp["L0_ffn_b2"])),
                        wg=np.ascontiguousarray(_np_f32(inp["L0_ffn_w_gate"])),
                        w1=np.ascontiguousarray(_np_f32(inp["L0_ffn_w1"])),
                        w2=np.ascontiguousarray(_np_f32(inp["L0_ffn_w2"])),
                        selm=selm, ident=ident))
    r = _run("moe0", ins)
    x1_all = np.stack([np.concatenate([np.asarray(r[2 * b]["x1o"], np.float32),
                                       np.asarray(r[2 * b + 1]["x1o"], np.float32)], axis=1)
                       for b in range(B)])
    xn1_all = np.stack([np.concatenate([np.asarray(r[2 * b]["xn1o"], np.float32),
                                        np.asarray(r[2 * b + 1]["xn1o"], np.float32)], axis=1)
                        for b in range(B)])

    s1_all = ssm_launches(xn1_all, inp)

    _prog("moe1", build_moe, "moe1", False)
    HID = 2816
    pad1 = np.zeros((8, 1024, HID), np.float32); pad1[:, :, :2730] = _np_f32(inp["L1_ffn_w1"])
    pad3 = np.zeros((8, 1024, HID), np.float32); pad3[:, :, :2730] = _np_f32(inp["L1_ffn_w3"])
    pad2 = np.zeros((8, HID, 1024), np.float32); pad2[:, :2730, :] = _np_f32(inp["L1_ffn_w2"])
    ins = []
    for c in range(8):
        b, hh = c // 2, c % 2
        sl = slice(TH * hh, TH * hh + TH)
        ins.append(dict(xn1=np.ascontiguousarray(xn1_all[b][:, sl]),
                        x1=np.ascontiguousarray(x1_all[b][:, sl]),
                        s1=np.ascontiguousarray(s1_all[b][:, sl]),
                        wg=np.ascontiguousarray(_np_f32(inp["L1_ffn_gate_w"])),
                        w1=pad1, w2=pad2, w3=pad3, selm=selm, ident=ident))
    r = _run("moe1", ins)
    out = np.zeros_like(x)
    for b in range(B):
        x2 = np.concatenate([np.asarray(r[2 * b]["x2o"], np.float32),
                             np.asarray(r[2 * b + 1]["x2o"], np.float32)], axis=1)
        out[b] = x2.T
    return out


def build_l0a(name):
    p = Prog(name)
    xT = p.din("xT", [D, T])
    wq = p.din("wq", [D, D]); wk = p.din("wk", [D, D])
    wv = p.din("wv", [D, D]); wo0 = p.din("wo0", [D, D])
    brows = p.din("brows", [1, 4 * D])
    augK = p.din("augK", [4, T], BF16)
    augQ = p.din("augQ", [4, T], BF16)
    wqkv = p.din("wqkv", [D, 3 * D])
    wo1 = p.din("wo1", [D, D])
    gcol = p.din("gcol", [128, 8])
    bcol = p.din("bcol", [128, 8])
    masks = p.din("masks", [4, 128, 512])
    sout0 = p.dout("sout0", [D, T])
    sout1 = p.dout("sout1", [D, T])
    xnd = p.nc.dram_tensor("xnd", [D, T], F32, kind="Internal").ap()

    xbf = p.sb("xbf", [128, 8 * T], BF16)
    wstage = p.sb("wstage", [128, TH])
    wbf = p.sb("wbf", [128, 8 * 128], BF16)
    qg = p.sb("qg", [128, 2 * T], BF16)
    kg = p.sb("kg", [128, 2 * T], BF16)
    vsb = p.sb("vsb", [128, 32 * 260], BF16)
    psb = p.sb("psb", [128, 512], BF16)
    osb = p.sb("osb", [128, 512])
    rd = p.sb("rd", [1, 512])
    onesr = p.sb("onesr", [1, 512], BF16)
    onesc = p.sb("onesc", [1, 64])
    brsb = p.sb("brsb", [1, 4 * D], BF16)
    brst = p.sb("brst", [1, 4 * D])
    maskb = p.sb("maskb", [128, 4 * 512])
    ontd = p.nc.dram_tensor("ontd", [D, T], BF16, kind="Internal").ap()
    ontb = p.sb("ontb", [128, 512], BF16)
    onrm = p.sb("onrm", [64, 512], BF16)
    aKb = p.sb("aKb", [4, T], BF16)
    aQb = p.sb("aQb", [4, T], BF16)
    stg = p.sb("stg", [128, 512])
    vrhs = p.sb("vrhs", [128, 8 * 256], BF16)
    x32 = p.sb("x32", [128, 512])
    sq = p.sb("sq", [128, 512])
    mrow = p.sb("mrow", [1, 512]); rrow = p.sb("rrow", [1, 512])
    vrow = p.sb("vrow", [1, 512])
    onescol = p.sb("onescol", [128, 1])
    eps_col = p.sb("eps_col", [128, 1])
    ones_row1 = p.sb("ones_row1", [1, 128])
    sgc = p.sb("sgc", [128, 8])
    sbc = p.sb("sbc", [128, 8])
    pA = p.ps("pA", [128, 512])
    pB = p.ps("pB", [128, 512])
    pC = p.ps("pC", [128, 512])
    pD = p.ps("pD", [128, 512])

    p.op("gpsimd", lambda h: h.memset(onesr[:], 1.0))
    p.op("gpsimd", lambda h: h.memset(onesc[:], 1.0))
    p.op("gpsimd", lambda h: h.memset(onescol[:], 1.0 / 1024.0))
    p.op("gpsimd", lambda h: h.memset(eps_col[:], EPS))
    p.op("gpsimd", lambda h: h.memset(ones_row1[:], 1.0))
    p.op("gpsimd", lambda h: h.memset(
        vsb[:].rearrange("p (t h c) -> p t h c", t=32, h=4)[:, :, :, 64:65], 1.0))
    for d in range(4):
        p.dma("sync", lambda h, d=d: h.dma_start(out=maskb[:, 512 * d:512 * d + 512], in_=masks[d]))
    p.dma("sync", lambda h: h.dma_start(out=brst[:], in_=brows))
    p.op("vector", lambda h: h.tensor_copy(brsb[:], brst[:]))
    p.dma("sync", lambda h: h.dma_start(out=aKb[:], in_=augK))
    p.dma("sync", lambda h: h.dma_start(out=aQb[:], in_=augQ))
    p.dma("sync", lambda h: h.dma_start(out=sgc[:], in_=gcol))
    p.dma("sync", lambda h: h.dma_start(out=sbc[:], in_=bcol))

    for tt in range(8):
        c0 = tt * 512
        for c in range(8):
            p.dma("sync", lambda h, c=c, c0=c0: h.dma_start(
                out=x32[:], in_=xT[128 * c:128 * c + 128, c0:c0 + 512]))
            p.op("tensor", lambda h, c=c: h.matmul(
                pA[0:1, :], onescol[:], x32[:], start=(c == 0), stop=(c == 7)))
            p.op("scalar", lambda h: h.activation(sq[:], x32[:], AF.Square))
            p.op("tensor", lambda h, c=c: h.matmul(
                pB[0:1, :], onescol[:], sq[:], start=(c == 0), stop=(c == 7)))
        p.op("vector", lambda h: h.tensor_copy(mrow[:], pA[0:1, :]))
        p.op("vector", lambda h: h.scalar_tensor_tensor(
            vrow[:], mrow[:], -1.0, mrow[:], op0=ALU.mult, op1=ALU.mult))
        p.op("vector", lambda h: h.tensor_add(vrow[:], vrow[:], pB[0:1, :]))
        p.op("scalar", lambda h: h.activation(vrow[:], vrow[:], AF.Ln, bias=eps_col[0:1, :], scale=1.0))
        p.op("scalar", lambda h: h.activation(rrow[:], vrow[:], AF.Exp, scale=-0.5))
        p.op("tensor", lambda h: h.matmul(pA[:], ones_row1[:], mrow[:], start=True, stop=True))
        p.op("tensor", lambda h: h.matmul(pB[:], ones_row1[:], rrow[:], start=True, stop=True))
        for c in range(8):
            p.dma("sync", lambda h, c=c, c0=c0: h.dma_start(
                out=x32[:], in_=xT[128 * c:128 * c + 128, c0:c0 + 512]))
            p.op("vector", lambda h: h.tensor_sub(stg[:], x32[:], pA[:]))
            p.op("vector", lambda h: h.tensor_mul(stg[:], stg[:], pB[:]))
            p.dma("gpsimd", lambda h, c=c, c0=c0: h.dma_start(
                out=xnd[128 * c:128 * c + 128, c0:c0 + 512], in_=stg[:]))

    def load_lhsT(src_ap, k, mcol0):
        p.dma("sync", lambda h, src_ap=src_ap, k=k, mcol0=mcol0: h.dma_start(
            out=wstage[:, 0:128], in_=src_ap[128 * k:128 * k + 128, mcol0:mcol0 + 128]))
        p.op("vector", lambda h: h.tensor_copy(wbf[:, 128 * k:128 * k + 128], wstage[:, 0:128]))

    NKT = T // 128
    for outer in (True, False):
        xnT = xnd
        sout = sout0 if outer else sout1
        wo = wo0 if outer else wo1
        if not outer:
            for tt in range(8):
                c0 = tt * 512
                for c in range(8):
                    p.dma("sync", lambda h, c=c, c0=c0: h.dma_start(
                        out=x32[:], in_=xnT[128 * c:128 * c + 128, c0:c0 + 512]))
                    p.op("tensor", lambda h, c=c: h.matmul(
                        pC[0:1, :], onescol[:], x32[:], start=(c == 0), stop=(c == 7)))
                    p.op("scalar", lambda h: h.activation(sq[:], x32[:], AF.Square))
                    p.op("tensor", lambda h, c=c: h.matmul(
                        pD[0:1, :], onescol[:], sq[:], start=(c == 0), stop=(c == 7)))
                p.op("vector", lambda h: h.tensor_copy(mrow[:], pC[0:1, :]))
                p.op("vector", lambda h: h.scalar_tensor_tensor(
                    vrow[:], mrow[:], -1.0, mrow[:], op0=ALU.mult, op1=ALU.mult))
                p.op("vector", lambda h: h.tensor_add(vrow[:], vrow[:], pD[0:1, :]))
                p.op("scalar", lambda h: h.activation(vrow[:], vrow[:], AF.Ln, bias=eps_col[0:1, :], scale=1.0))
                p.op("scalar", lambda h: h.activation(rrow[:], vrow[:], AF.Exp, scale=-0.5))
                p.op("tensor", lambda h: h.matmul(pC[:], ones_row1[:], mrow[:], start=True, stop=True))
                p.op("tensor", lambda h: h.matmul(pD[:], ones_row1[:], rrow[:], start=True, stop=True))
                for c in range(8):
                    p.dma("sync", lambda h, c=c, c0=c0: h.dma_start(
                        out=x32[:], in_=xnT[128 * c:128 * c + 128, c0:c0 + 512]))
                    p.op("vector", lambda h: h.tensor_sub(stg[:], x32[:], pC[:]))
                    p.op("vector", lambda h: h.tensor_mul(stg[:], stg[:], pD[:]))
                    p.op("vector", lambda h, c=c, c0=c0: h.tensor_scalar(
                        xbf[:, c * T + c0:c * T + c0 + 512], stg[:],
                        sgc[:, c:c + 1], sbc[:, c:c + 1], op0=ALU.mult, op1=ALU.add))
        else:
            for c in range(8):
                for hh2 in range(2):
                    p.dma("sync", lambda h, c=c, hh2=hh2: h.dma_start(
                        out=wstage[:], in_=xnT[128 * c:128 * c + 128, TH * hh2:TH * hh2 + TH]))
                    p.op("vector", lambda h, c=c, hh2=hh2: h.tensor_copy(
                        xbf[:, c * T + TH * hh2:c * T + TH * hh2 + TH], wstage[:]))
        for grp in range(4):
            h0 = grp * 4
            r0 = h0 * HD
            for which, dst, scale in (("q", qg, 0.125), ("k", kg, 1.0)):
                if outer:
                    wsrc = wq if which == "q" else wk
                    coff = r0
                else:
                    wsrc = wqkv
                    coff = r0 if which == "q" else D + r0
                for m in range(2):
                    for k in range(8):
                        load_lhsT(wsrc, k, coff + 128 * m)
                    for n in range(8):
                        for k in range(8):
                            last_mm = (k == 7 and not outer)
                            md = "chain" if k == 0 else ("tail" if last_mm else "mid")
                            p.op("tensor", lambda h, k=k, n=n: h.matmul(
                                pA[:], wbf[:, 128 * k:128 * k + 128],
                                xbf[:, k * T + 512 * n:k * T + 512 * n + 512],
                                start=(k == 0), stop=(k == 7 and not outer)), mode=md)
                        if outer:
                            bi = 0 if which == "q" else 1
                            p.op("tensor", lambda h, m=m, bi=bi, coff=coff: h.matmul(
                                pA[:], brsb[0:1, bi * D + coff + 128 * m:bi * D + coff + 128 * m + 128],
                                onesr[:], start=False, stop=True), mode="tail")
                        p.op("scalar", lambda h, dst=dst, m=m, n=n, scale=scale: h.activation(
                            dst[:, m * T + 512 * n:m * T + 512 * n + 512], pA[:],
                            AF.Copy, scale=scale))
            if outer:
                vw = wv; vcoff = r0
            else:
                vw = wqkv; vcoff = 2 * D + r0
            for k in range(8):
                p.dma("sync", lambda h, k=k, vw=vw, vcoff=vcoff: h.dma_start(
                    out=wstage[:, 0:256], in_=vw[128 * k:128 * k + 128, vcoff:vcoff + 256]))
                p.op("vector", lambda h, k=k: h.tensor_copy(
                    vrhs[:, 256 * k:256 * k + 256], wstage[:, 0:256]))
            for kt in range(NKT):
                for k in range(8):
                    md = "chain" if k == 0 else ("tail" if (k == 7 and not outer) else "mid")
                    p.op("tensor", lambda h, k=k, kt=kt: h.matmul(
                        pA[:, 0:256], xbf[:, k * T + 128 * kt:k * T + 128 * kt + 128],
                        vrhs[:, 256 * k:256 * k + 256], start=(k == 0), stop=(k == 7 and not outer)), mode=md)
                if outer:
                    p.op("tensor", lambda h, vcoff=vcoff: h.matmul(
                        pA[:, 0:256], onesr[0:1, 0:128], brsb[0:1, 2 * D + vcoff:2 * D + vcoff + 256],
                        start=False, stop=True), mode="tail")
                for hh in range(4):
                    p.op("vector", lambda h, kt=kt, hh=hh: h.tensor_copy(
                        vsb[:, 260 * kt + 65 * hh:260 * kt + 65 * hh + 64],
                        pA[:, 64 * hh:64 * hh + 64]))
            for qt in range(8):
                q0 = qt * 512
                nkb = min(32, 4 * qt + 4)
                for hh in range(4):
                    hloc_t = hh // 2
                    hbase = 64 * (hh % 2)
                    for kb in range(nkb):
                        p.op("tensor", lambda h, hloc_t=hloc_t, hbase=hbase, kb=kb, q0=q0, outer=outer: h.matmul(
                            pA[:], kg[hbase:hbase + 64, hloc_t * T + 128 * kb:hloc_t * T + 128 * kb + 128],
                            qg[hbase:hbase + 64, hloc_t * T + q0:hloc_t * T + q0 + 512],
                            start=True, stop=not outer), mode=("chain" if not outer else "mid") if False else "chain")
                        if outer:
                            p.op("tensor", lambda h, kb=kb, q0=q0: h.matmul(
                                pA[:], aKb[:, 128 * kb:128 * kb + 128], aQb[:, q0:q0 + 512],
                                start=False, stop=True), mode="tail")
                        d = 128 * kb - q0
                        if 0 <= d < 512:
                            mi = d // 128
                            p.op("vector", lambda h, mi=mi: h.tensor_add(
                                pA[:], pA[:], maskb[:, 512 * mi:512 * mi + 512]))
                        p.op("scalar", lambda h: h.activation(psb[:], pA[:], AF.Exp))
                        p.op("tensor", lambda h, kb=kb, hh=hh, nkb=nkb: h.matmul(
                            pB[0:65, :], vsb[:, 260 * kb + 65 * hh:260 * kb + 65 * hh + 65],
                            psb[:], start=(kb == 0), stop=(kb == nkb - 1)))
                    p.op("vector", lambda h: h.tensor_copy(osb[0:65, :], pB[0:65, :]))
                    p.op("vector", lambda h: h.reciprocal(rd[:], osb[64:65, :]))
                    p.op("tensor", lambda h: h.matmul(
                        pC[0:64, :], onesc[:], rd[:], start=True, stop=True))
                    p.op("vector", lambda h: h.tensor_mul(onrm[:], osb[0:64, :], pC[0:64, :]))
                    p.dma("gpsimd", lambda h, h0=h0, hh=hh, q0=q0: h.dma_start(
                        out=ontd[64 * (h0 + hh):64 * (h0 + hh) + 64, q0:q0 + 512], in_=onrm[:]))
        for m in range(8):
            for k in range(8):
                load_lhsT(wo, k, 128 * m)
            for n in range(8):
                for k in range(8):
                    p.dma("sync", lambda h, k=k, n=n: h.dma_start(
                        out=ontb[:], in_=ontd[128 * k:128 * k + 128, 512 * n:512 * n + 512]))
                    p.op("tensor", lambda h, k=k, n=n, outer=outer: h.matmul(
                        pD[:], wbf[:, 128 * k:128 * k + 128], ontb[:],
                        start=(k == 0), stop=(k == 7 and not outer)))
                if outer:
                    p.op("tensor", lambda h, m=m: h.matmul(
                        pD[:], brsb[0:1, 3 * D + 128 * m:3 * D + 128 * m + 128], onesr[:],
                        start=False, stop=True))
                    p.op("vector", lambda h: h.tensor_copy(stg[:], pD[:]))
                else:
                    p.dma("sync", lambda h, m=m, n=n: h.dma_start(
                        out=x32[:], in_=xnT[128 * m:128 * m + 128, 512 * n:512 * n + 512]))
                    p.op("vector", lambda h: h.tensor_add(stg[:], pD[:], x32[:]))
                p.dma("gpsimd", lambda h, m=m, n=n, sout=sout: h.dma_start(
                    out=sout[128 * m:128 * m + 128, 512 * n:512 * n + 512], in_=stg[:]))
    last = p.last
    p.op("gpsimd", lambda h: h.engine_nop(), deps=[last])
    return p


# version 04
# speedup vs baseline: 1.4274x; 1.0271x over previous
import numpy as np
import ml_dtypes
from contextlib import ExitStack

import concourse.bass as bass
import concourse.mybir as mybir
from concourse.bass_utils import run_bass_kernel_spmd

F32 = mybir.dt.float32
BF16 = mybir.dt.bfloat16
AF = mybir.ActivationFunctionType
ALU = mybir.AluOpType
NPBF = ml_dtypes.bfloat16

D = 1024
T = 4096
TH = 2048
NH = 16
HD = 64
EPS = 1e-5


class Prog:
    ENG = ("sync", "tensor", "vector", "scalar", "gpsimd")

    def __init__(self, name):
        self.nc = bass.Bass("TRN2", target_bir_lowering=False)
        self.name = name
        self.ctx = ExitStack()
        self.q = {e: [] for e in self.ENG}
        self.cnt = {e: 0 for e in self.ENG}
        self.last = None
        self._n = 0

    def din(self, name, shape, dt=F32):
        return self.nc.dram_tensor(name, list(shape), dt, kind="ExternalInput").ap()

    def dout(self, name, shape, dt=F32):
        return self.nc.dram_tensor(name, list(shape), dt, kind="ExternalOutput").ap()

    def sb(self, name, shape, dt=F32):
        return self.ctx.enter_context(self.nc.sbuf_tensor(name, list(shape), dt))

    def ps(self, name, shape, dt=F32):
        return self.ctx.enter_context(self.nc.psum_tensor(name, list(shape), dt))

    def _emit(self, eng, fn, inc, deps, mode="chain"):
        if mode == "chain":
            my = list(deps) if deps is not None else ([self.last] if self.last else [])
        else:
            my = []
        if mode == "mid":
            inc = 0
        def closure(h, _fn=fn, _deps=my, _eng=eng, _inc=inc):
            for d in _deps:
                if d is not None and not (d[0] == _eng and d[1] <= self._done[_eng]):
                    h.wait_ge(self._sems[d[0]], d[1])
            r = _fn(h)
            if _inc:
                r.then_inc(self._sems[_eng], _inc)
        self.q[eng].append(closure)
        if inc:
            self.cnt[eng] += inc
            tok = (eng, self.cnt[eng])
            self.last = tok
        else:
            tok = (eng, self.cnt[eng])
        return tok

    def op(self, eng, fn, deps=None, mode="chain"):
        return self._emit(eng, fn, 1, deps, mode)

    def dma(self, eng, fn, deps=None, mode="chain"):
        return self._emit(eng, fn, 16, deps, mode)

    def build(self):
        sems = {}
        for e in self.ENG:
            sems[e] = self.ctx.enter_context(self.nc.semaphore("sem_" + e))
        self._sems = sems
        self._done = {e: 0 for e in self.ENG}
        block = self.ctx.enter_context(self.nc.Block())

        def runq(e):
            def body(h):
                for f in self.q[e]:
                    f(h)
            return body

        for e in self.ENG:
            deco = getattr(block, e)
            deco(runq(e))
        self.ctx.close()
        return self.nc


def _np_f32(x):
    return np.ascontiguousarray(x, np.float32)


def _np_bf(x):
    return np.ascontiguousarray(np.asarray(x).astype(NPBF))


def build_ln(name, ncols, gamma=False):
    p = Prog(name)
    xin = p.din("xin", [D, ncols])
    xout = p.dout("xout", [D, ncols])
    if gamma:
        gcol = p.din("gcol", [128, 8])
        bcol = p.din("bcol", [128, 8])

    CT = 512
    nct = ncols // CT
    xs = [p.sb(f"x{i}", [128, CT]) for i in range(8)]
    sq = p.sb("sq", [128, CT])
    mrow = p.sb("mrow", [1, CT])
    vrow = p.sb("vrow", [1, CT])
    rrow = p.sb("rrow", [1, CT])
    t1 = p.sb("t1", [128, CT])
    ones_col = p.sb("ones_col", [128, 1])
    eps_col = p.sb("eps_col", [128, 1])
    ones_row1 = p.sb("ones_row1", [1, 128])
    outbuf = p.sb("outbuf", [128, CT])
    sgc = p.sb("sgc", [128, 8]) if gamma else None
    sbc = p.sb("sbc", [128, 8]) if gamma else None
    pm = p.ps("pm", [128, CT])
    pv = p.ps("pv", [128, CT])
    pb1 = p.ps("pb1", [128, CT])
    pb2 = p.ps("pb2", [128, CT])

    p.op("gpsimd", lambda h: h.memset(ones_col[:], 1.0 / 1024.0))
    p.op("gpsimd", lambda h: h.memset(ones_row1[:], 1.0))
    p.op("gpsimd", lambda h: h.memset(eps_col[:], EPS))
    if gamma:
        p.dma("sync", lambda h: h.dma_start(out=sgc[:], in_=gcol))
        p.dma("sync", lambda h: h.dma_start(out=sbc[:], in_=bcol))

    for tt in range(nct):
        c0 = tt * CT
        for c in range(8):
            p.dma("sync", lambda h, c=c, c0=c0: h.dma_start(
                out=xs[c][:], in_=xin[128 * c:128 * c + 128, c0:c0 + CT]))
        for c in range(8):
            p.op("tensor", lambda h, c=c: h.matmul(
                pm[0:1, :], ones_col[:], xs[c][:], start=(c == 0), stop=(c == 7)))
        for c in range(8):
            p.op("scalar", lambda h, c=c: h.activation(sq[:], xs[c][:], AF.Square))
            p.op("tensor", lambda h, c=c: h.matmul(
                pv[0:1, :], ones_col[:], sq[:], start=(c == 0), stop=(c == 7)))
        p.op("vector", lambda h: h.tensor_copy(mrow[:], pm[0:1, :]))
        p.op("vector", lambda h: h.scalar_tensor_tensor(
            vrow[:], mrow[:], -1.0, mrow[:], op0=ALU.mult, op1=ALU.mult))
        p.op("vector", lambda h: h.tensor_add(vrow[:], vrow[:], pv[0:1, :]))
        p.op("scalar", lambda h: h.activation(rrow[:], vrow[:], AF.Ln, bias=eps_col[0:1, :], scale=1.0))
        p.op("scalar", lambda h: h.activation(rrow[:], rrow[:], AF.Exp, scale=-0.5))
        p.op("tensor", lambda h: h.matmul(pb1[:], ones_row1[:], mrow[:], start=True, stop=True))
        p.op("tensor", lambda h: h.matmul(pb2[:], ones_row1[:], rrow[:], start=True, stop=True))
        for c in range(8):
            p.op("vector", lambda h, c=c: h.tensor_sub(t1[:], xs[c][:], pb1[:]))
            p.op("vector", lambda h, c=c: h.tensor_mul(outbuf[:], t1[:], pb2[:]))
            if gamma:
                p.op("vector", lambda h, c=c: h.tensor_scalar(
                    outbuf[:], outbuf[:], sgc[:, c:c + 1], sbc[:, c:c + 1],
                    op0=ALU.mult, op1=ALU.add))
            p.dma("gpsimd", lambda h, c=c, c0=c0: h.dma_start(
                out=xout[128 * c:128 * c + 128, c0:c0 + CT], in_=outbuf[:]))
    last = p.last
    p.op("gpsimd", lambda h: h.engine_nop(), deps=[last])
    return p


def build_attn(name, outer, debug=False):
    p = Prog(name)
    xnT = p.din("xnT", [D, T])
    if outer:
        wq = p.din("wq", [D, D]); wk = p.din("wk", [D, D])
        wv = p.din("wv", [D, D]); wo = p.din("wo", [D, D])
        brows = p.din("brows", [1, 4 * D])
        augK = p.din("augK", [4, T], BF16)
        augQ = p.din("augQ", [4, T], BF16)
    else:
        wqkv = p.din("wqkv", [D, 3 * D])
        wo = p.din("wo", [D, D])
        gcol = p.din("gcol", [128, 8])
        bcol = p.din("bcol", [128, 8])
    masks = p.din("masks", [4, 128, 512])
    sout = p.dout("sout", [D, T])

    xbf = p.sb("xbf", [128, 8 * T], BF16)
    wstage = p.sb("wstage", [128, TH])
    wbf = p.sb("wbf", [128, 8 * 128], BF16)
    qg = p.sb("qg", [128, 2 * T], BF16)
    kg = p.sb("kg", [128, 2 * T], BF16)
    vsb = p.sb("vsb", [128, 32 * 260], BF16)
    psb = p.sb("psb", [128, 512], BF16)
    osb = p.sb("osb", [128, 512])
    rd = p.sb("rd", [1, 512])
    onesr = p.sb("onesr", [1, 512], BF16)
    onesc = p.sb("onesc", [1, 64])
    brsb = p.sb("brsb", [1, 4 * D], BF16) if outer else None
    brst = p.sb("brst", [1, 4 * D]) if outer else None
    maskb = p.sb("maskb", [128, 4 * 512])
    ontd = p.nc.dram_tensor("ontd", [D, T], BF16,
                            kind="Internal" if not debug else "ExternalOutput").ap()
    ontb = p.sb("ontb", [128, 512], BF16)
    onrm = p.sb("onrm", [64, 512], BF16)
    aKb = p.sb("aKb", [4, T], BF16) if outer else None
    aQb = p.sb("aQb", [4, T], BF16) if outer else None
    stg = p.sb("stg", [128, 512])
    vrhs = p.sb("vrhs", [128, 8 * 256], BF16)
    x32 = p.sb("x32", [128, 512])
    sq = p.sb("sq", [128, 512])
    mrow = p.sb("mrow", [1, 512]); rrow = p.sb("rrow", [1, 512])
    vrow = p.sb("vrow", [1, 512])
    onescol = p.sb("onescol", [128, 1])
    eps_col = p.sb("eps_col", [128, 1])
    ones_row1 = p.sb("ones_row1", [1, 128])
    sgc = p.sb("sgc", [128, 8]) if not outer else None
    sbc = p.sb("sbc", [128, 8]) if not outer else None

    pA = p.ps("pA", [128, 512])
    pB = p.ps("pB", [128, 512])
    pC = p.ps("pC", [128, 512])
    pD = p.ps("pD", [128, 512])

    p.op("gpsimd", lambda h: h.memset(onesr[:], 1.0))
    p.op("gpsimd", lambda h: h.memset(onesc[:], 1.0))
    p.op("gpsimd", lambda h: h.memset(onescol[:], 1.0 / 1024.0))
    p.op("gpsimd", lambda h: h.memset(eps_col[:], EPS))
    p.op("gpsimd", lambda h: h.memset(ones_row1[:], 1.0))
    p.op("gpsimd", lambda h: h.memset(
        vsb[:].rearrange("p (t h c) -> p t h c", t=32, h=4)[:, :, :, 64:65], 1.0))
    for d in range(4):
        p.dma("sync", lambda h, d=d: h.dma_start(out=maskb[:, 512 * d:512 * d + 512], in_=masks[d]))
    if outer:
        p.dma("sync", lambda h: h.dma_start(out=brst[:], in_=brows))
        p.op("vector", lambda h: h.tensor_copy(brsb[:], brst[:]))
        p.dma("sync", lambda h: h.dma_start(out=aKb[:], in_=augK))
        p.dma("sync", lambda h: h.dma_start(out=aQb[:], in_=augQ))
    else:
        p.dma("sync", lambda h: h.dma_start(out=sgc[:], in_=gcol))
        p.dma("sync", lambda h: h.dma_start(out=sbc[:], in_=bcol))

    if not outer:
        for tt in range(8):
            c0 = tt * 512
            for c in range(8):
                p.dma("sync", lambda h, c=c, c0=c0: h.dma_start(
                    out=x32[:], in_=xnT[128 * c:128 * c + 128, c0:c0 + 512]))
                p.op("tensor", lambda h, c=c: h.matmul(
                    pC[0:1, :], onescol[:], x32[:], start=(c == 0), stop=(c == 7)))
                p.op("scalar", lambda h: h.activation(sq[:], x32[:], AF.Square))
                p.op("tensor", lambda h, c=c: h.matmul(
                    pD[0:1, :], onescol[:], sq[:], start=(c == 0), stop=(c == 7)))
            p.op("vector", lambda h: h.tensor_copy(mrow[:], pC[0:1, :]))
            p.op("vector", lambda h: h.scalar_tensor_tensor(
                vrow[:], mrow[:], -1.0, mrow[:], op0=ALU.mult, op1=ALU.mult))
            p.op("vector", lambda h: h.tensor_add(vrow[:], vrow[:], pD[0:1, :]))
            p.op("scalar", lambda h: h.activation(vrow[:], vrow[:], AF.Ln, bias=eps_col[0:1, :], scale=1.0))
            p.op("scalar", lambda h: h.activation(rrow[:], vrow[:], AF.Exp, scale=-0.5))
            p.op("tensor", lambda h: h.matmul(pC[:], ones_row1[:], mrow[:], start=True, stop=True))
            p.op("tensor", lambda h: h.matmul(pD[:], ones_row1[:], rrow[:], start=True, stop=True))
            for c in range(8):
                p.dma("sync", lambda h, c=c, c0=c0: h.dma_start(
                    out=x32[:], in_=xnT[128 * c:128 * c + 128, c0:c0 + 512]))
                p.op("vector", lambda h: h.tensor_sub(stg[:], x32[:], pC[:]))
                p.op("vector", lambda h: h.tensor_mul(stg[:], stg[:], pD[:]))
                p.op("vector", lambda h, c=c, c0=c0: h.tensor_scalar(
                    xbf[:, c * T + c0:c * T + c0 + 512], stg[:],
                    sgc[:, c:c + 1], sbc[:, c:c + 1], op0=ALU.mult, op1=ALU.add))
    else:
        for c in range(8):
            for hh2 in range(2):
                p.dma("sync", lambda h, c=c, hh2=hh2: h.dma_start(
                    out=wstage[:], in_=xnT[128 * c:128 * c + 128, TH * hh2:TH * hh2 + TH]))
                p.op("vector", lambda h, c=c, hh2=hh2: h.tensor_copy(
                    xbf[:, c * T + TH * hh2:c * T + TH * hh2 + TH], wstage[:]))

    def load_lhsT(src_ap, k, mcol0, tag):
        p.dma("sync", lambda h: h.dma_start(
            out=wstage[:, 0:128], in_=src_ap[128 * k:128 * k + 128, mcol0:mcol0 + 128]))
        p.op("vector", lambda h: h.tensor_copy(wbf[:, 128 * k:128 * k + 128], wstage[:, 0:128]))

    NKT = T // 128
    for grp in range(4):
        h0 = grp * 4
        r0 = h0 * HD
        for which, dst, scale in (("q", qg, 0.125), ("k", kg, 1.0)):
            if outer:
                wsrc = wq if which == "q" else wk
                coff = r0
            else:
                wsrc = wqkv
                coff = r0 if which == "q" else D + r0
            for m in range(2):
                for k in range(8):
                    load_lhsT(wsrc, k, coff + 128 * m, which)
                for n in range(8):
                    for k in range(8):
                        p.op("tensor", lambda h, k=k, n=n: h.matmul(
                            pA[:], wbf[:, 128 * k:128 * k + 128],
                            xbf[:, k * T + 512 * n:k * T + 512 * n + 512],
                            start=(k == 0), stop=(k == 7 and not outer)))
                    if outer:
                        bi = 0 if which == "q" else 1
                        p.op("tensor", lambda h, m=m, bi=bi, coff=coff: h.matmul(
                            pA[:], brsb[0:1, bi * D + coff + 128 * m:bi * D + coff + 128 * m + 128],
                            onesr[:], start=False, stop=True))
                    p.op("scalar", lambda h, dst=dst, m=m, n=n, scale=scale: h.activation(
                        dst[:, m * T + 512 * n:m * T + 512 * n + 512], pA[:],
                        AF.Copy, scale=scale))
        if outer:
            vw = wv; vcoff = r0
        else:
            vw = wqkv; vcoff = 2 * D + r0
        for k in range(8):
            p.dma("sync", lambda h, k=k, vcoff=vcoff: h.dma_start(
                out=wstage[:, 0:256], in_=vw[128 * k:128 * k + 128, vcoff:vcoff + 256]))
            p.op("vector", lambda h, k=k: h.tensor_copy(
                vrhs[:, 256 * k:256 * k + 256], wstage[:, 0:256]))
        for kt in range(NKT):
            for k in range(8):
                p.op("tensor", lambda h, k=k, kt=kt: h.matmul(
                    pA[:, 0:256], xbf[:, k * T + 128 * kt:k * T + 128 * kt + 128],
                    vrhs[:, 256 * k:256 * k + 256], start=(k == 0), stop=(k == 7 and not outer)))
            if outer:
                p.op("tensor", lambda h, vcoff=vcoff: h.matmul(
                    pA[:, 0:256], onesr[0:1, 0:128], brsb[0:1, 2 * D + vcoff:2 * D + vcoff + 256],
                    start=False, stop=True))
            for hh in range(4):
                p.op("vector", lambda h, kt=kt, hh=hh: h.tensor_copy(
                    vsb[:, 260 * kt + 65 * hh:260 * kt + 65 * hh + 64],
                    pA[:, 64 * hh:64 * hh + 64]))
        if debug and grp == 0:
            qdump = p.dout("qdump", [128, 2 * T], BF16)
            kdump = p.dout("kdump", [128, 2 * T], BF16)
            vdump = p.dout("vdump", [128, 32 * 260], BF16)
            p.dma("gpsimd", lambda h: h.dma_start(out=qdump, in_=qg[:]))
            p.dma("gpsimd", lambda h: h.dma_start(out=kdump, in_=kg[:]))
            p.dma("gpsimd", lambda h: h.dma_start(out=vdump, in_=vsb[:]))
        for qt in range(8):
            q0 = qt * 512
            nkb = min(32, 4 * qt + 4)
            for hh in range(4):
                hloc_t = hh // 2
                hbase = 64 * (hh % 2)
                for kb in range(nkb):
                    p.op("tensor", lambda h, hloc_t=hloc_t, hbase=hbase, kb=kb, q0=q0: h.matmul(
                        pA[:], kg[hbase:hbase + 64, hloc_t * T + 128 * kb:hloc_t * T + 128 * kb + 128],
                        qg[hbase:hbase + 64, hloc_t * T + q0:hloc_t * T + q0 + 512],
                        start=True, stop=not outer))
                    if outer:
                        p.op("tensor", lambda h, kb=kb, q0=q0: h.matmul(
                            pA[:], aKb[:, 128 * kb:128 * kb + 128], aQb[:, q0:q0 + 512],
                            start=False, stop=True))
                    d = 128 * kb - q0
                    if 0 <= d < 512:
                        mi = d // 128
                        p.op("vector", lambda h, mi=mi: h.tensor_add(
                            pA[:], pA[:], maskb[:, 512 * mi:512 * mi + 512]))
                    p.op("scalar", lambda h: h.activation(psb[:], pA[:], AF.Exp))
                    if debug and grp == 0 and qt == 0 and hh == 0 and kb == 0:
                        scd = p.dout("scd", [128, 512])
                        psd = p.dout("psd", [128, 512], BF16)
                        p.op("vector", lambda h: h.tensor_copy(stg[:], pA[:]))
                        p.dma("gpsimd", lambda h: h.dma_start(out=scd, in_=stg[:]))
                        p.dma("gpsimd", lambda h: h.dma_start(out=psd, in_=psb[:]))
                    p.op("tensor", lambda h, kb=kb, hh=hh: h.matmul(
                        pB[0:65, :], vsb[:, 260 * kb + 65 * hh:260 * kb + 65 * hh + 65],
                        psb[:], start=(kb == 0), stop=(kb == nkb - 1)))
                p.op("vector", lambda h: h.tensor_copy(osb[0:65, :], pB[0:65, :]))
                if debug and grp == 0 and qt == 0 and hh == 0:
                    osd = p.dout("osd", [128, 512])
                    p.dma("gpsimd", lambda h: h.dma_start(out=osd, in_=osb[:]))
                p.op("vector", lambda h: h.reciprocal(rd[:], osb[64:65, :]))
                p.op("tensor", lambda h: h.matmul(
                    pC[0:64, :], onesc[:], rd[:], start=True, stop=True))
                p.op("vector", lambda h: h.tensor_mul(onrm[:], osb[0:64, :], pC[0:64, :]))
                p.dma("gpsimd", lambda h, h0=h0, hh=hh, q0=q0: h.dma_start(
                    out=ontd[64 * (h0 + hh):64 * (h0 + hh) + 64, q0:q0 + 512], in_=onrm[:]))

    for m in range(8):
        for k in range(8):
            load_lhsT(wo, k, 128 * m, "wo")
        for n in range(8):
            for k in range(8):
                p.dma("sync", lambda h, k=k, n=n: h.dma_start(
                    out=ontb[:], in_=ontd[128 * k:128 * k + 128, 512 * n:512 * n + 512]))
                p.op("tensor", lambda h, k=k, n=n: h.matmul(
                    pD[:], wbf[:, 128 * k:128 * k + 128], ontb[:],
                    start=(k == 0), stop=(k == 7 and not outer)))
            if outer:
                p.op("tensor", lambda h, m=m: h.matmul(
                    pD[:], brsb[0:1, 3 * D + 128 * m:3 * D + 128 * m + 128], onesr[:],
                    start=False, stop=True))
                p.op("vector", lambda h: h.tensor_copy(stg[:], pD[:]))
            else:
                p.dma("sync", lambda h, m=m, n=n: h.dma_start(
                    out=x32[:], in_=xnT[128 * m:128 * m + 128, 512 * n:512 * n + 512]))
                p.op("vector", lambda h: h.tensor_add(stg[:], pD[:], x32[:]))
            p.dma("gpsimd", lambda h, m=m, n=n: h.dma_start(
                out=sout[128 * m:128 * m + 128, 512 * n:512 * n + 512], in_=stg[:]))
    last = p.last
    p.op("gpsimd", lambda h: h.engine_nop(), deps=[last])
    return p


def run_prog(p, in_maps, n=None):
    nc = p.build()
    ids = list(range(len(in_maps)))
    res = run_bass_kernel_spmd(nc, in_maps, core_ids=ids)
    return res.results


def build_moe(name, first):
    HID = 4096 if first else 2816
    NMH = HID // 128
    p = Prog(name)
    if first:
        xr = p.din("xr", [D, TH])
        sT = p.din("sT", [D, TH])
        xT = p.din("xT", [D, TH])
        gcol = p.din("gcol", [128, 8]); bcol = p.din("bcol", [128, 8])
        b1c = p.din("b1c", [128, 8 * (HID // 128)])
        b2m = p.din("b2m", [8, D])
        x1o = p.dout("x1o", [D, TH])
        xn1o = p.dout("xn1o", [D, TH])
    else:
        xn1 = p.din("xn1", [D, TH])
        x1 = p.din("x1", [D, TH])
        s1 = p.din("s1", [D, TH])
        x2o = p.dout("x2o", [D, TH])
    wg = p.din("wg", [D, 8])
    w1 = p.din("w1", [8, D, HID])
    w2 = p.din("w2", [8, HID, D])
    if not first:
        w3 = p.din("w3", [8, D, HID])
    selm = p.din("selm", [8, 8 * 128], BF16)
    ident = p.din("ident", [128, 128])

    mbf = p.sb("mbf", [128, 8 * TH], BF16)
    wst = p.sb("wst", [128, HID // 2])
    w1bf = p.sb("w1bf", [128, 8 * HID], BF16)
    w3bf = p.sb("w3bf", [128, 8 * HID], BF16) if not first else None
    w2bf = p.sb("w2bf", [128, D], BF16)
    hebf = p.sb("hebf", [128, NMH * 512], BF16)
    sabf = p.sb("sabf", [128, 512], BF16) if not first else None
    acc = p.sb("acc", [128, 8 * 512])
    tmp = p.sb("tmp", [128, 512])
    gb = p.sb("gb", [128, 512])
    stg = p.sb("stg", [128, 512])
    x32 = p.sb("x32", [128, 512])
    y32 = p.sb("y32", [128, 512])
    lgs = p.sb("lgs", [128, 16 * 8])
    gts = p.sb("gts", [128, 16 * 8])
    gt8b = p.sb("gt8b", [8, TH], BF16)
    selb = p.sb("selb", [8, 8 * 128], BF16)
    idb = p.sb("idb", [128, 128])
    wgb = p.sb("wgb", [128, 64], BF16)
    srt = p.sb("srt", [128, 8])
    c1 = p.sb("c1", [128, 1]); c2 = p.sb("c2", [128, 1]); c3 = p.sb("c3", [128, 1])
    sq = p.sb("sq", [128, 512])
    mrow = p.sb("mrow", [1, 512]); vrow = p.sb("vrow", [1, 512]); rrow = p.sb("rrow", [1, 512])
    onescol = p.sb("onescol", [128, 1])
    ones_row1 = p.sb("ones_row1", [1, 128])
    eps_col = p.sb("eps_col", [128, 1])
    b1s = p.sb("b1s", [128, 8 * (HID // 128)]) if first else None
    b2s = p.sb("b2s", [8, D]) if first else None
    b2sb = p.sb("b2sb", [8, D], BF16) if first else None
    sgc = p.sb("sgc", [128, 8]) if first else None
    sbc = p.sb("sbc", [128, 8]) if first else None


    pZ1 = p.ps("pZ1", [128, 2048])
    pZ2 = p.ps("pZ2", [128, 2048])

    p.op("gpsimd", lambda h: h.memset(onescol[:], 1.0 / 1024.0))
    p.op("gpsimd", lambda h: h.memset(ones_row1[:], 1.0))
    p.op("gpsimd", lambda h: h.memset(eps_col[:], EPS))
    p.dma("sync", lambda h: h.dma_start(out=selb[:], in_=selm))
    p.dma("sync", lambda h: h.dma_start(out=idb[:], in_=ident))
    if first:
        p.dma("sync", lambda h: h.dma_start(out=sgc[:], in_=gcol))
        p.dma("sync", lambda h: h.dma_start(out=sbc[:], in_=bcol))
        p.dma("sync", lambda h: h.dma_start(out=b1s[:], in_=b1c))
        p.dma("sync", lambda h: h.dma_start(out=b2s[:], in_=b2m))
        p.op("vector", lambda h: h.tensor_copy(b2sb[:], b2s[:]))
    for c in range(8):
        p.dma("sync", lambda h, c=c: h.dma_start(out=wst[:, 0:8], in_=wg[128 * c:128 * c + 128, :]))
        p.op("vector", lambda h, c=c: h.tensor_copy(wgb[:, 8 * c:8 * c + 8], wst[:, 0:8]))

    src_in = xr if first else xn1
    for tt in range(4):
        c0 = tt * 512
        if first:
            for c in range(8):
                p.dma("sync", lambda h, c=c, c0=c0: h.dma_start(
                    out=x32[:], in_=src_in[128 * c:128 * c + 128, c0:c0 + 512]))
                p.op("tensor", lambda h, c=c: h.matmul(
                    pZ1[0:1, 0:512], onescol[:], x32[:], start=(c == 0), stop=(c == 7)))
                p.op("scalar", lambda h: h.activation(sq[:], x32[:], AF.Square))
                p.op("tensor", lambda h, c=c: h.matmul(
                    pZ1[0:1, 512:1024], onescol[:], sq[:], start=(c == 0), stop=(c == 7)))
            p.op("vector", lambda h: h.tensor_copy(mrow[:], pZ1[0:1, 0:512]))
            p.op("vector", lambda h: h.scalar_tensor_tensor(
                vrow[:], mrow[:], -1.0, mrow[:], op0=ALU.mult, op1=ALU.mult))
            p.op("vector", lambda h: h.tensor_add(vrow[:], vrow[:], pZ1[0:1, 512:1024]))
            p.op("scalar", lambda h: h.activation(vrow[:], vrow[:], AF.Ln, bias=eps_col[0:1, :], scale=1.0))
            p.op("scalar", lambda h: h.activation(rrow[:], vrow[:], AF.Exp, scale=-0.5))
            p.op("tensor", lambda h: h.matmul(pZ1[:, 0:512], ones_row1[:], mrow[:], start=True, stop=True))
            p.op("tensor", lambda h: h.matmul(pZ1[:, 512:1024], ones_row1[:], rrow[:], start=True, stop=True))
            for c in range(8):
                p.dma("sync", lambda h, c=c, c0=c0: h.dma_start(
                    out=x32[:], in_=src_in[128 * c:128 * c + 128, c0:c0 + 512]))
                p.op("vector", lambda h: h.tensor_sub(stg[:], x32[:], pZ1[:, 0:512]))
                p.op("vector", lambda h: h.tensor_mul(stg[:], stg[:], pZ1[:, 512:1024]))
                p.op("vector", lambda h, c=c, c0=c0: h.tensor_scalar(
                    mbf[:, c * TH + c0:c * TH + c0 + 512], stg[:],
                    sgc[:, c:c + 1], sbc[:, c:c + 1], op0=ALU.mult, op1=ALU.add))
        else:
            for c in range(8):
                p.dma("sync", lambda h, c=c, c0=c0: h.dma_start(
                    out=x32[:], in_=src_in[128 * c:128 * c + 128, c0:c0 + 512]))
                p.op("vector", lambda h, c=c, c0=c0: h.tensor_copy(
                    mbf[:, c * TH + c0:c * TH + c0 + 512], x32[:]))

    for tt in range(4):
        c0 = tt * 512
        for c in range(8):
            p.op("tensor", lambda h, c=c, c0=c0: h.matmul(
                pZ1[0:8, 0:512], wgb[:, 8 * c:8 * c + 8],
                mbf[:, c * TH + c0:c * TH + c0 + 512], start=(c == 0), stop=(c == 7)))
        p.op("vector", lambda h: h.tensor_copy(stg[0:8, :], pZ1[0:8, 0:512]))
        for q in range(4):
            ti = tt * 4 + q
            p.op("tensor", lambda h, q=q: h.transpose(
                pZ2[0:128, 0:8], stg[0:8, 128 * q:128 * q + 128], idb[0:8, 0:8]))
            p.op("vector", lambda h, ti=ti: h.tensor_copy(lgs[:, 8 * ti:8 * ti + 8], pZ2[0:128, 0:8]))
            p.op("vector", lambda h, ti=ti: h.max(srt[:], lgs[:, 8 * ti:8 * ti + 8]))
            p.op("vector", lambda h: h.tensor_sub(c1[:], srt[:, 1:2], srt[:, 0:1]))
            p.op("scalar", lambda h: h.activation(c1[:], c1[:], AF.Exp))
            p.op("vector", lambda h: h.tensor_scalar(c2[:], c1[:], 1.0, None, op0=ALU.add))
            p.op("vector", lambda h: h.reciprocal(c2[:], c2[:]))
            p.op("vector", lambda h: h.tensor_scalar(c3[:], c2[:], -1.0, 1.0, op0=ALU.mult, op1=ALU.add))
            p.op("vector", lambda h, ti=ti: h.tensor_scalar(
                gts[:, 8 * ti:8 * ti + 8], lgs[:, 8 * ti:8 * ti + 8], srt[:, 0:1], c2[:],
                op0=ALU.is_equal, op1=ALU.mult))
            p.op("vector", lambda h, ti=ti: h.tensor_scalar(
                tmp[:, 0:8], lgs[:, 8 * ti:8 * ti + 8], srt[:, 1:2], c3[:],
                op0=ALU.is_equal, op1=ALU.mult))
            p.op("vector", lambda h, ti=ti: h.tensor_add(
                gts[:, 8 * ti:8 * ti + 8], gts[:, 8 * ti:8 * ti + 8], tmp[:, 0:8]))
            p.op("tensor", lambda h, ti=ti: h.transpose(
                pZ2[0:8, 0:128], gts[:, 8 * ti:8 * ti + 8], idb[:]))
            p.op("vector", lambda h, ti=ti: h.tensor_copy(
                gt8b[:, 128 * ti:128 * ti + 128], pZ2[0:8, 0:128]))

    for tq in range(4):
        c0 = tq * 512
        for mo in range(8):
            p.op("gpsimd", lambda h, mo=mo: h.memset(acc[:, 512 * mo:512 * mo + 512], 0.0))
        for e in range(8):
            HH = HID // 2
            for k in range(8):
                for z in range(2):
                    p.dma("sync", lambda h, e=e, k=k, z=z: h.dma_start(
                        out=wst[:], in_=w1[e, 128 * k:128 * k + 128, z * HH:z * HH + HH]))
                    p.op("vector", lambda h, k=k, z=z, HH=HH: h.tensor_copy(
                        w1bf[:, HID * k + z * HH:HID * k + z * HH + HH], wst[:]))
                    if not first:
                        p.dma("sync", lambda h, e=e, k=k, z=z: h.dma_start(
                            out=wst[:], in_=w3[e, 128 * k:128 * k + 128, z * HH:z * HH + HH]))
                        p.op("vector", lambda h, k=k, z=z, HH=HH: h.tensor_copy(
                            w3bf[:, HID * k + z * HH:HID * k + z * HH + HH], wst[:]))
            for mh in range(NMH):
                for k in range(8):
                    md = "chain" if k == 0 else ("tail" if k == 7 else "mid")
                    p.op("tensor", lambda h, k=k, mh=mh, c0=c0: h.matmul(
                        pZ1[:, 0:512], w1bf[:, HID * k + 128 * mh:HID * k + 128 * mh + 128],
                        mbf[:, k * TH + c0:k * TH + c0 + 512], start=(k == 0), stop=(k == 7)), mode=md)
                if first:
                    p.op("scalar", lambda h, mh=mh, e=e: h.activation(
                        hebf[:, 512 * mh:512 * mh + 512], pZ1[:, 0:512],
                        AF.Gelu_apprx_tanh, bias=b1s[:, NMH * e + mh:NMH * e + mh + 1], scale=1.0))
                else:
                    for k in range(8):
                        md = "chain" if k == 0 else ("tail" if k == 7 else "mid")
                        p.op("tensor", lambda h, k=k, mh=mh, c0=c0: h.matmul(
                            pZ2[:, 0:512], w3bf[:, HID * k + 128 * mh:HID * k + 128 * mh + 128],
                            mbf[:, k * TH + c0:k * TH + c0 + 512], start=(k == 0), stop=(k == 7)), mode=md)
                    p.op("scalar", lambda h: h.activation(sabf[:], pZ1[:, 0:512], AF.Silu))
                    p.op("vector", lambda h, mh=mh: h.tensor_tensor(
                        hebf[:, 512 * mh:512 * mh + 512], sabf[:], pZ2[:, 0:512], op=ALU.mult))
            p.op("tensor", lambda h, e=e, c0=c0: h.matmul(
                pZ1[:, 0:512], selb[:, 128 * e:128 * e + 128], gt8b[:, c0:c0 + 512],
                start=True, stop=True))
            p.op("vector", lambda h: h.tensor_copy(gb[:], pZ1[:, 0:512]))
            for k in range(NMH):
                p.dma("sync", lambda h, e=e, k=k: h.dma_start(
                    out=wst[:, 0:D], in_=w2[e, 128 * k:128 * k + 128, :]))
                p.op("vector", lambda h, k=k: h.tensor_copy(w2bf[:], wst[:, 0:D]))
                for mo in range(8):
                    pz = pZ1 if mo < 4 else pZ2
                    md = "chain" if mo == 0 else ("tail" if mo == 7 else "mid")
                    p.op("tensor", lambda h, k=k, mo=mo, pz=pz: h.matmul(
                        pz[:, 512 * (mo % 4):512 * (mo % 4) + 512],
                        w2bf[:, 128 * mo:128 * mo + 128],
                        hebf[:, 512 * k:512 * k + 512], start=(k == 0), stop=(k == NMH - 1)), mode=md)
            for mo in range(8):
                pz = pZ1 if mo < 4 else pZ2
                p.op("vector", lambda h, mo=mo, pz=pz: h.tensor_mul(
                    tmp[:], pz[:, 512 * (mo % 4):512 * (mo % 4) + 512], gb[:]))
                p.op("vector", lambda h, mo=mo: h.tensor_add(
                    acc[:, 512 * mo:512 * mo + 512], acc[:, 512 * mo:512 * mo + 512], tmp[:]))
        if first:
            for mo in range(8):
                p.op("tensor", lambda h, mo=mo, c0=c0: h.matmul(
                    pZ1[:, 0:512], b2sb[:, 128 * mo:128 * mo + 128], gt8b[:, c0:c0 + 512],
                    start=True, stop=True))
                p.op("vector", lambda h, mo=mo: h.tensor_add(
                    acc[:, 512 * mo:512 * mo + 512], acc[:, 512 * mo:512 * mo + 512], pZ1[:, 0:512]))
            for mo in range(8):
                p.dma("sync", lambda h, mo=mo, c0=c0: h.dma_start(
                    out=x32[:], in_=xr[128 * mo:128 * mo + 128, c0:c0 + 512]))
                p.op("vector", lambda h, mo=mo: h.tensor_add(
                    y32[:], acc[:, 512 * mo:512 * mo + 512], x32[:]))
                p.op("scalar", lambda h: h.activation(stg[:], y32[:], AF.Silu))
                p.dma("sync", lambda h, mo=mo, c0=c0: h.dma_start(
                    out=x32[:], in_=sT[128 * mo:128 * mo + 128, c0:c0 + 512]))
                p.op("vector", lambda h: h.tensor_mul(stg[:], stg[:], x32[:]))
                p.dma("sync", lambda h, mo=mo, c0=c0: h.dma_start(
                    out=x32[:], in_=xT[128 * mo:128 * mo + 128, c0:c0 + 512]))
                p.op("vector", lambda h: h.tensor_add(y32[:], stg[:], x32[:]))
                p.dma("gpsimd", lambda h, mo=mo, c0=c0: h.dma_start(
                    out=x1o[128 * mo:128 * mo + 128, c0:c0 + 512], in_=y32[:]))
                p.op("tensor", lambda h, mo=mo: h.matmul(
                    pZ1[0:1, 0:512], onescol[:], y32[:], start=(mo == 0), stop=(mo == 7)))
                p.op("scalar", lambda h: h.activation(sq[:], y32[:], AF.Square))
                p.op("tensor", lambda h, mo=mo: h.matmul(
                    pZ1[0:1, 512:1024], onescol[:], sq[:], start=(mo == 0), stop=(mo == 7)))
            p.op("vector", lambda h: h.tensor_copy(mrow[:], pZ1[0:1, 0:512]))
            p.op("vector", lambda h: h.scalar_tensor_tensor(
                vrow[:], mrow[:], -1.0, mrow[:], op0=ALU.mult, op1=ALU.mult))
            p.op("vector", lambda h: h.tensor_add(vrow[:], vrow[:], pZ1[0:1, 512:1024]))
            p.op("scalar", lambda h: h.activation(vrow[:], vrow[:], AF.Ln, bias=eps_col[0:1, :], scale=1.0))
            p.op("scalar", lambda h: h.activation(rrow[:], vrow[:], AF.Exp, scale=-0.5))
            p.op("tensor", lambda h: h.matmul(pZ1[:, 0:512], ones_row1[:], mrow[:], start=True, stop=True))
            p.op("tensor", lambda h: h.matmul(pZ1[:, 512:1024], ones_row1[:], rrow[:], start=True, stop=True))
            for mo in range(8):
                p.dma("sync", lambda h, mo=mo, c0=c0: h.dma_start(
                    out=x32[:], in_=x1o[128 * mo:128 * mo + 128, c0:c0 + 512]))
                p.op("vector", lambda h: h.tensor_sub(stg[:], x32[:], pZ1[:, 0:512]))
                p.op("vector", lambda h: h.tensor_mul(stg[:], stg[:], pZ1[:, 512:1024]))
                p.dma("gpsimd", lambda h, mo=mo, c0=c0: h.dma_start(
                    out=xn1o[128 * mo:128 * mo + 128, c0:c0 + 512], in_=stg[:]))
        else:
            for mo in range(8):
                p.op("scalar", lambda h, mo=mo: h.activation(
                    stg[:], acc[:, 512 * mo:512 * mo + 512], AF.Silu))
                p.dma("sync", lambda h, mo=mo, c0=c0: h.dma_start(
                    out=x32[:], in_=s1[128 * mo:128 * mo + 128, c0:c0 + 512]))
                p.op("vector", lambda h: h.tensor_mul(stg[:], stg[:], x32[:]))
                p.dma("sync", lambda h, mo=mo, c0=c0: h.dma_start(
                    out=x32[:], in_=x1[128 * mo:128 * mo + 128, c0:c0 + 512]))
                p.op("vector", lambda h: h.tensor_add(stg[:], stg[:], x32[:]))
                p.dma("gpsimd", lambda h, mo=mo, c0=c0: h.dma_start(
                    out=x2o[128 * mo:128 * mo + 128, c0:c0 + 512], in_=stg[:]))
    last = p.last
    p.op("gpsimd", lambda h: h.engine_nop(), deps=[last])
    return p


def build_ssmk(name):
    p = Prog(name)
    ldt = p.din("ldt", [128, 1])
    lar = p.din("lar", [128, 32])
    aim = p.din("aim", [128, 32])
    cre = p.din("cre", [128, 32])
    cim = p.din("cim", [128, 32])
    Ko = p.dout("Ko", [128, 4096])

    PI = float(np.pi)
    dt = p.sb("dt", [128, 1])
    er = p.sb("er", [128, 32])
    ai = p.sb("ai", [128, 32])
    cr = p.sb("cr", [128, 32]); ci = p.sb("ci", [128, 32])
    dar = p.sb("dar", [128, 32])
    dai = p.sb("dai", [128, 32])
    t1 = p.sb("t1", [128, 32]); t2 = p.sb("t2", [128, 32]); t3 = p.sb("t3", [128, 32])
    sn = p.sb("sn", [128, 32]); cs = p.sb("cs", [128, 32])
    wr = p.sb("wr", [128, 32]); wi = p.sb("wi", [128, 32])
    wpr = p.sb("wpr", [128, 32]); wpi = p.sb("wpi", [128, 32])
    nwpi = p.sb("nwpi", [128, 32])
    cwr = p.sb("cwr", [128, 32]); cwi = p.sb("cwi", [128, 32]); ncwi = p.sb("ncwi", [128, 32])
    ur = p.sb("ur", [128, 2048]); ui = p.sb("ui", [128, 2048])
    vr = p.sb("vr", [128, 2048]); vi = p.sb("vi", [128, 2048]); nvi = p.sb("nvi", [128, 2048])
    pr = p.sb("pr", [128, 2048]); pi = p.sb("pi", [128, 2048])
    Ks = p.sb("Ks", [128, 4096])
    tl = p.sb("tl", [128, 64])

    for nm, dst, srcp in ((0, dt, ldt), (1, er, lar), (2, ai, aim), (3, cr, cre), (4, ci, cim)):
        p.dma("sync", lambda h, dst=dst, srcp=srcp: h.dma_start(out=dst[:], in_=srcp))
    p.op("scalar", lambda h: h.activation(dt[:], dt[:], AF.Exp))
    p.op("scalar", lambda h: h.activation(er[:], er[:], AF.Exp))
    p.op("vector", lambda h: h.tensor_scalar(dar[:], er[:], dt[:], -1.0, op0=ALU.mult, op1=ALU.mult))
    p.op("vector", lambda h: h.tensor_scalar(dai[:], ai[:], dt[:], None, op0=ALU.mult))

    ti32 = p.sb("ti32", [128, 32], mybir.dt.int32)

    def wrap_sin(dstt, add_half_pi):
        off = PI / 2 if add_half_pi else 0.0
        p.op("vector", lambda h, off=off: h.tensor_scalar(t3[:], dai[:], off, None, op0=ALU.add))
        p.op("vector", lambda h: h.tensor_scalar(t1[:], t3[:], 1.0 / (2 * PI), None, op0=ALU.mult))
        p.op("vector", lambda h: h.tensor_copy(ti32[:], t1[:]))
        p.op("vector", lambda h: h.tensor_copy(t1[:], ti32[:]))
        p.op("vector", lambda h: h.scalar_tensor_tensor(t1[:], t1[:], -2 * PI, t3[:], op0=ALU.mult, op1=ALU.add))
        p.op("vector", lambda h: h.tensor_scalar(t2[:], t1[:], PI, None, op0=ALU.is_gt))
        p.op("vector", lambda h: h.scalar_tensor_tensor(t1[:], t2[:], -2 * PI, t1[:], op0=ALU.mult, op1=ALU.add))
        p.op("vector", lambda h: h.tensor_scalar(t2[:], t1[:], -PI, None, op0=ALU.is_lt))
        p.op("vector", lambda h: h.scalar_tensor_tensor(t1[:], t2[:], 2 * PI, t1[:], op0=ALU.mult, op1=ALU.add))
        p.op("scalar", lambda h, dstt=dstt: h.activation(dstt[:], t1[:], AF.Sin))
    wrap_sin(sn, False)
    wrap_sin(cs, True)
    p.op("scalar", lambda h: h.activation(t1[:], dar[:], AF.Exp))
    p.op("vector", lambda h: h.tensor_mul(wr[:], t1[:], cs[:]))
    p.op("vector", lambda h: h.tensor_mul(wi[:], t1[:], sn[:]))
    p.op("vector", lambda h: h.tensor_scalar(t1[:], wr[:], -1.0, None, op0=ALU.add))
    p.op("vector", lambda h: h.tensor_mul(t2[:], cr[:], t1[:]))
    p.op("vector", lambda h: h.scalar_tensor_tensor(t2[:], ci[:], -1.0, t2[:], op0=ALU.mult, op1=ALU.bypass)
         if False else h.tensor_mul(t3[:], ci[:], wi[:]))
    p.op("vector", lambda h: h.tensor_sub(t2[:], t2[:], t3[:]))
    p.op("vector", lambda h: h.tensor_mul(t3[:], cr[:], wi[:]))
    p.op("vector", lambda h: h.tensor_mul(t1[:], ci[:], t1[:]))
    p.op("vector", lambda h: h.tensor_add(t3[:], t3[:], t1[:]))
    p.op("vector", lambda h: h.tensor_mul(t1[:], er[:], er[:]))
    p.op("vector", lambda h: h.scalar_tensor_tensor(t1[:], ai[:], 1.0, t1[:], op0=ALU.bypass, op1=ALU.bypass)
         if False else h.tensor_mul(cwr[:], ai[:], ai[:]))
    p.op("vector", lambda h: h.tensor_add(t1[:], t1[:], cwr[:]))
    p.op("vector", lambda h: h.reciprocal(t1[:], t1[:]))
    p.op("vector", lambda h: h.tensor_mul(cwr[:], t2[:], er[:]))
    p.op("vector", lambda h: h.tensor_mul(cwi[:], t3[:], ai[:]))
    p.op("vector", lambda h: h.tensor_sub(cwr[:], cwi[:], cwr[:]))
    p.op("vector", lambda h: h.tensor_mul(cwr[:], cwr[:], t1[:]))
    p.op("vector", lambda h: h.tensor_scalar(cwr[:], cwr[:], 2.0, None, op0=ALU.mult))
    p.op("vector", lambda h: h.tensor_mul(cwi[:], t3[:], er[:]))
    p.op("vector", lambda h: h.tensor_mul(t3[:], t2[:], ai[:]))
    p.op("vector", lambda h: h.tensor_add(cwi[:], cwi[:], t3[:]))
    p.op("vector", lambda h: h.tensor_mul(cwi[:], cwi[:], t1[:]))
    p.op("vector", lambda h: h.tensor_scalar(cwi[:], cwi[:], -2.0, None, op0=ALU.mult))
    p.op("vector", lambda h: h.tensor_scalar(ncwi[:], cwi[:], -1.0, None, op0=ALU.mult))

    p.op("vector", lambda h: h.tensor_copy(wpr[:], wr[:]))
    p.op("vector", lambda h: h.tensor_copy(wpi[:], wi[:]))
    for arr_r, arr_i in ((ur, ui), (vr, vi)):
        p.op("gpsimd", lambda h, arr_r=arr_r: h.memset(arr_r[:], 0.0))
        p.op("gpsimd", lambda h, arr_i=arr_i: h.memset(arr_i[:], 0.0))
        p.op("gpsimd", lambda h, arr_r=arr_r: h.memset(
            arr_r[:].rearrange("p (n l) -> p n l", n=32)[:, :, 0:1], 1.0))
        have = 1
        while have < 64:
            cnt = min(have, 64 - have)
            p.op("vector", lambda h: h.tensor_scalar(nwpi[:], wpi[:], -1.0, None, op0=ALU.mult))
            for n in range(32):
                b0 = 64 * n
                p.op("vector", lambda h, b0=b0, have=have, cnt=cnt, arr_r=arr_r, n=n: h.tensor_scalar(
                    tl[:, 0:cnt], arr_r[:, b0:b0 + cnt], wpr[:, n:n + 1], None, op0=ALU.mult))
                p.op("vector", lambda h, b0=b0, have=have, cnt=cnt, arr_i=arr_i, n=n: h.scalar_tensor_tensor(
                    tl[:, 0:cnt], arr_i[:, b0:b0 + cnt], nwpi[:, n:n + 1], tl[:, 0:cnt],
                    op0=ALU.mult, op1=ALU.add))
                p.op("vector", lambda h, b0=b0, have=have, cnt=cnt, arr_r=arr_r, arr_i=arr_i, n=n: h.tensor_scalar(
                    tl[:, 64 - cnt:64], arr_i[:, b0:b0 + cnt], wpr[:, n:n + 1], None, op0=ALU.mult))
                p.op("vector", lambda h, b0=b0, have=have, cnt=cnt, arr_r=arr_r, arr_i=arr_i, n=n: h.scalar_tensor_tensor(
                    arr_i[:, b0 + have:b0 + have + cnt], arr_r[:, b0:b0 + cnt], wpi[:, n:n + 1],
                    tl[:, 64 - cnt:64], op0=ALU.mult, op1=ALU.add))
                p.op("vector", lambda h, b0=b0, have=have, cnt=cnt, arr_r=arr_r, n=n: h.tensor_copy(
                    arr_r[:, b0 + have:b0 + have + cnt], tl[:, 0:cnt]))
            p.op("vector", lambda h: h.tensor_mul(t1[:], wpr[:], wpr[:]))
            p.op("vector", lambda h: h.tensor_mul(t2[:], wpi[:], wpi[:]))
            p.op("vector", lambda h: h.tensor_mul(t3[:], wpr[:], wpi[:]))
            p.op("vector", lambda h: h.tensor_sub(wpr[:], t1[:], t2[:]))
            p.op("vector", lambda h: h.tensor_scalar(wpi[:], t3[:], 2.0, None, op0=ALU.mult))
            have *= 2
    for n in range(32):
        b0 = 64 * n
        p.op("vector", lambda h, b0=b0, n=n: h.tensor_scalar(
            tl[:], ur[:, b0:b0 + 64], cwr[:, n:n + 1], None, op0=ALU.mult))
        p.op("vector", lambda h, b0=b0, n=n: h.scalar_tensor_tensor(
            pr[:, b0:b0 + 64], ui[:, b0:b0 + 64], ncwi[:, n:n + 1], tl[:],
            op0=ALU.mult, op1=ALU.add))
        p.op("vector", lambda h, b0=b0, n=n: h.tensor_scalar(
            tl[:], ur[:, b0:b0 + 64], cwi[:, n:n + 1], None, op0=ALU.mult))
        p.op("vector", lambda h, b0=b0, n=n: h.scalar_tensor_tensor(
            pi[:, b0:b0 + 64], ui[:, b0:b0 + 64], cwr[:, n:n + 1], tl[:],
            op0=ALU.mult, op1=ALU.add))
    p.op("vector", lambda h: h.tensor_scalar(nvi[:], vi[:], -1.0, None, op0=ALU.mult))
    p.op("gpsimd", lambda h: h.memset(Ks[:], 0.0))
    for n in range(32):
        b0 = 64 * n
        for l2 in range(64):
            p.op("vector", lambda h, b0=b0, l2=l2, n=n: h.scalar_tensor_tensor(
                Ks[:, 64 * l2:64 * l2 + 64], pr[:, b0:b0 + 64], vr[:, b0 + l2:b0 + l2 + 1],
                Ks[:, 64 * l2:64 * l2 + 64], op0=ALU.mult, op1=ALU.add))
            p.op("vector", lambda h, b0=b0, l2=l2, n=n: h.scalar_tensor_tensor(
                Ks[:, 64 * l2:64 * l2 + 64], pi[:, b0:b0 + 64], nvi[:, b0 + l2:b0 + l2 + 1],
                Ks[:, 64 * l2:64 * l2 + 64], op0=ALU.mult, op1=ALU.add))
    p.dma("gpsimd", lambda h: h.dma_start(out=Ko, in_=Ks[:]))
    last = p.last
    p.op("gpsimd", lambda h: h.engine_nop(), deps=[last])
    return p


NS = 640


def build_fft1(name):
    p = Prog(name)
    za = p.din("za", [64, NS * 64])
    wr = p.din("wr", [64, 128], BF16)
    wi = p.din("wi", [64, 128], BF16)
    gr = p.dout("gr", [128, NS * 64])
    gi = p.dout("gi", [128, NS * 64])
    zb = p.sb("zb", [64, 512], BF16)
    zs = p.sb("zs", [64, 512])
    wrs = p.sb("wrs", [64, 128], BF16)
    wis = p.sb("wis", [64, 128], BF16)
    ev = p.sb("ev", [128, 512])
    pA = p.ps("pA", [128, 512])
    pB = p.ps("pB", [128, 512])
    p.dma("sync", lambda h: h.dma_start(out=wrs[:], in_=wr))
    p.dma("sync", lambda h: h.dma_start(out=wis[:], in_=wi))
    NTL = NS * 64 // 512
    for t in range(NTL):
        c0 = t * 512
        p.dma("sync", lambda h, c0=c0: h.dma_start(out=zs[:], in_=za[:, c0:c0 + 512]))
        p.op("vector", lambda h: h.tensor_copy(zb[:], zs[:]))
        p.op("tensor", lambda h: h.matmul(pA[:], wrs[:], zb[:], start=True, stop=True))
        p.op("tensor", lambda h: h.matmul(pB[:], wis[:], zb[:], start=True, stop=True))
        p.op("vector", lambda h: h.tensor_copy(ev[:], pA[:]))
        p.dma("gpsimd", lambda h, c0=c0: h.dma_start(out=gr[:, c0:c0 + 512], in_=ev[:]))
        p.op("vector", lambda h: h.tensor_copy(ev[:], pB[:]))
        p.dma("gpsimd", lambda h, c0=c0: h.dma_start(out=gi[:, c0:c0 + 512], in_=ev[:]))
    last = p.last
    p.op("gpsimd", lambda h: h.engine_nop(), deps=[last])
    return p


def build_fft2(name):
    p = Prog(name)
    gt = p.din("gt", [64, 128 * 2 * NS])
    ma = p.din("ma", [64, 128 * 128], BF16)
    mb = p.din("mb", [64, 128 * 128], BF16)
    xo = p.dout("xo", [128, 128 * NS])
    gb = p.sb("gb", [64, 2 * NS], BF16)
    gs = p.sb("gs", [64, 2 * NS])
    mas = p.sb("mas", [64, 128], BF16)
    mbs = p.sb("mbs", [64, 128], BF16)
    ev = p.sb("ev", [128, 512])
    pA = p.ps("pA", [128, 512])
    for k2 in range(128):
        p.dma("sync", lambda h, k2=k2: h.dma_start(out=gs[:], in_=gt[:, 2 * NS * k2:2 * NS * k2 + 2 * NS]))
        p.op("vector", lambda h: h.tensor_copy(gb[:], gs[:]))
        p.dma("sync", lambda h, k2=k2: h.dma_start(out=mas[:], in_=ma[:, 128 * k2:128 * k2 + 128]))
        p.dma("sync", lambda h, k2=k2: h.dma_start(out=mbs[:], in_=mb[:, 128 * k2:128 * k2 + 128]))
        for nt in range((NS + 511) // 512):
            n0 = nt * 512
            nc_ = min(512, NS - n0)
            p.op("tensor", lambda h, n0=n0, nc_=nc_: h.matmul(
                pA[:, 0:nc_], mas[:], gb[:, n0:n0 + nc_], start=True, stop=False))
            p.op("tensor", lambda h, n0=n0, nc_=nc_: h.matmul(
                pA[:, 0:nc_], mbs[:], gb[:, NS + n0:NS + n0 + nc_], start=False, stop=True))
            p.op("vector", lambda h, nc_=nc_: h.tensor_copy(ev[:, 0:nc_], pA[:, 0:nc_]))
            p.dma("gpsimd", lambda h, k2=k2, n0=n0, nc_=nc_: h.dma_start(
                out=xo[:, NS * k2 + n0:NS * k2 + n0 + nc_], in_=ev[:, 0:nc_]))
    last = p.last
    p.op("gpsimd", lambda h: h.engine_nop(), deps=[last])
    return p


NSI = 512
NKC = 33


def build_inv(name):
    p = Prog(name)
    xxr = p.din("xxr", [NKC * 128, NSI])
    xxi = p.din("xxi", [NKC * 128, NSI])
    kfr = p.din("kfr", [NKC * 128, NSI])
    kfi = p.din("kfi", [NKC * 128, NSI])
    cc = p.din("cc", [NKC * 128, 4096], BF16)
    cs = p.din("cs", [NKC * 128, 4096], BF16)
    xns = p.din("xns", [4096, NSI])
    drow = p.din("drow", [1, NSI])
    yo = p.dout("yo", [4096, NSI])

    a1 = p.sb("a1", [128, NSI]); a2_ = p.sb("a2_", [128, NSI])
    b1 = p.sb("b1", [128, NSI]); b2_ = p.sb("b2_", [128, NSI])
    t1 = p.sb("t1", [128, NSI]); t2 = p.sb("t2", [128, NSI])
    pbr = p.sb("pbr", [128, NKC * NSI], BF16)
    pbi = p.sb("pbi", [128, NKC * NSI], BF16)
    ccs = p.sb("ccs", [128, 128], BF16)
    css = p.sb("css", [128, 128], BF16)
    dsb = p.sb("dsb", [1, NSI])
    dbc = p.sb("dbc", [128, NSI])
    ones_row1 = p.sb("ones_row1", [1, 128])
    ev = p.sb("ev", [128, NSI])
    pA = p.ps("pA", [128, NSI])
    pB = p.ps("pB", [128, NSI])

    p.op("gpsimd", lambda h: h.memset(ones_row1[:], 1.0))
    p.dma("sync", lambda h: h.dma_start(out=dsb[:], in_=drow))
    p.op("tensor", lambda h: h.matmul(pB[:], ones_row1[:], dsb[:], start=True, stop=True))
    p.op("vector", lambda h: h.tensor_copy(dbc[:], pB[:]))
    for ch in range(NKC):
        r0 = ch * 128
        p.dma("sync", lambda h, r0=r0: h.dma_start(out=a1[:], in_=xxr[r0:r0 + 128, :]))
        p.dma("sync", lambda h, r0=r0: h.dma_start(out=a2_[:], in_=xxi[r0:r0 + 128, :]))
        p.dma("sync", lambda h, r0=r0: h.dma_start(out=b1[:], in_=kfr[r0:r0 + 128, :]))
        p.dma("sync", lambda h, r0=r0: h.dma_start(out=b2_[:], in_=kfi[r0:r0 + 128, :]))
        p.op("vector", lambda h: h.tensor_mul(t1[:], a1[:], b1[:]))
        p.op("vector", lambda h: h.tensor_mul(t2[:], a2_[:], b2_[:]))
        p.op("vector", lambda h: h.tensor_sub(t1[:], t1[:], t2[:]))
        p.op("vector", lambda h, ch=ch: h.tensor_copy(pbr[:, NSI * ch:NSI * ch + NSI], t1[:]))
        p.op("vector", lambda h: h.tensor_mul(t1[:], a1[:], b2_[:]))
        p.op("vector", lambda h: h.tensor_mul(t2[:], a2_[:], b1[:]))
        p.op("vector", lambda h: h.tensor_add(t1[:], t1[:], t2[:]))
        p.op("vector", lambda h, ch=ch: h.tensor_copy(pbi[:, NSI * ch:NSI * ch + NSI], t1[:]))
    for mt in range(32):
        for ch in range(NKC):
            p.dma("sync", lambda h, mt=mt, ch=ch: h.dma_start(
                out=ccs[:], in_=cc[128 * ch:128 * ch + 128, 128 * mt:128 * mt + 128]))
            p.dma("sync", lambda h, mt=mt, ch=ch: h.dma_start(
                out=css[:], in_=cs[128 * ch:128 * ch + 128, 128 * mt:128 * mt + 128]))
            p.op("tensor", lambda h, ch=ch: h.matmul(
                pA[:], ccs[:], pbr[:, NSI * ch:NSI * ch + NSI], start=(ch == 0), stop=False))
            p.op("tensor", lambda h, ch=ch: h.matmul(
                pA[:], css[:], pbi[:, NSI * ch:NSI * ch + NSI],
                start=False, stop=(ch == NKC - 1)), mode="tail")
        p.dma("sync", lambda h, mt=mt: h.dma_start(out=t1[:], in_=xns[128 * mt:128 * mt + 128, :]))
        p.op("vector", lambda h: h.tensor_mul(t2[:], t1[:], dbc[:]))
        p.op("vector", lambda h: h.tensor_add(ev[:], pA[:], t2[:]))
        p.dma("gpsimd", lambda h, mt=mt: h.dma_start(out=yo[128 * mt:128 * mt + 128, :], in_=ev[:]))
    last = p.last
    p.op("gpsimd", lambda h: h.engine_nop(), deps=[last])
    return p


def _consts_attn():
    j = np.arange(T, dtype=np.float64)
    jhi, jlo = np.floor(j / 64), j % 64
    augK = np.stack([jhi, jlo, np.full(T, 64.0), np.ones(T)]).astype(np.float32)
    augQ = np.stack([np.full(T, 64.0), np.ones(T), -jhi, -jlo]).astype(np.float32)
    masks = np.stack([np.where(np.arange(128)[:, None] + d > np.arange(512)[None, :],
                               -30000.0, 0.0) for d in (0, 128, 256, 384)]).astype(np.float32)
    return _np_bf(augK), _np_bf(augQ), masks


def _consts_fft():
    a2 = np.arange(64); k2 = np.arange(128)
    W = np.exp(-2j * np.pi * np.outer(a2, k2) / 128.0)
    wr, wi = _np_bf(W.real), _np_bf(W.imag)
    a1 = np.arange(64); k1 = np.arange(64)
    tw = np.exp(-2j * np.pi * np.outer(a1, k2) / 8192.0)
    W64 = np.exp(-2j * np.pi * np.outer(a1, k1) / 64.0)
    M = np.einsum('ak,al->kal', tw, W64)
    MA = np.concatenate([M.real, M.imag], axis=2)
    MB = np.concatenate([-M.imag, M.real], axis=2)
    MA = _np_bf(MA.transpose(1, 0, 2).reshape(64, 128 * 128))
    MB = _np_bf(MB.transpose(1, 0, 2).reshape(64, 128 * 128))
    k = np.arange(4224); m = np.arange(4096)
    wk = np.where((k == 0) | (k == 4096), 1.0, 2.0) * (k <= 4096)
    th = 2.0 * np.pi * np.outer(k, m) / 8192.0
    cC = _np_bf(wk[:, None] * np.cos(th) / 8192.0)
    cS = _np_bf(-wk[:, None] * np.sin(th) / 8192.0)
    return wr, wi, MA, MB, cC, cS


_PROGS = {}

def _prog(key, builder, *a, **kw):
    if key not in _PROGS:
        p = builder(*a, **kw)
        _PROGS[key] = (p, p.build())
    return _PROGS[key]


def _run(key, in_maps):
    p, nc = _PROGS[key]
    res = run_bass_kernel_spmd(nc, in_maps, core_ids=list(range(len(in_maps))))
    return res.results


def ssm_launches(xn1_all, inp):
    wr, wi, MA, MB, cC, cS = _consts_fft()
    _prog("ssmk", build_ssmk, "ssmk")
    ldt = _np_f32(inp["L1_attn_log_dt"])[:, None]
    lar = _np_f32(inp["L1_attn_log_A_real"])
    aim = _np_f32(inp["L1_attn_A_imag"])
    Cp = _np_f32(inp["L1_attn_C"])
    ins = [dict(ldt=np.ascontiguousarray(ldt[128*c:128*c+128]),
                lar=np.ascontiguousarray(lar[128*c:128*c+128]),
                aim=np.ascontiguousarray(aim[128*c:128*c+128]),
                cre=np.ascontiguousarray(Cp[128*c:128*c+128, :, 0]),
                cim=np.ascontiguousarray(Cp[128*c:128*c+128, :, 1])) for c in range(8)]
    r = _run("ssmk", ins)
    Kfull = np.concatenate([np.asarray(r[c]["Ko"], np.float32) for c in range(8)], axis=0)
    sigs = np.concatenate([xn1_all.reshape(4 * 1024, 4096), Kfull], axis=0)
    _prog("fft1", build_fft1, "fft1")
    ins = []
    for c in range(8):
        sl = sigs[NS * c:NS * c + NS]
        za = sl.reshape(NS, 64, 64).transpose(1, 0, 2).reshape(64, NS * 64)
        ins.append(dict(za=np.ascontiguousarray(za), wr=wr, wi=wi))
    r = _run("fft1", ins)
    Gr = np.stack([np.asarray(r[c]["gr"], np.float32) for c in range(8)])
    Gi = np.stack([np.asarray(r[c]["gi"], np.float32) for c in range(8)])
    _prog("fft2", build_fft2, "fft2")
    ins = []
    for c in range(8):
        grc = Gr[c].reshape(128, NS, 64)
        gic = Gi[c].reshape(128, NS, 64)
        gt = np.stack([grc, gic], axis=1)
        gt = gt.transpose(3, 0, 1, 2).reshape(64, 128 * 2 * NS)
        ins.append(dict(gt=np.ascontiguousarray(gt), ma=MA, mb=MB))
    r = _run("fft2", ins)
    Xr = np.zeros((5120, 8192), np.float32)
    Xi = np.zeros((5120, 8192), np.float32)
    for c in range(8):
        xo = np.asarray(r[c]["xo"], np.float32).reshape(128, 128, NS)
        Xr[NS*c:NS*c+NS] = xo[0:64].transpose(2, 0, 1).reshape(NS, 8192)
        Xi[NS*c:NS*c+NS] = xo[64:128].transpose(2, 0, 1).reshape(NS, 8192)
    _prog("inv", build_inv, "inv")
    Dp = _np_f32(inp["L1_attn_D"])
    ins = []
    for c in range(8):
        ss = np.arange(NSI * c, NSI * c + NSI)
        hh = ss % 1024
        xxr = np.zeros((NKC * 128, NSI), np.float32); xxr[0:4097] = Xr[ss, 0:4097].T
        xxi = np.zeros((NKC * 128, NSI), np.float32); xxi[0:4097] = Xi[ss, 0:4097].T
        kfr = np.zeros((NKC * 128, NSI), np.float32); kfr[0:4097] = Xr[4096 + hh][:, 0:4097].T
        kfi = np.zeros((NKC * 128, NSI), np.float32); kfi[0:4097] = Xi[4096 + hh][:, 0:4097].T
        xns = sigs[ss].T.copy()
        drow = Dp[hh][None, :]
        ins.append(dict(xxr=xxr, xxi=xxi, kfr=kfr, kfi=kfi, cc=cC, cs=cS,
                        xns=np.ascontiguousarray(xns), drow=np.ascontiguousarray(drow)))
    r = _run("inv", ins)
    s1_all = np.zeros((4, 1024, 4096), np.float32)
    for c in range(8):
        y = np.asarray(r[c]["yo"], np.float32)
        ss = np.arange(NSI * c, NSI * c + NSI)
        s1_all.reshape(4096, 4096)[ss] = y.T
    return s1_all


def kernel(**inp):
    inp = {k: np.asarray(v) for k, v in inp.items()}
    x = _np_f32(inp["L0_attn_x"])
    B = x.shape[0]
    xT_all = np.ascontiguousarray(x.transpose(0, 2, 1))
    augK, augQ, masks = _consts_attn()
    ident = np.eye(128, dtype=np.float32)
    selm = np.zeros((8, 8, 128), np.float32)
    for e in range(8):
        selm[e, e, :] = 1.0
    selm = _np_bf(selm.reshape(8, 8 * 128))

    _prog("l0a", build_l0a, "l0a")
    wq = np.ascontiguousarray(_np_f32(inp["L0_attn_Wq"]).T)
    wk = np.ascontiguousarray(_np_f32(inp["L0_attn_Wk"]).T)
    wv = np.ascontiguousarray(_np_f32(inp["L0_attn_Wv"]).T)
    wo = np.ascontiguousarray(_np_f32(inp["L0_attn_Wo"]).T)
    brows = np.ascontiguousarray(np.concatenate(
        [_np_f32(inp["L0_attn_bq"]), _np_f32(inp["L0_attn_bk"]),
         _np_f32(inp["L0_attn_bv"]), _np_f32(inp["L0_attn_bo"])])[None, :])
    g1c = np.ascontiguousarray(_np_f32(inp["L0_ffn_ln1_g"]).reshape(8, 128).T)
    b1c_ = np.ascontiguousarray(_np_f32(inp["L0_ffn_ln1_b"]).reshape(8, 128).T)
    wqkv = np.ascontiguousarray(_np_f32(inp["L0_ffn_w_qkv"]))
    wo1 = np.ascontiguousarray(_np_f32(inp["L0_ffn_w_o"]))
    ins = [dict(xT=np.ascontiguousarray(xT_all[c // 2]), wq=wq, wk=wk, wv=wv, wo0=wo,
                brows=brows, augK=augK, augQ=augQ, wqkv=wqkv, wo1=wo1,
                gcol=g1c, bcol=b1c_, masks=masks) for c in range(8)]
    r = _run("l0a", ins)
    s_all = np.stack([np.asarray(r[2 * b]["sout0"], np.float32) for b in range(B)])
    xr_all = np.stack([np.asarray(r[2 * b]["sout1"], np.float32) for b in range(B)])
    xn_all = None

    _prog("moe0", build_moe, "moe0", True)
    g2c = np.ascontiguousarray(_np_f32(inp["L0_ffn_ln2_g"]).reshape(8, 128).T)
    b2c = np.ascontiguousarray(_np_f32(inp["L0_ffn_ln2_b"]).reshape(8, 128).T)
    b1cols = np.ascontiguousarray(_np_f32(inp["L0_ffn_b1"]).reshape(8, 32, 128)
                                  .transpose(2, 0, 1).reshape(128, 256))
    ins = []
    for c in range(8):
        b, hh = c // 2, c % 2
        sl = slice(TH * hh, TH * hh + TH)
        ins.append(dict(xr=np.ascontiguousarray(xr_all[b][:, sl]),
                        sT=np.ascontiguousarray(s_all[b][:, sl]),
                        xT=np.ascontiguousarray(xT_all[b][:, sl]),
                        gcol=g2c, bcol=b2c, b1c=b1cols,
                        b2m=np.ascontiguousarray(_np_f32(inp["L0_ffn_b2"])),
                        wg=np.ascontiguousarray(_np_f32(inp["L0_ffn_w_gate"])),
                        w1=np.ascontiguousarray(_np_f32(inp["L0_ffn_w1"])),
                        w2=np.ascontiguousarray(_np_f32(inp["L0_ffn_w2"])),
                        selm=selm, ident=ident))
    r = _run("moe0", ins)
    x1_all = np.stack([np.concatenate([np.asarray(r[2 * b]["x1o"], np.float32),
                                       np.asarray(r[2 * b + 1]["x1o"], np.float32)], axis=1)
                       for b in range(B)])
    xn1_all = np.stack([np.concatenate([np.asarray(r[2 * b]["xn1o"], np.float32),
                                        np.asarray(r[2 * b + 1]["xn1o"], np.float32)], axis=1)
                        for b in range(B)])

    s1_all = ssm_launches(xn1_all, inp)

    _prog("moe1", build_moe, "moe1", False)
    HID = 2816
    pad1 = np.zeros((8, 1024, HID), np.float32); pad1[:, :, :2730] = _np_f32(inp["L1_ffn_w1"])
    pad3 = np.zeros((8, 1024, HID), np.float32); pad3[:, :, :2730] = _np_f32(inp["L1_ffn_w3"])
    pad2 = np.zeros((8, HID, 1024), np.float32); pad2[:, :2730, :] = _np_f32(inp["L1_ffn_w2"])
    ins = []
    for c in range(8):
        b, hh = c // 2, c % 2
        sl = slice(TH * hh, TH * hh + TH)
        ins.append(dict(xn1=np.ascontiguousarray(xn1_all[b][:, sl]),
                        x1=np.ascontiguousarray(x1_all[b][:, sl]),
                        s1=np.ascontiguousarray(s1_all[b][:, sl]),
                        wg=np.ascontiguousarray(_np_f32(inp["L1_ffn_gate_w"])),
                        w1=pad1, w2=pad2, w3=pad3, selm=selm, ident=ident))
    r = _run("moe1", ins)
    out = np.zeros_like(x)
    for b in range(B):
        x2 = np.concatenate([np.asarray(r[2 * b]["x2o"], np.float32),
                             np.asarray(r[2 * b + 1]["x2o"], np.float32)], axis=1)
        out[b] = x2.T
    return out


def build_l0a(name):
    p = Prog(name)
    xT = p.din("xT", [D, T])
    wq = p.din("wq", [D, D]); wk = p.din("wk", [D, D])
    wv = p.din("wv", [D, D]); wo0 = p.din("wo0", [D, D])
    brows = p.din("brows", [1, 4 * D])
    augK = p.din("augK", [4, T], BF16)
    augQ = p.din("augQ", [4, T], BF16)
    wqkv = p.din("wqkv", [D, 3 * D])
    wo1 = p.din("wo1", [D, D])
    gcol = p.din("gcol", [128, 8])
    bcol = p.din("bcol", [128, 8])
    masks = p.din("masks", [4, 128, 512])
    sout0 = p.dout("sout0", [D, T])
    sout1 = p.dout("sout1", [D, T])
    xnd = p.nc.dram_tensor("xnd", [D, T], F32, kind="Internal").ap()

    xbf = p.sb("xbf", [128, 8 * T], BF16)
    wstage = p.sb("wstage", [128, TH])
    wbf = p.sb("wbf", [128, 8 * 128], BF16)
    qg = p.sb("qg", [128, 2 * T], BF16)
    kg = p.sb("kg", [128, 2 * T], BF16)
    vsb = p.sb("vsb", [128, 32 * 260], BF16)
    psb = p.sb("psb", [128, 512], BF16)
    osb = p.sb("osb", [128, 512])
    rd = p.sb("rd", [1, 512])
    onesr = p.sb("onesr", [1, 512], BF16)
    onesc = p.sb("onesc", [1, 64])
    brsb = p.sb("brsb", [1, 4 * D], BF16)
    brst = p.sb("brst", [1, 4 * D])
    maskb = p.sb("maskb", [128, 4 * 512])
    ontd = p.nc.dram_tensor("ontd", [D, T], BF16, kind="Internal").ap()
    ontb = p.sb("ontb", [128, 512], BF16)
    onrm = p.sb("onrm", [64, 512], BF16)
    aKb = p.sb("aKb", [4, T], BF16)
    aQb = p.sb("aQb", [4, T], BF16)
    stg = p.sb("stg", [128, 512])
    vrhs = p.sb("vrhs", [128, 8 * 256], BF16)
    x32 = p.sb("x32", [128, 512])
    sq = p.sb("sq", [128, 512])
    mrow = p.sb("mrow", [1, 512]); rrow = p.sb("rrow", [1, 512])
    vrow = p.sb("vrow", [1, 512])
    onescol = p.sb("onescol", [128, 1])
    eps_col = p.sb("eps_col", [128, 1])
    ones_row1 = p.sb("ones_row1", [1, 128])
    sgc = p.sb("sgc", [128, 8])
    sbc = p.sb("sbc", [128, 8])
    pA = p.ps("pA", [128, 512])
    pB = p.ps("pB", [128, 512])
    pC = p.ps("pC", [128, 512])
    pD = p.ps("pD", [128, 512])
    pS = p.ps("pS", [128, 2048])
    psb4 = p.sb("psb4", [128, 2048], BF16)

    p.op("gpsimd", lambda h: h.memset(onesr[:], 1.0))
    p.op("gpsimd", lambda h: h.memset(onesc[:], 1.0))
    p.op("gpsimd", lambda h: h.memset(onescol[:], 1.0 / 1024.0))
    p.op("gpsimd", lambda h: h.memset(eps_col[:], EPS))
    p.op("gpsimd", lambda h: h.memset(ones_row1[:], 1.0))
    p.op("gpsimd", lambda h: h.memset(
        vsb[:].rearrange("p (t h c) -> p t h c", t=32, h=4)[:, :, :, 64:65], 1.0))
    for d in range(4):
        p.dma("sync", lambda h, d=d: h.dma_start(out=maskb[:, 512 * d:512 * d + 512], in_=masks[d]))
    p.dma("sync", lambda h: h.dma_start(out=brst[:], in_=brows))
    p.op("vector", lambda h: h.tensor_copy(brsb[:], brst[:]))
    p.dma("sync", lambda h: h.dma_start(out=aKb[:], in_=augK))
    p.dma("sync", lambda h: h.dma_start(out=aQb[:], in_=augQ))
    p.dma("sync", lambda h: h.dma_start(out=sgc[:], in_=gcol))
    p.dma("sync", lambda h: h.dma_start(out=sbc[:], in_=bcol))

    for tt in range(8):
        c0 = tt * 512
        for c in range(8):
            p.dma("sync", lambda h, c=c, c0=c0: h.dma_start(
                out=x32[:], in_=xT[128 * c:128 * c + 128, c0:c0 + 512]))
            p.op("tensor", lambda h, c=c: h.matmul(
                pA[0:1, :], onescol[:], x32[:], start=(c == 0), stop=(c == 7)))
            p.op("scalar", lambda h: h.activation(sq[:], x32[:], AF.Square))
            p.op("tensor", lambda h, c=c: h.matmul(
                pB[0:1, :], onescol[:], sq[:], start=(c == 0), stop=(c == 7)))
        p.op("vector", lambda h: h.tensor_copy(mrow[:], pA[0:1, :]))
        p.op("vector", lambda h: h.scalar_tensor_tensor(
            vrow[:], mrow[:], -1.0, mrow[:], op0=ALU.mult, op1=ALU.mult))
        p.op("vector", lambda h: h.tensor_add(vrow[:], vrow[:], pB[0:1, :]))
        p.op("scalar", lambda h: h.activation(vrow[:], vrow[:], AF.Ln, bias=eps_col[0:1, :], scale=1.0))
        p.op("scalar", lambda h: h.activation(rrow[:], vrow[:], AF.Exp, scale=-0.5))
        p.op("tensor", lambda h: h.matmul(pA[:], ones_row1[:], mrow[:], start=True, stop=True))
        p.op("tensor", lambda h: h.matmul(pB[:], ones_row1[:], rrow[:], start=True, stop=True))
        for c in range(8):
            p.dma("sync", lambda h, c=c, c0=c0: h.dma_start(
                out=x32[:], in_=xT[128 * c:128 * c + 128, c0:c0 + 512]))
            p.op("vector", lambda h: h.tensor_sub(stg[:], x32[:], pA[:]))
            p.op("vector", lambda h: h.tensor_mul(stg[:], stg[:], pB[:]))
            p.dma("gpsimd", lambda h, c=c, c0=c0: h.dma_start(
                out=xnd[128 * c:128 * c + 128, c0:c0 + 512], in_=stg[:]))

    def load_lhsT(src_ap, k, mcol0):
        p.dma("sync", lambda h, src_ap=src_ap, k=k, mcol0=mcol0: h.dma_start(
            out=wstage[:, 0:128], in_=src_ap[128 * k:128 * k + 128, mcol0:mcol0 + 128]))
        p.op("vector", lambda h: h.tensor_copy(wbf[:, 128 * k:128 * k + 128], wstage[:, 0:128]))

    NKT = T // 128
    for outer in (True, False):
        xnT = xnd
        sout = sout0 if outer else sout1
        wo = wo0 if outer else wo1
        if not outer:
            for tt in range(8):
                c0 = tt * 512
                for c in range(8):
                    p.dma("sync", lambda h, c=c, c0=c0: h.dma_start(
                        out=x32[:], in_=xnT[128 * c:128 * c + 128, c0:c0 + 512]))
                    p.op("tensor", lambda h, c=c: h.matmul(
                        pC[0:1, :], onescol[:], x32[:], start=(c == 0), stop=(c == 7)))
                    p.op("scalar", lambda h: h.activation(sq[:], x32[:], AF.Square))
                    p.op("tensor", lambda h, c=c: h.matmul(
                        pD[0:1, :], onescol[:], sq[:], start=(c == 0), stop=(c == 7)))
                p.op("vector", lambda h: h.tensor_copy(mrow[:], pC[0:1, :]))
                p.op("vector", lambda h: h.scalar_tensor_tensor(
                    vrow[:], mrow[:], -1.0, mrow[:], op0=ALU.mult, op1=ALU.mult))
                p.op("vector", lambda h: h.tensor_add(vrow[:], vrow[:], pD[0:1, :]))
                p.op("scalar", lambda h: h.activation(vrow[:], vrow[:], AF.Ln, bias=eps_col[0:1, :], scale=1.0))
                p.op("scalar", lambda h: h.activation(rrow[:], vrow[:], AF.Exp, scale=-0.5))
                p.op("tensor", lambda h: h.matmul(pC[:], ones_row1[:], mrow[:], start=True, stop=True))
                p.op("tensor", lambda h: h.matmul(pD[:], ones_row1[:], rrow[:], start=True, stop=True))
                for c in range(8):
                    p.dma("sync", lambda h, c=c, c0=c0: h.dma_start(
                        out=x32[:], in_=xnT[128 * c:128 * c + 128, c0:c0 + 512]))
                    p.op("vector", lambda h: h.tensor_sub(stg[:], x32[:], pC[:]))
                    p.op("vector", lambda h: h.tensor_mul(stg[:], stg[:], pD[:]))
                    p.op("vector", lambda h, c=c, c0=c0: h.tensor_scalar(
                        xbf[:, c * T + c0:c * T + c0 + 512], stg[:],
                        sgc[:, c:c + 1], sbc[:, c:c + 1], op0=ALU.mult, op1=ALU.add))
        else:
            for c in range(8):
                for hh2 in range(2):
                    p.dma("sync", lambda h, c=c, hh2=hh2: h.dma_start(
                        out=wstage[:], in_=xnT[128 * c:128 * c + 128, TH * hh2:TH * hh2 + TH]))
                    p.op("vector", lambda h, c=c, hh2=hh2: h.tensor_copy(
                        xbf[:, c * T + TH * hh2:c * T + TH * hh2 + TH], wstage[:]))
        for grp in range(4):
            h0 = grp * 4
            r0 = h0 * HD
            for which, dst, scale in (("q", qg, 0.125), ("k", kg, 1.0)):
                if outer:
                    wsrc = wq if which == "q" else wk
                    coff = r0
                else:
                    wsrc = wqkv
                    coff = r0 if which == "q" else D + r0
                for m in range(2):
                    for k in range(8):
                        load_lhsT(wsrc, k, coff + 128 * m)
                    for n in range(8):
                        for k in range(8):
                            last_mm = (k == 7 and not outer)
                            md = "chain" if k == 0 else ("tail" if last_mm else "mid")
                            p.op("tensor", lambda h, k=k, n=n: h.matmul(
                                pA[:], wbf[:, 128 * k:128 * k + 128],
                                xbf[:, k * T + 512 * n:k * T + 512 * n + 512],
                                start=(k == 0), stop=(k == 7 and not outer)), mode=md)
                        if outer:
                            bi = 0 if which == "q" else 1
                            p.op("tensor", lambda h, m=m, bi=bi, coff=coff: h.matmul(
                                pA[:], brsb[0:1, bi * D + coff + 128 * m:bi * D + coff + 128 * m + 128],
                                onesr[:], start=False, stop=True), mode="tail")
                        p.op("scalar", lambda h, dst=dst, m=m, n=n, scale=scale: h.activation(
                            dst[:, m * T + 512 * n:m * T + 512 * n + 512], pA[:],
                            AF.Copy, scale=scale))
            if outer:
                vw = wv; vcoff = r0
            else:
                vw = wqkv; vcoff = 2 * D + r0
            for k in range(8):
                p.dma("sync", lambda h, k=k, vw=vw, vcoff=vcoff: h.dma_start(
                    out=wstage[:, 0:256], in_=vw[128 * k:128 * k + 128, vcoff:vcoff + 256]))
                p.op("vector", lambda h, k=k: h.tensor_copy(
                    vrhs[:, 256 * k:256 * k + 256], wstage[:, 0:256]))
            for kt in range(NKT):
                for k in range(8):
                    md = "chain" if k == 0 else ("tail" if (k == 7 and not outer) else "mid")
                    p.op("tensor", lambda h, k=k, kt=kt: h.matmul(
                        pA[:, 0:256], xbf[:, k * T + 128 * kt:k * T + 128 * kt + 128],
                        vrhs[:, 256 * k:256 * k + 256], start=(k == 0), stop=(k == 7 and not outer)), mode=md)
                if outer:
                    p.op("tensor", lambda h, vcoff=vcoff: h.matmul(
                        pA[:, 0:256], onesr[0:1, 0:128], brsb[0:1, 2 * D + vcoff:2 * D + vcoff + 256],
                        start=False, stop=True), mode="tail")
                for hh in range(4):
                    p.op("vector", lambda h, kt=kt, hh=hh: h.tensor_copy(
                        vsb[:, 260 * kt + 65 * hh:260 * kt + 65 * hh + 64],
                        pA[:, 64 * hh:64 * hh + 64]))
            for qt in range(8):
                q0 = qt * 512
                nkb = min(32, 4 * qt + 4)
                for hh in range(4):
                    hloc_t = hh // 2
                    hbase = 64 * (hh % 2)
                    for kbg in range(nkb // 4):
                        for j in range(4):
                            kb = 4 * kbg + j
                            md = "chain" if j == 0 else "mid"
                            lastmm = (j == 3 and not outer)
                            p.op("tensor", lambda h, hloc_t=hloc_t, hbase=hbase, kb=kb, q0=q0, outer=outer, j=j: h.matmul(
                                pS[:, 512 * j:512 * j + 512],
                                kg[hbase:hbase + 64, hloc_t * T + 128 * kb:hloc_t * T + 128 * kb + 128],
                                qg[hbase:hbase + 64, hloc_t * T + q0:hloc_t * T + q0 + 512],
                                start=True, stop=not outer),
                                mode=("tail" if lastmm else md))
                            if outer:
                                p.op("tensor", lambda h, kb=kb, q0=q0, j=j: h.matmul(
                                    pS[:, 512 * j:512 * j + 512],
                                    aKb[:, 128 * kb:128 * kb + 128], aQb[:, q0:q0 + 512],
                                    start=False, stop=True), mode=("tail" if j == 3 else "mid"))
                        if kbg == qt:
                            for j in range(4):
                                p.op("vector", lambda h, j=j: h.tensor_add(
                                    pS[:, 512 * j:512 * j + 512], pS[:, 512 * j:512 * j + 512],
                                    maskb[:, 512 * j:512 * j + 512]))
                        p.op("scalar", lambda h: h.activation(psb4[:], pS[:], AF.Exp))
                        for j in range(4):
                            kb = 4 * kbg + j
                            p.op("tensor", lambda h, kb=kb, hh=hh, nkb=nkb, j=j: h.matmul(
                                pB[0:65, :], vsb[:, 260 * kb + 65 * hh:260 * kb + 65 * hh + 65],
                                psb4[:, 512 * j:512 * j + 512], start=(kb == 0), stop=(kb == nkb - 1)),
                                mode="chain" if j == 0 else ("tail" if j == 3 else "mid"))
                    p.op("vector", lambda h: h.tensor_copy(osb[0:65, :], pB[0:65, :]))
                    p.op("vector", lambda h: h.reciprocal(rd[:], osb[64:65, :]))
                    p.op("tensor", lambda h: h.matmul(
                        pC[0:64, :], onesc[:], rd[:], start=True, stop=True))
                    p.op("vector", lambda h: h.tensor_mul(onrm[:], osb[0:64, :], pC[0:64, :]))
                    p.dma("gpsimd", lambda h, h0=h0, hh=hh, q0=q0: h.dma_start(
                        out=ontd[64 * (h0 + hh):64 * (h0 + hh) + 64, q0:q0 + 512], in_=onrm[:]))
        for m in range(8):
            for k in range(8):
                load_lhsT(wo, k, 128 * m)
            for n in range(8):
                for k in range(8):
                    p.dma("sync", lambda h, k=k, n=n: h.dma_start(
                        out=ontb[:], in_=ontd[128 * k:128 * k + 128, 512 * n:512 * n + 512]))
                    p.op("tensor", lambda h, k=k, n=n, outer=outer: h.matmul(
                        pD[:], wbf[:, 128 * k:128 * k + 128], ontb[:],
                        start=(k == 0), stop=(k == 7 and not outer)))
                if outer:
                    p.op("tensor", lambda h, m=m: h.matmul(
                        pD[:], brsb[0:1, 3 * D + 128 * m:3 * D + 128 * m + 128], onesr[:],
                        start=False, stop=True))
                    p.op("vector", lambda h: h.tensor_copy(stg[:], pD[:]))
                else:
                    p.dma("sync", lambda h, m=m, n=n: h.dma_start(
                        out=x32[:], in_=xnT[128 * m:128 * m + 128, 512 * n:512 * n + 512]))
                    p.op("vector", lambda h: h.tensor_add(stg[:], pD[:], x32[:]))
                p.dma("gpsimd", lambda h, m=m, n=n, sout=sout: h.dma_start(
                    out=sout[128 * m:128 * m + 128, 512 * n:512 * n + 512], in_=stg[:]))
    last = p.last
    p.op("gpsimd", lambda h: h.engine_nop(), deps=[last])
    return p


# version 05 (final)
# speedup vs baseline: 1.4495x; 1.0155x over previous
import numpy as np
import ml_dtypes
from contextlib import ExitStack

import concourse.bass as bass
import concourse.mybir as mybir
from concourse.bass_utils import run_bass_kernel_spmd

F32 = mybir.dt.float32
BF16 = mybir.dt.bfloat16
AF = mybir.ActivationFunctionType
ALU = mybir.AluOpType
NPBF = ml_dtypes.bfloat16

D = 1024
T = 4096
TH = 2048
NH = 16
HD = 64
EPS = 1e-5


class Prog:
    ENG = ("sync", "tensor", "vector", "scalar", "gpsimd")

    def __init__(self, name):
        self.nc = bass.Bass("TRN2", target_bir_lowering=False)
        self.name = name
        self.ctx = ExitStack()
        self.q = {e: [] for e in self.ENG}
        self.cnt = {e: 0 for e in list(self.ENG) + ["gdma"]}
        self.last = None
        self._n = 0

    def din(self, name, shape, dt=F32):
        return self.nc.dram_tensor(name, list(shape), dt, kind="ExternalInput").ap()

    def dout(self, name, shape, dt=F32):
        return self.nc.dram_tensor(name, list(shape), dt, kind="ExternalOutput").ap()

    def sb(self, name, shape, dt=F32):
        return self.ctx.enter_context(self.nc.sbuf_tensor(name, list(shape), dt))

    def ps(self, name, shape, dt=F32):
        return self.ctx.enter_context(self.nc.psum_tensor(name, list(shape), dt))

    def _emit(self, eng, fn, inc, deps, mode="chain"):
        if mode == "chain":
            my = list(deps) if deps is not None else ([self.last] if self.last else [])
        else:
            my = []
        if mode == "mid":
            inc = 0
        sk = "gdma" if (eng == "gpsimd" and inc == 16) else eng
        def closure(h, _fn=fn, _deps=my, _sk=sk, _inc=inc):
            for d in _deps:
                if d is not None:
                    h.wait_ge(self._sems[d[0]], d[1])
            r = _fn(h)
            if _inc:
                r.then_inc(self._sems[_sk], _inc)
        self.q[eng].append(closure)
        if inc:
            self.cnt[sk] += inc
            tok = (sk, self.cnt[sk])
            self.last = tok
        else:
            tok = (sk, self.cnt[sk])
        return tok

    def op(self, eng, fn, deps=None, mode="chain"):
        return self._emit(eng, fn, 1, deps, mode)

    def dma(self, eng, fn, deps=None, mode="chain"):
        return self._emit(eng, fn, 16, deps, mode)

    def build(self):
        sems = {}
        for e in list(self.ENG) + ["gdma"]:
            sems[e] = self.ctx.enter_context(self.nc.semaphore("sem_" + e))
        self._sems = sems
        self._done = {e: 0 for e in self.ENG}
        block = self.ctx.enter_context(self.nc.Block())

        def runq(e):
            def body(h):
                for f in self.q[e]:
                    f(h)
            return body

        for e in self.ENG:
            deco = getattr(block, e)
            deco(runq(e))
        self.ctx.close()
        return self.nc


def _np_f32(x):
    return np.ascontiguousarray(x, np.float32)


def _np_bf(x):
    return np.ascontiguousarray(np.asarray(x).astype(NPBF))


def build_ln(name, ncols, gamma=False):
    p = Prog(name)
    xin = p.din("xin", [D, ncols])
    xout = p.dout("xout", [D, ncols])
    if gamma:
        gcol = p.din("gcol", [128, 8])
        bcol = p.din("bcol", [128, 8])

    CT = 512
    nct = ncols // CT
    xs = [p.sb(f"x{i}", [128, CT]) for i in range(8)]
    sq = p.sb("sq", [128, CT])
    mrow = p.sb("mrow", [1, CT])
    vrow = p.sb("vrow", [1, CT])
    rrow = p.sb("rrow", [1, CT])
    t1 = p.sb("t1", [128, CT])
    ones_col = p.sb("ones_col", [128, 1])
    eps_col = p.sb("eps_col", [128, 1])
    ones_row1 = p.sb("ones_row1", [1, 128])
    outbuf = p.sb("outbuf", [128, CT])
    sgc = p.sb("sgc", [128, 8]) if gamma else None
    sbc = p.sb("sbc", [128, 8]) if gamma else None
    pm = p.ps("pm", [128, CT])
    pv = p.ps("pv", [128, CT])
    pb1 = p.ps("pb1", [128, CT])
    pb2 = p.ps("pb2", [128, CT])

    p.op("gpsimd", lambda h: h.memset(ones_col[:], 1.0 / 1024.0))
    p.op("gpsimd", lambda h: h.memset(ones_row1[:], 1.0))
    p.op("gpsimd", lambda h: h.memset(eps_col[:], EPS))
    if gamma:
        p.dma("sync", lambda h: h.dma_start(out=sgc[:], in_=gcol))
        p.dma("sync", lambda h: h.dma_start(out=sbc[:], in_=bcol))

    for tt in range(nct):
        c0 = tt * CT
        for c in range(8):
            p.dma("sync", lambda h, c=c, c0=c0: h.dma_start(
                out=xs[c][:], in_=xin[128 * c:128 * c + 128, c0:c0 + CT]))
        for c in range(8):
            p.op("tensor", lambda h, c=c: h.matmul(
                pm[0:1, :], ones_col[:], xs[c][:], start=(c == 0), stop=(c == 7)))
        for c in range(8):
            p.op("scalar", lambda h, c=c: h.activation(sq[:], xs[c][:], AF.Square))
            p.op("tensor", lambda h, c=c: h.matmul(
                pv[0:1, :], ones_col[:], sq[:], start=(c == 0), stop=(c == 7)))
        p.op("vector", lambda h: h.tensor_copy(mrow[:], pm[0:1, :]))
        p.op("vector", lambda h: h.scalar_tensor_tensor(
            vrow[:], mrow[:], -1.0, mrow[:], op0=ALU.mult, op1=ALU.mult))
        p.op("vector", lambda h: h.tensor_add(vrow[:], vrow[:], pv[0:1, :]))
        p.op("scalar", lambda h: h.activation(rrow[:], vrow[:], AF.Ln, bias=eps_col[0:1, :], scale=1.0))
        p.op("scalar", lambda h: h.activation(rrow[:], rrow[:], AF.Exp, scale=-0.5))
        p.op("tensor", lambda h: h.matmul(pb1[:], ones_row1[:], mrow[:], start=True, stop=True))
        p.op("tensor", lambda h: h.matmul(pb2[:], ones_row1[:], rrow[:], start=True, stop=True))
        for c in range(8):
            p.op("vector", lambda h, c=c: h.tensor_sub(t1[:], xs[c][:], pb1[:]))
            p.op("vector", lambda h, c=c: h.tensor_mul(outbuf[:], t1[:], pb2[:]))
            if gamma:
                p.op("vector", lambda h, c=c: h.tensor_scalar(
                    outbuf[:], outbuf[:], sgc[:, c:c + 1], sbc[:, c:c + 1],
                    op0=ALU.mult, op1=ALU.add))
            p.dma("gpsimd", lambda h, c=c, c0=c0: h.dma_start(
                out=xout[128 * c:128 * c + 128, c0:c0 + CT], in_=outbuf[:]))
    last = p.last
    p.op("gpsimd", lambda h: h.engine_nop(), deps=[last])
    return p


def build_attn(name, outer, debug=False):
    p = Prog(name)
    xnT = p.din("xnT", [D, T])
    if outer:
        wq = p.din("wq", [D, D]); wk = p.din("wk", [D, D])
        wv = p.din("wv", [D, D]); wo = p.din("wo", [D, D])
        brows = p.din("brows", [1, 4 * D])
        augK = p.din("augK", [4, T], BF16)
        augQ = p.din("augQ", [4, T], BF16)
    else:
        wqkv = p.din("wqkv", [D, 3 * D])
        wo = p.din("wo", [D, D])
        gcol = p.din("gcol", [128, 8])
        bcol = p.din("bcol", [128, 8])
    masks = p.din("masks", [4, 128, 512])
    sout = p.dout("sout", [D, T])

    xbf = p.sb("xbf", [128, 8 * T], BF16)
    wstage = p.sb("wstage", [128, TH])
    wbf = p.sb("wbf", [128, 8 * 128], BF16)
    qg = p.sb("qg", [128, 2 * T], BF16)
    kg = p.sb("kg", [128, 2 * T], BF16)
    vsb = p.sb("vsb", [128, 32 * 260], BF16)
    psb = p.sb("psb", [128, 512], BF16)
    osb = p.sb("osb", [128, 512])
    rd = p.sb("rd", [1, 512])
    onesr = p.sb("onesr", [1, 512], BF16)
    onesc = p.sb("onesc", [1, 64])
    brsb = p.sb("brsb", [1, 4 * D], BF16) if outer else None
    brst = p.sb("brst", [1, 4 * D]) if outer else None
    maskb = p.sb("maskb", [128, 4 * 512])
    ontd = p.nc.dram_tensor("ontd", [D, T], BF16,
                            kind="Internal" if not debug else "ExternalOutput").ap()
    ontb = p.sb("ontb", [128, 512], BF16)
    onrm = p.sb("onrm", [64, 512], BF16)
    aKb = p.sb("aKb", [4, T], BF16) if outer else None
    aQb = p.sb("aQb", [4, T], BF16) if outer else None
    stg = p.sb("stg", [128, 512])
    vrhs = p.sb("vrhs", [128, 8 * 256], BF16)
    x32 = p.sb("x32", [128, 512])
    sq = p.sb("sq", [128, 512])
    mrow = p.sb("mrow", [1, 512]); rrow = p.sb("rrow", [1, 512])
    vrow = p.sb("vrow", [1, 512])
    onescol = p.sb("onescol", [128, 1])
    eps_col = p.sb("eps_col", [128, 1])
    ones_row1 = p.sb("ones_row1", [1, 128])
    sgc = p.sb("sgc", [128, 8]) if not outer else None
    sbc = p.sb("sbc", [128, 8]) if not outer else None

    pA = p.ps("pA", [128, 512])
    pB = p.ps("pB", [128, 512])
    pC = p.ps("pC", [128, 512])
    pD = p.ps("pD", [128, 512])

    p.op("gpsimd", lambda h: h.memset(onesr[:], 1.0))
    p.op("gpsimd", lambda h: h.memset(onesc[:], 1.0))
    p.op("gpsimd", lambda h: h.memset(onescol[:], 1.0 / 1024.0))
    p.op("gpsimd", lambda h: h.memset(eps_col[:], EPS))
    p.op("gpsimd", lambda h: h.memset(ones_row1[:], 1.0))
    p.op("gpsimd", lambda h: h.memset(
        vsb[:].rearrange("p (t h c) -> p t h c", t=32, h=4)[:, :, :, 64:65], 1.0))
    for d in range(4):
        p.dma("sync", lambda h, d=d: h.dma_start(out=maskb[:, 512 * d:512 * d + 512], in_=masks[d]))
    if outer:
        p.dma("sync", lambda h: h.dma_start(out=brst[:], in_=brows))
        p.op("vector", lambda h: h.tensor_copy(brsb[:], brst[:]))
        p.dma("sync", lambda h: h.dma_start(out=aKb[:], in_=augK))
        p.dma("sync", lambda h: h.dma_start(out=aQb[:], in_=augQ))
    else:
        p.dma("sync", lambda h: h.dma_start(out=sgc[:], in_=gcol))
        p.dma("sync", lambda h: h.dma_start(out=sbc[:], in_=bcol))

    if not outer:
        for tt in range(8):
            c0 = tt * 512
            for c in range(8):
                p.dma("sync", lambda h, c=c, c0=c0: h.dma_start(
                    out=x32[:], in_=xnT[128 * c:128 * c + 128, c0:c0 + 512]))
                p.op("tensor", lambda h, c=c: h.matmul(
                    pC[0:1, :], onescol[:], x32[:], start=(c == 0), stop=(c == 7)))
                p.op("scalar", lambda h: h.activation(sq[:], x32[:], AF.Square))
                p.op("tensor", lambda h, c=c: h.matmul(
                    pD[0:1, :], onescol[:], sq[:], start=(c == 0), stop=(c == 7)))
            p.op("vector", lambda h: h.tensor_copy(mrow[:], pC[0:1, :]))
            p.op("vector", lambda h: h.scalar_tensor_tensor(
                vrow[:], mrow[:], -1.0, mrow[:], op0=ALU.mult, op1=ALU.mult))
            p.op("vector", lambda h: h.tensor_add(vrow[:], vrow[:], pD[0:1, :]))
            p.op("scalar", lambda h: h.activation(vrow[:], vrow[:], AF.Ln, bias=eps_col[0:1, :], scale=1.0))
            p.op("scalar", lambda h: h.activation(rrow[:], vrow[:], AF.Exp, scale=-0.5))
            p.op("tensor", lambda h: h.matmul(pC[:], ones_row1[:], mrow[:], start=True, stop=True))
            p.op("tensor", lambda h: h.matmul(pD[:], ones_row1[:], rrow[:], start=True, stop=True))
            for c in range(8):
                p.dma("sync", lambda h, c=c, c0=c0: h.dma_start(
                    out=x32[:], in_=xnT[128 * c:128 * c + 128, c0:c0 + 512]))
                p.op("vector", lambda h: h.tensor_sub(stg[:], x32[:], pC[:]))
                p.op("vector", lambda h: h.tensor_mul(stg[:], stg[:], pD[:]))
                p.op("vector", lambda h, c=c, c0=c0: h.tensor_scalar(
                    xbf[:, c * T + c0:c * T + c0 + 512], stg[:],
                    sgc[:, c:c + 1], sbc[:, c:c + 1], op0=ALU.mult, op1=ALU.add))
    else:
        for c in range(8):
            for hh2 in range(2):
                p.dma("sync", lambda h, c=c, hh2=hh2: h.dma_start(
                    out=wstage[:], in_=xnT[128 * c:128 * c + 128, TH * hh2:TH * hh2 + TH]))
                p.op("vector", lambda h, c=c, hh2=hh2: h.tensor_copy(
                    xbf[:, c * T + TH * hh2:c * T + TH * hh2 + TH], wstage[:]))

    def load_lhsT(src_ap, k, mcol0, tag):
        p.dma("sync", lambda h: h.dma_start(
            out=wstage[:, 0:128], in_=src_ap[128 * k:128 * k + 128, mcol0:mcol0 + 128]))
        p.op("vector", lambda h: h.tensor_copy(wbf[:, 128 * k:128 * k + 128], wstage[:, 0:128]))

    NKT = T // 128
    for grp in range(4):
        h0 = grp * 4
        r0 = h0 * HD
        for which, dst, scale in (("q", qg, 0.125), ("k", kg, 1.0)):
            if outer:
                wsrc = wq if which == "q" else wk
                coff = r0
            else:
                wsrc = wqkv
                coff = r0 if which == "q" else D + r0
            for m in range(2):
                for k in range(8):
                    load_lhsT(wsrc, k, coff + 128 * m, which)
                for n in range(8):
                    for k in range(8):
                        p.op("tensor", lambda h, k=k, n=n: h.matmul(
                            pA[:], wbf[:, 128 * k:128 * k + 128],
                            xbf[:, k * T + 512 * n:k * T + 512 * n + 512],
                            start=(k == 0), stop=(k == 7 and not outer)))
                    if outer:
                        bi = 0 if which == "q" else 1
                        p.op("tensor", lambda h, m=m, bi=bi, coff=coff: h.matmul(
                            pA[:], brsb[0:1, bi * D + coff + 128 * m:bi * D + coff + 128 * m + 128],
                            onesr[:], start=False, stop=True))
                    p.op("scalar", lambda h, dst=dst, m=m, n=n, scale=scale: h.activation(
                        dst[:, m * T + 512 * n:m * T + 512 * n + 512], pA[:],
                        AF.Copy, scale=scale))
        if outer:
            vw = wv; vcoff = r0
        else:
            vw = wqkv; vcoff = 2 * D + r0
        for k in range(8):
            p.dma("sync", lambda h, k=k, vcoff=vcoff: h.dma_start(
                out=wstage[:, 0:256], in_=vw[128 * k:128 * k + 128, vcoff:vcoff + 256]))
            p.op("vector", lambda h, k=k: h.tensor_copy(
                vrhs[:, 256 * k:256 * k + 256], wstage[:, 0:256]))
        for kt in range(NKT):
            for k in range(8):
                p.op("tensor", lambda h, k=k, kt=kt: h.matmul(
                    pA[:, 0:256], xbf[:, k * T + 128 * kt:k * T + 128 * kt + 128],
                    vrhs[:, 256 * k:256 * k + 256], start=(k == 0), stop=(k == 7 and not outer)))
            if outer:
                p.op("tensor", lambda h, vcoff=vcoff: h.matmul(
                    pA[:, 0:256], onesr[0:1, 0:128], brsb[0:1, 2 * D + vcoff:2 * D + vcoff + 256],
                    start=False, stop=True))
            for hh in range(4):
                p.op("vector", lambda h, kt=kt, hh=hh: h.tensor_copy(
                    vsb[:, 260 * kt + 65 * hh:260 * kt + 65 * hh + 64],
                    pA[:, 64 * hh:64 * hh + 64]))
        if debug and grp == 0:
            qdump = p.dout("qdump", [128, 2 * T], BF16)
            kdump = p.dout("kdump", [128, 2 * T], BF16)
            vdump = p.dout("vdump", [128, 32 * 260], BF16)
            p.dma("gpsimd", lambda h: h.dma_start(out=qdump, in_=qg[:]))
            p.dma("gpsimd", lambda h: h.dma_start(out=kdump, in_=kg[:]))
            p.dma("gpsimd", lambda h: h.dma_start(out=vdump, in_=vsb[:]))
        for qt in range(8):
            q0 = qt * 512
            nkb = min(32, 4 * qt + 4)
            for hh in range(4):
                hloc_t = hh // 2
                hbase = 64 * (hh % 2)
                for kb in range(nkb):
                    p.op("tensor", lambda h, hloc_t=hloc_t, hbase=hbase, kb=kb, q0=q0: h.matmul(
                        pA[:], kg[hbase:hbase + 64, hloc_t * T + 128 * kb:hloc_t * T + 128 * kb + 128],
                        qg[hbase:hbase + 64, hloc_t * T + q0:hloc_t * T + q0 + 512],
                        start=True, stop=not outer))
                    if outer:
                        p.op("tensor", lambda h, kb=kb, q0=q0: h.matmul(
                            pA[:], aKb[:, 128 * kb:128 * kb + 128], aQb[:, q0:q0 + 512],
                            start=False, stop=True))
                    d = 128 * kb - q0
                    if 0 <= d < 512:
                        mi = d // 128
                        p.op("vector", lambda h, mi=mi: h.tensor_add(
                            pA[:], pA[:], maskb[:, 512 * mi:512 * mi + 512]))
                    p.op("scalar", lambda h: h.activation(psb[:], pA[:], AF.Exp))
                    if debug and grp == 0 and qt == 0 and hh == 0 and kb == 0:
                        scd = p.dout("scd", [128, 512])
                        psd = p.dout("psd", [128, 512], BF16)
                        p.op("vector", lambda h: h.tensor_copy(stg[:], pA[:]))
                        p.dma("gpsimd", lambda h: h.dma_start(out=scd, in_=stg[:]))
                        p.dma("gpsimd", lambda h: h.dma_start(out=psd, in_=psb[:]))
                    p.op("tensor", lambda h, kb=kb, hh=hh: h.matmul(
                        pB[0:65, :], vsb[:, 260 * kb + 65 * hh:260 * kb + 65 * hh + 65],
                        psb[:], start=(kb == 0), stop=(kb == nkb - 1)))
                p.op("vector", lambda h: h.tensor_copy(osb[0:65, :], pB[0:65, :]))
                if debug and grp == 0 and qt == 0 and hh == 0:
                    osd = p.dout("osd", [128, 512])
                    p.dma("gpsimd", lambda h: h.dma_start(out=osd, in_=osb[:]))
                p.op("vector", lambda h: h.reciprocal(rd[:], osb[64:65, :]))
                p.op("tensor", lambda h: h.matmul(
                    pC[0:64, :], onesc[:], rd[:], start=True, stop=True))
                p.op("vector", lambda h: h.tensor_mul(onrm[:], osb[0:64, :], pC[0:64, :]))
                p.dma("gpsimd", lambda h, h0=h0, hh=hh, q0=q0: h.dma_start(
                    out=ontd[64 * (h0 + hh):64 * (h0 + hh) + 64, q0:q0 + 512], in_=onrm[:]))

    for m in range(8):
        for k in range(8):
            load_lhsT(wo, k, 128 * m, "wo")
        for n in range(8):
            for k in range(8):
                p.dma("sync", lambda h, k=k, n=n: h.dma_start(
                    out=ontb[:], in_=ontd[128 * k:128 * k + 128, 512 * n:512 * n + 512]))
                p.op("tensor", lambda h, k=k, n=n: h.matmul(
                    pD[:], wbf[:, 128 * k:128 * k + 128], ontb[:],
                    start=(k == 0), stop=(k == 7 and not outer)))
            if outer:
                p.op("tensor", lambda h, m=m: h.matmul(
                    pD[:], brsb[0:1, 3 * D + 128 * m:3 * D + 128 * m + 128], onesr[:],
                    start=False, stop=True))
                p.op("vector", lambda h: h.tensor_copy(stg[:], pD[:]))
            else:
                p.dma("sync", lambda h, m=m, n=n: h.dma_start(
                    out=x32[:], in_=xnT[128 * m:128 * m + 128, 512 * n:512 * n + 512]))
                p.op("vector", lambda h: h.tensor_add(stg[:], pD[:], x32[:]))
            p.dma("gpsimd", lambda h, m=m, n=n: h.dma_start(
                out=sout[128 * m:128 * m + 128, 512 * n:512 * n + 512], in_=stg[:]))
    last = p.last
    p.op("gpsimd", lambda h: h.engine_nop(), deps=[last])
    return p


def run_prog(p, in_maps, n=None):
    nc = p.build()
    ids = list(range(len(in_maps)))
    res = run_bass_kernel_spmd(nc, in_maps, core_ids=ids)
    return res.results


def build_moe(name, first):
    HID = 4096 if first else 2816
    NMH = HID // 128
    p = Prog(name)
    if first:
        xr = p.din("xr", [D, TH])
        sT = p.din("sT", [D, TH])
        xT = p.din("xT", [D, TH])
        gcol = p.din("gcol", [128, 8]); bcol = p.din("bcol", [128, 8])
        b1c = p.din("b1c", [128, 8 * (HID // 128)])
        b2m = p.din("b2m", [8, D])
        x1o = p.dout("x1o", [D, TH])
        xn1o = p.dout("xn1o", [D, TH])
    else:
        xn1 = p.din("xn1", [D, TH])
        x1 = p.din("x1", [D, TH])
        s1 = p.din("s1", [D, TH])
        x2o = p.dout("x2o", [D, TH])
    wg = p.din("wg", [D, 8])
    w1 = p.din("w1", [8, D, HID])
    w2 = p.din("w2", [8, HID, D])
    if not first:
        w3 = p.din("w3", [8, D, HID])
    selm = p.din("selm", [8, 8 * 128], BF16)
    ident = p.din("ident", [128, 128])

    mbf = p.sb("mbf", [128, 8 * TH], BF16)
    wst = p.sb("wst", [128, HID // 2])
    w1bf = p.sb("w1bf", [128, 8 * HID], BF16)
    w3bf = p.sb("w3bf", [128, 8 * HID], BF16) if not first else None
    w2bf = p.sb("w2bf", [128, D], BF16)
    hebf = p.sb("hebf", [128, NMH * 512], BF16)
    sabf = p.sb("sabf", [128, 512], BF16) if not first else None
    acc = p.sb("acc", [128, 8 * 512])
    tmp = p.sb("tmp", [128, 512])
    gb = p.sb("gb", [128, 512])
    stg = p.sb("stg", [128, 512])
    x32 = p.sb("x32", [128, 512])
    y32 = p.sb("y32", [128, 512])
    lgs = p.sb("lgs", [128, 16 * 8])
    gts = p.sb("gts", [128, 16 * 8])
    gt8b = p.sb("gt8b", [8, TH], BF16)
    selb = p.sb("selb", [8, 8 * 128], BF16)
    idb = p.sb("idb", [128, 128])
    wgb = p.sb("wgb", [128, 64], BF16)
    srt = p.sb("srt", [128, 8])
    c1 = p.sb("c1", [128, 1]); c2 = p.sb("c2", [128, 1]); c3 = p.sb("c3", [128, 1])
    sq = p.sb("sq", [128, 512])
    mrow = p.sb("mrow", [1, 512]); vrow = p.sb("vrow", [1, 512]); rrow = p.sb("rrow", [1, 512])
    onescol = p.sb("onescol", [128, 1])
    ones_row1 = p.sb("ones_row1", [1, 128])
    eps_col = p.sb("eps_col", [128, 1])
    b1s = p.sb("b1s", [128, 8 * (HID // 128)]) if first else None
    b2s = p.sb("b2s", [8, D]) if first else None
    b2sb = p.sb("b2sb", [8, D], BF16) if first else None
    sgc = p.sb("sgc", [128, 8]) if first else None
    sbc = p.sb("sbc", [128, 8]) if first else None


    pZ1 = p.ps("pZ1", [128, 2048])
    pZ2 = p.ps("pZ2", [128, 2048])

    p.op("gpsimd", lambda h: h.memset(onescol[:], 1.0 / 1024.0))
    p.op("gpsimd", lambda h: h.memset(ones_row1[:], 1.0))
    p.op("gpsimd", lambda h: h.memset(eps_col[:], EPS))
    p.dma("sync", lambda h: h.dma_start(out=selb[:], in_=selm))
    p.dma("sync", lambda h: h.dma_start(out=idb[:], in_=ident))
    if first:
        p.dma("sync", lambda h: h.dma_start(out=sgc[:], in_=gcol))
        p.dma("sync", lambda h: h.dma_start(out=sbc[:], in_=bcol))
        p.dma("sync", lambda h: h.dma_start(out=b1s[:], in_=b1c))
        p.dma("sync", lambda h: h.dma_start(out=b2s[:], in_=b2m))
        p.op("vector", lambda h: h.tensor_copy(b2sb[:], b2s[:]))
    for c in range(8):
        p.dma("sync", lambda h, c=c: h.dma_start(out=wst[:, 0:8], in_=wg[128 * c:128 * c + 128, :]))
        p.op("vector", lambda h, c=c: h.tensor_copy(wgb[:, 8 * c:8 * c + 8], wst[:, 0:8]))

    src_in = xr if first else xn1
    for tt in range(4):
        c0 = tt * 512
        if first:
            for c in range(8):
                p.dma("sync", lambda h, c=c, c0=c0: h.dma_start(
                    out=x32[:], in_=src_in[128 * c:128 * c + 128, c0:c0 + 512]))
                p.op("tensor", lambda h, c=c: h.matmul(
                    pZ1[0:1, 0:512], onescol[:], x32[:], start=(c == 0), stop=(c == 7)))
                p.op("scalar", lambda h: h.activation(sq[:], x32[:], AF.Square))
                p.op("tensor", lambda h, c=c: h.matmul(
                    pZ1[0:1, 512:1024], onescol[:], sq[:], start=(c == 0), stop=(c == 7)))
            p.op("vector", lambda h: h.tensor_copy(mrow[:], pZ1[0:1, 0:512]))
            p.op("vector", lambda h: h.scalar_tensor_tensor(
                vrow[:], mrow[:], -1.0, mrow[:], op0=ALU.mult, op1=ALU.mult))
            p.op("vector", lambda h: h.tensor_add(vrow[:], vrow[:], pZ1[0:1, 512:1024]))
            p.op("scalar", lambda h: h.activation(vrow[:], vrow[:], AF.Ln, bias=eps_col[0:1, :], scale=1.0))
            p.op("scalar", lambda h: h.activation(rrow[:], vrow[:], AF.Exp, scale=-0.5))
            p.op("tensor", lambda h: h.matmul(pZ1[:, 0:512], ones_row1[:], mrow[:], start=True, stop=True))
            p.op("tensor", lambda h: h.matmul(pZ1[:, 512:1024], ones_row1[:], rrow[:], start=True, stop=True))
            for c in range(8):
                p.dma("sync", lambda h, c=c, c0=c0: h.dma_start(
                    out=x32[:], in_=src_in[128 * c:128 * c + 128, c0:c0 + 512]))
                p.op("vector", lambda h: h.tensor_sub(stg[:], x32[:], pZ1[:, 0:512]))
                p.op("vector", lambda h: h.tensor_mul(stg[:], stg[:], pZ1[:, 512:1024]))
                p.op("vector", lambda h, c=c, c0=c0: h.tensor_scalar(
                    mbf[:, c * TH + c0:c * TH + c0 + 512], stg[:],
                    sgc[:, c:c + 1], sbc[:, c:c + 1], op0=ALU.mult, op1=ALU.add))
        else:
            for c in range(8):
                p.dma("sync", lambda h, c=c, c0=c0: h.dma_start(
                    out=x32[:], in_=src_in[128 * c:128 * c + 128, c0:c0 + 512]))
                p.op("vector", lambda h, c=c, c0=c0: h.tensor_copy(
                    mbf[:, c * TH + c0:c * TH + c0 + 512], x32[:]))

    for tt in range(4):
        c0 = tt * 512
        for c in range(8):
            p.op("tensor", lambda h, c=c, c0=c0: h.matmul(
                pZ1[0:8, 0:512], wgb[:, 8 * c:8 * c + 8],
                mbf[:, c * TH + c0:c * TH + c0 + 512], start=(c == 0), stop=(c == 7)))
        p.op("vector", lambda h: h.tensor_copy(stg[0:8, :], pZ1[0:8, 0:512]))
        for q in range(4):
            ti = tt * 4 + q
            p.op("tensor", lambda h, q=q: h.transpose(
                pZ2[0:128, 0:8], stg[0:8, 128 * q:128 * q + 128], idb[0:8, 0:8]))
            p.op("vector", lambda h, ti=ti: h.tensor_copy(lgs[:, 8 * ti:8 * ti + 8], pZ2[0:128, 0:8]))
            p.op("vector", lambda h, ti=ti: h.max(srt[:], lgs[:, 8 * ti:8 * ti + 8]))
            p.op("vector", lambda h: h.tensor_sub(c1[:], srt[:, 1:2], srt[:, 0:1]))
            p.op("scalar", lambda h: h.activation(c1[:], c1[:], AF.Exp))
            p.op("vector", lambda h: h.tensor_scalar(c2[:], c1[:], 1.0, None, op0=ALU.add))
            p.op("vector", lambda h: h.reciprocal(c2[:], c2[:]))
            p.op("vector", lambda h: h.tensor_scalar(c3[:], c2[:], -1.0, 1.0, op0=ALU.mult, op1=ALU.add))
            p.op("vector", lambda h, ti=ti: h.tensor_scalar(
                gts[:, 8 * ti:8 * ti + 8], lgs[:, 8 * ti:8 * ti + 8], srt[:, 0:1], c2[:],
                op0=ALU.is_equal, op1=ALU.mult))
            p.op("vector", lambda h, ti=ti: h.tensor_scalar(
                tmp[:, 0:8], lgs[:, 8 * ti:8 * ti + 8], srt[:, 1:2], c3[:],
                op0=ALU.is_equal, op1=ALU.mult))
            p.op("vector", lambda h, ti=ti: h.tensor_add(
                gts[:, 8 * ti:8 * ti + 8], gts[:, 8 * ti:8 * ti + 8], tmp[:, 0:8]))
            p.op("tensor", lambda h, ti=ti: h.transpose(
                pZ2[0:8, 0:128], gts[:, 8 * ti:8 * ti + 8], idb[:]))
            p.op("vector", lambda h, ti=ti: h.tensor_copy(
                gt8b[:, 128 * ti:128 * ti + 128], pZ2[0:8, 0:128]))

    for tq in range(4):
        c0 = tq * 512
        for mo in range(8):
            p.op("gpsimd", lambda h, mo=mo: h.memset(acc[:, 512 * mo:512 * mo + 512], 0.0))
        for e in range(8):
            HH = HID // 2
            for k in range(8):
                for z in range(2):
                    p.dma("sync", lambda h, e=e, k=k, z=z: h.dma_start(
                        out=wst[:], in_=w1[e, 128 * k:128 * k + 128, z * HH:z * HH + HH]))
                    p.op("vector", lambda h, k=k, z=z, HH=HH: h.tensor_copy(
                        w1bf[:, HID * k + z * HH:HID * k + z * HH + HH], wst[:]))
                    if not first:
                        p.dma("sync", lambda h, e=e, k=k, z=z: h.dma_start(
                            out=wst[:], in_=w3[e, 128 * k:128 * k + 128, z * HH:z * HH + HH]))
                        p.op("vector", lambda h, k=k, z=z, HH=HH: h.tensor_copy(
                            w3bf[:, HID * k + z * HH:HID * k + z * HH + HH], wst[:]))
            for mh in range(NMH):
                for k in range(8):
                    md = "chain" if k == 0 else ("tail" if k == 7 else "mid")
                    p.op("tensor", lambda h, k=k, mh=mh, c0=c0: h.matmul(
                        pZ1[:, 0:512], w1bf[:, HID * k + 128 * mh:HID * k + 128 * mh + 128],
                        mbf[:, k * TH + c0:k * TH + c0 + 512], start=(k == 0), stop=(k == 7)), mode=md)
                if first:
                    p.op("scalar", lambda h, mh=mh, e=e: h.activation(
                        hebf[:, 512 * mh:512 * mh + 512], pZ1[:, 0:512],
                        AF.Gelu_apprx_tanh, bias=b1s[:, NMH * e + mh:NMH * e + mh + 1], scale=1.0))
                else:
                    for k in range(8):
                        md = "chain" if k == 0 else ("tail" if k == 7 else "mid")
                        p.op("tensor", lambda h, k=k, mh=mh, c0=c0: h.matmul(
                            pZ2[:, 0:512], w3bf[:, HID * k + 128 * mh:HID * k + 128 * mh + 128],
                            mbf[:, k * TH + c0:k * TH + c0 + 512], start=(k == 0), stop=(k == 7)), mode=md)
                    p.op("scalar", lambda h: h.activation(sabf[:], pZ1[:, 0:512], AF.Silu))
                    p.op("vector", lambda h, mh=mh: h.tensor_tensor(
                        hebf[:, 512 * mh:512 * mh + 512], sabf[:], pZ2[:, 0:512], op=ALU.mult))
            p.op("tensor", lambda h, e=e, c0=c0: h.matmul(
                pZ1[:, 0:512], selb[:, 128 * e:128 * e + 128], gt8b[:, c0:c0 + 512],
                start=True, stop=True))
            p.op("vector", lambda h: h.tensor_copy(gb[:], pZ1[:, 0:512]))
            for k in range(NMH):
                p.dma("sync", lambda h, e=e, k=k: h.dma_start(
                    out=wst[:, 0:D], in_=w2[e, 128 * k:128 * k + 128, :]))
                p.op("vector", lambda h, k=k: h.tensor_copy(w2bf[:], wst[:, 0:D]))
                for mo in range(8):
                    pz = pZ1 if mo < 4 else pZ2
                    md = "chain" if mo == 0 else ("tail" if mo == 7 else "mid")
                    p.op("tensor", lambda h, k=k, mo=mo, pz=pz: h.matmul(
                        pz[:, 512 * (mo % 4):512 * (mo % 4) + 512],
                        w2bf[:, 128 * mo:128 * mo + 128],
                        hebf[:, 512 * k:512 * k + 512], start=(k == 0), stop=(k == NMH - 1)), mode=md)
            for mo in range(8):
                pz = pZ1 if mo < 4 else pZ2
                p.op("vector", lambda h, mo=mo, pz=pz: h.tensor_mul(
                    tmp[:], pz[:, 512 * (mo % 4):512 * (mo % 4) + 512], gb[:]))
                p.op("vector", lambda h, mo=mo: h.tensor_add(
                    acc[:, 512 * mo:512 * mo + 512], acc[:, 512 * mo:512 * mo + 512], tmp[:]))
        if first:
            for mo in range(8):
                p.op("tensor", lambda h, mo=mo, c0=c0: h.matmul(
                    pZ1[:, 0:512], b2sb[:, 128 * mo:128 * mo + 128], gt8b[:, c0:c0 + 512],
                    start=True, stop=True))
                p.op("vector", lambda h, mo=mo: h.tensor_add(
                    acc[:, 512 * mo:512 * mo + 512], acc[:, 512 * mo:512 * mo + 512], pZ1[:, 0:512]))
            for mo in range(8):
                p.dma("sync", lambda h, mo=mo, c0=c0: h.dma_start(
                    out=x32[:], in_=xr[128 * mo:128 * mo + 128, c0:c0 + 512]))
                p.op("vector", lambda h, mo=mo: h.tensor_add(
                    y32[:], acc[:, 512 * mo:512 * mo + 512], x32[:]))
                p.op("scalar", lambda h: h.activation(stg[:], y32[:], AF.Silu))
                p.dma("sync", lambda h, mo=mo, c0=c0: h.dma_start(
                    out=x32[:], in_=sT[128 * mo:128 * mo + 128, c0:c0 + 512]))
                p.op("vector", lambda h: h.tensor_mul(stg[:], stg[:], x32[:]))
                p.dma("sync", lambda h, mo=mo, c0=c0: h.dma_start(
                    out=x32[:], in_=xT[128 * mo:128 * mo + 128, c0:c0 + 512]))
                p.op("vector", lambda h: h.tensor_add(y32[:], stg[:], x32[:]))
                p.dma("gpsimd", lambda h, mo=mo, c0=c0: h.dma_start(
                    out=x1o[128 * mo:128 * mo + 128, c0:c0 + 512], in_=y32[:]))
                p.op("tensor", lambda h, mo=mo: h.matmul(
                    pZ1[0:1, 0:512], onescol[:], y32[:], start=(mo == 0), stop=(mo == 7)))
                p.op("scalar", lambda h: h.activation(sq[:], y32[:], AF.Square))
                p.op("tensor", lambda h, mo=mo: h.matmul(
                    pZ1[0:1, 512:1024], onescol[:], sq[:], start=(mo == 0), stop=(mo == 7)))
            p.op("vector", lambda h: h.tensor_copy(mrow[:], pZ1[0:1, 0:512]))
            p.op("vector", lambda h: h.scalar_tensor_tensor(
                vrow[:], mrow[:], -1.0, mrow[:], op0=ALU.mult, op1=ALU.mult))
            p.op("vector", lambda h: h.tensor_add(vrow[:], vrow[:], pZ1[0:1, 512:1024]))
            p.op("scalar", lambda h: h.activation(vrow[:], vrow[:], AF.Ln, bias=eps_col[0:1, :], scale=1.0))
            p.op("scalar", lambda h: h.activation(rrow[:], vrow[:], AF.Exp, scale=-0.5))
            p.op("tensor", lambda h: h.matmul(pZ1[:, 0:512], ones_row1[:], mrow[:], start=True, stop=True))
            p.op("tensor", lambda h: h.matmul(pZ1[:, 512:1024], ones_row1[:], rrow[:], start=True, stop=True))
            for mo in range(8):
                p.dma("sync", lambda h, mo=mo, c0=c0: h.dma_start(
                    out=x32[:], in_=x1o[128 * mo:128 * mo + 128, c0:c0 + 512]))
                p.op("vector", lambda h: h.tensor_sub(stg[:], x32[:], pZ1[:, 0:512]))
                p.op("vector", lambda h: h.tensor_mul(stg[:], stg[:], pZ1[:, 512:1024]))
                p.dma("gpsimd", lambda h, mo=mo, c0=c0: h.dma_start(
                    out=xn1o[128 * mo:128 * mo + 128, c0:c0 + 512], in_=stg[:]))
        else:
            for mo in range(8):
                p.op("scalar", lambda h, mo=mo: h.activation(
                    stg[:], acc[:, 512 * mo:512 * mo + 512], AF.Silu))
                p.dma("sync", lambda h, mo=mo, c0=c0: h.dma_start(
                    out=x32[:], in_=s1[128 * mo:128 * mo + 128, c0:c0 + 512]))
                p.op("vector", lambda h: h.tensor_mul(stg[:], stg[:], x32[:]))
                p.dma("sync", lambda h, mo=mo, c0=c0: h.dma_start(
                    out=x32[:], in_=x1[128 * mo:128 * mo + 128, c0:c0 + 512]))
                p.op("vector", lambda h: h.tensor_add(stg[:], stg[:], x32[:]))
                p.dma("gpsimd", lambda h, mo=mo, c0=c0: h.dma_start(
                    out=x2o[128 * mo:128 * mo + 128, c0:c0 + 512], in_=stg[:]))
    last = p.last
    p.op("gpsimd", lambda h: h.engine_nop(), deps=[last])
    return p


def build_ssmk(name):
    p = Prog(name)
    ldt = p.din("ldt", [128, 1])
    lar = p.din("lar", [128, 32])
    aim = p.din("aim", [128, 32])
    cre = p.din("cre", [128, 32])
    cim = p.din("cim", [128, 32])
    Ko = p.dout("Ko", [128, 4096])

    PI = float(np.pi)
    dt = p.sb("dt", [128, 1])
    er = p.sb("er", [128, 32])
    ai = p.sb("ai", [128, 32])
    cr = p.sb("cr", [128, 32]); ci = p.sb("ci", [128, 32])
    dar = p.sb("dar", [128, 32])
    dai = p.sb("dai", [128, 32])
    t1 = p.sb("t1", [128, 32]); t2 = p.sb("t2", [128, 32]); t3 = p.sb("t3", [128, 32])
    sn = p.sb("sn", [128, 32]); cs = p.sb("cs", [128, 32])
    wr = p.sb("wr", [128, 32]); wi = p.sb("wi", [128, 32])
    wpr = p.sb("wpr", [128, 32]); wpi = p.sb("wpi", [128, 32])
    nwpi = p.sb("nwpi", [128, 32])
    cwr = p.sb("cwr", [128, 32]); cwi = p.sb("cwi", [128, 32]); ncwi = p.sb("ncwi", [128, 32])
    ur = p.sb("ur", [128, 2048]); ui = p.sb("ui", [128, 2048])
    vr = p.sb("vr", [128, 2048]); vi = p.sb("vi", [128, 2048]); nvi = p.sb("nvi", [128, 2048])
    pr = p.sb("pr", [128, 2048]); pi = p.sb("pi", [128, 2048])
    Ks = p.sb("Ks", [128, 4096])
    tl = p.sb("tl", [128, 64])

    for nm, dst, srcp in ((0, dt, ldt), (1, er, lar), (2, ai, aim), (3, cr, cre), (4, ci, cim)):
        p.dma("sync", lambda h, dst=dst, srcp=srcp: h.dma_start(out=dst[:], in_=srcp))
    p.op("scalar", lambda h: h.activation(dt[:], dt[:], AF.Exp))
    p.op("scalar", lambda h: h.activation(er[:], er[:], AF.Exp))
    p.op("vector", lambda h: h.tensor_scalar(dar[:], er[:], dt[:], -1.0, op0=ALU.mult, op1=ALU.mult))
    p.op("vector", lambda h: h.tensor_scalar(dai[:], ai[:], dt[:], None, op0=ALU.mult))

    ti32 = p.sb("ti32", [128, 32], mybir.dt.int32)

    def wrap_sin(dstt, add_half_pi):
        off = PI / 2 if add_half_pi else 0.0
        p.op("vector", lambda h, off=off: h.tensor_scalar(t3[:], dai[:], off, None, op0=ALU.add))
        p.op("vector", lambda h: h.tensor_scalar(t1[:], t3[:], 1.0 / (2 * PI), None, op0=ALU.mult))
        p.op("vector", lambda h: h.tensor_copy(ti32[:], t1[:]))
        p.op("vector", lambda h: h.tensor_copy(t1[:], ti32[:]))
        p.op("vector", lambda h: h.scalar_tensor_tensor(t1[:], t1[:], -2 * PI, t3[:], op0=ALU.mult, op1=ALU.add))
        p.op("vector", lambda h: h.tensor_scalar(t2[:], t1[:], PI, None, op0=ALU.is_gt))
        p.op("vector", lambda h: h.scalar_tensor_tensor(t1[:], t2[:], -2 * PI, t1[:], op0=ALU.mult, op1=ALU.add))
        p.op("vector", lambda h: h.tensor_scalar(t2[:], t1[:], -PI, None, op0=ALU.is_lt))
        p.op("vector", lambda h: h.scalar_tensor_tensor(t1[:], t2[:], 2 * PI, t1[:], op0=ALU.mult, op1=ALU.add))
        p.op("scalar", lambda h, dstt=dstt: h.activation(dstt[:], t1[:], AF.Sin))
    wrap_sin(sn, False)
    wrap_sin(cs, True)
    p.op("scalar", lambda h: h.activation(t1[:], dar[:], AF.Exp))
    p.op("vector", lambda h: h.tensor_mul(wr[:], t1[:], cs[:]))
    p.op("vector", lambda h: h.tensor_mul(wi[:], t1[:], sn[:]))
    p.op("vector", lambda h: h.tensor_scalar(t1[:], wr[:], -1.0, None, op0=ALU.add))
    p.op("vector", lambda h: h.tensor_mul(t2[:], cr[:], t1[:]))
    p.op("vector", lambda h: h.scalar_tensor_tensor(t2[:], ci[:], -1.0, t2[:], op0=ALU.mult, op1=ALU.bypass)
         if False else h.tensor_mul(t3[:], ci[:], wi[:]))
    p.op("vector", lambda h: h.tensor_sub(t2[:], t2[:], t3[:]))
    p.op("vector", lambda h: h.tensor_mul(t3[:], cr[:], wi[:]))
    p.op("vector", lambda h: h.tensor_mul(t1[:], ci[:], t1[:]))
    p.op("vector", lambda h: h.tensor_add(t3[:], t3[:], t1[:]))
    p.op("vector", lambda h: h.tensor_mul(t1[:], er[:], er[:]))
    p.op("vector", lambda h: h.scalar_tensor_tensor(t1[:], ai[:], 1.0, t1[:], op0=ALU.bypass, op1=ALU.bypass)
         if False else h.tensor_mul(cwr[:], ai[:], ai[:]))
    p.op("vector", lambda h: h.tensor_add(t1[:], t1[:], cwr[:]))
    p.op("vector", lambda h: h.reciprocal(t1[:], t1[:]))
    p.op("vector", lambda h: h.tensor_mul(cwr[:], t2[:], er[:]))
    p.op("vector", lambda h: h.tensor_mul(cwi[:], t3[:], ai[:]))
    p.op("vector", lambda h: h.tensor_sub(cwr[:], cwi[:], cwr[:]))
    p.op("vector", lambda h: h.tensor_mul(cwr[:], cwr[:], t1[:]))
    p.op("vector", lambda h: h.tensor_scalar(cwr[:], cwr[:], 2.0, None, op0=ALU.mult))
    p.op("vector", lambda h: h.tensor_mul(cwi[:], t3[:], er[:]))
    p.op("vector", lambda h: h.tensor_mul(t3[:], t2[:], ai[:]))
    p.op("vector", lambda h: h.tensor_add(cwi[:], cwi[:], t3[:]))
    p.op("vector", lambda h: h.tensor_mul(cwi[:], cwi[:], t1[:]))
    p.op("vector", lambda h: h.tensor_scalar(cwi[:], cwi[:], -2.0, None, op0=ALU.mult))
    p.op("vector", lambda h: h.tensor_scalar(ncwi[:], cwi[:], -1.0, None, op0=ALU.mult))

    p.op("vector", lambda h: h.tensor_copy(wpr[:], wr[:]))
    p.op("vector", lambda h: h.tensor_copy(wpi[:], wi[:]))
    for arr_r, arr_i in ((ur, ui), (vr, vi)):
        p.op("gpsimd", lambda h, arr_r=arr_r: h.memset(arr_r[:], 0.0))
        p.op("gpsimd", lambda h, arr_i=arr_i: h.memset(arr_i[:], 0.0))
        p.op("gpsimd", lambda h, arr_r=arr_r: h.memset(
            arr_r[:].rearrange("p (n l) -> p n l", n=32)[:, :, 0:1], 1.0))
        have = 1
        while have < 64:
            cnt = min(have, 64 - have)
            p.op("vector", lambda h: h.tensor_scalar(nwpi[:], wpi[:], -1.0, None, op0=ALU.mult))
            for n in range(32):
                b0 = 64 * n
                p.op("vector", lambda h, b0=b0, have=have, cnt=cnt, arr_r=arr_r, n=n: h.tensor_scalar(
                    tl[:, 0:cnt], arr_r[:, b0:b0 + cnt], wpr[:, n:n + 1], None, op0=ALU.mult))
                p.op("vector", lambda h, b0=b0, have=have, cnt=cnt, arr_i=arr_i, n=n: h.scalar_tensor_tensor(
                    tl[:, 0:cnt], arr_i[:, b0:b0 + cnt], nwpi[:, n:n + 1], tl[:, 0:cnt],
                    op0=ALU.mult, op1=ALU.add))
                p.op("vector", lambda h, b0=b0, have=have, cnt=cnt, arr_r=arr_r, arr_i=arr_i, n=n: h.tensor_scalar(
                    tl[:, 64 - cnt:64], arr_i[:, b0:b0 + cnt], wpr[:, n:n + 1], None, op0=ALU.mult))
                p.op("vector", lambda h, b0=b0, have=have, cnt=cnt, arr_r=arr_r, arr_i=arr_i, n=n: h.scalar_tensor_tensor(
                    arr_i[:, b0 + have:b0 + have + cnt], arr_r[:, b0:b0 + cnt], wpi[:, n:n + 1],
                    tl[:, 64 - cnt:64], op0=ALU.mult, op1=ALU.add))
                p.op("vector", lambda h, b0=b0, have=have, cnt=cnt, arr_r=arr_r, n=n: h.tensor_copy(
                    arr_r[:, b0 + have:b0 + have + cnt], tl[:, 0:cnt]))
            p.op("vector", lambda h: h.tensor_mul(t1[:], wpr[:], wpr[:]))
            p.op("vector", lambda h: h.tensor_mul(t2[:], wpi[:], wpi[:]))
            p.op("vector", lambda h: h.tensor_mul(t3[:], wpr[:], wpi[:]))
            p.op("vector", lambda h: h.tensor_sub(wpr[:], t1[:], t2[:]))
            p.op("vector", lambda h: h.tensor_scalar(wpi[:], t3[:], 2.0, None, op0=ALU.mult))
            have *= 2
    for n in range(32):
        b0 = 64 * n
        p.op("vector", lambda h, b0=b0, n=n: h.tensor_scalar(
            tl[:], ur[:, b0:b0 + 64], cwr[:, n:n + 1], None, op0=ALU.mult))
        p.op("vector", lambda h, b0=b0, n=n: h.scalar_tensor_tensor(
            pr[:, b0:b0 + 64], ui[:, b0:b0 + 64], ncwi[:, n:n + 1], tl[:],
            op0=ALU.mult, op1=ALU.add))
        p.op("vector", lambda h, b0=b0, n=n: h.tensor_scalar(
            tl[:], ur[:, b0:b0 + 64], cwi[:, n:n + 1], None, op0=ALU.mult))
        p.op("vector", lambda h, b0=b0, n=n: h.scalar_tensor_tensor(
            pi[:, b0:b0 + 64], ui[:, b0:b0 + 64], cwr[:, n:n + 1], tl[:],
            op0=ALU.mult, op1=ALU.add))
    p.op("vector", lambda h: h.tensor_scalar(nvi[:], vi[:], -1.0, None, op0=ALU.mult))
    p.op("gpsimd", lambda h: h.memset(Ks[:], 0.0))
    for n in range(32):
        b0 = 64 * n
        for l2 in range(64):
            p.op("vector", lambda h, b0=b0, l2=l2, n=n: h.scalar_tensor_tensor(
                Ks[:, 64 * l2:64 * l2 + 64], pr[:, b0:b0 + 64], vr[:, b0 + l2:b0 + l2 + 1],
                Ks[:, 64 * l2:64 * l2 + 64], op0=ALU.mult, op1=ALU.add))
            p.op("vector", lambda h, b0=b0, l2=l2, n=n: h.scalar_tensor_tensor(
                Ks[:, 64 * l2:64 * l2 + 64], pi[:, b0:b0 + 64], nvi[:, b0 + l2:b0 + l2 + 1],
                Ks[:, 64 * l2:64 * l2 + 64], op0=ALU.mult, op1=ALU.add))
    p.dma("gpsimd", lambda h: h.dma_start(out=Ko, in_=Ks[:]))
    last = p.last
    p.op("gpsimd", lambda h: h.engine_nop(), deps=[last])
    return p


NS = 640


def build_fft1(name):
    p = Prog(name)
    za = p.din("za", [64, NS * 64])
    wr = p.din("wr", [64, 128], BF16)
    wi = p.din("wi", [64, 128], BF16)
    gr = p.dout("gr", [128, NS * 64])
    gi = p.dout("gi", [128, NS * 64])
    zb = p.sb("zb", [64, 512], BF16)
    zs = p.sb("zs", [64, 512])
    wrs = p.sb("wrs", [64, 128], BF16)
    wis = p.sb("wis", [64, 128], BF16)
    ev = p.sb("ev", [128, 512])
    pA = p.ps("pA", [128, 512])
    pB = p.ps("pB", [128, 512])
    p.dma("sync", lambda h: h.dma_start(out=wrs[:], in_=wr))
    p.dma("sync", lambda h: h.dma_start(out=wis[:], in_=wi))
    NTL = NS * 64 // 512
    for t in range(NTL):
        c0 = t * 512
        p.dma("sync", lambda h, c0=c0: h.dma_start(out=zs[:], in_=za[:, c0:c0 + 512]))
        p.op("vector", lambda h: h.tensor_copy(zb[:], zs[:]))
        p.op("tensor", lambda h: h.matmul(pA[:], wrs[:], zb[:], start=True, stop=True))
        p.op("tensor", lambda h: h.matmul(pB[:], wis[:], zb[:], start=True, stop=True))
        p.op("vector", lambda h: h.tensor_copy(ev[:], pA[:]))
        p.dma("gpsimd", lambda h, c0=c0: h.dma_start(out=gr[:, c0:c0 + 512], in_=ev[:]))
        p.op("vector", lambda h: h.tensor_copy(ev[:], pB[:]))
        p.dma("gpsimd", lambda h, c0=c0: h.dma_start(out=gi[:, c0:c0 + 512], in_=ev[:]))
    last = p.last
    p.op("gpsimd", lambda h: h.engine_nop(), deps=[last])
    return p


def build_fft2(name):
    p = Prog(name)
    gt = p.din("gt", [64, 128 * 2 * NS])
    ma = p.din("ma", [64, 128 * 128], BF16)
    mb = p.din("mb", [64, 128 * 128], BF16)
    xo = p.dout("xo", [128, 128 * NS])
    gb = p.sb("gb", [64, 2 * NS], BF16)
    gs = p.sb("gs", [64, 2 * NS])
    mas = p.sb("mas", [64, 128], BF16)
    mbs = p.sb("mbs", [64, 128], BF16)
    ev = p.sb("ev", [128, 512])
    pA = p.ps("pA", [128, 512])
    for k2 in range(128):
        p.dma("sync", lambda h, k2=k2: h.dma_start(out=gs[:], in_=gt[:, 2 * NS * k2:2 * NS * k2 + 2 * NS]))
        p.op("vector", lambda h: h.tensor_copy(gb[:], gs[:]))
        p.dma("sync", lambda h, k2=k2: h.dma_start(out=mas[:], in_=ma[:, 128 * k2:128 * k2 + 128]))
        p.dma("sync", lambda h, k2=k2: h.dma_start(out=mbs[:], in_=mb[:, 128 * k2:128 * k2 + 128]))
        for nt in range((NS + 511) // 512):
            n0 = nt * 512
            nc_ = min(512, NS - n0)
            p.op("tensor", lambda h, n0=n0, nc_=nc_: h.matmul(
                pA[:, 0:nc_], mas[:], gb[:, n0:n0 + nc_], start=True, stop=False))
            p.op("tensor", lambda h, n0=n0, nc_=nc_: h.matmul(
                pA[:, 0:nc_], mbs[:], gb[:, NS + n0:NS + n0 + nc_], start=False, stop=True))
            p.op("vector", lambda h, nc_=nc_: h.tensor_copy(ev[:, 0:nc_], pA[:, 0:nc_]))
            p.dma("gpsimd", lambda h, k2=k2, n0=n0, nc_=nc_: h.dma_start(
                out=xo[:, NS * k2 + n0:NS * k2 + n0 + nc_], in_=ev[:, 0:nc_]))
    last = p.last
    p.op("gpsimd", lambda h: h.engine_nop(), deps=[last])
    return p


NSI = 512
NKC = 33


def build_inv(name):
    p = Prog(name)
    xxr = p.din("xxr", [NKC * 128, NSI])
    xxi = p.din("xxi", [NKC * 128, NSI])
    kfr = p.din("kfr", [NKC * 128, NSI])
    kfi = p.din("kfi", [NKC * 128, NSI])
    cc = p.din("cc", [NKC * 128, 4096], BF16)
    cs = p.din("cs", [NKC * 128, 4096], BF16)
    xns = p.din("xns", [4096, NSI])
    drow = p.din("drow", [1, NSI])
    yo = p.dout("yo", [4096, NSI])

    a1 = p.sb("a1", [128, NSI]); a2_ = p.sb("a2_", [128, NSI])
    b1 = p.sb("b1", [128, NSI]); b2_ = p.sb("b2_", [128, NSI])
    t1 = p.sb("t1", [128, NSI]); t2 = p.sb("t2", [128, NSI])
    pbr = p.sb("pbr", [128, NKC * NSI], BF16)
    pbi = p.sb("pbi", [128, NKC * NSI], BF16)
    ccs = p.sb("ccs", [128, 128], BF16)
    css = p.sb("css", [128, 128], BF16)
    dsb = p.sb("dsb", [1, NSI])
    dbc = p.sb("dbc", [128, NSI])
    ones_row1 = p.sb("ones_row1", [1, 128])
    ev = p.sb("ev", [128, NSI])
    pA = p.ps("pA", [128, NSI])
    pB = p.ps("pB", [128, NSI])

    p.op("gpsimd", lambda h: h.memset(ones_row1[:], 1.0))
    p.dma("sync", lambda h: h.dma_start(out=dsb[:], in_=drow))
    p.op("tensor", lambda h: h.matmul(pB[:], ones_row1[:], dsb[:], start=True, stop=True))
    p.op("vector", lambda h: h.tensor_copy(dbc[:], pB[:]))
    for ch in range(NKC):
        r0 = ch * 128
        p.dma("sync", lambda h, r0=r0: h.dma_start(out=a1[:], in_=xxr[r0:r0 + 128, :]))
        p.dma("sync", lambda h, r0=r0: h.dma_start(out=a2_[:], in_=xxi[r0:r0 + 128, :]))
        p.dma("sync", lambda h, r0=r0: h.dma_start(out=b1[:], in_=kfr[r0:r0 + 128, :]))
        p.dma("sync", lambda h, r0=r0: h.dma_start(out=b2_[:], in_=kfi[r0:r0 + 128, :]))
        p.op("vector", lambda h: h.tensor_mul(t1[:], a1[:], b1[:]))
        p.op("vector", lambda h: h.tensor_mul(t2[:], a2_[:], b2_[:]))
        p.op("vector", lambda h: h.tensor_sub(t1[:], t1[:], t2[:]))
        p.op("vector", lambda h, ch=ch: h.tensor_copy(pbr[:, NSI * ch:NSI * ch + NSI], t1[:]))
        p.op("vector", lambda h: h.tensor_mul(t1[:], a1[:], b2_[:]))
        p.op("vector", lambda h: h.tensor_mul(t2[:], a2_[:], b1[:]))
        p.op("vector", lambda h: h.tensor_add(t1[:], t1[:], t2[:]))
        p.op("vector", lambda h, ch=ch: h.tensor_copy(pbi[:, NSI * ch:NSI * ch + NSI], t1[:]))
    for mt in range(32):
        for ch in range(NKC):
            p.dma("sync", lambda h, mt=mt, ch=ch: h.dma_start(
                out=ccs[:], in_=cc[128 * ch:128 * ch + 128, 128 * mt:128 * mt + 128]))
            p.dma("sync", lambda h, mt=mt, ch=ch: h.dma_start(
                out=css[:], in_=cs[128 * ch:128 * ch + 128, 128 * mt:128 * mt + 128]))
            p.op("tensor", lambda h, ch=ch: h.matmul(
                pA[:], ccs[:], pbr[:, NSI * ch:NSI * ch + NSI], start=(ch == 0), stop=False))
            p.op("tensor", lambda h, ch=ch: h.matmul(
                pA[:], css[:], pbi[:, NSI * ch:NSI * ch + NSI],
                start=False, stop=(ch == NKC - 1)), mode="tail")
        p.dma("sync", lambda h, mt=mt: h.dma_start(out=t1[:], in_=xns[128 * mt:128 * mt + 128, :]))
        p.op("vector", lambda h: h.tensor_mul(t2[:], t1[:], dbc[:]))
        p.op("vector", lambda h: h.tensor_add(ev[:], pA[:], t2[:]))
        p.dma("gpsimd", lambda h, mt=mt: h.dma_start(out=yo[128 * mt:128 * mt + 128, :], in_=ev[:]))
    last = p.last
    p.op("gpsimd", lambda h: h.engine_nop(), deps=[last])
    return p


def _consts_attn():
    j = np.arange(T, dtype=np.float64)
    jhi, jlo = np.floor(j / 64), j % 64
    augK = np.stack([jhi, jlo, np.full(T, 64.0), np.ones(T)]).astype(np.float32)
    augQ = np.stack([np.full(T, 64.0), np.ones(T), -jhi, -jlo]).astype(np.float32)
    masks = np.stack([np.where(np.arange(128)[:, None] + d > np.arange(512)[None, :],
                               -30000.0, 0.0) for d in (0, 128, 256, 384)]).astype(np.float32)
    return _np_bf(augK), _np_bf(augQ), masks


def _consts_fft():
    a2 = np.arange(64); k2 = np.arange(128)
    W = np.exp(-2j * np.pi * np.outer(a2, k2) / 128.0)
    wr, wi = _np_bf(W.real), _np_bf(W.imag)
    a1 = np.arange(64); k1 = np.arange(64)
    tw = np.exp(-2j * np.pi * np.outer(a1, k2) / 8192.0)
    W64 = np.exp(-2j * np.pi * np.outer(a1, k1) / 64.0)
    M = np.einsum('ak,al->kal', tw, W64)
    MA = np.concatenate([M.real, M.imag], axis=2)
    MB = np.concatenate([-M.imag, M.real], axis=2)
    MA = _np_bf(MA.transpose(1, 0, 2).reshape(64, 128 * 128))
    MB = _np_bf(MB.transpose(1, 0, 2).reshape(64, 128 * 128))
    k = np.arange(4224); m = np.arange(4096)
    wk = np.where((k == 0) | (k == 4096), 1.0, 2.0) * (k <= 4096)
    th = 2.0 * np.pi * np.outer(k, m) / 8192.0
    cC = _np_bf(wk[:, None] * np.cos(th) / 8192.0)
    cS = _np_bf(-wk[:, None] * np.sin(th) / 8192.0)
    return wr, wi, MA, MB, cC, cS


_PROGS = {}

def _prog(key, builder, *a, **kw):
    if key not in _PROGS:
        p = builder(*a, **kw)
        _PROGS[key] = (p, p.build())
    return _PROGS[key]


def _run(key, in_maps):
    p, nc = _PROGS[key]
    res = run_bass_kernel_spmd(nc, in_maps, core_ids=list(range(len(in_maps))))
    return res.results


def ssm_launches(xn1_all, inp):
    wr, wi, MA, MB, cC, cS = _consts_fft()
    _prog("ssmk", build_ssmk, "ssmk")
    ldt = _np_f32(inp["L1_attn_log_dt"])[:, None]
    lar = _np_f32(inp["L1_attn_log_A_real"])
    aim = _np_f32(inp["L1_attn_A_imag"])
    Cp = _np_f32(inp["L1_attn_C"])
    ins = [dict(ldt=np.ascontiguousarray(ldt[128*c:128*c+128]),
                lar=np.ascontiguousarray(lar[128*c:128*c+128]),
                aim=np.ascontiguousarray(aim[128*c:128*c+128]),
                cre=np.ascontiguousarray(Cp[128*c:128*c+128, :, 0]),
                cim=np.ascontiguousarray(Cp[128*c:128*c+128, :, 1])) for c in range(8)]
    r = _run("ssmk", ins)
    Kfull = np.concatenate([np.asarray(r[c]["Ko"], np.float32) for c in range(8)], axis=0)
    sigs = np.concatenate([xn1_all.reshape(4 * 1024, 4096), Kfull], axis=0)
    _prog("fft1", build_fft1, "fft1")
    ins = []
    for c in range(8):
        sl = sigs[NS * c:NS * c + NS]
        za = sl.reshape(NS, 64, 64).transpose(1, 0, 2).reshape(64, NS * 64)
        ins.append(dict(za=np.ascontiguousarray(za), wr=wr, wi=wi))
    r = _run("fft1", ins)
    Gr = np.stack([np.asarray(r[c]["gr"], np.float32) for c in range(8)])
    Gi = np.stack([np.asarray(r[c]["gi"], np.float32) for c in range(8)])
    _prog("fft2", build_fft2, "fft2")
    ins = []
    for c in range(8):
        grc = Gr[c].reshape(128, NS, 64)
        gic = Gi[c].reshape(128, NS, 64)
        gt = np.stack([grc, gic], axis=1)
        gt = gt.transpose(3, 0, 1, 2).reshape(64, 128 * 2 * NS)
        ins.append(dict(gt=np.ascontiguousarray(gt), ma=MA, mb=MB))
    r = _run("fft2", ins)
    Xr = np.zeros((5120, 8192), np.float32)
    Xi = np.zeros((5120, 8192), np.float32)
    for c in range(8):
        xo = np.asarray(r[c]["xo"], np.float32).reshape(128, 128, NS)
        Xr[NS*c:NS*c+NS] = xo[0:64].transpose(2, 0, 1).reshape(NS, 8192)
        Xi[NS*c:NS*c+NS] = xo[64:128].transpose(2, 0, 1).reshape(NS, 8192)
    _prog("inv", build_inv, "inv")
    Dp = _np_f32(inp["L1_attn_D"])
    ins = []
    for c in range(8):
        ss = np.arange(NSI * c, NSI * c + NSI)
        hh = ss % 1024
        xxr = np.zeros((NKC * 128, NSI), np.float32); xxr[0:4097] = Xr[ss, 0:4097].T
        xxi = np.zeros((NKC * 128, NSI), np.float32); xxi[0:4097] = Xi[ss, 0:4097].T
        kfr = np.zeros((NKC * 128, NSI), np.float32); kfr[0:4097] = Xr[4096 + hh][:, 0:4097].T
        kfi = np.zeros((NKC * 128, NSI), np.float32); kfi[0:4097] = Xi[4096 + hh][:, 0:4097].T
        xns = sigs[ss].T.copy()
        drow = Dp[hh][None, :]
        ins.append(dict(xxr=xxr, xxi=xxi, kfr=kfr, kfi=kfi, cc=cC, cs=cS,
                        xns=np.ascontiguousarray(xns), drow=np.ascontiguousarray(drow)))
    r = _run("inv", ins)
    s1_all = np.zeros((4, 1024, 4096), np.float32)
    for c in range(8):
        y = np.asarray(r[c]["yo"], np.float32)
        ss = np.arange(NSI * c, NSI * c + NSI)
        s1_all.reshape(4096, 4096)[ss] = y.T
    return s1_all


def kernel(**inp):
    inp = {k: np.asarray(v) for k, v in inp.items()}
    x = _np_f32(inp["L0_attn_x"])
    B = x.shape[0]
    xT_all = np.ascontiguousarray(x.transpose(0, 2, 1))
    augK, augQ, masks = _consts_attn()
    ident = np.eye(128, dtype=np.float32)
    selm = np.zeros((8, 8, 128), np.float32)
    for e in range(8):
        selm[e, e, :] = 1.0
    selm = _np_bf(selm.reshape(8, 8 * 128))

    _prog("l0a", build_l0a, "l0a")
    wq = np.ascontiguousarray(_np_f32(inp["L0_attn_Wq"]).T)
    wk = np.ascontiguousarray(_np_f32(inp["L0_attn_Wk"]).T)
    wv = np.ascontiguousarray(_np_f32(inp["L0_attn_Wv"]).T)
    wo = np.ascontiguousarray(_np_f32(inp["L0_attn_Wo"]).T)
    brows = np.ascontiguousarray(np.concatenate(
        [_np_f32(inp["L0_attn_bq"]), _np_f32(inp["L0_attn_bk"]),
         _np_f32(inp["L0_attn_bv"]), _np_f32(inp["L0_attn_bo"])])[None, :])
    g1c = np.ascontiguousarray(_np_f32(inp["L0_ffn_ln1_g"]).reshape(8, 128).T)
    b1c_ = np.ascontiguousarray(_np_f32(inp["L0_ffn_ln1_b"]).reshape(8, 128).T)
    wqkv = np.ascontiguousarray(_np_f32(inp["L0_ffn_w_qkv"]))
    wo1 = np.ascontiguousarray(_np_f32(inp["L0_ffn_w_o"]))
    ins = [dict(xT=np.ascontiguousarray(xT_all[c // 2]), wq=wq, wk=wk, wv=wv, wo0=wo,
                brows=brows, augK=augK, augQ=augQ, wqkv=wqkv, wo1=wo1,
                gcol=g1c, bcol=b1c_, masks=masks) for c in range(8)]
    r = _run("l0a", ins)
    s_all = np.stack([np.asarray(r[2 * b]["sout0"], np.float32) for b in range(B)])
    xr_all = np.stack([np.asarray(r[2 * b]["sout1"], np.float32) for b in range(B)])
    xn_all = None

    _prog("moe0", build_moe, "moe0", True)
    g2c = np.ascontiguousarray(_np_f32(inp["L0_ffn_ln2_g"]).reshape(8, 128).T)
    b2c = np.ascontiguousarray(_np_f32(inp["L0_ffn_ln2_b"]).reshape(8, 128).T)
    b1cols = np.ascontiguousarray(_np_f32(inp["L0_ffn_b1"]).reshape(8, 32, 128)
                                  .transpose(2, 0, 1).reshape(128, 256))
    ins = []
    for c in range(8):
        b, hh = c // 2, c % 2
        sl = slice(TH * hh, TH * hh + TH)
        ins.append(dict(xr=np.ascontiguousarray(xr_all[b][:, sl]),
                        sT=np.ascontiguousarray(s_all[b][:, sl]),
                        xT=np.ascontiguousarray(xT_all[b][:, sl]),
                        gcol=g2c, bcol=b2c, b1c=b1cols,
                        b2m=np.ascontiguousarray(_np_f32(inp["L0_ffn_b2"])),
                        wg=np.ascontiguousarray(_np_f32(inp["L0_ffn_w_gate"])),
                        w1=np.ascontiguousarray(_np_f32(inp["L0_ffn_w1"])),
                        w2=np.ascontiguousarray(_np_f32(inp["L0_ffn_w2"])),
                        selm=selm, ident=ident))
    r = _run("moe0", ins)
    x1_all = np.stack([np.concatenate([np.asarray(r[2 * b]["x1o"], np.float32),
                                       np.asarray(r[2 * b + 1]["x1o"], np.float32)], axis=1)
                       for b in range(B)])
    xn1_all = np.stack([np.concatenate([np.asarray(r[2 * b]["xn1o"], np.float32),
                                        np.asarray(r[2 * b + 1]["xn1o"], np.float32)], axis=1)
                        for b in range(B)])

    s1_all = ssm_launches(xn1_all, inp)

    _prog("moe1", build_moe, "moe1", False)
    HID = 2816
    pad1 = np.zeros((8, 1024, HID), np.float32); pad1[:, :, :2730] = _np_f32(inp["L1_ffn_w1"])
    pad3 = np.zeros((8, 1024, HID), np.float32); pad3[:, :, :2730] = _np_f32(inp["L1_ffn_w3"])
    pad2 = np.zeros((8, HID, 1024), np.float32); pad2[:, :2730, :] = _np_f32(inp["L1_ffn_w2"])
    ins = []
    for c in range(8):
        b, hh = c // 2, c % 2
        sl = slice(TH * hh, TH * hh + TH)
        ins.append(dict(xn1=np.ascontiguousarray(xn1_all[b][:, sl]),
                        x1=np.ascontiguousarray(x1_all[b][:, sl]),
                        s1=np.ascontiguousarray(s1_all[b][:, sl]),
                        wg=np.ascontiguousarray(_np_f32(inp["L1_ffn_gate_w"])),
                        w1=pad1, w2=pad2, w3=pad3, selm=selm, ident=ident))
    r = _run("moe1", ins)
    out = np.zeros_like(x)
    for b in range(B):
        x2 = np.concatenate([np.asarray(r[2 * b]["x2o"], np.float32),
                             np.asarray(r[2 * b + 1]["x2o"], np.float32)], axis=1)
        out[b] = x2.T
    return out


def build_l0a(name):
    p = Prog(name)
    xT = p.din("xT", [D, T])
    wq = p.din("wq", [D, D]); wk = p.din("wk", [D, D])
    wv = p.din("wv", [D, D]); wo0 = p.din("wo0", [D, D])
    brows = p.din("brows", [1, 4 * D])
    augK = p.din("augK", [4, T], BF16)
    augQ = p.din("augQ", [4, T], BF16)
    wqkv = p.din("wqkv", [D, 3 * D])
    wo1 = p.din("wo1", [D, D])
    gcol = p.din("gcol", [128, 8])
    bcol = p.din("bcol", [128, 8])
    masks = p.din("masks", [4, 128, 512])
    sout0 = p.dout("sout0", [D, T])
    sout1 = p.dout("sout1", [D, T])
    xnd = p.nc.dram_tensor("xnd", [D, T], F32, kind="Internal").ap()

    xbf = p.sb("xbf", [128, 8 * T], BF16)
    wstage = p.sb("wstage", [128, TH])
    wbf = p.sb("wbf", [128, 8 * 128], BF16)
    qg = p.sb("qg", [128, 2 * T], BF16)
    kg = p.sb("kg", [128, 2 * T], BF16)
    vsb = p.sb("vsb", [128, 32 * 260], BF16)
    psb = p.sb("psb", [128, 512], BF16)
    osb = p.sb("osb", [128, 512])
    rd = p.sb("rd", [1, 512])
    onesr = p.sb("onesr", [1, 512], BF16)
    onesc = p.sb("onesc", [1, 64])
    brsb = p.sb("brsb", [1, 4 * D], BF16)
    brst = p.sb("brst", [1, 4 * D])
    maskb = p.sb("maskb", [128, 4 * 512])
    ontd = p.nc.dram_tensor("ontd", [D, T], BF16, kind="Internal").ap()
    ontb = p.sb("ontb", [128, 512], BF16)
    onrm = p.sb("onrm", [64, 512], BF16)
    aKb = p.sb("aKb", [4, T], BF16)
    aQb = p.sb("aQb", [4, T], BF16)
    stg = p.sb("stg", [128, 512])
    vrhs = p.sb("vrhs", [128, 8 * 256], BF16)
    x32 = p.sb("x32", [128, 512])
    sq = p.sb("sq", [128, 512])
    mrow = p.sb("mrow", [1, 512]); rrow = p.sb("rrow", [1, 512])
    vrow = p.sb("vrow", [1, 512])
    onescol = p.sb("onescol", [128, 1])
    eps_col = p.sb("eps_col", [128, 1])
    ones_row1 = p.sb("ones_row1", [1, 128])
    sgc = p.sb("sgc", [128, 8])
    sbc = p.sb("sbc", [128, 8])
    pA = p.ps("pA", [128, 512])
    pB = p.ps("pB", [128, 512])
    pC = p.ps("pC", [128, 512])
    pD = p.ps("pD", [128, 512])
    pS = p.ps("pS", [128, 2048])
    psb4 = p.sb("psb4", [128, 2048], BF16)

    p.op("gpsimd", lambda h: h.memset(onesr[:], 1.0))
    p.op("gpsimd", lambda h: h.memset(onesc[:], 1.0))
    p.op("gpsimd", lambda h: h.memset(onescol[:], 1.0 / 1024.0))
    p.op("gpsimd", lambda h: h.memset(eps_col[:], EPS))
    p.op("gpsimd", lambda h: h.memset(ones_row1[:], 1.0))
    p.op("gpsimd", lambda h: h.memset(
        vsb[:].rearrange("p (t h c) -> p t h c", t=32, h=4)[:, :, :, 64:65], 1.0))
    for d in range(4):
        p.dma("sync", lambda h, d=d: h.dma_start(out=maskb[:, 512 * d:512 * d + 512], in_=masks[d]))
    p.dma("sync", lambda h: h.dma_start(out=brst[:], in_=brows))
    p.op("vector", lambda h: h.tensor_copy(brsb[:], brst[:]))
    p.dma("sync", lambda h: h.dma_start(out=aKb[:], in_=augK))
    p.dma("sync", lambda h: h.dma_start(out=aQb[:], in_=augQ))
    p.dma("sync", lambda h: h.dma_start(out=sgc[:], in_=gcol))
    p.dma("sync", lambda h: h.dma_start(out=sbc[:], in_=bcol))

    for tt in range(8):
        c0 = tt * 512
        for c in range(8):
            p.dma("sync", lambda h, c=c, c0=c0: h.dma_start(
                out=x32[:], in_=xT[128 * c:128 * c + 128, c0:c0 + 512]))
            p.op("tensor", lambda h, c=c: h.matmul(
                pA[0:1, :], onescol[:], x32[:], start=(c == 0), stop=(c == 7)))
            p.op("scalar", lambda h: h.activation(sq[:], x32[:], AF.Square))
            p.op("tensor", lambda h, c=c: h.matmul(
                pB[0:1, :], onescol[:], sq[:], start=(c == 0), stop=(c == 7)))
        p.op("vector", lambda h: h.tensor_copy(mrow[:], pA[0:1, :]))
        p.op("vector", lambda h: h.scalar_tensor_tensor(
            vrow[:], mrow[:], -1.0, mrow[:], op0=ALU.mult, op1=ALU.mult))
        p.op("vector", lambda h: h.tensor_add(vrow[:], vrow[:], pB[0:1, :]))
        p.op("scalar", lambda h: h.activation(vrow[:], vrow[:], AF.Ln, bias=eps_col[0:1, :], scale=1.0))
        p.op("scalar", lambda h: h.activation(rrow[:], vrow[:], AF.Exp, scale=-0.5))
        p.op("tensor", lambda h: h.matmul(pA[:], ones_row1[:], mrow[:], start=True, stop=True))
        p.op("tensor", lambda h: h.matmul(pB[:], ones_row1[:], rrow[:], start=True, stop=True))
        for c in range(8):
            p.dma("sync", lambda h, c=c, c0=c0: h.dma_start(
                out=x32[:], in_=xT[128 * c:128 * c + 128, c0:c0 + 512]))
            p.op("vector", lambda h: h.tensor_sub(stg[:], x32[:], pA[:]))
            p.op("vector", lambda h: h.tensor_mul(stg[:], stg[:], pB[:]))
            p.dma("gpsimd", lambda h, c=c, c0=c0: h.dma_start(
                out=xnd[128 * c:128 * c + 128, c0:c0 + 512], in_=stg[:]))

    def load_lhsT(src_ap, k, mcol0):
        p.dma("sync", lambda h, src_ap=src_ap, k=k, mcol0=mcol0: h.dma_start(
            out=wstage[:, 0:128], in_=src_ap[128 * k:128 * k + 128, mcol0:mcol0 + 128]))
        p.op("vector", lambda h: h.tensor_copy(wbf[:, 128 * k:128 * k + 128], wstage[:, 0:128]))

    NKT = T // 128
    for outer in (True, False):
        xnT = xnd
        sout = sout0 if outer else sout1
        wo = wo0 if outer else wo1
        if not outer:
            for tt in range(8):
                c0 = tt * 512
                for c in range(8):
                    p.dma("sync", lambda h, c=c, c0=c0: h.dma_start(
                        out=x32[:], in_=xnT[128 * c:128 * c + 128, c0:c0 + 512]))
                    p.op("tensor", lambda h, c=c: h.matmul(
                        pC[0:1, :], onescol[:], x32[:], start=(c == 0), stop=(c == 7)))
                    p.op("scalar", lambda h: h.activation(sq[:], x32[:], AF.Square))
                    p.op("tensor", lambda h, c=c: h.matmul(
                        pD[0:1, :], onescol[:], sq[:], start=(c == 0), stop=(c == 7)))
                p.op("vector", lambda h: h.tensor_copy(mrow[:], pC[0:1, :]))
                p.op("vector", lambda h: h.scalar_tensor_tensor(
                    vrow[:], mrow[:], -1.0, mrow[:], op0=ALU.mult, op1=ALU.mult))
                p.op("vector", lambda h: h.tensor_add(vrow[:], vrow[:], pD[0:1, :]))
                p.op("scalar", lambda h: h.activation(vrow[:], vrow[:], AF.Ln, bias=eps_col[0:1, :], scale=1.0))
                p.op("scalar", lambda h: h.activation(rrow[:], vrow[:], AF.Exp, scale=-0.5))
                p.op("tensor", lambda h: h.matmul(pC[:], ones_row1[:], mrow[:], start=True, stop=True))
                p.op("tensor", lambda h: h.matmul(pD[:], ones_row1[:], rrow[:], start=True, stop=True))
                for c in range(8):
                    p.dma("sync", lambda h, c=c, c0=c0: h.dma_start(
                        out=x32[:], in_=xnT[128 * c:128 * c + 128, c0:c0 + 512]))
                    p.op("vector", lambda h: h.tensor_sub(stg[:], x32[:], pC[:]))
                    p.op("vector", lambda h: h.tensor_mul(stg[:], stg[:], pD[:]))
                    p.op("vector", lambda h, c=c, c0=c0: h.tensor_scalar(
                        xbf[:, c * T + c0:c * T + c0 + 512], stg[:],
                        sgc[:, c:c + 1], sbc[:, c:c + 1], op0=ALU.mult, op1=ALU.add))
        else:
            for c in range(8):
                for hh2 in range(2):
                    p.dma("sync", lambda h, c=c, hh2=hh2: h.dma_start(
                        out=wstage[:], in_=xnT[128 * c:128 * c + 128, TH * hh2:TH * hh2 + TH]))
                    p.op("vector", lambda h, c=c, hh2=hh2: h.tensor_copy(
                        xbf[:, c * T + TH * hh2:c * T + TH * hh2 + TH], wstage[:]))
        for grp in range(4):
            h0 = grp * 4
            r0 = h0 * HD
            for which, dst, scale in (("q", qg, 0.125), ("k", kg, 1.0)):
                if outer:
                    wsrc = wq if which == "q" else wk
                    coff = r0
                else:
                    wsrc = wqkv
                    coff = r0 if which == "q" else D + r0
                for m in range(2):
                    for k in range(8):
                        load_lhsT(wsrc, k, coff + 128 * m)
                    for n in range(8):
                        for k in range(8):
                            last_mm = (k == 7 and not outer)
                            md = "chain" if k == 0 else ("tail" if last_mm else "mid")
                            p.op("tensor", lambda h, k=k, n=n: h.matmul(
                                pA[:], wbf[:, 128 * k:128 * k + 128],
                                xbf[:, k * T + 512 * n:k * T + 512 * n + 512],
                                start=(k == 0), stop=(k == 7 and not outer)), mode=md)
                        if outer:
                            bi = 0 if which == "q" else 1
                            p.op("tensor", lambda h, m=m, bi=bi, coff=coff: h.matmul(
                                pA[:], brsb[0:1, bi * D + coff + 128 * m:bi * D + coff + 128 * m + 128],
                                onesr[:], start=False, stop=True), mode="tail")
                        p.op("scalar", lambda h, dst=dst, m=m, n=n, scale=scale: h.activation(
                            dst[:, m * T + 512 * n:m * T + 512 * n + 512], pA[:],
                            AF.Copy, scale=scale))
            if outer:
                vw = wv; vcoff = r0
            else:
                vw = wqkv; vcoff = 2 * D + r0
            for k in range(8):
                p.dma("sync", lambda h, k=k, vw=vw, vcoff=vcoff: h.dma_start(
                    out=wstage[:, 0:256], in_=vw[128 * k:128 * k + 128, vcoff:vcoff + 256]))
                p.op("vector", lambda h, k=k: h.tensor_copy(
                    vrhs[:, 256 * k:256 * k + 256], wstage[:, 0:256]))
            for kt in range(NKT):
                for k in range(8):
                    md = "chain" if k == 0 else ("tail" if (k == 7 and not outer) else "mid")
                    p.op("tensor", lambda h, k=k, kt=kt: h.matmul(
                        pA[:, 0:256], xbf[:, k * T + 128 * kt:k * T + 128 * kt + 128],
                        vrhs[:, 256 * k:256 * k + 256], start=(k == 0), stop=(k == 7 and not outer)), mode=md)
                if outer:
                    p.op("tensor", lambda h, vcoff=vcoff: h.matmul(
                        pA[:, 0:256], onesr[0:1, 0:128], brsb[0:1, 2 * D + vcoff:2 * D + vcoff + 256],
                        start=False, stop=True), mode="tail")
                for hh in range(4):
                    p.op("vector", lambda h, kt=kt, hh=hh: h.tensor_copy(
                        vsb[:, 260 * kt + 65 * hh:260 * kt + 65 * hh + 64],
                        pA[:, 64 * hh:64 * hh + 64]))
            for qt in range(8):
                q0 = qt * 512
                nkb = min(32, 4 * qt + 4)
                for hh in range(4):
                    hloc_t = hh // 2
                    hbase = 64 * (hh % 2)
                    for kbg in range(nkb // 4):
                        for j in range(4):
                            kb = 4 * kbg + j
                            md = "chain" if j == 0 else "mid"
                            lastmm = (j == 3 and not outer)
                            p.op("tensor", lambda h, hloc_t=hloc_t, hbase=hbase, kb=kb, q0=q0, outer=outer, j=j: h.matmul(
                                pS[:, 512 * j:512 * j + 512],
                                kg[hbase:hbase + 64, hloc_t * T + 128 * kb:hloc_t * T + 128 * kb + 128],
                                qg[hbase:hbase + 64, hloc_t * T + q0:hloc_t * T + q0 + 512],
                                start=True, stop=not outer),
                                mode=("tail" if lastmm else md))
                            if outer:
                                p.op("tensor", lambda h, kb=kb, q0=q0, j=j: h.matmul(
                                    pS[:, 512 * j:512 * j + 512],
                                    aKb[:, 128 * kb:128 * kb + 128], aQb[:, q0:q0 + 512],
                                    start=False, stop=True), mode=("tail" if j == 3 else "mid"))
                        if kbg == qt:
                            for j in range(4):
                                p.op("vector", lambda h, j=j: h.tensor_add(
                                    pS[:, 512 * j:512 * j + 512], pS[:, 512 * j:512 * j + 512],
                                    maskb[:, 512 * j:512 * j + 512]))
                        p.op("scalar", lambda h: h.activation(psb4[:], pS[:], AF.Exp))
                        for j in range(4):
                            kb = 4 * kbg + j
                            p.op("tensor", lambda h, kb=kb, hh=hh, nkb=nkb, j=j: h.matmul(
                                pB[0:65, :], vsb[:, 260 * kb + 65 * hh:260 * kb + 65 * hh + 65],
                                psb4[:, 512 * j:512 * j + 512], start=(kb == 0), stop=(kb == nkb - 1)),
                                mode="chain" if j == 0 else ("tail" if j == 3 else "mid"))
                    p.op("vector", lambda h: h.tensor_copy(osb[0:65, :], pB[0:65, :]))
                    p.op("vector", lambda h: h.reciprocal(rd[:], osb[64:65, :]))
                    p.op("tensor", lambda h: h.matmul(
                        pC[0:64, :], onesc[:], rd[:], start=True, stop=True))
                    p.op("vector", lambda h: h.tensor_mul(onrm[:], osb[0:64, :], pC[0:64, :]))
                    p.dma("gpsimd", lambda h, h0=h0, hh=hh, q0=q0: h.dma_start(
                        out=ontd[64 * (h0 + hh):64 * (h0 + hh) + 64, q0:q0 + 512], in_=onrm[:]))
        for m in range(8):
            for k in range(8):
                load_lhsT(wo, k, 128 * m)
            for n in range(8):
                for k in range(8):
                    p.dma("sync", lambda h, k=k, n=n: h.dma_start(
                        out=ontb[:], in_=ontd[128 * k:128 * k + 128, 512 * n:512 * n + 512]))
                    p.op("tensor", lambda h, k=k, n=n, outer=outer: h.matmul(
                        pD[:], wbf[:, 128 * k:128 * k + 128], ontb[:],
                        start=(k == 0), stop=(k == 7 and not outer)))
                if outer:
                    p.op("tensor", lambda h, m=m: h.matmul(
                        pD[:], brsb[0:1, 3 * D + 128 * m:3 * D + 128 * m + 128], onesr[:],
                        start=False, stop=True))
                    p.op("vector", lambda h: h.tensor_copy(stg[:], pD[:]))
                else:
                    p.dma("sync", lambda h, m=m, n=n: h.dma_start(
                        out=x32[:], in_=xnT[128 * m:128 * m + 128, 512 * n:512 * n + 512]))
                    p.op("vector", lambda h: h.tensor_add(stg[:], pD[:], x32[:]))
                p.dma("gpsimd", lambda h, m=m, n=n, sout=sout: h.dma_start(
                    out=sout[128 * m:128 * m + 128, 512 * n:512 * n + 512], in_=stg[:]))
    last = p.last
    p.op("gpsimd", lambda h: h.engine_nop(), deps=[last])
    return p
